# version 07
# baseline (speedup 1.0000x reference)
import contextlib
import numpy as np
import concourse.bass as bass
import concourse.mybir as mybir

F32 = mybir.dt.float32
BF16 = mybir.dt.bfloat16
AF = mybir.ActivationFunctionType
ALU = mybir.AluOpType
AX = mybir.AxisListType

ENGS = ("sync", "tensor", "vector", "scalar", "gpsimd")

B = 2
S = 2048
D = 1024
T = B * S
NT = T // 128
NH0, HD0 = 16, 64
FH, FHD = 4, 256
E, FF = 8, 4096
NDET = 256
L = 16


class Sem:
    __slots__ = ("name", "h", "n")
    def __init__(self, name):
        self.name, self.h, self.n = name, None, 0


SEM_ALIAS = {}
for _n in ("LV2", "LVd", "LV5", "LV6"):
    SEM_ALIAS[_n] = "LNV"
for _n in ("LS2", "LSd", "LS5", "LS6"):
    SEM_ALIAS[_n] = "LNS"
for _n in ("gv1", "gv2", "g1v", "t16", "scv", "s8v", "aev", "cadd", "cedone",
           "ksdone", "dzero", "dmemset", "crec", "bdev", "zev", "hy2", "x1c",
           "QE", "KE", "ELUV", "ONES", "EPS"):
    SEM_ALIAS[_n] = "VM"
for _n in ("ELUS", "gs1", "g1s", "eexp", "lnf5"):
    SEM_ALIAS[_n] = "SCM"
for _n in ("oev1", "gev", "g1ev", "umv", "runv", "rkv", "hy4", "s81", "sm1v"):
    SEM_ALIAS[_n] = "VM"
for _n in ("PMM", "wemm", "cemm", "amm", "atmm", "cntmm", "dmm", "g1mm", "gmm",
           "h1mm", "sc1mm", "smm", "rkmm", "ctxmm", "av1", "kcons", "pfmm"):
    SEM_ALIAS[_n] = "TM"


class KB:
    def __init__(self, nc):
        self.nc = nc
        self.ops = {e: [] for e in ENGS}
        self.sems = {}
        self.stack = contextlib.ExitStack()
        self.sb = {}
        self.dr = {}
        self.pyctr = {}

    def sem(self, name):
        if isinstance(name, Sem):
            return name
        name = SEM_ALIAS.get(name, name)
        if name not in self.sems:
            s = Sem(name)
            s.h = self.stack.enter_context(self.nc.semaphore(name))
            self.sems[name] = s
        return self.sems[name]

    def sbuf(self, name, shape, dtype=F32):
        t = self.stack.enter_context(self.nc.sbuf_tensor(name, list(shape), dtype))
        self.sb[name] = t
        return t

    def psum(self, name, shape, dtype=F32):
        t = self.stack.enter_context(self.nc.psum_tensor(name, list(shape), dtype))
        self.sb[name] = t
        return t

    def dram(self, name, shape, dtype=F32, kind=None):
        if kind:
            t = self.nc.dram_tensor(name, list(shape), dtype, kind=kind)
        else:
            t = self.nc.dram_tensor(name, list(shape), dtype)
        self.dr[name] = t
        return t

    def rec(self, eng, fn):
        self.ops[eng].append(fn)

    def _waits(self, eng, waits):
        for w in waits:
            if not w:
                continue
            sem, val = w
            sem = self.sem(sem)
            v = sem.n if val is None else val
            if v <= 0:
                continue
            self.rec(eng, lambda e, h=sem.h, v=v: e.wait_ge(h, v))

    def wait(self, eng, sem, val=None):
        self._waits(eng, ((sem, val),))

    def emit(self, eng, fn, waits=(), incs=(), drain=None):
        if drain is None:
            drain = eng in ("vector", "scalar", "gpsimd")
        if drain:
            self.rec(eng, lambda e: e.drain())
        self._waits(eng, waits)
        incs = [(self.sem(s), a) for (s, a) in incs]
        assert len(incs) <= 1, "hardware allows one sync update per instruction"
        for (sem, amt) in incs:
            sem.n += amt
        def g(e, fn=fn, incs=tuple(incs)):
            i = fn(e)
            for (sem, amt) in incs:
                i.then_inc(sem.h, amt)
        self.rec(eng, g)

    def tok(self, name):
        s = self.sem(name)
        return (s, s.n)

    def dma(self, eng, out, in_, sem=None, waits=(), accum=False):
        incs = ((sem, 16),) if sem is not None else ()
        if accum:
            fn = lambda e, o=out, i=in_: e.dma_start(out=o, in_=i, accum_op=ALU.add)
        else:
            fn = lambda e, o=out, i=in_: e.dma_start(out=o, in_=i)
        self.emit(eng, fn, waits=waits, incs=incs, drain=False)

    def mm(self, out, lhsT, rhs, start, stop, waits=(), incs=(), tile_position=None):
        self.emit("tensor",
                  lambda e, o=out, l=lhsT, r=rhs, st=start, sp=stop, tp=tile_position:
                      e.matmul(o, l, r, start=st, stop=sp, tile_position=tp),
                  waits=waits, incs=incs)

    def transpose(self, out, in_, waits=(), incs=()):
        ident = self.sb["ident"]
        p = in_.shape[0]
        self.emit("tensor",
                  lambda e, o=out, i=in_, idn=ident, p=p: e.transpose(o, i, idn[0:p, 0:p]),
                  waits=waits, incs=incs)

    def tponce_sel(self, ps_full, dst_ap, src_ap, sel, extra_waits=(), evac_waits=()):
        raise NotImplementedError

    def cp(self, eng, out, in_, waits=(), incs=()):
        self.emit(eng, lambda e, o=out, i=in_: e.tensor_copy(o, i), waits=waits, incs=incs)

    def tt(self, eng, out, a, b, op, waits=(), incs=()):
        self.emit(eng, lambda e, o=out, x=a, y=b, p=op: e.tensor_tensor(o, x, y, op=p),
                  waits=waits, incs=incs)

    def ts(self, eng, out, a, scalar, op, waits=(), incs=()):
        self.emit(eng, lambda e, o=out, x=a, s=scalar, p=op: e.tensor_single_scalar(o, x, s, p),
                  waits=waits, incs=incs)

    def tsc(self, eng, out, in0, scalar1, scalar2, op0, op1, waits=(), incs=()):
        self.emit(eng,
                  lambda e, o=out, a=in0, s1=scalar1, s2=scalar2, p0=op0, p1=op1:
                      e.tensor_scalar(o, a, s1, s2, p0, p1),
                  waits=waits, incs=incs)

    def stt(self, eng, out, in0, scalar, in1, op0, op1, waits=(), incs=()):
        self.emit(eng,
                  lambda e, o=out, a=in0, s=scalar, b=in1, p0=op0, p1=op1:
                      e.scalar_tensor_tensor(o, a, s, b, p0, p1),
                  waits=waits, incs=incs)

    def act(self, out, in_, func, bias=0.0, scale=1.0, accum_out=None, waits=(), incs=()):
        if accum_out is not None:
            fn = lambda e, o=out, i=in_, f=func, b=bias, s=scale, a=accum_out: \
                e.activation(out=o, in_=i, func=f, bias=b, scale=s, accum_out=a)
        else:
            fn = lambda e, o=out, i=in_, f=func, b=bias, s=scale: \
                e.activation(out=o, in_=i, func=f, bias=b, scale=s)
        self.emit("scalar", fn, waits=waits, incs=incs)

    def red(self, eng, out, in_, op, negate=False, waits=(), incs=()):
        self.emit(eng,
                  lambda e, o=out, i=in_, p=op, n=negate: e.tensor_reduce(o, i, AX.X, p, negate=n),
                  waits=waits, incs=incs)

    def memset(self, eng, ap, val, waits=(), incs=()):
        self.emit(eng, lambda e, a=ap, v=val: e.memset(a, v), waits=waits, incs=incs)

    def collective(self, kind, ins, outs, op=ALU.bypass, waits=(), incs=()):
        from concourse.aot_env import allow_ici_all_to_all
        self._waits("gpsimd", waits)
        incs = [(self.sem(s), a) for (s, a) in incs]
        for (sem, amt) in incs:
            sem.n += amt
        def g(e, k=kind, i=ins, o=outs, p=op, incs=tuple(incs)):
            with allow_ici_all_to_all():
                inst = e.collective_compute(k, p, replica_groups=[list(range(8))], ins=i, outs=o)
            for (sem, amt) in incs:
                inst.then_inc(sem.h, amt)
        self.rec("gpsimd", g)

    def ln(self, out_ap, x_ap, sq_scratch, st, eps, cols, gamma=None, beta=None,
           pre=(), incs=(), sem_v="LNV", sem_s="LNS"):
        v = "vector"
        SV, SS = self.sem(sem_v), self.sem(sem_s)
        self.red(v, st[:, 0:1], x_ap, ALU.add, waits=pre, incs=((SV, 1),))
        self.act(sq_scratch, x_ap, AF.Square, accum_out=st[:, 1:2], waits=pre, incs=((SS, 1),))
        self.ts(v, st[:, 2:3], st[:, 0:1], 1.0 / cols, ALU.mult)
        self.ts(v, st[:, 3:4], st[:, 1:2], 1.0 / cols, ALU.mult, waits=(self.tok(sem_s),))
        self.tt(v, st[:, 4:5], st[:, 2:3], st[:, 2:3], ALU.mult)
        self.tt(v, st[:, 3:4], st[:, 3:4], st[:, 4:5], ALU.subtract, incs=((SV, 1),))
        self.act(st[:, 7:8], st[:, 3:4], AF.Sqrt,
                 bias=(eps if not isinstance(eps, float) else float(eps)),
                 waits=(self.tok(sem_v),), incs=((SS, 1),))
        self.emit(v, lambda e, o=st[:, 5:6], i=st[:, 7:8]: e.reciprocal(o, i),
                  waits=(self.tok(sem_s),))
        self.tt(v, st[:, 6:7], st[:, 2:3], st[:, 5:6], ALU.mult)
        self.ts(v, st[:, 6:7], st[:, 6:7], -1.0, ALU.mult, incs=((SV, 1),))
        self.act(out_ap, x_ap, AF.Identity, bias=st[:, 6:7], scale=st[:, 5:6],
                 waits=(self.tok(sem_v),), incs=((SS, 1),))
        if gamma is not None:
            self.tt(v, out_ap, out_ap, gamma, ALU.mult, waits=(self.tok(sem_s),))
            self.tt(v, out_ap, out_ap, beta, ALU.add, incs=((SV, 1),))
            return self.tok(sem_v)
        return self.tok(sem_s)

    def build_block(self):
        with self.nc.Block() as block:
            for eng in ENGS:
                fns = self.ops[eng]
                def runner(e, fns=fns):
                    for f in fns:
                        f(e)
                getattr(block, eng)(runner)
        self.stack.close()



STAGES = ["xn", "s", "qdet", "y1", "g", "x1", "s1", "full"]


def build_core(stage="full"):
    nc = bass.Bass("TRN2", target_bir_lowering=False)
    k = KB(nc)
    sti = STAGES.index(stage)
    SM = k.sem

    xin = k.dram("x_in", [T, D], kind="ExternalInput").ap()
    wq_sl = k.dram("wq_sl", [128, 1024], kind="ExternalInput").ap()
    wk_sl = k.dram("wk_sl", [128, 1024], kind="ExternalInput").ap()
    wv_sl = k.dram("wv_sl", [128, 1024], kind="ExternalInput").ap()
    wd_sl = k.dram("wd_sl", [128, 1024], kind="ExternalInput").ap()
    ip_sl = k.dram("ip_sl", [384, 1024], kind="ExternalInput").ap()
    op_sl = k.dram("op_sl", [128, 1024], kind="ExternalInput").ap()
    w1c = k.dram("w1c", [1024, 4096], kind="ExternalInput").ap()
    w2c = k.dram("w2c", [4096, 1024], kind="ExternalInput").ap()
    b1c = k.dram("b1c", [128, 32], kind="ExternalInput").ap()
    b2_r = k.dram("b2_r", [1, 1024], kind="ExternalInput").ap()
    wg0_in = k.dram("wg0_in", [1024, 8], kind="ExternalInput").ap()
    wqkv1_sl = k.dram("wqkv1_sl", [128, 3072], kind="ExternalInput").ap()
    wout1_sl = k.dram("wout1_sl", [128, 1024], kind="ExternalInput").ap()
    winc = k.dram("winc", [1024, 4096], kind="ExternalInput").ap()
    woutc = k.dram("woutc", [4096, 1024], kind="ExternalInput").ap()
    binc = k.dram("binc", [128, 256], kind="ExternalInput").ap()
    bout_r = k.dram("bout_r", [1, 8192], kind="ExternalInput").ap()
    wg1_in = k.dram("wg1_in", [1024, 8], kind="ExternalInput").ap()
    ipb_bc = k.dram("ipb_bc", [128, 3072], kind="ExternalInput").ap()
    m1c_in = k.dram("m1c_in", [128, 16], kind="ExternalInput").ap()
    selS_in = k.dram("selS_in", [128, 16], kind="ExternalInput").ap()
    selST_in = k.dram("selST_in", [16, 128], kind="ExternalInput").ap()
    tri_in = k.dram("tri_in", [128, 128], kind="ExternalInput").ap()
    tri0_in = k.dram("tri0_in", [128, 128], kind="ExternalInput").ap()
    ident_in = k.dram("ident_in", [128, 128], kind="ExternalInput").ap()
    cvec = k.dram("cvec", [11, 128, 1024], kind="ExternalInput").ap()

    yout = k.dram("y_out", [T, D], kind="ExternalOutput").ap()

    def idram(name, shape, dtype=F32):
        return k.dram(name, shape, dtype).ap()

    ag_specs = [("wq", wq_sl), ("wk", wk_sl), ("wv", wv_sl), ("wd", wd_sl),
                ("ip", ip_sl), ("op", op_sl),
                ("wqkv1", wqkv1_sl), ("wout1", wout1_sl),
                ("win", winc), ("wout", woutc)]
    ag_full = {"wq": [1024, 1024], "wk": [1024, 1024], "wv": [1024, 1024],
               "wd": [1024, 1024], "ip": [3072, 1024], "op": [1024, 1024],
               "wqkv1": [1024, 3072], "wout1": [1024, 1024],
               "win": [8192, 4096], "wout": [32768, 1024]}
    ag = {nm: (idram("agi_" + nm, list(sl.shape)),
               idram("ago_" + nm, ag_full[nm])) for nm, sl in ag_specs}
    q_scr = idram("q_scr", [T, D])
    s_scr = idram("s_scr", [T, D])
    cc1_in = idram("cc1_in", [T, D])
    cc1_out = idram("cc1_out", [T, D])
    qkvd_scr = idram("qkvd_scr", [T, 3 * D])
    y1_scr = idram("y1_scr", [T, D])
    y1T_scr = idram("y1T_scr", [D, T])
    moe0_scr = idram("moe0_scr", [T, D])
    cc2_in = idram("cc2_in", [T, D])
    cc2_out = idram("cc2_out", [T, D])
    agi_we = idram("agi_we", [1024, 1024])
    ago_we = idram("ago_we", [8 * 1024, 1024])
    agi_ce = idram("agi_ce", [1, 1024])
    ago_ce = idram("ago_ce", [8, 1024])
    agi_cnt = idram("agi_cnt", [16, 8])
    ago_cnt = idram("ago_cnt", [128, 8])
    x1_scr = idram("x1_scr", [T, D])
    xn1_scr = idram("xn1_scr", [T, D])
    qkv1_scr = idram("qkv1_scr", [T, 3 * D])
    s1_scr = idram("s1_scr", [T, D])
    o1_scr = idram("o1_scr", [T, D])
    CPE = 1536
    tok_slot = k.dram("tok_slot", [13312, 1], mybir.dt.int32).ap()
    slots_scr = k.dram("slots_scr", [T, 2], mybir.dt.int32).ap()
    wv_scr = idram("wv_scr", [T, 2])
    ycomp_scr = idram("ycomp_scr", [8 * CPE + 128, D])

    BIGA = k.sbuf("BIGA", [128, 24576])
    BIGB = k.sbuf("BIGB", [128, 12288])
    TRANS = k.sbuf("TRANS", [128, 5376])
    CONSTS = k.sbuf("CONSTS", [128, 3072])
    SMALL = k.sbuf("SMALL", [128, 1024])
    S8 = k.sbuf("S8", [128, 512])
    GT = k.sbuf("GT", [128, 1024])
    RUN = k.sbuf("RUN", [16, 512])
    ident = k.sbuf("ident", [128, 128])
    tri = k.sbuf("tri", [128, 128])
    tri0 = k.sbuf("tri0", [128, 128])
    selS = k.sbuf("selS", [128, 16])
    selST = k.sbuf("selST", [16, 128])
    m1c = k.sbuf("m1c", [128, 16])
    wg0 = k.sbuf("wg0", [128, 64])
    wg1 = k.sbuf("wg1", [128, 64])
    binsb = k.sbuf("binsb", [128, 256])
    IDXS = k.sbuf("IDXS", [128, 16], mybir.dt.int32)
    cpe_base = k.sbuf("cpe_base", [1, 8])
    BIGAh = BIGA[:, :].bitcast(BF16)
    BIGBh = BIGB[:, :].bitcast(BF16)
    b1sb = k.sbuf("b1sb", [128, 32])
    onescol = k.sbuf("onescol", [128, 8])
    onesrow = k.sbuf("onesrow", [1, 128])
    rowbuf = k.sbuf("rowbuf", [1, 1024])
    rowbuf2 = k.sbuf("rowbuf2", [1, 1024])

    PS_A = k.psum("PS_A", [128, 1024])
    PS_B = k.psum("PS_B", [128, 1024])
    PS_C = k.psum("PS_C", [128, 512])
    PS_D = k.psum("PS_D", [128, 512])
    PS_E = k.psum("PS_E", [128, 512])
    PS_F = k.psum("PS_F", [128, 512])

    TPEV, PEV = SM("TPEV"), SM("PEV")

    def cslot(si):
        return CONSTS[:, si * 1024:(si + 1) * 1024]

    def stat8(slot):
        return SMALL[:, slot * 8:(slot + 1) * 8]

    k.dma("sync", ident[:], ident_in, sem=SM("idl"))
    if sti >= STAGES.index("qdet"):
        k.dma("sync", tri[:], tri_in, sem=SM("idl"))
        k.dma("sync", tri0[:], tri0_in, sem=SM("idl"))
        k.dma("sync", selS[:], selS_in, sem=SM("idl"))
        k.dma("sync", selST[:], selST_in, sem=SM("idl"))
        k.dma("sync", m1c[:], m1c_in, sem=SM("idl"))
        k.dma("sync", wg0[:, 0:64].rearrange("p (jc e) -> p jc e", jc=8),
              wg0_in.rearrange("(jc p) e -> p jc e", p=128), sem=SM("idl"))
        k.dma("sync", b1sb[:], b1c, sem=SM("idl"))
        k.dma("sync", rowbuf[0:1, 0:1024], b2_r, sem=SM("idl"))
    if sti >= STAGES.index("x1"):
        k.dma("sync", wg1[:, 0:64].rearrange("p (jc e) -> p jc e", jc=8),
              wg1_in.rearrange("(jc p) e -> p jc e", p=128), sem=SM("idl"))
        k.dma("sync", binsb[:], binc, sem=SM("idl"))
        for e_ in range(8):
            k.memset("vector", cpe_base[0:1, e_:e_ + 1], float(e_ * 1536),
                     incs=((("VM", 1),) if e_ == 7 else ()))
        k.memset("vector", GT[:, 0:104], 0.0, incs=(("VM", 1),))
        zt = k.tok("VM")
        k.dma("gpsimd", tok_slot.rearrange("(p a) c -> p (a c)", p=128),
              GT[:, 0:104].bitcast(mybir.dt.int32), sem=SM("tofz"), waits=(zt,))
    k.memset("vector", onescol[:], 1.0, incs=(("ONES", 1),))
    k.memset("vector", onesrow[:], 1.0, incs=(("ONES", 1),))
    eps5 = SMALL[:, 96:97]; eps12 = SMALL[:, 97:98]
    k.memset("vector", eps5, 1e-5)
    k.memset("vector", eps12, 1e-12, incs=(("EPS", 1),))

    def load_consts(slots, waits=()):
        for si, row in slots:
            k.dma("sync", cslot(si), cvec[row], sem=SM("cl"), waits=waits)
            waits = ()
        return k.tok("cl")

    import os
    if stage != "xn" or os.environ.get("KB_FORCE_AG"):
        for nm, sl in ag_specs:
            k.dma("gpsimd", ag[nm][0], sl, sem=SM("agb"))
        k.wait("gpsimd", SM("agb"), SM("agb").n)
        for nm, sl in ag_specs:
            k.collective("AllGather", [ag[nm][0].opt()], [ag[nm][1].opt()],
                         incs=(("AG_" + nm, 1),))

    def tponce(dst_ap, src_ap, extra_waits=(), evac_waits=(), sel=None):
        k.pyctr["tpar"] = k.pyctr.get("tpar", 0) + 1
        pslot = k.pyctr["tpar"] % 2
        cols = src_ap.shape[1] if hasattr(src_ap, "shape") else 128
        ps = PS_A[:, pslot * 512: pslot * 512 + cols]
        k.emit("tensor",
               lambda e, o=ps, i=src_ap, idn=ident: e.transpose(o, i, idn[0:i.shape[0], 0:i.shape[0]]),
               waits=tuple(extra_waits) + ((TPEV, TPEV.n - 1),),
               incs=(("tpmm", 1),))
        rd = ps[0:dst_ap.shape[0], :] if sel is None else sel(ps)
        k.cp("vector", dst_ap, rd,
             waits=(k.tok("tpmm"),) + tuple(evac_waits),
             incs=((TPEV, 1),))

    def build_WT(base, ago_ap, first_waits, tag, rows=1024, dstbuf=None):
        ld = SM("wtl")
        dstbuf = BIGA if dstbuf is None else dstbuf
        sbufs = (TRANS[:, 2304:3328], TRANS[:, 3328:4352])
        import os as _os
        dbuf = bool(_os.environ.get("KB_WT_DBUF"))
        for ic in range(rows // 128):
            w = list(first_waits) if ic == 0 else []
            if dbuf:
                w.append((TPEV, TPEV.n - 8))
            else:
                w.append((TPEV, TPEV.n))
            stage_buf = sbufs[ic % 2 if dbuf else 0]
            k.dma("sync", stage_buf, ago_ap[ic * 128:(ic + 1) * 128, :], sem=ld, waits=tuple(w))
            tk = k.tok("wtl")
            for jc in range(8):
                o0 = base + jc * rows + ic * 128
                dst = dstbuf[:, o0:o0 + 128]
                tponce(dst, stage_buf[:, jc * 128:(jc + 1) * 128],
                       extra_waits=(tk,) if jc == 0 else ())
        return k.tok("TPEV")

    if sti >= STAGES.index("qdet"):
        idl_tok = k.tok("idl")
        for qq in range(4):
            t_w1 = build_WT(0, w1c[:, qq * 1024:(qq + 1) * 1024],
                            ((k.tok("wemm"),) if qq > 0 else (k.tok("idl"),)),
                            "W1Q%d" % qq, rows=1024, dstbuf=BIGB)
            for fc in range(8):
                k.dma("sync", BIGA[:, fc * 1024:(fc + 1) * 1024],
                      w2c[qq * 1024 + fc * 128: qq * 1024 + (fc + 1) * 128, :],
                      sem=SM("w2l"), waits=((k.tok("wemm"),) if qq > 0 and fc == 0 else ()))
            t_w2 = k.tok("w2l")
            for fc in range(8):
                for dh in range(2):
                    ceps = PS_C if dh == 0 else PS_E
                    w = [t_w2, idl_tok] if fc == 0 and dh == 0 else []
                    k.mm(ceps[0:1, 0:512], b1sb[:, qq * 8 + fc: qq * 8 + fc + 1],
                         BIGA[:, fc * 1024 + dh * 512: fc * 1024 + dh * 512 + 512],
                         start=(qq == 0 and fc == 0), stop=(qq == 3 and fc == 7),
                         waits=tuple(w),
                         incs=((("cemm", 1),) if (qq == 3 and fc == 7 and dh == 1) else ()))
            for jc in range(8):
                for dh in range(2):
                    ps = PS_B[:, (dh % 2) * 512:(dh % 2) * 512 + 512]
                    k.wait("tensor", PEV, PEV.n - 1)
                    for fc in range(8):
                        w = [t_w1, t_w2] if (jc == 0 and dh == 0 and fc == 0) else []
                        k.mm(ps, BIGB[:, fc * 1024 + jc * 128: fc * 1024 + jc * 128 + 128],
                             BIGA[:, fc * 1024 + dh * 512: fc * 1024 + dh * 512 + 512],
                             start=(fc == 0), stop=(fc == 7), waits=tuple(w),
                             incs=((("wemm", 1),) if fc == 7 else ()))
                    dst = BIGA[:, 8192 + jc * 1024 + dh * 512: 8192 + jc * 1024 + dh * 512 + 512]
                    if qq == 0:
                        k.cp("vector", dst, ps, waits=(k.tok("wemm"),), incs=((PEV, 1),))
                    else:
                        k.tt("vector", dst, dst, ps, ALU.add, waits=(k.tok("wemm"),),
                             incs=((PEV, 1),))
        k.cp("vector", rowbuf2[0:1, 0:512], PS_C[0:1, 0:512], waits=(k.tok("cemm"),))
        k.cp("vector", rowbuf2[0:1, 512:1024], PS_E[0:1, 0:512])
        k.tt("vector", rowbuf2[0:1, 0:1024], rowbuf2[0:1, 0:1024],
             rowbuf[0:1, 0:1024], ALU.add, incs=(("cedone", 1),))
        k.dma("gpsimd", agi_ce, rowbuf2[0:1, 0:1024], sem=SM("weout"),
              waits=(k.tok("cedone"),))
        for jc in range(8):
            k.dma("gpsimd", agi_we[jc * 128:(jc + 1) * 128, :],
                  BIGA[:, 8192 + jc * 1024: 8192 + (jc + 1) * 1024],
                  sem=SM("weout"), waits=((k.tok("PEV"),) if jc == 0 else ()))
        k.wait("gpsimd", SM("weout"), SM("weout").n)
        k.collective("AllGather", [agi_we.opt()], [ago_we.opt()], incs=(("AG_we", 1),))
        k.collective("AllGather", [agi_ce.opt()], [ago_ce.opt()], incs=(("AG_ce", 1),))
        tok_p2 = (k.tok("wemm"), k.tok("weout"), k.tok("cedone"), k.tok("PEV"))
    else:
        tok_p2 = ()

    if stage != "xn" or os.environ.get("KB_FORCE_WT"):
        tok_wq = build_WT(0, ag["wq"][1], (k.tok("AG_wq"), k.tok("idl")) + tuple(tok_p2), "TQ")
        tok_wk = build_WT(8192, ag["wk"][1], (k.tok("AG_wk"),), "TK")
        tok_wv = build_WT(16384, ag["wv"][1], (k.tok("AG_wv"),), "TV")
    else:
        tok_wq = tok_wk = tok_wv = None

    def WT_mv(wbase, jc, i0, iw, istride=1024, buf=None):
        buf = BIGA if buf is None else buf
        return buf[:, wbase + jc * istride + i0: wbase + jc * istride + i0 + iw]

    def proj_tm(xnT, wbase, nout, evac_fn, first_waits=(), istride=1024, wbuf=None):
        for h in range(nout // 512):
            ps = PS_B[:, (h % 2) * 512:(h % 2) * 512 + 512]
            k.wait("tensor", PEV, PEV.n - 1)
            for jc in range(8):
                w = list(first_waits) if (h == 0 and jc == 0) else []
                k.mm(ps, xnT[:, jc * 128:(jc + 1) * 128],
                     WT_mv(wbase, jc, h * 512, 512, istride, wbuf),
                     start=(jc == 0), stop=(jc == 7), waits=tuple(w),
                     incs=((("PMM", 1),) if jc == 7 else ()))
            evac_fn(h, ps, (k.tok("PMM"),))

    def elu_p1(dst, src, scratch, waits=(), incs=()):
        k.ts("vector", scratch, src, 0.0, ALU.min, waits=waits, incs=(("ELUV", 1),))
        k.act(scratch, scratch, AF.Exp, waits=(k.tok("ELUV"),), incs=(("ELUS", 1),))
        k.ts("vector", dst, src, 0.0, ALU.max, waits=(k.tok("ELUS"),))
        k.tt("vector", dst, dst, scratch, ALU.add, incs=incs)

    xb = BIGB[:, 0:1024]
    xnb = BIGB[:, 1024:2048]
    xnTb = BIGB[:, 2048:3072]
    qb = BIGB[:, 3072:4096]
    kbf = BIGB[:, 4096:5120]
    vbf = BIGB[:, 5120:6144]
    sqb = BIGB[:, 6144:7168]
    elub = BIGB[:, 7168:8192]
    kvev = TRANS[:, 0:1024]
    ksev = TRANS[:, 1024:1056]

    cl1 = load_consts([(0, 2), (1, 3), (2, 4)])

    prev = {}
    for ti in range(NT):
        b_loc, s0 = ti // 16, (ti % 16) * 128
        k.dma("sync", xb, xin[ti * 128:(ti + 1) * 128, :], sem=SM("xld"),
              waits=(prev.get("xn_done"),) + (tuple(tok_p2) if ti == 0 else ()))
        pre = [(SM("xld"), SM("xld").n)]
        if ti >= 1:
            pre += [(SM("ccw"), 16 * ti)] + ([prev["xnT_mm"]] if "xnT_mm" in prev else [])
            if stage == "xn":
                pre += [(SM("yw"), 16 * ti)]
        tok_xn = k.ln(xnb, xb, sqb, stat8(0), eps5, D,
                      pre=tuple(pre) + ((k.tok("EPS"),) if ti == 0 else ()))
        prev["xn_done"] = tok_xn
        j = s0 // 256
        r0 = j * 512 + b_loc * 256 + (s0 % 256)
        k.dma("gpsimd", cc1_in[r0:r0 + 128, :], xnb, sem=SM("ccw"), waits=(tok_xn,))
        if stage == "xn":
            k.dma("gpsimd", yout[ti * 128:(ti + 1) * 128, :], xnb, sem=SM("yw"),
                  waits=(tok_xn,))
            continue
        for c in range(8):
            tponce(xnTb[:, c * 128:(c + 1) * 128], xnb[:, c * 128:(c + 1) * 128],
                   extra_waits=(tok_xn,) if c == 0 else (),
                   evac_waits=(prev["proj_mm"],) if (c == 0 and ti >= 1) else ())
        prev["xnT_mm"] = k.tok("tpmm")
        tok_xnT = k.tok("TPEV")

        def mkev(buf, slot, extra):
            def ev(h, ps, w):
                k.tt("vector", buf[:, h * 512:(h + 1) * 512], ps,
                     cslot(slot)[:, h * 512:(h + 1) * 512], ALU.add,
                     waits=tuple(list(w) + (extra if h == 0 else [])),
                     incs=((PEV, 1),))
            return ev

        fw = [tok_xnT] + ([cl1, tok_wq, tok_wk, tok_wv] if ti == 0 else [])
        proj_tm(xnTb, 0, 1024, mkev(qb, 0, [(SM("qsp"), 16 * ti)] if ti >= 1 else []),
                first_waits=fw)
        tok_qev = k.tok("PEV")
        proj_tm(xnTb, 8192, 1024, mkev(kbf, 1, [prev["kcons"]] if ti >= 1 else []))
        proj_tm(xnTb, 16384, 1024, mkev(vbf, 2, [prev["kcons"]] if ti >= 1 else []))
        tok_vev = k.tok("PEV")
        prev["proj_mm"] = k.tok("PMM")
        elu_p1(qb, qb, elub, incs=(("QE", 1),))
        elu_p1(kbf, kbf, elub, incs=(("KE", 1),))
        k.dma("gpsimd", q_scr[ti * 128:(ti + 1) * 128, :], qb, sem=SM("qsp"),
              waits=(k.tok("QE"),))
        row_start = (ti % 16 == 0)
        par = ti % 2
        kvps = PS_C if par == 0 else PS_E
        for h in range(NH0):
            pp, cc_ = 64 * (h % 2), 64 * (h // 2)
            w = []
            if h == 0:
                w = [k.tok("KE"), tok_vev]
                if ti <= 1:
                    w.append(k.tok("cemm"))
                if ti >= 2:
                    w.append((SM("kvadd"), 17 * (ti - 1)))
            k.mm(kvps[pp:pp + 64, cc_:cc_ + 64],
                 kbf[:, h * 64:(h + 1) * 64], vbf[:, h * 64:(h + 1) * 64],
                 start=True, stop=True, waits=tuple(w),
                 incs=(("kcons", 1),))
        ksps = PS_D if par == 0 else PS_F
        for c in range(8):
            k.mm(ksps[:, c: c + 1],
                 kbf[:, c * 128:(c + 1) * 128], onescol[:, 0:1],
                 start=True, stop=True,
                 waits=((k.tok("ONES"),) if (ti == 0 and c == 0) else ()),
                 incs=(("kcons", 1),))
        prev["kcons"] = k.tok("kcons")
        tok_kvmm = k.tok("kcons")
        for h in range(NH0):
            pp, cc_ = 64 * (h % 2), 64 * (h // 2)
            dst = kvev[pp:pp + 64, b_loc * 512 + cc_: b_loc * 512 + cc_ + 64]
            srcp = kvps[pp:pp + 64, cc_:cc_ + 64]
            if row_start:
                k.cp("vector", dst, srcp, waits=(tok_kvmm,) if h == 0 else (),
                     incs=(("kvadd", 1),))
            else:
                k.tt("vector", dst, dst, srcp, ALU.add,
                     waits=(tok_kvmm,) if h == 0 else (), incs=(("kvadd", 1),))
        dst = ksev[:, b_loc * 8: b_loc * 8 + 8]
        srcp = ksps[:, 0:8]
        if row_start:
            k.cp("vector", dst, srcp, incs=(("kvadd", 1),))
        else:
            k.tt("vector", dst, dst, srcp, ALU.add, incs=(("kvadd", 1),))

    if stage == "xn":
        k.wait("gpsimd", SM("yw"), SM("yw").n)
        k.build_block()
        return nc

    k.ts("vector", ksev[:, 0:16], ksev[:, 0:16], 1e-6, ALU.add,
         waits=(k.tok("kvadd"),), incs=(("ksdone", 1),))

    if os.environ.get("KB_P1ONLY"):
        for ti in range(NT):
            k.dma("gpsimd", yout[ti * 128:(ti + 1) * 128, :], kvev[:, 0:1024],
                  sem=SM("yw"), waits=(k.tok("ksdone"), (SM("qsp"), 16 * NT)))
        k.wait("gpsimd", SM("yw"), SM("yw").n)
        k.build_block()
        return nc

    tok_wd = build_WT(0, ag["wd"][1], (k.tok("AG_wd"), k.tok("PEV")), "TD")
    cl2 = load_consts([(0, 5), (1, 0), (2, 1)], waits=(k.tok("PEV"),))

    qb2 = BIGB[:, 0:1024]
    qT2 = BIGB[:, 1024:2048]
    xn2 = BIGB[:, 2048:3072]
    ctxb = BIGB[:, 3072:4096]
    ctxT = BIGB[:, 4096:5120]
    hsb = BIGB[:, 5120:6144]
    sb_o = BIGB[:, 6144:7168]
    sq2 = BIGB[:, 7168:8192]
    zb = SMALL[:, 64:80]

    prev = {}
    for ti in range(NT):
        b_loc, s0 = ti // 16, (ti % 16) * 128
        j = s0 // 256
        r0 = j * 512 + b_loc * 256 + (s0 % 256)
        p1w = []
        if ti == 0:
            p1w = [(SM("qsp"), 16 * NT), k.tok("LNS"), k.tok("ccw"), k.tok("PMM"),
                   k.tok("kcons"), k.tok("ELUS")]
        k.dma("sync", qb2, q_scr[ti * 128:(ti + 1) * 128, :], sem=SM("q2l"),
              waits=tuple(p1w) if ti == 0 else (prev["qT_mm"],))
        k.dma("sync", xn2, cc1_in[r0:r0 + 128, :], sem=SM("xn2l"),
              waits=tuple(p1w) if ti == 0 else ((SM("hsx"), ti),))
        tq = (SM("q2l"), SM("q2l").n)
        for c in range(8):
            ev_w = ()
            if c == 0:
                ev_w = tuple(p1w) if ti == 0 else (prev["ctx_mm"],)
            tponce(qT2[:, c * 128:(c + 1) * 128], qb2[:, c * 128:(c + 1) * 128],
                   extra_waits=(tq,) if c == 0 else (),
                   evac_waits=ev_w)
        prev["qT_mm"] = k.tok("tpmm")
        tok_qT = k.tok("TPEV")
        for h in range(NH0):
            pb = 64 * (h % 2)
            w = []
            if h == 0:
                w = [tok_qT]
                w.append((SM("ksdone"), 1) if ti == 0 else (SM("zev"), SM("zev").n))
            dps = PS_D if h % 2 == 0 else PS_F
            k.mm(dps[:, h // 2: h // 2 + 1],
                 qT2[pb:pb + 64, (h // 2) * 128:(h // 2) * 128 + 128],
                 ksev[pb:pb + 64, b_loc * 8 + h // 2: b_loc * 8 + h // 2 + 1],
                 start=True, stop=True, waits=tuple(w),
                 incs=((("dmm", 1),) if h == NH0 - 1 else ()))
        k.emit("vector", lambda e, o=zb[:, 0:8], i=PS_D[:, 0:8]: e.reciprocal(o, i),
               waits=(k.tok("dmm"),))
        k.emit("vector", lambda e, o=zb[:, 8:16], i=PS_F[:, 0:8]: e.reciprocal(o, i),
               incs=(("zev", 1),))
        for h in range(NH0):
            pb = 64 * (h % 2)
            w = []
            if h == 0:
                w = [(PEV, PEV.n)]
                if ti == 0:
                    w.append(k.tok("kvadd"))
            pcol = (h % 2) * 512 + (h // 2) * 64
            k.mm(PS_B[:, pcol:pcol + 64],
                 qT2[pb:pb + 64, (h // 2) * 128:(h // 2) * 128 + 128],
                 kvev[pb:pb + 64, b_loc * 512 + (h // 2) * 64: b_loc * 512 + (h // 2) * 64 + 64],
                 start=True, stop=True,
                 waits=tuple(w), incs=(("ctxmm", 1),))
        prev["ctx_mm"] = k.tok("ctxmm")
        for half in range(2):
            wv_ = [(SM("ctxmm"), SM("ctxmm").n), k.tok("zev")]
            if ti == 0 and half == 0:
                wv_ += [(SM("qsp"), 16 * NT), k.tok("kcons")]
            for hh in range(8):
                h = half + 2 * hh
                pcol = half * 512 + hh * 64
                k.ts("vector", ctxb[:, h * 64:(h + 1) * 64], PS_B[:, pcol:pcol + 64],
                     zb[:, half * 8 + hh: half * 8 + hh + 1], ALU.mult,
                     waits=tuple(wv_) if hh == 0 else (),
                     incs=((PEV, 1),) if hh == 7 else ())
        tok_ctxev = k.tok("PEV")
        for c in range(8):
            ev_w = ()
            if c == 0:
                ev_w = (prev["oproj_mm"],) if ti >= 1 else (k.tok("PMM"),)
            tponce(ctxT[:, c * 128:(c + 1) * 128], ctxb[:, c * 128:(c + 1) * 128],
                   extra_waits=(tok_ctxev,) if c == 0 else (), evac_waits=ev_w)

        def evhs(h, ps, w):
            w = list(w) + ([cl2, k.tok("kcons")] if ti == 0 else []) + \
                ([(SM("LS2"), SM("LS2").n)] if ti >= 1 else [])
            k.tt("vector", hsb[:, h * 512:(h + 1) * 512], ps,
                 cslot(0)[:, h * 512:(h + 1) * 512], ALU.add, waits=tuple(w),
                 incs=((PEV, 1),))
            k.tt("vector", hsb[:, h * 512:(h + 1) * 512], hsb[:, h * 512:(h + 1) * 512],
                 xn2[:, h * 512:(h + 1) * 512], ALU.add,
                 waits=((SM("xn2l"), SM("xn2l").n),),
                 incs=((("hsx", 1),) if h == 1 else ()))
        proj_tm(ctxT, 0, 1024, evhs,
                first_waits=(k.tok("TPEV"),) + ((tok_wd,) if ti == 0 else ()))
        prev["oproj_mm"] = k.tok("PMM")
        pre_ln = [k.tok("hsx")]
        if ti >= 1:
            pre_ln.append(prev["s_w"])
        tok_s = k.ln(sb_o, hsb, sq2, stat8(1), eps12, D, gamma=cslot(1), beta=cslot(2),
                     pre=tuple(pre_ln), sem_v="LV2", sem_s="LS2")
        k.dma("gpsimd", s_scr[ti * 128:(ti + 1) * 128, :], sb_o, sem=SM("sw"),
              waits=(tok_s,))
        if stage == "s":
            k.dma("gpsimd", yout[ti * 128:(ti + 1) * 128, :], sb_o, sem=SM("sw"))
        prev["s_w"] = k.tok("sw")

    if stage == "s":
        k.wait("gpsimd", SM("sw"), SM("sw").n)
        k.build_block()
        return nc

    k.collective("AllToAll", [cc1_in.opt()], [cc1_out.opt()],
                 waits=((SM("ccw"), 16 * NT),), incs=(("A2A1", 1),))

    def ccdet_ap(buf, dt):
        return buf.rearrange("(i lp n) f -> n i lp f", i=8, lp=2, n=256)[8 * dt:8 * dt + 8]

    tok_ip = build_WT(0, ag["ip"][1], (k.tok("AG_ip"), prev["oproj_mm"], k.tok("PEV")),
                      "TIP", rows=3072)
    w_c = (k.tok("LS2"), k.tok("PEV"))
    for si in range(3):
        k.dma("sync", cslot(si), ipb_bc[:, si * 1024:(si + 1) * 1024], sem=SM("cld"),
              waits=w_c if si == 0 else ())
    cld1 = k.tok("cld")

    xd = BIGB[:, 0:1024]
    xdT = BIGB[:, 1024:2048]
    qkvsb = BIGB[:, 2048:5120]
    prevd = {}
    for dt in range(NT):
        w = [(SM("A2A1"), 1)] if dt == 0 else [prevd["xdT_mm"]]
        if dt == 0:
            w += [k.tok("LS2"), k.tok("sw"), k.tok("ctxmm"), k.tok("PEV"), k.tok("tpmm")]
        k.dma("sync", xd, ccdet_ap(cc1_out, dt), sem=SM("xdl"), waits=tuple(w))
        for c in range(8):
            tponce(xdT[:, c * 128:(c + 1) * 128], xd[:, c * 128:(c + 1) * 128],
                   extra_waits=((SM("xdl"), SM("xdl").n),) if c == 0 else (),
                   evac_waits=(prevd["qkv_mm"],) if (c == 0 and dt >= 1) else ())
        prevd["xdT_mm"] = k.tok("tpmm")
        tok_xdT = k.tok("TPEV")

        def evd(h, ps, w2):
            w2 = list(w2) + ([cld1] if dt == 0 else []) + \
                 ([prevd["qdsp_t"]] if dt >= 1 else [])
            cs = cslot(h // 2)[:, (h % 2) * 512:(h % 2) * 512 + 512]
            k.tt("vector", qkvsb[:, h * 512:(h + 1) * 512], ps, cs, ALU.add,
                 waits=tuple(w2), incs=((PEV, 1),))
        proj_tm(xdT, 0, 3072, evd, istride=3072,
                first_waits=(tok_xdT,) + ((tok_ip,) if dt == 0 else ()))
        prevd["qkv_mm"] = k.tok("PMM")
        tok_ev = k.tok("PEV")
        k.dma("gpsimd", qkvd_scr[dt * 128:(dt + 1) * 128, :], qkvsb, sem=SM("qdsp"),
              waits=(tok_ev,))
        if stage == "qdet":
            k.dma("gpsimd", yout[dt * 128:(dt + 1) * 128, :], qkvsb[:, 0:1024],
                  sem=SM("qdsp"))
        prevd["qdsp_t"] = k.tok("qdsp")
    if stage == "qdet":
        k.wait("gpsimd", SM("qdsp"), SM("qdsp").n)
        k.build_block()
        return nc

    tok_op = build_WT(0, ag["op"][1], (k.tok("AG_op"), prevd["qkv_mm"], k.tok("PEV")), "TOP")
    for si, row in [(0, 10), (1, 6), (2, 7)]:
        k.dma("sync", cslot(si), cvec[row], sem=SM("cld"),
              waits=((k.tok("PEV"),) if si == 0 else ()))
    cld2 = k.tok("cld")
    qd = BIGB[:, 0:1024]
    kd = BIGB[:, 1024:2048]
    vd = BIGB[:, 2048:3072]
    qdT = BIGB[:, 3072:4096]
    kdT = BIGB[:, 4096:5120]
    o_sb = BIGB[:, 5120:7168]
    oT = BIGB[:, 7168:9216]
    mha = BIGB[:, 9216:10240]
    y1b = BIGB[:, 10240:11264]
    y1Tb = BIGB[:, 11264:12288]
    BD = [TRANS[:, 0:128], TRANS[:, 128:256]]
    vpad = [TRANS[:, 256:1280], TRANS[:, 1280:2304]]
    xd2 = TRANS[:, 3328:4352]
    sqd = TRANS[:, 4352:5376]
    E_sb = SMALL[:, 104:120]
    cb = SMALL[:, 120:128]
    k.memset("vector", BD[0][:], 0.0,
             waits=(k.tok("qdsp"), k.tok("kvadd"), k.tok("ctxmm"), k.tok("LS2"),
                    k.tok("dmm")))
    k.memset("vector", BD[1][:], 0.0)
    k.memset("vector", vpad[0][:], 0.0)
    k.memset("vector", vpad[1][:], 0.0, incs=(("dzero", 1),))

    for dt in range(NT):
        w0 = [prevd["qdsp_t"], tok_op] if dt == 0 else [prevd["qkT_mm"]]
        k.dma("sync", qd, qkvd_scr[dt * 128:(dt + 1) * 128, 0:1024], sem=SM("d2l"),
              waits=tuple(w0))
        k.dma("sync", kd, qkvd_scr[dt * 128:(dt + 1) * 128, 1024:2048], sem=SM("d2l"))
        k.dma("sync", vd, qkvd_scr[dt * 128:(dt + 1) * 128, 2048:3072], sem=SM("d2l"),
              waits=((SM("vrp"), 128 * dt) if dt >= 1 else None,))
        k.dma("sync", xd2, ccdet_ap(cc1_out, dt), sem=SM("xd2l"),
              waits=((SM("mhax"), 2 * dt) if dt >= 1 else None,))
        tql = (SM("d2l"), SM("d2l").n - 16)
        tvl = (SM("d2l"), SM("d2l").n)
        for c in range(8):
            tponce(qdT[:, c * 128:(c + 1) * 128], qd[:, c * 128:(c + 1) * 128],
                   extra_waits=((SM("d2l"), SM("d2l").n - 32),) if c == 0 else (),
                   evac_waits=(prevd["sc_mm"],) if (c == 0 and dt >= 1) else ())
            tponce(kdT[:, c * 128:(c + 1) * 128], kd[:, c * 128:(c + 1) * 128],
                   extra_waits=(tql,) if c == 0 else ())
        prevd["qkT_mm"] = k.tok("tpmm")
        tok_qkT = k.tok("TPEV")
        for g2 in range(2):
            for a in range(4):
                nn = 4 * g2 + a
                k.dma("gpsimd", vpad[g2][32 * a:32 * a + 16, :],
                      vd[16 * nn:16 * nn + 16, :], sem=SM("vrp"),
                      waits=(tvl, (SM("avmm"), 8 * dt), k.tok("dzero")) if (g2 == 0 and a == 0) else ())
        tok_vrp = k.tok("vrp")
        for g2 in range(2):
            for h in range(FH):
                it = dt * 8 + g2 * 4 + h
                for a in range(4):
                    nn = 4 * g2 + a
                    for kc in range(2):
                        w2 = []
                        if a == 0 and kc == 0:
                            w2 = [tok_qkT]
                            if it >= 1:
                                w2.append(prevd["eexp"])
                        k.mm(PS_D[32 * a:32 * a + 16, 0:16],
                             qdT[:, (2 * h + kc) * 128 + nn * 16: (2 * h + kc) * 128 + nn * 16 + 16],
                             kdT[:, (2 * h + kc) * 128 + nn * 16: (2 * h + kc) * 128 + nn * 16 + 16],
                             start=(kc == 0), stop=(kc == 1), tile_position=(0, 32 * a),
                             waits=tuple(w2), incs=((("smm", 1),) if (a == 3 and kc == 1) else ()))
                tok_smm = k.tok("smm")
                k.memset("vector", E_sb[:], 1.0,
                         waits=(prevd["atmm"],) if it >= 1 else (),
                         incs=(("dmemset", 1),))
                for a in range(4):
                    k.act(E_sb[32 * a:32 * a + 16, :], PS_D[32 * a:32 * a + 16, 0:16],
                          AF.Exp, scale=1.0 / 16.0,
                          waits=(tok_smm, k.tok("dmemset")) if a == 0 else (),
                          incs=((("eexp", 1),) if a == 3 else ()))
                prevd["eexp"] = k.tok("eexp")
                k.red("vector", cb[:, 4:5], E_sb[:], ALU.add, waits=(prevd["eexp"],))
                k.emit("vector", lambda e, o=cb[:, h:h + 1], i=cb[:, 4:5]: e.reciprocal(o, i),
                       incs=(("crec", 1),))
                for a in range(4):
                    w2 = []
                    if a == 0:
                        w2 = [prevd["eexp"]]
                        if it >= 1:
                            w2.append(prevd["bdev"])
                    k.mm(PS_E[32 * a:32 * a + 16, 32 * a:32 * a + 16],
                         E_sb[:, 0:16], ident[:, 32 * a:32 * a + 16],
                         start=True, stop=True, tile_position=(0, 32 * a), waits=tuple(w2),
                         incs=((("atmm", 1),) if a == 3 else ()))
                prevd["atmm"] = k.tok("atmm")
                bdp = BD[it % 2]
                for a in range(4):
                    k.cp("vector", bdp[32 * a:32 * a + 16, 32 * a:32 * a + 16],
                         PS_E[32 * a:32 * a + 16, 32 * a:32 * a + 16],
                         waits=(prevd["atmm"],) if a == 0 else (),
                         incs=((("bdev", 1),) if a == 3 else ()))
                prevd["bdev"] = k.tok("bdev")
                w2 = [prevd["bdev"]]
                if g2 == 0 and h == 0:
                    w2.append(tok_vrp)
                    w2.append((PEV, PEV.n))
                if h == 0 and g2 == 1:
                    w2.append((PEV, PEV.n))
                k.mm(PS_B[:, h * 256:(h + 1) * 256], bdp[:, 0:128],
                     vpad[g2][:, h * 256:(h + 1) * 256],
                     start=True, stop=True, waits=tuple(w2), incs=(("avmm", 1),))
            tok_av = k.tok("avmm")
            for h in range(FH):
                w4 = ()
                if h == 0:
                    w4 = (tok_av, k.tok("crec"))
                    if g2 == 0 and dt >= 1:
                        w4 = w4 + (prevd["oT_mm"],)
                k.ts("vector", o_sb[:, g2 * 1024 + h * 256: g2 * 1024 + (h + 1) * 256],
                     PS_B[:, h * 256:(h + 1) * 256], cb[:, h:h + 1], ALU.mult,
                     waits=w4, incs=((PEV, 1),) if h == 3 else ())
        tok_oev = k.tok("PEV")
        for g2 in range(2):
            for c in range(8):
                dst = oT[:, c * 128 + g2 * 64: c * 128 + g2 * 64 + 64].rearrange(
                    "p (a l) -> p a l", a=4)
                tponce(dst,
                       o_sb[:, g2 * 1024 + c * 128: g2 * 1024 + (c + 1) * 128],
                       extra_waits=(tok_oev,) if (g2 == 0 and c == 0) else (),
                       evac_waits=(prevd["op_mm"],) if (g2 == 0 and c == 0 and dt >= 1) else (),
                       sel=lambda ps: ps.rearrange("p (a r) -> p a r", a=4)[:, :, 0:16])
        prevd["oT_mm"] = k.tok("tpmm")
        tok_oT = k.tok("TPEV")
        for h in range(2):
            ps = PS_B[:, (h % 2) * 512:(h % 2) * 512 + 512]
            k.wait("tensor", PEV, PEV.n - 1)
            for jc in range(8):
                w2 = [tok_oT] + ([tok_op, cld2] if dt == 0 else []) if (h == 0 and jc == 0) else []
                k.mm(ps, oT[:, jc * 128:(jc + 1) * 128], WT_mv(0, jc, h * 512, 512),
                     start=(jc == 0), stop=(jc == 7), waits=tuple(w2),
                     incs=((("PMM", 1),) if jc == 7 else ()))
            w3 = [k.tok("PMM")] + ([(SM("LSd"), SM("LSd").n)] if dt >= 1 else [])
            k.tt("vector", mha[:, h * 512:(h + 1) * 512], ps,
                 cslot(0)[:, h * 512:(h + 1) * 512], ALU.add, waits=tuple(w3),
                 incs=((PEV, 1),))
            k.tt("vector", mha[:, h * 512:(h + 1) * 512], mha[:, h * 512:(h + 1) * 512],
                 xd2[:, h * 512:(h + 1) * 512], ALU.add,
                 waits=((SM("xd2l"), SM("xd2l").n),), incs=(("mhax", 1),))
        prevd["op_mm"] = k.tok("PMM")
        prevd["sc_mm"] = k.tok("smm")
        pre_ln = [k.tok("mhax")] + ([prevd["y1_w"], prevd["y1T_mm"]] if dt >= 1 else [])
        tok_y1 = k.ln(y1b, mha, sqd, stat8(2), eps5, D, gamma=cslot(1), beta=cslot(2),
                      pre=tuple(pre_ln), sem_v="LVd", sem_s="LSd")
        k.dma("gpsimd", y1_scr[dt * 128:(dt + 1) * 128, :], y1b, sem=SM("y1w"),
              waits=(tok_y1,))
        if stage == "y1":
            k.dma("gpsimd", yout[dt * 128:(dt + 1) * 128, :], y1b, sem=SM("y1w"))
        prevd["y1_w"] = k.tok("y1w")
        for c in range(8):
            ev_w = ()
            if c == 0 and dt >= 1:
                ev_w = (prevd["gt_mm"], (SM("y1tw"), 16 * dt))
            tponce(y1Tb[:, c * 128:(c + 1) * 128], y1b[:, c * 128:(c + 1) * 128],
                   extra_waits=(tok_y1,) if c == 0 else (),
                   evac_waits=ev_w)
        prevd["y1T_mm"] = k.tok("tpmm")
        tok_y1T = k.tok("TPEV")
        k.dma("gpsimd", y1T_scr.rearrange("(c p) t -> p c t", c=8)[:, :, dt * 128:(dt + 1) * 128],
              y1Tb, sem=SM("y1tw"), waits=(tok_y1T,))
        for jc in range(8):
            w2 = []
            if jc == 0:
                w2 = [tok_y1T, (SM("idl"), SM("idl").n)]
                if dt >= 1:
                    w2.append((SM("gev"), SM("gev").n))
            k.mm(PS_F[:, 0:8], y1Tb[:, jc * 128:(jc + 1) * 128], wg0[:, jc * 8:(jc + 1) * 8],
                 start=(jc == 0), stop=(jc == 7), waits=tuple(w2),
                 incs=((("gmm", 1),) if jc == 7 else ()))
        gl = SMALL[:, 128:136]
        nmx = SMALL[:, 136:137]
        ge = SMALL[:, 144:152]
        den = SMALL[:, 152:153]
        rcp = SMALL[:, 153:154]
        emx = SMALL[:, 154:155]
        mask_sl = GT[:, dt * 24: dt * 24 + 8]
        pref_sl = GT[:, dt * 24 + 8: dt * 24 + 16]
        gv_sl = GT[:, dt * 24 + 16: dt * 24 + 17]
        prevd["gt_mm"] = k.tok("gmm")
        k.cp("vector", gl[:], PS_F[:, 0:8], waits=(k.tok("gmm"),))
        k.red("vector", nmx[:], gl[:], ALU.max, negate=True, incs=(("gv1", 1),))
        k.act(ge[:], gl[:], AF.Exp, bias=nmx[:], waits=(k.tok("gv1"),), incs=(("gs1", 1),))
        k.red("vector", den[:], ge[:], ALU.add, waits=(k.tok("gs1"),))
        k.emit("vector", lambda e, o=rcp[:], i=den[:]: e.reciprocal(o, i))
        k.red("vector", emx[:], ge[:], ALU.max, incs=(("gv2", 1),))
        k.tt("vector", gv_sl, emx[:], rcp[:], ALU.mult, waits=(k.tok("gv2"),))
        k.ts("vector", mask_sl, ge[:], emx[:], ALU.is_equal,
             waits=(((SM("tofz"), SM("tofz").n),) if dt == 0 else ()),
             incs=(("gev", 1),))
        for (reg, lhs, inc) in [((0, 16, 8, 16), selS[:], "cntmm"), ((0, 128, 16, 24), tri[:], "cntmm")]:
            p0, p1, c0, c1 = reg
            k.mm(PS_F[p0:p1, c0:c1], lhs, mask_sl, start=True, stop=True,
                 waits=((k.tok("gev"),) if c0 == 8 else ()) +
                       (((SM("cadd"), SM("cadd").n),) if (c0 == 8 and dt >= 1) else ()),
                 incs=((inc, 1),))
        tok_cnt = k.tok("cntmm")
        cntacc = RUN[:, 280:288]
        if dt == 0:
            k.memset("vector", RUN[0:16, 280:288], 0.0)
        k.cp("vector", RUN[0:16, dt * 8:(dt + 1) * 8], RUN[0:16, 280:288],
             waits=(tok_cnt,))
        k.tt("vector", RUN[0:16, 280:288], RUN[0:16, 280:288], PS_F[0:16, 8:16], ALU.add)
        k.cp("vector", pref_sl, PS_F[:, 16:24], incs=(("cadd", 1),))

    k.dma("gpsimd", agi_cnt, RUN[0:16, 280:288], sem=SM("cntw"), waits=(k.tok("cadd"),))
    k.collective("AllGather", [agi_cnt.opt()], [ago_cnt.opt()],
                 waits=((SM("cntw"), SM("cntw").n),), incs=(("A2Ac", 1),))
    agc = SMALL[:, 160:168]
    k.dma("sync", agc, ago_cnt, sem=SM("agcl"), waits=((SM("A2Ac"), 1),))
    k.mm(PS_F[0:16, 24:32], m1c[:], agc[:], start=True, stop=True,
         waits=((SM("agcl"), SM("agcl").n), k.tok("cadd")), incs=(("amm", 1),))
    A_sb = RUN[0:16, 288:296]
    k.cp("vector", A_sb, PS_F[0:16, 24:32], waits=(k.tok("amm"),), incs=(("aev", 1),))
    tmp16 = RUN[0:16, 296:304]
    rank = SMALL[:, 168:176]
    tmp8 = SMALL[:, 176:184]
    ro = SMALL[:, 184:185]
    kp = SMALL[:, 185:186]
    sc = SMALL[:, 186:187]
    for dt in range(NT):
        mask_sl = GT[:, dt * 24: dt * 24 + 8]
        pref_sl = GT[:, dt * 24 + 8: dt * 24 + 16]
        gv_sl = GT[:, dt * 24 + 16: dt * 24 + 17]
        k.tt("vector", tmp16, A_sb, RUN[0:16, dt * 8:(dt + 1) * 8], ALU.add,
             waits=(k.tok("aev"),) if dt == 0 else ((SM("rkmm"), SM("rkmm").n),),
             incs=(("t16", 1),))
        k.mm(PS_F[:, 32:40], selST[:], tmp16, start=True, stop=True,
             waits=(k.tok("t16"),), incs=(("rkmm", 1),))
        k.stt("vector", rank, PS_F[:, 32:40], 1.0, pref_sl, ALU.mult, ALU.add,
              waits=(k.tok("rkmm"),))
        k.tt("vector", tmp8, rank, mask_sl, ALU.mult)
        k.red("vector", ro[:], tmp8, ALU.add)
        k.ts("vector", kp[:], ro[:], 4096.0, ALU.is_lt)
        k.tt("vector", sc[:], kp[:], gv_sl, ALU.mult, incs=(("scv", 1),))
        k.ts("vector", S8[:, dt * 8:(dt + 1) * 8], mask_sl, sc[:], ALU.mult,
             incs=(("s8v", 1),))


    ytmp = [TRANS[:, 256:1280], TRANS[:, 1280:2304]]
    y1t_b = [TRANS[:, 3328:4352], TRANS[:, 4352:5376]]
    for e in range(E):
        for jc in range(8):
            w2 = [(SM("AG_we"), 1)] if (e == 0 and jc == 0) else []
            if e == 0 and jc == 0:
                w2 += [k.tok("avmm"), k.tok("PMM"), k.tok("bdev"), (SM("vrp"), SM("vrp").n)]
            if e >= 1 and jc == 0:
                w2.append(moe_prev_mm)
            k.dma("sync", BIGB[:, jc * 1024:(jc + 1) * 1024],
                  ago_we[e * 1024 + jc * 128: e * 1024 + (jc + 1) * 128, :],
                  sem=SM("wel"), waits=tuple(w2))
        k.dma("sync", rowbuf2[0:1, 0:1024], ago_ce[e:e + 1, :], sem=SM("cel"),
              waits=((SM("AG_ce"), 1),) if e == 0 else (k.tok("moemm"),))
        tok_wel = k.tok("wel")
        tok_cel = k.tok("cel")
        ROT4 = (PS_B[:, 0:512], PS_B[:, 512:1024], PS_C[:, 0:512], PS_E[:, 0:512])
        for dt in range(NT):
            par = dt % 2
            idx = e * NT + dt
            k.dma("sync", y1t_b[par], y1T_scr.rearrange("(c p) t -> p c t", c=8)
                  [:, :, dt * 128:(dt + 1) * 128], sem=SM("y1tl"),
                  waits=((SM("y1tw"), 16 * NT) if (e == 0 and dt == 0) else None,
                         (SM("moemm"), 2 * (idx - 1)) if idx >= 2 else None))
            tok_yl = k.tok("y1tl")
            for dh in range(2):
                ps = ROT4[(dt * 2 + dh) % 4]
                k.wait("tensor", PEV, PEV.n - 3)
                for jc in range(8):
                    w2 = []
                    if jc == 0:
                        w2 = [tok_yl]
                        if dt == 0:
                            w2.append(tok_wel)

                    k.mm(ps, y1t_b[par][:, jc * 128:(jc + 1) * 128],
                         BIGB[:, jc * 1024 + dh * 512: jc * 1024 + dh * 512 + 512],
                         start=(jc == 0), stop=False, waits=tuple(w2))
                k.mm(ps, onesrow[0:1, :], rowbuf2[0:1, dh * 512: dh * 512 + 512],
                     start=False, stop=True,
                     waits=(tok_cel,) if (dt == 0 and dh == 0) else (),
                     incs=(("moemm", 1),))
                k.ts("vector", ytmp[par][:, dh * 512:(dh + 1) * 512], ps,
                     S8[:, dt * 8 + e: dt * 8 + e + 1], ALU.mult,
                     waits=(k.tok("moemm"), k.tok("s8v") if (e == 0 and dt == 0) else None,
                            (SM("macc"), SM("macc").n - 16) if (e * NT + dt >= 2) else None),
                     incs=((PEV, 1),) + (() if dh == 0 else ()))
            k.dma("gpsimd", moe0_scr[dt * 128:(dt + 1) * 128, :], ytmp[par],
                  sem=SM("macc"), waits=(k.tok("PEV"),), accum=(e > 0))
        moe_prev_mm = k.tok("moemm")
    for si, row in [(0, 8), (1, 9)]:
        k.dma("sync", cslot(si), cvec[row], sem=SM("cld"),
              waits=((SM("LSd"), SM("LSd").n), k.tok("PEV")) if si == 0 else ())
    cld3 = k.tok("cld")
    d5a = BIGB[:, 0:1024]
    d5b = BIGB[:, 1024:2048]
    d5y = BIGB[:, 2048:3072]
    d5g = BIGB[:, 3072:4096]
    d5sq = BIGB[:, 4096:5120]
    for dt in range(NT):
        w2 = [(SM("macc"), SM("macc").n)] if dt == 0 else [(SM("lnf5"), SM("lnf5").n)]
        if dt == 0:
            w2 += [k.tok("moemm"), k.tok("PEV")]
        k.dma("sync", d5a, y1_scr[dt * 128:(dt + 1) * 128, :], sem=SM("d5l"), waits=tuple(w2))
        k.dma("sync", d5b, moe0_scr[dt * 128:(dt + 1) * 128, :], sem=SM("d5l"))
        k.tt("vector", d5y, d5a, d5b, ALU.add, waits=((SM("d5l"), SM("d5l").n),),
             incs=(("hy2", 1),))
        pre_ln = [k.tok("hy2")] + ([cld3] if dt == 0 else []) + \
                 ([(SM("ccb"), 16 * dt * (2 if stage == "g" else 1))] if dt >= 1 else [])
        tok_y2 = k.ln(d5y, d5y, d5sq, stat8(3), eps5, D, gamma=cslot(0), beta=cslot(1),
                      pre=tuple(pre_ln), sem_v="LV5", sem_s="LS5")
        k.act(d5g, d5y, AF.Silu, waits=(tok_y2,), incs=(("lnf5", 1),))
        k.dma("gpsimd", ccdet_ap(cc2_in, dt), d5g, sem=SM("ccb"),
              waits=(k.tok("lnf5"),))
        if stage == "g":
            k.dma("gpsimd", yout[dt * 128:(dt + 1) * 128, :], d5g, sem=SM("ccb"))
    if stage in ("y1", "g"):
        k.wait("gpsimd", SM("y1w" if stage == "y1" else "ccb"),
               SM("y1w" if stage == "y1" else "ccb").n)
        k.build_block()
        return nc

    k.collective("AllToAll", [cc2_in.opt()], [cc2_out.opt()],
                 waits=((SM("ccb"), SM("ccb").n),), incs=(("A2A2", 1),))

    for jc in range(8):
        k.dma("sync", BIGA[:, jc * 3072:(jc + 1) * 3072],
              ag["wqkv1"][1][jc * 128:(jc + 1) * 128, :], sem=SM("wq1l"),
              waits=((k.tok("AG_wqkv1"), k.tok("moemm"), k.tok("PMM"), k.tok("wemm"))
                     if jc == 0 else ()))
    tok_wq1 = k.tok("wq1l")
    c1x = BIGB[:, 0:1024]
    c1s = BIGB[:, 1024:2048]
    c1g = BIGB[:, 2048:3072]
    c1x1 = BIGB[:, 3072:4096]
    c1xn = BIGB[:, 4096:5120]
    c1xnT = BIGB[:, 5120:6144]
    c1qkv = BIGB[:, 6144:9216]
    c1sq = BIGB[:, 9216:10240]
    prev1 = {}
    for ti in range(NT):
        b_loc, s0 = ti // 16, (ti % 16) * 128
        i_blk = s0 // 256
        r0 = i_blk * 512 + b_loc * 256 + (s0 % 256)
        w0 = [(SM("A2A2"), 1)] if ti == 0 else [prev1["x1cmb"]]
        if ti == 0:
            w0 += [k.tok("lnf5"), k.tok("LS5"), k.tok("moemm"), k.tok("PEV"),
                   k.tok("hy2"), (SM("d5l"), SM("d5l").n)]
        k.dma("sync", c1x, xin[ti * 128:(ti + 1) * 128, :], sem=SM("c1l"), waits=tuple(w0))
        k.dma("sync", c1s, s_scr[ti * 128:(ti + 1) * 128, :], sem=SM("c1l"))
        k.dma("sync", c1g, cc2_out[r0:r0 + 128, :], sem=SM("c1l"))
        tok_l = k.tok("c1l")
        w1_ = [tok_l] + ([prev1["x1w"], prev1["xn1_tok"]] if ti >= 1 else [])
        k.tt("vector", c1s, c1s, c1g, ALU.mult, waits=tuple(w1_))
        k.tt("vector", c1x1, c1x, c1s, ALU.add, incs=(("x1c", 1),))
        prev1["x1cmb"] = k.tok("x1c")
        k.dma("gpsimd", x1_scr[ti * 128:(ti + 1) * 128, :], c1x1, sem=SM("x1w"),
              waits=(k.tok("x1c"),))
        if stage == "x1":
            if os.environ.get("KB_X1DUMP") == "cc2raw":
                k.dma("gpsimd", yout[ti * 128:(ti + 1) * 128, :],
                      cc2_in[ti * 128:(ti + 1) * 128, :], sem=SM("x1w"),
                      waits=((SM("ccb"), 16 * NT) if ti == 0 else None,))
            elif os.environ.get("KB_X1DUMP") == "cc2oraw":
                k.dma("gpsimd", yout[ti * 128:(ti + 1) * 128, :],
                      cc2_out[ti * 128:(ti + 1) * 128, :], sem=SM("x1w"),
                      waits=(((SM("A2A2"), 1)) if ti == 0 else None,))
            elif os.environ.get("KB_X1DUMP"):
                dumpsrc = {"g": c1g, "s": c1s, "x": c1x}.get(os.environ.get("KB_X1DUMP", ""), c1x1)
                k.dma("gpsimd", yout[ti * 128:(ti + 1) * 128, :], dumpsrc, sem=SM("x1w"))
        prev1["x1w"] = k.tok("x1w")
        pre = [k.tok("x1c")] + ([prev1["xnT_mm"]] if ti >= 1 else [])
        tok_xn1 = k.ln(c1xn, c1x1, c1sq, stat8(4), eps5, D, pre=tuple(pre),
                       sem_v="LV6", sem_s="LS6")
        prev1["xn1_tok"] = tok_xn1
        k.dma("gpsimd", xn1_scr[ti * 128:(ti + 1) * 128, :], c1xn, sem=SM("xnTw"),
              waits=(tok_xn1,))
        for c in range(8):
            tponce(c1xnT[:, c * 128:(c + 1) * 128], c1xn[:, c * 128:(c + 1) * 128],
                   extra_waits=(tok_xn1,) if c == 0 else (),
                   evac_waits=(prev1["qkv_mm"], (SM("xnTw"), 16 * ti)) if (c == 0 and ti >= 1) else ())
        prev1["xnT_mm"] = k.tok("tpmm")
        tok_xnT = k.tok("TPEV")
        def evq1(h, ps, w2):
            w2 = list(w2) + ([prev1["q1sp"]] if ti >= 1 else [])
            k.cp("vector", c1qkv[:, h * 512:(h + 1) * 512], ps, waits=tuple(w2),
                 incs=((PEV, 1),))
        proj_tm(c1xnT, 0, 3072, evq1, istride=3072,
                first_waits=(tok_xnT,) + ((tok_wq1,) if ti == 0 else ()))
        prev1["qkv_mm"] = k.tok("PMM")
        k.dma("gpsimd", qkv1_scr[ti * 128:(ti + 1) * 128, :], c1qkv, sem=SM("q1sp"),
              waits=(k.tok("PEV"),))
        prev1["q1sp"] = k.tok("q1sp")
        for jc in range(8):
            w2 = []
            if jc == 0:
                w2 = [tok_xnT, (SM("idl"), SM("idl").n)]
                if ti >= 1:
                    w2.append((SM("g1ev"), SM("g1ev").n))
            k.mm(PS_F[:, 0:8], c1xnT[:, jc * 128:(jc + 1) * 128], wg1[:, jc * 8:(jc + 1) * 8],
                 start=(jc == 0), stop=(jc == 7), waits=tuple(w2),
                 incs=((("g1mm", 1),) if jc == 7 else ()))
        gl = SMALL[:, 192:200]
        mm1 = SMALL[:, 200:201]
        mm2 = SMALL[:, 201:202]
        mlg = SMALL[:, 208:216]
        w12 = SMALL[:, 216:218]
        msk1 = SMALL[:, 224:232]
        msk2 = SMALL[:, 232:240]
        k.cp("vector", gl[:], PS_F[:, 0:8], waits=(k.tok("g1mm"),), incs=(("g1ev", 1),))
        k.red("vector", mm1[:], gl[:], ALU.max)
        k.ts("vector", msk1[:], gl[:], mm1[:], ALU.is_equal, incs=(("g1v", 1),))
        k.stt("vector", mlg[:], msk1[:], -1e30, gl[:], ALU.mult, ALU.add)
        k.red("vector", mm2[:], mlg[:], ALU.max)
        k.ts("vector", msk2[:], mlg[:], mm2[:], ALU.is_equal)
        k.tt("vector", w12[:, 0:1], mm1[:], mm2[:], ALU.subtract,
             waits=(((SM("slw"), SM("slw").n),) if ti >= 1 else ()),
             incs=(("g1v", 1),))
        k.act(w12[:, 0:1], w12[:, 0:1], AF.Sigmoid, waits=(k.tok("g1v"),),
              incs=(("g1s", 1),))
        k.ts("vector", w12[:, 1:2], w12[:, 0:1], -1.0, ALU.mult,
             waits=(k.tok("g1s"),))
        k.ts("vector", w12[:, 1:2], w12[:, 1:2], 1.0, ALU.add, incs=(("VM", 1),))
        um8 = SMALL[:, 256:264]
        k.tt("vector", um8[:], msk1[:], msk2[:], ALU.add,
             waits=((k.tok("pfmm"),) if ti >= 1 else ()), incs=(("umv", 1),))
        k.dma("gpsimd", wv_scr[ti * 128:(ti + 1) * 128, :], w12[:, 0:2], sem=SM("slw"),
              waits=(k.tok("g1s"), (SM("VM"), SM("VM").n)))
        k.mm(PS_D[:, 0:8], tri0[:], um8[:], start=True, stop=True,
             waits=(k.tok("umv"),) + ((k.tok("rkv"),) if ti >= 1 else ()),
             incs=(("pfmm", 1),))
        k.mm(PS_D[0:1, 8:16], onescol[:, 0:1], um8[:], start=True, stop=True,
             incs=(("pfmm", 1),))
        run8 = SMALL[:, 264:272]
        if ti == 0:
            k.memset("vector", run8[0:1, :], 0.0, incs=(("runv", 1),))
        k.tt("vector", SMALL[0:1, 272:280], run8[0:1, :], cpe_base[0:1, :], ALU.add,
             waits=((SM("runv"), SM("runv").n),) + ((k.tok("pfmm"),) if ti >= 1 else ()),
             incs=(("runv", 1),))
        k.mm(PS_E[:, 0:8], onesrow[0:1, :], SMALL[0:1, 272:280], start=True, stop=True,
             waits=(k.tok("runv"),), incs=(("pfmm", 1),))
        k.tt("vector", run8[0:1, :], run8[0:1, :], PS_D[0:1, 8:16], ALU.add,
             waits=(k.tok("pfmm"),), incs=(("runv", 1),))
        rank8 = SMALL[:, 280:288]
        k.cp("vector", rank8[:], PS_D[:, 0:8], waits=(k.tok("pfmm"),))
        k.tt("vector", rank8[:], rank8[:], PS_E[:, 0:8], ALU.add)
        tmp8b = SMALL[:, 288:296]
        slotf = SMALL[:, 296:298]
        for kk_, mk in ((0, msk1), (1, msk2)):
            k.tt("vector", tmp8b[:], rank8[:], mk[:], ALU.mult)
            k.red("vector", slotf[:, kk_:kk_ + 1], tmp8b[:], ALU.add)
        k.ts("vector", slotf[:, 0:2], slotf[:, 0:2], float(8 * 1536), ALU.min,
             incs=(("rkv", 1),))
        k.cp("vector", IDXS[:, 0:2], slotf[:, 0:2],
             waits=(k.tok("rkv"),) + (((SM("sctw"), SM("sctw").n),) if ti >= 1 else ()),
             incs=(("rkv", 1),))
        k.dma("gpsimd", slots_scr[ti * 128:(ti + 1) * 128, :], IDXS[:, 0:2],
              sem=SM("slw"), waits=(k.tok("rkv"),))
        k.emit("gpsimd", lambda e, o=IDXS[:, 2:3], b=ti * 128:
               e.iota(o, pattern=[[0, 1]], base=b, channel_multiplier=1),
               waits=((SM("tofz"), SM("tofz").n), (SM("sctw"), SM("sctw").n)),
               incs=(("sctw", 16),))
        from concourse.bass_core_rust import IndirectOffsetOnAxis as _IOA
        for kk_ in range(2):
            k.emit("gpsimd", lambda e, ix=IDXS[:, kk_:kk_ + 1], tk=IDXS[:, 2:3]:
                   e.indirect_dma_start(tok_slot[:, :], _IOA(ix, 0), tk, None),
                   waits=((SM("sctw"), SM("sctw").n), k.tok("rkv")),
                   incs=(("sctw", 16),), drain=False)
        k.ts("vector", msk1[:], msk1[:], w12[:, 0:1], ALU.mult,
             waits=((SM("slw"), SM("slw").n),))
        k.ts("vector", msk2[:], msk2[:], w12[:, 1:2], ALU.mult)
        k.tt("vector", S8[:, 256 + ti * 8: 256 + (ti + 1) * 8], msk1[:], msk2[:],
             ALU.add, incs=(("s81", 1),))
    if stage == "x1":
        if not os.environ.get("KB_X1DUMP"):
            for ti in range(NT):
                k.dma("gpsimd", yout[ti * 128:(ti + 1) * 128, :],
                      x1_scr[ti * 128:(ti + 1) * 128, :], sem=SM("yw"),
                      waits=((SM("x1w"), SM("x1w").n) if ti == 0 else None,))
            k.wait("gpsimd", SM("yw"), SM("yw").n)
        k.wait("gpsimd", SM("x1w"), SM("x1w").n)
        k.build_block()
        return nc

    def vtile(kt):
        if kt < 8:
            return BIGA[:, 16384 + kt * 1024: 16384 + (kt + 1) * 1024]
        return BIGB[:, (kt - 8) * 1024: (kt - 7) * 1024]

    qtb = BIGB[:, 8192:9216]
    qTb = BIGB[:, 9216:10240]
    probs = BIGB[:, 10240:12288]
    pT = [GT[:, 0:128], GT[:, 128:256]]
    o1sb = TRANS[:, 0:1024]
    rr1 = SMALL[:, 240:241]
    mx4 = SMALL[:, 244:248]
    nmb = SMALL[:, 248:249]
    prev2 = {}
    for r in range(B):
        for kt in range(16):
            tt_g = [(SM("q1sp"), 16 * NT)] if (r == 0 and kt == 0) else []
            if r == 1 and kt == 0:
                tt_g = [prev2["sc1_mm"], k.tok("PMM")]
            k.dma("sync", TRANS[:, 1024:2048],
                  qkv1_scr[r * 2048 + kt * 128: r * 2048 + (kt + 1) * 128, 1024:2048],
                  sem=SM("ktl"), waits=tuple(tt_g) + ((TPEV, TPEV.n),))
            tk = k.tok("ktl")
            for c in range(8):
                tponce(BIGA[:, c * 2048 + kt * 128: c * 2048 + (kt + 1) * 128],
                       TRANS[:, 1024 + c * 128: 1024 + (c + 1) * 128],
                       extra_waits=(tk,) if c == 0 else ())
        tok_kT = k.tok("TPEV")
        for kt in range(16):
            w2 = []
            if kt == 0:
                w2 = [tok_kT] + ([prev2["av_mm"]] if r == 1 else [k.tok("wq1l"), k.tok("g1mm")])
            k.dma("sync", vtile(kt),
                  qkv1_scr[r * 2048 + kt * 128: r * 2048 + (kt + 1) * 128, 2048:3072],
                  sem=SM("v1l"), waits=tuple(w2))
        tok_v1 = k.tok("v1l")
        for qt in range(16):
            ti_g = r * 16 + qt
            k.dma("sync", qtb, qkv1_scr[r * 2048 + qt * 128: r * 2048 + (qt + 1) * 128, 0:1024],
                  sem=SM("q1l"), waits=(prev2.get("qT_mm"),))
            tq = (SM("q1l"), SM("q1l").n)
            for c in range(8):
                tponce(qTb[:, c * 128:(c + 1) * 128], qtb[:, c * 128:(c + 1) * 128],
                       extra_waits=(tq, tok_v1) if c == 0 else (),
                       evac_waits=(prev2["sc1_mm"],) if "sc1_mm" in prev2 else ())
            prev2["qT_mm"] = k.tok("tpmm")
            tok_qT = k.tok("TPEV")
            for h in range(16):
                pb = 64 * (h % 2)
                segs = [PS_B[:, 0:512], PS_B[:, 512:1024], PS_C[:, 0:512], PS_E[:, 0:512]]
                for kb_i in range(4):
                    w2 = []
                    if kb_i == 0:
                        w2 = [tok_qT]
                        w2.append((SM("SCM"), SM("SCM").n))
                        if h == 0:
                            w2.append((PEV, PEV.n))
                    k.mm(segs[kb_i],
                         qTb[pb:pb + 64, (h // 2) * 128:(h // 2) * 128 + 128],
                         BIGA[pb:pb + 64, (h // 2) * 2048 + kb_i * 512: (h // 2) * 2048 + (kb_i + 1) * 512],
                         start=True, stop=True, waits=tuple(w2),
                         incs=((("sc1mm", 1),) if kb_i == 3 else ()))
                tok_sc = k.tok("sc1mm")
                for kb_i in range(4):
                    k.red("vector", mx4[:, kb_i:kb_i + 1], segs[kb_i], ALU.max,
                          waits=(tok_sc,) if kb_i == 0 else ())
                k.red("vector", nmb[:], mx4[:, 0:4], ALU.max, negate=True)
                k.ts("vector", nmb[:], nmb[:], 1.0 / 32.0, ALU.mult, incs=(("sm1v", 1),))
                for kb_i in range(4):
                    k.act(probs[:, kb_i * 512:(kb_i + 1) * 512], segs[kb_i], AF.Exp,
                          bias=nmb[:], scale=1.0 / 32.0,
                          waits=(k.tok("sm1v"),) if kb_i == 0 else (),
                          incs=((("SCM", 1),) if kb_i == 3 else ()))
                tok_exp = (SM("SCM"), SM("SCM").n)
                k.red("vector", rr1[:], probs[:], ALU.add, waits=(tok_exp,))
                k.emit("vector", lambda e, o=rr1[:], i=rr1[:]: e.reciprocal(o, i),
                       incs=(("VM", 1),))
                tok_rr = (SM("VM"), SM("VM").n)
                ops_ = PS_D if h % 2 == 0 else PS_F
                pTbuf = TRANS[:, 2304:4352]
                for kb4 in range(4):
                    k.pyctr["tpar"] = k.pyctr.get("tpar", 0) + 1
                    pslot = k.pyctr["tpar"] % 2
                    for q4 in range(4):
                        kt = kb4 * 4 + q4
                        k.emit("tensor",
                               lambda ee, o=PS_A[:, pslot * 512 + q4 * 128: pslot * 512 + (q4 + 1) * 128],
                               i=probs[:, kt * 128:(kt + 1) * 128]:
                                   ee.transpose(o, i, ident[0:128, 0:128]),
                               waits=((tok_exp,) if kt == 0 else ()) +
                                     (((TPEV, TPEV.n - 1),) if q4 == 0 else ()),
                               incs=(("tpmm", 1),))
                    lag = 4 * (3 - kb4)
                    gv_ = SM("av1").n - lag
                    k.cp("vector", pTbuf[:, kb4 * 512:(kb4 + 1) * 512],
                         PS_A[:, pslot * 512: pslot * 512 + 512],
                         waits=(k.tok("tpmm"),) +
                               (((SM("av1"), gv_),) if gv_ >= 1 else ()),
                         incs=((TPEV, 1),))
                tok_pT = k.tok("TPEV")
                for kt in range(16):
                    w2 = []
                    if kt == 0:
                        w2 = [tok_pT, (SM("oev1"), SM("oev1").n)]
                    elif kt % 4 == 0:
                        pass
                    k.mm(ops_[:, 0:64], pTbuf[:, kt * 128:(kt + 1) * 128],
                         vtile(kt)[:, h * 64:(h + 1) * 64],
                         start=(kt == 0), stop=(kt == 15), waits=tuple(w2),
                         incs=(("av1", 1),))
                k.ts("vector", o1sb[:, h * 64:(h + 1) * 64], ops_[:, 0:64], rr1[:],
                     ALU.mult, waits=(k.tok("av1"), tok_rr), incs=(("oev1", 1),))
            prev2["sc1_mm"] = k.tok("sc1mm")
            prev2["av_mm"] = k.tok("av1")
            k.dma("gpsimd", o1_scr[ti_g * 128:(ti_g + 1) * 128, :], o1sb,
                  sem=SM("o1w"), waits=(k.tok("oev1"),))
            prev2["o1_w"] = k.tok("o1w")
    for jc in range(8):
        k.dma("sync", BIGA[:, jc * 1024:(jc + 1) * 1024],
              ag["wout1"][1][jc * 128:(jc + 1) * 128, :], sem=SM("wo1l"),
              waits=((k.tok("AG_wout1"), k.tok("sc1mm"), k.tok("av1"), k.tok("TPEV"))
                     if jc == 0 else ()))
    tok_wo1 = k.tok("wo1l")
    c2b_o = BIGB[:, 0:1024]
    c2b_oT = BIGB[:, 1024:2048]
    c2b_s = BIGB[:, 2048:3072]
    prev3 = {}
    for ti in range(NT):
        k.dma("sync", c2b_o, o1_scr[ti * 128:(ti + 1) * 128, :], sem=SM("o1l"),
              waits=((SM("o1w"), 16 * NT), k.tok("v1l"), (SM("q1l"), SM("q1l").n))
                    if ti == 0 else (prev3["oT_mm"],))
        to_ = (SM("o1l"), SM("o1l").n)
        for c in range(8):
            tponce(c2b_oT[:, c * 128:(c + 1) * 128], c2b_o[:, c * 128:(c + 1) * 128],
                   extra_waits=(to_,) if c == 0 else (),
                   evac_waits=(prev3["sp_mm"],) if (c == 0 and ti >= 1) else ())
        prev3["oT_mm"] = k.tok("tpmm")
        tok_oT2 = k.tok("TPEV")

        def evs1(h, ps, w2):
            w2 = list(w2) + ([(SM("s1w"), 16 * ti)] if ti >= 1 else [])
            k.cp("vector", c2b_s[:, h * 512:(h + 1) * 512], ps, waits=tuple(w2),
                 incs=((PEV, 1),))
        proj_tm(c2b_oT, 0, 1024, evs1,
                first_waits=(tok_oT2,) + ((tok_wo1,) if ti == 0 else ()))
        prev3["sp_mm"] = k.tok("PMM")
        k.dma("gpsimd", s1_scr[ti * 128:(ti + 1) * 128, :], c2b_s, sem=SM("s1w"),
              waits=(k.tok("PEV"),))
        if stage == "s1":
            k.dma("gpsimd", yout[ti * 128:(ti + 1) * 128, :], c2b_s, sem=SM("s1y"))
    if stage == "s1":
        k.wait("gpsimd", SM("s1y"), SM("s1y").n)
        k.build_block()
        return nc

    from concourse.bass_core_rust import IndirectOffsetOnAxis as _IOA2
    CG = 3
    hq = TRANS[:, 0:4096]
    hq_h = TRANS[:, 0:2048].bitcast(BF16)
    hq_l = TRANS[:, 2048:4096].bitcast(BF16)
    stg = TRANS[:, 0:4096]
    ytmp1 = TRANS[:, 4096:5120]
    cxT = BIGB[:, 0:12288]
    for e in range(E):
        for ct in range(12):
            sbase = e * CPE + ct * 128
            w2 = []
            if e == 0 and ct == 0:
                w2 = [(SM("sctw"), SM("sctw").n), (SM("xnTw"), 16 * NT),
                      k.tok("PMM"), k.tok("PEV"), (SM("s1w"), SM("s1w").n),
                      (SM("o1l"), SM("o1l").n)]
            if ct >= 1 or e >= 1:
                w2.append(k.tok("cxg"))
            k.dma("sync", IDXS[:, 4:5], tok_slot[sbase:sbase + 128, :], sem=SM("ixl"),
                  waits=tuple(w2))
            gw = [(SM("ixl"), SM("ixl").n)]
            if ct >= 4 or e >= 1:
                gw.append((SM("tpmm"), SM("tpmm").n - 24))
            if e >= 1 and ct == 0:
                gw.append(k.tok("m1mm"))
            k.emit("gpsimd", lambda ee, o=stg[:, (ct % 4) * 1024:(ct % 4) * 1024 + 1024],
                   ix=IDXS[:, 4:5]:
                   ee.indirect_dma_start(o, None, xn1_scr[:, :], _IOA2(ix, 0)),
                   waits=tuple(gw), incs=(("cxg", 16),), drain=False)
            tok_g = k.tok("cxg")
            for c in range(8):
                ev_w = ()
                if c == 0 and e >= 1 and ct == 0:
                    ev_w = (k.tok("h1mm"),)
                k.pyctr["tpar"] = k.pyctr.get("tpar", 0) + 1
                pslot = k.pyctr["tpar"] % 2
                pst = PS_A[:, pslot * 512: pslot * 512 + 128]
                k.emit("tensor",
                       lambda ee, o=pst, i=stg[:, (ct % 4) * 1024 + c * 128:(ct % 4) * 1024 + (c + 1) * 128]:
                           ee.transpose(o, i, ident[0:128, 0:128]),
                       waits=((tok_g,) if c == 0 else ()) + ((TPEV, TPEV.n - 1),),
                       incs=(("tpmm", 1),))
                co = c * 1536 + ct * 128
                hi_ap = BIGBh[:, co:co + 128]
                lo_ap = BIGBh[:, 12288 + co: 12288 + co + 128]
                k.cp("vector", hi_ap, pst, waits=(k.tok("tpmm"),) + tuple(ev_w))
                k.stt("vector", lo_ap, hi_ap, -1.0, pst, ALU.mult, ALU.add,
                      incs=((TPEV, 1),))
        tok_cxT = k.tok("TPEV")
        k.dma("sync", rowbuf[0:1, 0:1024], bout_r[0:1, e * 1024:(e + 1) * 1024],
              sem=SM("bo1l"),
              waits=(((SM("idl"), SM("idl").n),) if e == 0 else (k.tok("m1mm"),)))
        tok_bo = k.tok("bo1l")
        for fq in range(4):
            for jc in range(8):
                w2 = [(SM("wspl"), SM("wspl").n)]
                if jc == 0:
                    w2 += [(SM("AG_win"), 1)]
                k.dma("sync", BIGA[:, 16384 + jc * 1024: 16384 + (jc + 1) * 1024],
                      ag["win"][1][e * 1024 + jc * 128: e * 1024 + (jc + 1) * 128,
                                   fq * 1024:(fq + 1) * 1024],
                      sem=SM("wi1l"), waits=tuple(w2))
            twl = k.tok("wi1l")
            for jc in range(8):
                stga = BIGA[:, 16384 + jc * 1024: 16384 + (jc + 1) * 1024]
                w2 = [twl] if jc == 0 else []
                if jc == 0:
                    w2.append(k.tok("h1mm"))
                k.cp("vector", BIGAh[:, jc * 1024:(jc + 1) * 1024], stga, waits=tuple(w2))
                k.stt("vector", BIGAh[:, 8192 + jc * 1024: 8192 + (jc + 1) * 1024],
                      BIGAh[:, jc * 1024:(jc + 1) * 1024], -1.0, stga,
                      ALU.mult, ALU.add, incs=(("wspl", 1),))
            for fc in range(8):
                w2 = [(SM("wspl"), SM("wspl").n)]
                if fc == 0:
                    w2 += [(SM("AG_wout"), 1)]
                k.dma("sync", BIGA[:, 16384 + fc * 1024: 16384 + (fc + 1) * 1024],
                      ag["wout"][1][e * 4096 + fq * 1024 + fc * 128:
                                    e * 4096 + fq * 1024 + (fc + 1) * 128, :],
                      sem=SM("wi2l"), waits=tuple(w2))
            two = k.tok("wi2l")
            for fc in range(8):
                stga = BIGA[:, 16384 + fc * 1024: 16384 + (fc + 1) * 1024]
                w2 = [two, k.tok("m1mm")] if fc == 0 else []
                k.cp("vector", BIGAh[:, 16384 + fc * 1024: 16384 + (fc + 1) * 1024],
                     stga, waits=tuple(w2))
                k.stt("vector", BIGAh[:, 24576 + fc * 1024: 24576 + (fc + 1) * 1024],
                      BIGAh[:, 16384 + fc * 1024: 16384 + (fc + 1) * 1024], -1.0, stga,
                      ALU.mult, ALU.add, incs=(("wspl", 1),))
            tok_wi = k.tok("wspl")
            ROT = (PS_B[:, 0:512], PS_B[:, 512:1024], PS_C[:, 0:512], PS_E[:, 0:512])
            for gi in range(CG):
                for fc in range(8):
                    ps = ROT[fc % 4]
                    k.wait("tensor", PEV, PEV.n - 3)
                    for jc in range(8):
                        w2 = [tok_wi, tok_cxT] if (fc == 0 and jc == 0) else []
                        wi_h = BIGAh[:, jc * 1024 + fc * 128: jc * 1024 + fc * 128 + 128]
                        wi_l = BIGAh[:, 8192 + jc * 1024 + fc * 128: 8192 + jc * 1024 + fc * 128 + 128]
                        cx_h = BIGBh[:, jc * 1536 + gi * 512: jc * 1536 + (gi + 1) * 512]
                        cx_l = BIGBh[:, 12288 + jc * 1536 + gi * 512: 12288 + jc * 1536 + (gi + 1) * 512]
                        k.mm(ps, wi_h, cx_h, start=(jc == 0), stop=False, waits=tuple(w2))
                        k.mm(ps, wi_h, cx_l, start=False, stop=False)
                        k.mm(ps, wi_l, cx_h, start=False, stop=(jc == 7),
                             incs=((("h1mm", 1),) if jc == 7 else ()))
                    k.act(ytmp1[:, 0:512], ps, AF.Gelu,
                          bias=binsb[:, e * 32 + fq * 8 + fc: e * 32 + fq * 8 + fc + 1],
                          waits=(k.tok("h1mm"), (SM("m1w"), SM("m1w").n)),
                          incs=(("SCM", 1),))
                    k.cp("vector", hq_h[:, fc * 512:(fc + 1) * 512], ytmp1[:, 0:512],
                         waits=((SM("SCM"), SM("SCM").n),))
                    k.stt("vector", hq_l[:, fc * 512:(fc + 1) * 512],
                          hq_h[:, fc * 512:(fc + 1) * 512], -1.0, ytmp1[:, 0:512],
                          ALU.mult, ALU.add, incs=((PEV, 1),))
                tok_h = k.tok("PEV")
                for tsub in range(4):
                    for dh in range(2):
                        ps = ROT[(tsub * 2 + dh) % 4]
                        k.wait("tensor", PEV, PEV.n - 3)
                        for fc in range(8):
                            w2 = [tok_h] if (fc == 0 and dh == 0 and tsub == 0) else []
                            last = (fc == 7 and fq > 0)
                            hh_ = hq_h[:, fc * 512 + tsub * 128: fc * 512 + tsub * 128 + 128]
                            hl_ = hq_l[:, fc * 512 + tsub * 128: fc * 512 + tsub * 128 + 128]
                            wo_h = BIGAh[:, 16384 + fc * 1024 + dh * 512: 16384 + fc * 1024 + dh * 512 + 512]
                            wo_l = BIGAh[:, 24576 + fc * 1024 + dh * 512: 24576 + fc * 1024 + dh * 512 + 512]
                            k.mm(ps, hh_, wo_h, start=(fc == 0), stop=False, waits=tuple(w2))
                            k.mm(ps, hh_, wo_l, start=False, stop=False)
                            k.mm(ps, hl_, wo_h, start=False, stop=last,
                                 incs=((("m1mm", 1),) if last else ()))
                        if fq == 0:
                            k.mm(ps, onesrow[0:1, :], rowbuf[0:1, dh * 512:(dh + 1) * 512],
                                 start=False, stop=True,
                                 waits=(tok_bo,) if (gi == 0 and tsub == 0 and dh == 0) else (),
                                 incs=(("m1mm", 1),))
                        wv2 = [k.tok("m1mm"), (SM("m1w"), SM("m1w").n)]
                        k.cp("vector", ytmp1[:, dh * 512:(dh + 1) * 512], ps,
                             waits=tuple(wv2), incs=((PEV, 1),))
                    rbase = e * CPE + gi * 512 + tsub * 128
                    k.dma("gpsimd", ycomp_scr[rbase:rbase + 128, :], ytmp1,
                          sem=SM("m1w"), waits=(k.tok("PEV"),), accum=(fq > 0))
    f4a = BIGB[:, 0:1024]
    f4b = BIGB[:, 1024:2048]
    f4c = BIGB[:, 2048:3072]
    f4g1 = BIGB[:, 3072:4096]
    f4g2 = BIGB[:, 4096:5120]
    f4w = SMALL[:, 304:306]
    for ti in range(NT):
        w2 = [(SM("m1w"), SM("m1w").n), k.tok("m1mm"), k.tok("PEV")] if ti == 0 else \
             [(SM("yw"), 16 * ti)]
        k.dma("sync", f4b, x1_scr[ti * 128:(ti + 1) * 128, :], sem=SM("f4l"),
              waits=tuple(w2))
        k.dma("sync", f4c, s1_scr[ti * 128:(ti + 1) * 128, :], sem=SM("f4l"),
              waits=((SM("s1w"), 16 * NT) if ti == 0 else None,))
        k.dma("sync", f4w, wv_scr[ti * 128:(ti + 1) * 128, :], sem=SM("f4l"),
              waits=((SM("slw"), SM("slw").n) if ti == 0 else None,))
        k.dma("sync", IDXS[:, 4:6], slots_scr[ti * 128:(ti + 1) * 128, :], sem=SM("f4i"),
              waits=((SM("c4g"), 32 * ti) if ti >= 1 else None,))
        ti4 = k.tok("f4i")
        k.emit("gpsimd", lambda ee, o=f4g1, ix=IDXS[:, 4:5]:
               ee.indirect_dma_start(o, None, ycomp_scr[:, :], _IOA2(ix, 0)),
               waits=(ti4, ((SM("VM"), SM("VM").n) if ti >= 1 else None)),
               incs=(("c4g", 16),), drain=False)
        k.emit("gpsimd", lambda ee, o=f4g2, ix=IDXS[:, 5:6]:
               ee.indirect_dma_start(o, None, ycomp_scr[:, :], _IOA2(ix, 0)),
               waits=(), incs=(("c4g", 16),), drain=False)
        tg4 = k.tok("c4g")
        tl = k.tok("f4l")
        k.ts("vector", f4a, f4g1, f4w[:, 0:1], ALU.mult,
             waits=(tg4, tl) + (((SM("yw"), 16 * ti),) if ti >= 1 else ()))
        k.stt("vector", f4a, f4g2, f4w[:, 1:2], f4a, ALU.mult, ALU.add,
              incs=(("hy4", 1),))
        k.act(f4a, f4a, AF.Silu, waits=(k.tok("hy4"),), incs=(("SCM", 1),))
        k.tt("vector", f4a, f4a, f4c, ALU.mult, waits=((SM("SCM"), SM("SCM").n),))
        k.tt("vector", f4a, f4a, f4b, ALU.add, incs=(("VM", 1),))
        k.dma("gpsimd", yout[ti * 128:(ti + 1) * 128, :], f4a, sem=SM("yw"),
              waits=(k.tok("VM"),))
    k.wait("gpsimd", SM("yw"), SM("yw").n)
    k.build_block()
    return nc


def _make_inmaps(ins):
    f32 = np.float32
    cores = []
    Wq, Wk, Wv, Wd = (ins["L0_attn_W" + n].astype(f32) for n in "qkvd")
    bvecs = [ins["L0_attn_gamma"], ins["L0_attn_beta"], ins["L0_attn_bq"], ins["L0_attn_bk"],
             ins["L0_attn_bv"], ins["L0_attn_bd"], ins["L0_ffn_ln1_g"], ins["L0_ffn_ln1_b"],
             ins["L0_ffn_ln2_g"], ins["L0_ffn_ln2_b"], ins["L0_ffn_out_proj_b"]]
    cvec = np.stack([np.ascontiguousarray(np.broadcast_to(np.asarray(v, f32), (128, 1024)))
                     for v in bvecs])
    x = np.asarray(ins["L0_attn_input_tensor"], f32)
    tt_ = np.arange(128)
    tri = ((tt_[:, None] % 16 == tt_[None, :] % 16) &
           (tt_[:, None] // 16 < tt_[None, :] // 16)).astype(f32)
    selS = (tt_[:, None] % 16 == np.arange(16)[None, :]).astype(f32)
    cl_ = np.arange(128) // 16
    ll_ = np.arange(128) % 16
    for c in range(8):
        m1 = ((ll_[:, None] < np.arange(16)[None, :]) |
              ((ll_[:, None] == np.arange(16)[None, :]) & (cl_[:, None] < c))).astype(f32)
        m = {
            "x_in": np.ascontiguousarray(x[2 * c:2 * c + 2].reshape(4096, 1024)),
            "wq_sl": np.ascontiguousarray(Wq[128 * c:128 * (c + 1)]),
            "wk_sl": np.ascontiguousarray(Wk[128 * c:128 * (c + 1)]),
            "wv_sl": np.ascontiguousarray(Wv[128 * c:128 * (c + 1)]),
            "wd_sl": np.ascontiguousarray(Wd[128 * c:128 * (c + 1)]),
            "ip_sl": np.ascontiguousarray(np.asarray(ins["L0_ffn_in_proj_w"], f32)[384 * c:384 * (c + 1)]),
            "op_sl": np.ascontiguousarray(np.asarray(ins["L0_ffn_out_proj_w"], f32)[128 * c:128 * (c + 1)]),
            "w1c": np.ascontiguousarray(np.asarray(ins["L0_ffn_w1"], f32)[c]),
            "w2c": np.ascontiguousarray(np.asarray(ins["L0_ffn_w2"], f32)[c]),
            "b1c": np.ascontiguousarray(np.asarray(ins["L0_ffn_b1"], f32)[c].reshape(32, 128).T),
            "b2_r": np.ascontiguousarray(np.asarray(ins["L0_ffn_b2"], f32)[c:c + 1]),
            "wg0_in": np.ascontiguousarray(np.asarray(ins["L0_ffn_wg"], f32)),
            "ipb_bc": np.ascontiguousarray(np.broadcast_to(np.asarray(ins["L0_ffn_in_proj_b"], f32), (128, 3072))),
            "m1c_in": m1,
            "selS_in": selS,
            "selST_in": np.ascontiguousarray(selS.T),
            "tri_in": tri,
            "tri0_in": (tt_[:, None] < tt_[None, :]).astype(f32),
            "ident_in": np.eye(128, dtype=f32),
            "cvec": cvec,
            "wqkv1_sl": np.ascontiguousarray(np.asarray(ins["L1_attn_w_qkv"], f32)[128 * c:128 * (c + 1)]),
            "wout1_sl": np.ascontiguousarray(np.asarray(ins["L1_attn_w_out"], f32)[128 * c:128 * (c + 1)]),
            "winc": np.ascontiguousarray(np.asarray(ins["L1_ffn_W_in"], f32)[c]),
            "woutc": np.ascontiguousarray(np.asarray(ins["L1_ffn_W_out"], f32)[c]),
            "binc": np.ascontiguousarray(
                np.asarray(ins["L1_ffn_b_in"], f32).reshape(8, 32, 128).transpose(2, 0, 1).reshape(128, 256)),
            "bout_r": np.ascontiguousarray(np.asarray(ins["L1_ffn_b_out"], f32).reshape(1, 8192)),
            "wg1_in": np.ascontiguousarray(np.asarray(ins["L1_ffn_W_gate"], f32)),
        }
        cores.append(m)
    return cores


def kernel(**inputs):
    from concourse.bass_utils import run_bass_kernel_spmd
    nc = build_core("full")
    in_maps = _make_inmaps(inputs)
    res = run_bass_kernel_spmd(nc, in_maps, core_ids=list(range(8)))
    out = np.stack([res.results[c]["y_out"].reshape(2, 2048, 1024) for c in range(8)])
    return np.ascontiguousarray(out.reshape(16, 2048, 1024))


# version 08
# speedup vs baseline: 1.0045x; 1.0045x over previous
import contextlib
import numpy as np
import concourse.bass as bass
import concourse.mybir as mybir

F32 = mybir.dt.float32
BF16 = mybir.dt.bfloat16
AF = mybir.ActivationFunctionType
ALU = mybir.AluOpType
AX = mybir.AxisListType

ENGS = ("sync", "tensor", "vector", "scalar", "gpsimd")

B = 2
S = 2048
D = 1024
T = B * S
NT = T // 128
NH0, HD0 = 16, 64
FH, FHD = 4, 256
E, FF = 8, 4096
NDET = 256
L = 16


class Sem:
    __slots__ = ("name", "h", "n")
    def __init__(self, name):
        self.name, self.h, self.n = name, None, 0


SEM_ALIAS = {}
for _n in ("LV2", "LVd", "LV5", "LV6"):
    SEM_ALIAS[_n] = "LNV"
for _n in ("LS2", "LSd", "LS5", "LS6"):
    SEM_ALIAS[_n] = "LNS"
for _n in ("gv1", "gv2", "g1v", "t16", "scv", "s8v", "aev", "cadd", "cedone",
           "ksdone", "dzero", "dmemset", "crec", "bdev", "zev", "hy2", "x1c",
           "QE", "KE", "ELUV", "ONES", "EPS"):
    SEM_ALIAS[_n] = "VM"
for _n in ("ELUS", "gs1", "g1s", "eexp", "lnf5"):
    SEM_ALIAS[_n] = "SCM"
for _n in ("oev1", "gev", "g1ev", "umv", "runv", "rkv", "hy4", "s81", "sm1v"):
    SEM_ALIAS[_n] = "VM"
for _n in ("PMM", "wemm", "cemm", "amm", "atmm", "cntmm", "dmm", "g1mm", "gmm",
           "h1mm", "sc1mm", "smm", "rkmm", "ctxmm", "av1", "kcons", "pfmm"):
    SEM_ALIAS[_n] = "TM"


class KB:
    def __init__(self, nc):
        self.nc = nc
        self.ops = {e: [] for e in ENGS}
        self.sems = {}
        self.stack = contextlib.ExitStack()
        self.sb = {}
        self.dr = {}
        self.pyctr = {}

    def sem(self, name):
        if isinstance(name, Sem):
            return name
        name = SEM_ALIAS.get(name, name)
        if name not in self.sems:
            s = Sem(name)
            s.h = self.stack.enter_context(self.nc.semaphore(name))
            self.sems[name] = s
        return self.sems[name]

    def sbuf(self, name, shape, dtype=F32):
        t = self.stack.enter_context(self.nc.sbuf_tensor(name, list(shape), dtype))
        self.sb[name] = t
        return t

    def psum(self, name, shape, dtype=F32):
        t = self.stack.enter_context(self.nc.psum_tensor(name, list(shape), dtype))
        self.sb[name] = t
        return t

    def dram(self, name, shape, dtype=F32, kind=None):
        if kind:
            t = self.nc.dram_tensor(name, list(shape), dtype, kind=kind)
        else:
            t = self.nc.dram_tensor(name, list(shape), dtype)
        self.dr[name] = t
        return t

    def rec(self, eng, fn):
        self.ops[eng].append(fn)

    def _waits(self, eng, waits):
        for w in waits:
            if not w:
                continue
            sem, val = w
            sem = self.sem(sem)
            v = sem.n if val is None else val
            if v <= 0:
                continue
            self.rec(eng, lambda e, h=sem.h, v=v: e.wait_ge(h, v))

    def wait(self, eng, sem, val=None):
        self._waits(eng, ((sem, val),))

    def emit(self, eng, fn, waits=(), incs=(), drain=None):
        if drain is None:
            drain = eng in ("vector", "scalar", "gpsimd")
        if drain:
            self.rec(eng, lambda e: e.drain())
        self._waits(eng, waits)
        incs = [(self.sem(s), a) for (s, a) in incs]
        assert len(incs) <= 1, "hardware allows one sync update per instruction"
        for (sem, amt) in incs:
            sem.n += amt
        def g(e, fn=fn, incs=tuple(incs)):
            i = fn(e)
            for (sem, amt) in incs:
                i.then_inc(sem.h, amt)
        self.rec(eng, g)

    def tok(self, name):
        s = self.sem(name)
        return (s, s.n)

    def dma(self, eng, out, in_, sem=None, waits=(), accum=False):
        incs = ((sem, 16),) if sem is not None else ()
        if accum:
            fn = lambda e, o=out, i=in_: e.dma_start(out=o, in_=i, accum_op=ALU.add)
        else:
            fn = lambda e, o=out, i=in_: e.dma_start(out=o, in_=i)
        self.emit(eng, fn, waits=waits, incs=incs, drain=False)

    def mm(self, out, lhsT, rhs, start, stop, waits=(), incs=(), tile_position=None):
        self.emit("tensor",
                  lambda e, o=out, l=lhsT, r=rhs, st=start, sp=stop, tp=tile_position:
                      e.matmul(o, l, r, start=st, stop=sp, tile_position=tp),
                  waits=waits, incs=incs)

    def transpose(self, out, in_, waits=(), incs=()):
        ident = self.sb["ident"]
        p = in_.shape[0]
        self.emit("tensor",
                  lambda e, o=out, i=in_, idn=ident, p=p: e.transpose(o, i, idn[0:p, 0:p]),
                  waits=waits, incs=incs)

    def tponce_sel(self, ps_full, dst_ap, src_ap, sel, extra_waits=(), evac_waits=()):
        raise NotImplementedError

    def cp(self, eng, out, in_, waits=(), incs=()):
        self.emit(eng, lambda e, o=out, i=in_: e.tensor_copy(o, i), waits=waits, incs=incs)

    def tt(self, eng, out, a, b, op, waits=(), incs=()):
        self.emit(eng, lambda e, o=out, x=a, y=b, p=op: e.tensor_tensor(o, x, y, op=p),
                  waits=waits, incs=incs)

    def ts(self, eng, out, a, scalar, op, waits=(), incs=()):
        self.emit(eng, lambda e, o=out, x=a, s=scalar, p=op: e.tensor_single_scalar(o, x, s, p),
                  waits=waits, incs=incs)

    def tsc(self, eng, out, in0, scalar1, scalar2, op0, op1, waits=(), incs=()):
        self.emit(eng,
                  lambda e, o=out, a=in0, s1=scalar1, s2=scalar2, p0=op0, p1=op1:
                      e.tensor_scalar(o, a, s1, s2, p0, p1),
                  waits=waits, incs=incs)

    def stt(self, eng, out, in0, scalar, in1, op0, op1, waits=(), incs=()):
        self.emit(eng,
                  lambda e, o=out, a=in0, s=scalar, b=in1, p0=op0, p1=op1:
                      e.scalar_tensor_tensor(o, a, s, b, p0, p1),
                  waits=waits, incs=incs)

    def act(self, out, in_, func, bias=0.0, scale=1.0, accum_out=None, waits=(), incs=()):
        if accum_out is not None:
            fn = lambda e, o=out, i=in_, f=func, b=bias, s=scale, a=accum_out: \
                e.activation(out=o, in_=i, func=f, bias=b, scale=s, accum_out=a)
        else:
            fn = lambda e, o=out, i=in_, f=func, b=bias, s=scale: \
                e.activation(out=o, in_=i, func=f, bias=b, scale=s)
        self.emit("scalar", fn, waits=waits, incs=incs)

    def red(self, eng, out, in_, op, negate=False, waits=(), incs=()):
        self.emit(eng,
                  lambda e, o=out, i=in_, p=op, n=negate: e.tensor_reduce(o, i, AX.X, p, negate=n),
                  waits=waits, incs=incs)

    def memset(self, eng, ap, val, waits=(), incs=()):
        self.emit(eng, lambda e, a=ap, v=val: e.memset(a, v), waits=waits, incs=incs)

    def collective(self, kind, ins, outs, op=ALU.bypass, waits=(), incs=()):
        from concourse.aot_env import allow_ici_all_to_all
        self._waits("gpsimd", waits)
        incs = [(self.sem(s), a) for (s, a) in incs]
        for (sem, amt) in incs:
            sem.n += amt
        def g(e, k=kind, i=ins, o=outs, p=op, incs=tuple(incs)):
            with allow_ici_all_to_all():
                inst = e.collective_compute(k, p, replica_groups=[list(range(8))], ins=i, outs=o)
            for (sem, amt) in incs:
                inst.then_inc(sem.h, amt)
        self.rec("gpsimd", g)

    def ln(self, out_ap, x_ap, sq_scratch, st, eps, cols, gamma=None, beta=None,
           pre=(), incs=(), sem_v="LNV", sem_s="LNS"):
        v = "vector"
        SV, SS = self.sem(sem_v), self.sem(sem_s)
        self.red(v, st[:, 0:1], x_ap, ALU.add, waits=pre, incs=((SV, 1),))
        self.act(sq_scratch, x_ap, AF.Square, accum_out=st[:, 1:2], waits=pre, incs=((SS, 1),))
        self.ts(v, st[:, 2:3], st[:, 0:1], 1.0 / cols, ALU.mult)
        self.ts(v, st[:, 3:4], st[:, 1:2], 1.0 / cols, ALU.mult, waits=(self.tok(sem_s),))
        self.tt(v, st[:, 4:5], st[:, 2:3], st[:, 2:3], ALU.mult)
        self.tt(v, st[:, 3:4], st[:, 3:4], st[:, 4:5], ALU.subtract, incs=((SV, 1),))
        self.act(st[:, 7:8], st[:, 3:4], AF.Sqrt,
                 bias=(eps if not isinstance(eps, float) else float(eps)),
                 waits=(self.tok(sem_v),), incs=((SS, 1),))
        self.emit(v, lambda e, o=st[:, 5:6], i=st[:, 7:8]: e.reciprocal(o, i),
                  waits=(self.tok(sem_s),))
        self.tt(v, st[:, 6:7], st[:, 2:3], st[:, 5:6], ALU.mult)
        self.ts(v, st[:, 6:7], st[:, 6:7], -1.0, ALU.mult, incs=((SV, 1),))
        self.act(out_ap, x_ap, AF.Identity, bias=st[:, 6:7], scale=st[:, 5:6],
                 waits=(self.tok(sem_v),), incs=((SS, 1),))
        if gamma is not None:
            self.tt(v, out_ap, out_ap, gamma, ALU.mult, waits=(self.tok(sem_s),))
            self.tt(v, out_ap, out_ap, beta, ALU.add, incs=((SV, 1),))
            return self.tok(sem_v)
        return self.tok(sem_s)

    def build_block(self):
        with self.nc.Block() as block:
            for eng in ENGS:
                fns = self.ops[eng]
                def runner(e, fns=fns):
                    for f in fns:
                        f(e)
                getattr(block, eng)(runner)
        self.stack.close()



STAGES = ["xn", "s", "qdet", "y1", "g", "x1", "s1", "full"]


def build_core(stage="full"):
    nc = bass.Bass("TRN2", target_bir_lowering=False)
    k = KB(nc)
    sti = STAGES.index(stage)
    SM = k.sem

    xin = k.dram("x_in", [T, D], kind="ExternalInput").ap()
    wq_sl = k.dram("wq_sl", [128, 1024], kind="ExternalInput").ap()
    wk_sl = k.dram("wk_sl", [128, 1024], kind="ExternalInput").ap()
    wv_sl = k.dram("wv_sl", [128, 1024], kind="ExternalInput").ap()
    wd_sl = k.dram("wd_sl", [128, 1024], kind="ExternalInput").ap()
    ip_sl = k.dram("ip_sl", [384, 1024], kind="ExternalInput").ap()
    op_sl = k.dram("op_sl", [128, 1024], kind="ExternalInput").ap()
    w1c = k.dram("w1c", [1024, 4096], kind="ExternalInput").ap()
    w2c = k.dram("w2c", [4096, 1024], kind="ExternalInput").ap()
    b1c = k.dram("b1c", [128, 32], kind="ExternalInput").ap()
    b2_r = k.dram("b2_r", [1, 1024], kind="ExternalInput").ap()
    wg0_in = k.dram("wg0_in", [1024, 8], kind="ExternalInput").ap()
    wqkv1_sl = k.dram("wqkv1_sl", [128, 3072], kind="ExternalInput").ap()
    wout1_sl = k.dram("wout1_sl", [128, 1024], kind="ExternalInput").ap()
    winc = k.dram("winc", [1024, 4096], kind="ExternalInput").ap()
    woutc = k.dram("woutc", [4096, 1024], kind="ExternalInput").ap()
    binc = k.dram("binc", [128, 256], kind="ExternalInput").ap()
    bout_r = k.dram("bout_r", [1, 8192], kind="ExternalInput").ap()
    wg1_in = k.dram("wg1_in", [1024, 8], kind="ExternalInput").ap()
    ipb_bc = k.dram("ipb_bc", [128, 3072], kind="ExternalInput").ap()
    m1c_in = k.dram("m1c_in", [128, 16], kind="ExternalInput").ap()
    selS_in = k.dram("selS_in", [128, 16], kind="ExternalInput").ap()
    selST_in = k.dram("selST_in", [16, 128], kind="ExternalInput").ap()
    tri_in = k.dram("tri_in", [128, 128], kind="ExternalInput").ap()
    tri0_in = k.dram("tri0_in", [128, 128], kind="ExternalInput").ap()
    ident_in = k.dram("ident_in", [128, 128], kind="ExternalInput").ap()
    cvec = k.dram("cvec", [11, 128, 1024], kind="ExternalInput").ap()

    yout = k.dram("y_out", [T, D], kind="ExternalOutput").ap()

    def idram(name, shape, dtype=F32):
        return k.dram(name, shape, dtype).ap()

    ag_specs = [("wq", wq_sl), ("wk", wk_sl), ("wv", wv_sl), ("wd", wd_sl),
                ("ip", ip_sl), ("op", op_sl),
                ("wqkv1", wqkv1_sl), ("wout1", wout1_sl),
                ("win", winc), ("wout", woutc)]
    ag_full = {"wq": [1024, 1024], "wk": [1024, 1024], "wv": [1024, 1024],
               "wd": [1024, 1024], "ip": [3072, 1024], "op": [1024, 1024],
               "wqkv1": [1024, 3072], "wout1": [1024, 1024],
               "win": [8192, 4096], "wout": [32768, 1024]}
    ag = {nm: (idram("agi_" + nm, list(sl.shape)),
               idram("ago_" + nm, ag_full[nm])) for nm, sl in ag_specs}
    q_scr = idram("q_scr", [T, D])
    s_scr = idram("s_scr", [T, D])
    cc1_in = idram("cc1_in", [T, D])
    cc1_out = idram("cc1_out", [T, D])
    qkvd_scr = idram("qkvd_scr", [T, 3 * D])
    y1_scr = idram("y1_scr", [T, D])
    y1T_scr = idram("y1T_scr", [D, T])
    moe0_scr = idram("moe0_scr", [T, D])
    cc2_in = idram("cc2_in", [T, D])
    cc2_out = idram("cc2_out", [T, D])
    agi_we = idram("agi_we", [1024, 1024])
    ago_we = idram("ago_we", [8 * 1024, 1024])
    agi_ce = idram("agi_ce", [1, 1024])
    ago_ce = idram("ago_ce", [8, 1024])
    agi_cnt = idram("agi_cnt", [16, 8])
    ago_cnt = idram("ago_cnt", [128, 8])
    x1_scr = idram("x1_scr", [T, D])
    xn1_scr = idram("xn1_scr", [T, D])
    qkv1_scr = idram("qkv1_scr", [T, 3 * D])
    s1_scr = idram("s1_scr", [T, D])
    o1_scr = idram("o1_scr", [T, D])
    CPE = 1536
    tok_slot = k.dram("tok_slot", [13312, 1], mybir.dt.int32).ap()
    slots_scr = k.dram("slots_scr", [T, 2], mybir.dt.int32).ap()
    wv_scr = idram("wv_scr", [T, 2])
    ycomp_scr = idram("ycomp_scr", [8 * CPE + 128, D])

    BIGA = k.sbuf("BIGA", [128, 24576])
    BIGB = k.sbuf("BIGB", [128, 12288])
    TRANS = k.sbuf("TRANS", [128, 5376])
    CONSTS = k.sbuf("CONSTS", [128, 3072])
    SMALL = k.sbuf("SMALL", [128, 1024])
    S8 = k.sbuf("S8", [128, 512])
    GT = k.sbuf("GT", [128, 1024])
    RUN = k.sbuf("RUN", [16, 512])
    ident = k.sbuf("ident", [128, 128])
    tri = k.sbuf("tri", [128, 128])
    tri0 = k.sbuf("tri0", [128, 128])
    selS = k.sbuf("selS", [128, 16])
    selST = k.sbuf("selST", [16, 128])
    m1c = k.sbuf("m1c", [128, 16])
    wg0 = k.sbuf("wg0", [128, 64])
    wg1 = k.sbuf("wg1", [128, 64])
    binsb = k.sbuf("binsb", [128, 256])
    IDXS = k.sbuf("IDXS", [128, 16], mybir.dt.int32)
    cpe_base = k.sbuf("cpe_base", [1, 8])
    BIGAh = BIGA[:, :].bitcast(BF16)
    BIGBh = BIGB[:, :].bitcast(BF16)
    b1sb = k.sbuf("b1sb", [128, 32])
    onescol = k.sbuf("onescol", [128, 8])
    onesrow = k.sbuf("onesrow", [1, 128])
    rowbuf = k.sbuf("rowbuf", [1, 1024])
    rowbuf2 = k.sbuf("rowbuf2", [1, 1024])

    PS_A = k.psum("PS_A", [128, 1024])
    PS_B = k.psum("PS_B", [128, 1024])
    PS_C = k.psum("PS_C", [128, 512])
    PS_D = k.psum("PS_D", [128, 512])
    PS_E = k.psum("PS_E", [128, 512])
    PS_F = k.psum("PS_F", [128, 512])

    TPEV, PEV = SM("TPEV"), SM("PEV")

    def cslot(si):
        return CONSTS[:, si * 1024:(si + 1) * 1024]

    def stat8(slot):
        return SMALL[:, slot * 8:(slot + 1) * 8]

    k.dma("sync", ident[:], ident_in, sem=SM("idl"))
    if sti >= STAGES.index("qdet"):
        k.dma("sync", tri[:], tri_in, sem=SM("idl"))
        k.dma("sync", tri0[:], tri0_in, sem=SM("idl"))
        k.dma("sync", selS[:], selS_in, sem=SM("idl"))
        k.dma("sync", selST[:], selST_in, sem=SM("idl"))
        k.dma("sync", m1c[:], m1c_in, sem=SM("idl"))
        k.dma("sync", wg0[:, 0:64].rearrange("p (jc e) -> p jc e", jc=8),
              wg0_in.rearrange("(jc p) e -> p jc e", p=128), sem=SM("idl"))
        k.dma("sync", b1sb[:], b1c, sem=SM("idl"))
        k.dma("sync", rowbuf[0:1, 0:1024], b2_r, sem=SM("idl"))
    if sti >= STAGES.index("x1"):
        k.dma("sync", wg1[:, 0:64].rearrange("p (jc e) -> p jc e", jc=8),
              wg1_in.rearrange("(jc p) e -> p jc e", p=128), sem=SM("idl"))
        k.dma("sync", binsb[:], binc, sem=SM("idl"))
        for e_ in range(8):
            k.memset("vector", cpe_base[0:1, e_:e_ + 1], float(e_ * 1536),
                     incs=((("VM", 1),) if e_ == 7 else ()))
        k.memset("vector", GT[:, 0:104], 0.0, incs=(("VM", 1),))
        zt = k.tok("VM")
        k.dma("gpsimd", tok_slot.rearrange("(p a) c -> p (a c)", p=128),
              GT[:, 0:104].bitcast(mybir.dt.int32), sem=SM("tofz"), waits=(zt,))
    k.memset("vector", onescol[:], 1.0, incs=(("ONES", 1),))
    k.memset("vector", onesrow[:], 1.0, incs=(("ONES", 1),))
    eps5 = SMALL[:, 96:97]; eps12 = SMALL[:, 97:98]
    k.memset("vector", eps5, 1e-5)
    k.memset("vector", eps12, 1e-12, incs=(("EPS", 1),))

    def load_consts(slots, waits=()):
        for si, row in slots:
            k.dma("sync", cslot(si), cvec[row], sem=SM("cl"), waits=waits)
            waits = ()
        return k.tok("cl")

    import os
    if stage != "xn" or os.environ.get("KB_FORCE_AG"):
        for nm, sl in ag_specs:
            k.dma("gpsimd", ag[nm][0], sl, sem=SM("agb"))
        k.wait("gpsimd", SM("agb"), SM("agb").n)
        for nm, sl in ag_specs:
            k.collective("AllGather", [ag[nm][0].opt()], [ag[nm][1].opt()],
                         incs=(("AG_" + nm, 1),))

    def tponce(dst_ap, src_ap, extra_waits=(), evac_waits=(), sel=None):
        k.pyctr["tpar"] = k.pyctr.get("tpar", 0) + 1
        pslot = k.pyctr["tpar"] % 2
        cols = src_ap.shape[1] if hasattr(src_ap, "shape") else 128
        ps = PS_A[:, pslot * 512: pslot * 512 + cols]
        k.emit("tensor",
               lambda e, o=ps, i=src_ap, idn=ident: e.transpose(o, i, idn[0:i.shape[0], 0:i.shape[0]]),
               waits=tuple(extra_waits) + ((TPEV, TPEV.n - 1),),
               incs=(("tpmm", 1),))
        rd = ps[0:dst_ap.shape[0], :] if sel is None else sel(ps)
        k.cp("vector", dst_ap, rd,
             waits=(k.tok("tpmm"),) + tuple(evac_waits),
             incs=((TPEV, 1),))

    def build_WT(base, ago_ap, first_waits, tag, rows=1024, dstbuf=None):
        ld = SM("wtl")
        dstbuf = BIGA if dstbuf is None else dstbuf
        sbufs = (TRANS[:, 2304:3328], TRANS[:, 3328:4352])
        import os as _os
        dbuf = bool(_os.environ.get("KB_WT_DBUF"))
        for ic in range(rows // 128):
            w = list(first_waits) if ic == 0 else []
            if dbuf:
                w.append((TPEV, TPEV.n - 8))
            else:
                w.append((TPEV, TPEV.n))
            stage_buf = sbufs[ic % 2 if dbuf else 0]
            k.dma("sync", stage_buf, ago_ap[ic * 128:(ic + 1) * 128, :], sem=ld, waits=tuple(w))
            tk = k.tok("wtl")
            for jc in range(8):
                o0 = base + jc * rows + ic * 128
                dst = dstbuf[:, o0:o0 + 128]
                tponce(dst, stage_buf[:, jc * 128:(jc + 1) * 128],
                       extra_waits=(tk,) if jc == 0 else ())
        return k.tok("TPEV")

    if sti >= STAGES.index("qdet"):
        idl_tok = k.tok("idl")
        for qq in range(4):
            t_w1 = build_WT(0, w1c[:, qq * 1024:(qq + 1) * 1024],
                            ((k.tok("wemm"),) if qq > 0 else (k.tok("idl"),)),
                            "W1Q%d" % qq, rows=1024, dstbuf=BIGB)
            for fc in range(8):
                k.dma("sync", BIGA[:, fc * 1024:(fc + 1) * 1024],
                      w2c[qq * 1024 + fc * 128: qq * 1024 + (fc + 1) * 128, :],
                      sem=SM("w2l"), waits=((k.tok("wemm"),) if qq > 0 and fc == 0 else ()))
            t_w2 = k.tok("w2l")
            for fc in range(8):
                for dh in range(2):
                    ceps = PS_C if dh == 0 else PS_E
                    w = [t_w2, idl_tok] if fc == 0 and dh == 0 else []
                    k.mm(ceps[0:1, 0:512], b1sb[:, qq * 8 + fc: qq * 8 + fc + 1],
                         BIGA[:, fc * 1024 + dh * 512: fc * 1024 + dh * 512 + 512],
                         start=(qq == 0 and fc == 0), stop=(qq == 3 and fc == 7),
                         waits=tuple(w),
                         incs=((("cemm", 1),) if (qq == 3 and fc == 7 and dh == 1) else ()))
            for jc in range(8):
                for dh in range(2):
                    ps = PS_B[:, (dh % 2) * 512:(dh % 2) * 512 + 512]
                    k.wait("tensor", PEV, PEV.n - 1)
                    for fc in range(8):
                        w = [t_w1, t_w2] if (jc == 0 and dh == 0 and fc == 0) else []
                        k.mm(ps, BIGB[:, fc * 1024 + jc * 128: fc * 1024 + jc * 128 + 128],
                             BIGA[:, fc * 1024 + dh * 512: fc * 1024 + dh * 512 + 512],
                             start=(fc == 0), stop=(fc == 7), waits=tuple(w),
                             incs=((("wemm", 1),) if fc == 7 else ()))
                    dst = BIGA[:, 8192 + jc * 1024 + dh * 512: 8192 + jc * 1024 + dh * 512 + 512]
                    if qq == 0:
                        k.cp("vector", dst, ps, waits=(k.tok("wemm"),), incs=((PEV, 1),))
                    else:
                        k.tt("vector", dst, dst, ps, ALU.add, waits=(k.tok("wemm"),),
                             incs=((PEV, 1),))
        k.cp("vector", rowbuf2[0:1, 0:512], PS_C[0:1, 0:512], waits=(k.tok("cemm"),))
        k.cp("vector", rowbuf2[0:1, 512:1024], PS_E[0:1, 0:512])
        k.tt("vector", rowbuf2[0:1, 0:1024], rowbuf2[0:1, 0:1024],
             rowbuf[0:1, 0:1024], ALU.add, incs=(("cedone", 1),))
        k.dma("gpsimd", agi_ce, rowbuf2[0:1, 0:1024], sem=SM("weout"),
              waits=(k.tok("cedone"),))
        for jc in range(8):
            k.dma("gpsimd", agi_we[jc * 128:(jc + 1) * 128, :],
                  BIGA[:, 8192 + jc * 1024: 8192 + (jc + 1) * 1024],
                  sem=SM("weout"), waits=((k.tok("PEV"),) if jc == 0 else ()))
        k.wait("gpsimd", SM("weout"), SM("weout").n)
        k.collective("AllGather", [agi_we.opt()], [ago_we.opt()], incs=(("AG_we", 1),))
        k.collective("AllGather", [agi_ce.opt()], [ago_ce.opt()], incs=(("AG_ce", 1),))
        tok_p2 = (k.tok("wemm"), k.tok("weout"), k.tok("cedone"), k.tok("PEV"))
    else:
        tok_p2 = ()

    if stage != "xn" or os.environ.get("KB_FORCE_WT"):
        tok_wq = build_WT(0, ag["wq"][1], (k.tok("AG_wq"), k.tok("idl")) + tuple(tok_p2), "TQ")
        tok_wk = build_WT(8192, ag["wk"][1], (k.tok("AG_wk"),), "TK")
        tok_wv = build_WT(16384, ag["wv"][1], (k.tok("AG_wv"),), "TV")
    else:
        tok_wq = tok_wk = tok_wv = None

    def WT_mv(wbase, jc, i0, iw, istride=1024, buf=None):
        buf = BIGA if buf is None else buf
        return buf[:, wbase + jc * istride + i0: wbase + jc * istride + i0 + iw]

    def proj_tm(xnT, wbase, nout, evac_fn, first_waits=(), istride=1024, wbuf=None):
        for h in range(nout // 512):
            ps = PS_B[:, (h % 2) * 512:(h % 2) * 512 + 512]
            k.wait("tensor", PEV, PEV.n - 1)
            for jc in range(8):
                w = list(first_waits) if (h == 0 and jc == 0) else []
                k.mm(ps, xnT[:, jc * 128:(jc + 1) * 128],
                     WT_mv(wbase, jc, h * 512, 512, istride, wbuf),
                     start=(jc == 0), stop=(jc == 7), waits=tuple(w),
                     incs=((("PMM", 1),) if jc == 7 else ()))
            evac_fn(h, ps, (k.tok("PMM"),))

    def elu_p1(dst, src, scratch, waits=(), incs=()):
        k.ts("vector", scratch, src, 0.0, ALU.min, waits=waits, incs=(("ELUV", 1),))
        k.act(scratch, scratch, AF.Exp, waits=(k.tok("ELUV"),), incs=(("ELUS", 1),))
        k.ts("vector", dst, src, 0.0, ALU.max, waits=(k.tok("ELUS"),))
        k.tt("vector", dst, dst, scratch, ALU.add, incs=incs)

    xb = BIGB[:, 0:1024]
    xnb = BIGB[:, 1024:2048]
    xnTb = BIGB[:, 2048:3072]
    qb = BIGB[:, 3072:4096]
    kbf = BIGB[:, 4096:5120]
    vbf = BIGB[:, 5120:6144]
    sqb = BIGB[:, 6144:7168]
    elub = BIGB[:, 7168:8192]
    kvev = TRANS[:, 0:1024]
    ksev = TRANS[:, 1024:1056]

    cl1 = load_consts([(0, 2), (1, 3), (2, 4)])

    prev = {}
    for ti in range(NT):
        b_loc, s0 = ti // 16, (ti % 16) * 128
        k.dma("sync", xb, xin[ti * 128:(ti + 1) * 128, :], sem=SM("xld"),
              waits=(prev.get("xn_done"),) + (tuple(tok_p2) if ti == 0 else ()))
        pre = [(SM("xld"), SM("xld").n)]
        if ti >= 1:
            pre += [(SM("ccw"), 16 * ti)] + ([prev["xnT_mm"]] if "xnT_mm" in prev else [])
            if stage == "xn":
                pre += [(SM("yw"), 16 * ti)]
        tok_xn = k.ln(xnb, xb, sqb, stat8(0), eps5, D,
                      pre=tuple(pre) + ((k.tok("EPS"),) if ti == 0 else ()))
        prev["xn_done"] = tok_xn
        j = s0 // 256
        r0 = j * 512 + b_loc * 256 + (s0 % 256)
        k.dma("gpsimd", cc1_in[r0:r0 + 128, :], xnb, sem=SM("ccw"), waits=(tok_xn,))
        if stage == "xn":
            k.dma("gpsimd", yout[ti * 128:(ti + 1) * 128, :], xnb, sem=SM("yw"),
                  waits=(tok_xn,))
            continue
        for c in range(8):
            tponce(xnTb[:, c * 128:(c + 1) * 128], xnb[:, c * 128:(c + 1) * 128],
                   extra_waits=(tok_xn,) if c == 0 else (),
                   evac_waits=(prev["proj_mm"],) if (c == 0 and ti >= 1) else ())
        prev["xnT_mm"] = k.tok("tpmm")
        tok_xnT = k.tok("TPEV")

        def mkev(buf, slot, extra):
            def ev(h, ps, w):
                k.tt("vector", buf[:, h * 512:(h + 1) * 512], ps,
                     cslot(slot)[:, h * 512:(h + 1) * 512], ALU.add,
                     waits=tuple(list(w) + (extra if h == 0 else [])),
                     incs=((PEV, 1),))
            return ev

        fw = [tok_xnT] + ([cl1, tok_wq, tok_wk, tok_wv] if ti == 0 else [])
        proj_tm(xnTb, 0, 1024, mkev(qb, 0, [(SM("qsp"), 16 * ti)] if ti >= 1 else []),
                first_waits=fw)
        tok_qev = k.tok("PEV")
        proj_tm(xnTb, 8192, 1024, mkev(kbf, 1, [prev["kcons"]] if ti >= 1 else []))
        proj_tm(xnTb, 16384, 1024, mkev(vbf, 2, [prev["kcons"]] if ti >= 1 else []))
        tok_vev = k.tok("PEV")
        prev["proj_mm"] = k.tok("PMM")
        elu_p1(qb, qb, elub, incs=(("QE", 1),))
        elu_p1(kbf, kbf, elub, incs=(("KE", 1),))
        k.dma("gpsimd", q_scr[ti * 128:(ti + 1) * 128, :], qb, sem=SM("qsp"),
              waits=(k.tok("QE"),))
        row_start = (ti % 16 == 0)
        par = ti % 2
        kvps = PS_C if par == 0 else PS_E
        for h in range(NH0):
            pp, cc_ = 64 * (h % 2), 64 * (h // 2)
            w = []
            if h == 0:
                w = [k.tok("KE"), tok_vev]
                if ti <= 1:
                    w.append(k.tok("cemm"))
                if ti >= 2:
                    w.append((SM("kvadd"), 17 * (ti - 1)))
            k.mm(kvps[pp:pp + 64, cc_:cc_ + 64],
                 kbf[:, h * 64:(h + 1) * 64], vbf[:, h * 64:(h + 1) * 64],
                 start=True, stop=True, waits=tuple(w),
                 incs=(("kcons", 1),))
        ksps = PS_D if par == 0 else PS_F
        for c in range(8):
            k.mm(ksps[:, c: c + 1],
                 kbf[:, c * 128:(c + 1) * 128], onescol[:, 0:1],
                 start=True, stop=True,
                 waits=((k.tok("ONES"),) if (ti == 0 and c == 0) else ()),
                 incs=(("kcons", 1),))
        prev["kcons"] = k.tok("kcons")
        tok_kvmm = k.tok("kcons")
        for h in range(NH0):
            pp, cc_ = 64 * (h % 2), 64 * (h // 2)
            dst = kvev[pp:pp + 64, b_loc * 512 + cc_: b_loc * 512 + cc_ + 64]
            srcp = kvps[pp:pp + 64, cc_:cc_ + 64]
            if row_start:
                k.cp("vector", dst, srcp, waits=(tok_kvmm,) if h == 0 else (),
                     incs=(("kvadd", 1),))
            else:
                k.tt("vector", dst, dst, srcp, ALU.add,
                     waits=(tok_kvmm,) if h == 0 else (), incs=(("kvadd", 1),))
        dst = ksev[:, b_loc * 8: b_loc * 8 + 8]
        srcp = ksps[:, 0:8]
        if row_start:
            k.cp("vector", dst, srcp, incs=(("kvadd", 1),))
        else:
            k.tt("vector", dst, dst, srcp, ALU.add, incs=(("kvadd", 1),))

    if stage == "xn":
        k.wait("gpsimd", SM("yw"), SM("yw").n)
        k.build_block()
        return nc

    k.ts("vector", ksev[:, 0:16], ksev[:, 0:16], 1e-6, ALU.add,
         waits=(k.tok("kvadd"),), incs=(("ksdone", 1),))

    if os.environ.get("KB_P1ONLY"):
        for ti in range(NT):
            k.dma("gpsimd", yout[ti * 128:(ti + 1) * 128, :], kvev[:, 0:1024],
                  sem=SM("yw"), waits=(k.tok("ksdone"), (SM("qsp"), 16 * NT)))
        k.wait("gpsimd", SM("yw"), SM("yw").n)
        k.build_block()
        return nc

    tok_wd = build_WT(0, ag["wd"][1], (k.tok("AG_wd"), k.tok("PEV")), "TD")
    cl2 = load_consts([(0, 5), (1, 0), (2, 1)], waits=(k.tok("PEV"),))

    qb2 = BIGB[:, 0:1024]
    qT2 = BIGB[:, 1024:2048]
    xn2 = BIGB[:, 2048:3072]
    ctxb = BIGB[:, 3072:4096]
    ctxT = BIGB[:, 4096:5120]
    hsb = BIGB[:, 5120:6144]
    sb_o = BIGB[:, 6144:7168]
    sq2 = BIGB[:, 7168:8192]
    zb = SMALL[:, 64:80]

    prev = {}
    for ti in range(NT):
        b_loc, s0 = ti // 16, (ti % 16) * 128
        j = s0 // 256
        r0 = j * 512 + b_loc * 256 + (s0 % 256)
        p1w = []
        if ti == 0:
            p1w = [(SM("qsp"), 16 * NT), k.tok("LNS"), k.tok("ccw"), k.tok("PMM"),
                   k.tok("kcons"), k.tok("ELUS")]
        k.dma("sync", qb2, q_scr[ti * 128:(ti + 1) * 128, :], sem=SM("q2l"),
              waits=tuple(p1w) if ti == 0 else (prev["qT_mm"],))
        k.dma("sync", xn2, cc1_in[r0:r0 + 128, :], sem=SM("xn2l"),
              waits=tuple(p1w) if ti == 0 else ((SM("hsx"), ti),))
        tq = (SM("q2l"), SM("q2l").n)
        for c in range(8):
            ev_w = ()
            if c == 0:
                ev_w = tuple(p1w) if ti == 0 else (prev["ctx_mm"],)
            tponce(qT2[:, c * 128:(c + 1) * 128], qb2[:, c * 128:(c + 1) * 128],
                   extra_waits=(tq,) if c == 0 else (),
                   evac_waits=ev_w)
        prev["qT_mm"] = k.tok("tpmm")
        tok_qT = k.tok("TPEV")
        for h in range(NH0):
            pb = 64 * (h % 2)
            w = []
            if h == 0:
                w = [tok_qT]
                w.append((SM("ksdone"), 1) if ti == 0 else (SM("zev"), SM("zev").n))
            dps = PS_D if h % 2 == 0 else PS_F
            k.mm(dps[:, h // 2: h // 2 + 1],
                 qT2[pb:pb + 64, (h // 2) * 128:(h // 2) * 128 + 128],
                 ksev[pb:pb + 64, b_loc * 8 + h // 2: b_loc * 8 + h // 2 + 1],
                 start=True, stop=True, waits=tuple(w),
                 incs=((("dmm", 1),) if h == NH0 - 1 else ()))
        k.emit("vector", lambda e, o=zb[:, 0:8], i=PS_D[:, 0:8]: e.reciprocal(o, i),
               waits=(k.tok("dmm"),))
        k.emit("vector", lambda e, o=zb[:, 8:16], i=PS_F[:, 0:8]: e.reciprocal(o, i),
               incs=(("zev", 1),))
        for h in range(NH0):
            pb = 64 * (h % 2)
            w = []
            if h == 0:
                w = [(PEV, PEV.n)]
                if ti == 0:
                    w.append(k.tok("kvadd"))
            pcol = (h % 2) * 512 + (h // 2) * 64
            k.mm(PS_B[:, pcol:pcol + 64],
                 qT2[pb:pb + 64, (h // 2) * 128:(h // 2) * 128 + 128],
                 kvev[pb:pb + 64, b_loc * 512 + (h // 2) * 64: b_loc * 512 + (h // 2) * 64 + 64],
                 start=True, stop=True,
                 waits=tuple(w), incs=(("ctxmm", 1),))
        prev["ctx_mm"] = k.tok("ctxmm")
        for half in range(2):
            wv_ = [(SM("ctxmm"), SM("ctxmm").n), k.tok("zev")]
            if ti == 0 and half == 0:
                wv_ += [(SM("qsp"), 16 * NT), k.tok("kcons")]
            for hh in range(8):
                h = half + 2 * hh
                pcol = half * 512 + hh * 64
                k.ts("vector", ctxb[:, h * 64:(h + 1) * 64], PS_B[:, pcol:pcol + 64],
                     zb[:, half * 8 + hh: half * 8 + hh + 1], ALU.mult,
                     waits=tuple(wv_) if hh == 0 else (),
                     incs=((PEV, 1),) if hh == 7 else ())
        tok_ctxev = k.tok("PEV")
        for c in range(8):
            ev_w = ()
            if c == 0:
                ev_w = (prev["oproj_mm"],) if ti >= 1 else (k.tok("PMM"),)
            tponce(ctxT[:, c * 128:(c + 1) * 128], ctxb[:, c * 128:(c + 1) * 128],
                   extra_waits=(tok_ctxev,) if c == 0 else (), evac_waits=ev_w)

        def evhs(h, ps, w):
            w = list(w) + ([cl2, k.tok("kcons")] if ti == 0 else []) + \
                ([(SM("LS2"), SM("LS2").n)] if ti >= 1 else [])
            k.tt("vector", hsb[:, h * 512:(h + 1) * 512], ps,
                 cslot(0)[:, h * 512:(h + 1) * 512], ALU.add, waits=tuple(w),
                 incs=((PEV, 1),))
            k.tt("vector", hsb[:, h * 512:(h + 1) * 512], hsb[:, h * 512:(h + 1) * 512],
                 xn2[:, h * 512:(h + 1) * 512], ALU.add,
                 waits=((SM("xn2l"), SM("xn2l").n),),
                 incs=((("hsx", 1),) if h == 1 else ()))
        proj_tm(ctxT, 0, 1024, evhs,
                first_waits=(k.tok("TPEV"),) + ((tok_wd,) if ti == 0 else ()))
        prev["oproj_mm"] = k.tok("PMM")
        pre_ln = [k.tok("hsx")]
        if ti >= 1:
            pre_ln.append(prev["s_w"])
        tok_s = k.ln(sb_o, hsb, sq2, stat8(1), eps12, D, gamma=cslot(1), beta=cslot(2),
                     pre=tuple(pre_ln), sem_v="LV2", sem_s="LS2")
        k.dma("gpsimd", s_scr[ti * 128:(ti + 1) * 128, :], sb_o, sem=SM("sw"),
              waits=(tok_s,))
        if stage == "s":
            k.dma("gpsimd", yout[ti * 128:(ti + 1) * 128, :], sb_o, sem=SM("sw"))
        prev["s_w"] = k.tok("sw")

    if stage == "s":
        k.wait("gpsimd", SM("sw"), SM("sw").n)
        k.build_block()
        return nc

    k.collective("AllToAll", [cc1_in.opt()], [cc1_out.opt()],
                 waits=((SM("ccw"), 16 * NT),), incs=(("A2A1", 1),))

    def ccdet_ap(buf, dt):
        return buf.rearrange("(i lp n) f -> n i lp f", i=8, lp=2, n=256)[8 * dt:8 * dt + 8]

    tok_ip = build_WT(0, ag["ip"][1], (k.tok("AG_ip"), prev["oproj_mm"], k.tok("PEV")),
                      "TIP", rows=3072)
    w_c = (k.tok("LS2"), k.tok("PEV"))
    for si in range(3):
        k.dma("sync", cslot(si), ipb_bc[:, si * 1024:(si + 1) * 1024], sem=SM("cld"),
              waits=w_c if si == 0 else ())
    cld1 = k.tok("cld")

    xd = BIGB[:, 0:1024]
    xdT = BIGB[:, 1024:2048]
    qkvsb = BIGB[:, 2048:5120]
    prevd = {}
    for dt in range(NT):
        w = [(SM("A2A1"), 1)] if dt == 0 else [prevd["xdT_mm"]]
        if dt == 0:
            w += [k.tok("LS2"), k.tok("sw"), k.tok("ctxmm"), k.tok("PEV"), k.tok("tpmm")]
        k.dma("sync", xd, ccdet_ap(cc1_out, dt), sem=SM("xdl"), waits=tuple(w))
        for c in range(8):
            tponce(xdT[:, c * 128:(c + 1) * 128], xd[:, c * 128:(c + 1) * 128],
                   extra_waits=((SM("xdl"), SM("xdl").n),) if c == 0 else (),
                   evac_waits=(prevd["qkv_mm"],) if (c == 0 and dt >= 1) else ())
        prevd["xdT_mm"] = k.tok("tpmm")
        tok_xdT = k.tok("TPEV")

        def evd(h, ps, w2):
            w2 = list(w2) + ([cld1] if dt == 0 else []) + \
                 ([prevd["qdsp_t"]] if dt >= 1 else [])
            cs = cslot(h // 2)[:, (h % 2) * 512:(h % 2) * 512 + 512]
            k.tt("vector", qkvsb[:, h * 512:(h + 1) * 512], ps, cs, ALU.add,
                 waits=tuple(w2), incs=((PEV, 1),))
        proj_tm(xdT, 0, 3072, evd, istride=3072,
                first_waits=(tok_xdT,) + ((tok_ip,) if dt == 0 else ()))
        prevd["qkv_mm"] = k.tok("PMM")
        tok_ev = k.tok("PEV")
        k.dma("gpsimd", qkvd_scr[dt * 128:(dt + 1) * 128, :], qkvsb, sem=SM("qdsp"),
              waits=(tok_ev,))
        if stage == "qdet":
            k.dma("gpsimd", yout[dt * 128:(dt + 1) * 128, :], qkvsb[:, 0:1024],
                  sem=SM("qdsp"))
        prevd["qdsp_t"] = k.tok("qdsp")
    if stage == "qdet":
        k.wait("gpsimd", SM("qdsp"), SM("qdsp").n)
        k.build_block()
        return nc

    tok_op = build_WT(0, ag["op"][1], (k.tok("AG_op"), prevd["qkv_mm"], k.tok("PEV")), "TOP")
    for si, row in [(0, 10), (1, 6), (2, 7)]:
        k.dma("sync", cslot(si), cvec[row], sem=SM("cld"),
              waits=((k.tok("PEV"),) if si == 0 else ()))
    cld2 = k.tok("cld")
    qd = BIGB[:, 0:1024]
    kd = BIGB[:, 1024:2048]
    vd = BIGB[:, 2048:3072]
    qdT = BIGB[:, 3072:4096]
    kdT = BIGB[:, 4096:5120]
    o_sb = BIGB[:, 5120:7168]
    oT = BIGB[:, 7168:9216]
    mha = BIGB[:, 9216:10240]
    y1b = BIGB[:, 10240:11264]
    y1Tb = BIGB[:, 11264:12288]
    BD = [TRANS[:, 0:128], TRANS[:, 128:256]]
    vpad = [TRANS[:, 256:1280], TRANS[:, 1280:2304]]
    xd2 = TRANS[:, 3328:4352]
    sqd = TRANS[:, 4352:5376]
    E_sb = SMALL[:, 104:120]
    cb = SMALL[:, 120:128]
    k.memset("vector", BD[0][:], 0.0,
             waits=(k.tok("qdsp"), k.tok("kvadd"), k.tok("ctxmm"), k.tok("LS2"),
                    k.tok("dmm")))
    k.memset("vector", BD[1][:], 0.0)
    k.memset("vector", vpad[0][:], 0.0)
    k.memset("vector", vpad[1][:], 0.0, incs=(("dzero", 1),))

    for dt in range(NT):
        w0 = [prevd["qdsp_t"], tok_op] if dt == 0 else [prevd["qkT_mm"]]
        k.dma("sync", qd, qkvd_scr[dt * 128:(dt + 1) * 128, 0:1024], sem=SM("d2l"),
              waits=tuple(w0))
        k.dma("sync", kd, qkvd_scr[dt * 128:(dt + 1) * 128, 1024:2048], sem=SM("d2l"))
        k.dma("sync", vd, qkvd_scr[dt * 128:(dt + 1) * 128, 2048:3072], sem=SM("d2l"),
              waits=((SM("vrp"), 128 * dt) if dt >= 1 else None,))
        k.dma("sync", xd2, ccdet_ap(cc1_out, dt), sem=SM("xd2l"),
              waits=((SM("mhax"), 2 * dt) if dt >= 1 else None,))
        tql = (SM("d2l"), SM("d2l").n)
        tvl = (SM("d2l"), SM("d2l").n)
        for c in range(8):
            tponce(qdT[:, c * 128:(c + 1) * 128], qd[:, c * 128:(c + 1) * 128],
                   extra_waits=(tql,) if c == 0 else (),
                   evac_waits=(prevd["sc_mm"],) if (c == 0 and dt >= 1) else ())
            tponce(kdT[:, c * 128:(c + 1) * 128], kd[:, c * 128:(c + 1) * 128],
                   extra_waits=(tql,) if c == 0 else ())
        prevd["qkT_mm"] = k.tok("tpmm")
        tok_qkT = k.tok("TPEV")
        for g2 in range(2):
            for a in range(4):
                nn = 4 * g2 + a
                k.dma("gpsimd", vpad[g2][32 * a:32 * a + 16, :],
                      vd[16 * nn:16 * nn + 16, :], sem=SM("vrp"),
                      waits=(tvl, (SM("avmm"), 8 * dt), k.tok("dzero")) if (g2 == 0 and a == 0) else ())
        tok_vrp = k.tok("vrp")
        for g2 in range(2):
            for h in range(FH):
                it = dt * 8 + g2 * 4 + h
                for a in range(4):
                    nn = 4 * g2 + a
                    for kc in range(2):
                        w2 = []
                        if a == 0 and kc == 0:
                            w2 = [tok_qkT]
                            if it >= 1:
                                w2.append(prevd["eexp"])
                        k.mm(PS_D[32 * a:32 * a + 16, 0:16],
                             qdT[:, (2 * h + kc) * 128 + nn * 16: (2 * h + kc) * 128 + nn * 16 + 16],
                             kdT[:, (2 * h + kc) * 128 + nn * 16: (2 * h + kc) * 128 + nn * 16 + 16],
                             start=(kc == 0), stop=(kc == 1), tile_position=(0, 32 * a),
                             waits=tuple(w2), incs=((("smm", 1),) if (a == 3 and kc == 1) else ()))
                tok_smm = k.tok("smm")
                k.memset("vector", E_sb[:], 1.0,
                         waits=(prevd["atmm"],) if it >= 1 else (),
                         incs=(("dmemset", 1),))
                for a in range(4):
                    k.act(E_sb[32 * a:32 * a + 16, :], PS_D[32 * a:32 * a + 16, 0:16],
                          AF.Exp, scale=1.0 / 16.0,
                          waits=(tok_smm, k.tok("dmemset")) if a == 0 else (),
                          incs=((("eexp", 1),) if a == 3 else ()))
                prevd["eexp"] = k.tok("eexp")
                k.red("vector", cb[:, 4:5], E_sb[:], ALU.add, waits=(prevd["eexp"],))
                k.emit("vector", lambda e, o=cb[:, h:h + 1], i=cb[:, 4:5]: e.reciprocal(o, i),
                       incs=(("crec", 1),))
                for a in range(4):
                    w2 = []
                    if a == 0:
                        w2 = [prevd["eexp"]]
                        if it >= 1:
                            w2.append(prevd["bdev"])
                    k.mm(PS_E[32 * a:32 * a + 16, 32 * a:32 * a + 16],
                         E_sb[:, 0:16], ident[:, 32 * a:32 * a + 16],
                         start=True, stop=True, tile_position=(0, 32 * a), waits=tuple(w2),
                         incs=((("atmm", 1),) if a == 3 else ()))
                prevd["atmm"] = k.tok("atmm")
                bdp = BD[it % 2]
                for a in range(4):
                    k.cp("vector", bdp[32 * a:32 * a + 16, 32 * a:32 * a + 16],
                         PS_E[32 * a:32 * a + 16, 32 * a:32 * a + 16],
                         waits=(prevd["atmm"],) if a == 0 else (),
                         incs=((("bdev", 1),) if a == 3 else ()))
                prevd["bdev"] = k.tok("bdev")
                w2 = [prevd["bdev"]]
                if g2 == 0 and h == 0:
                    w2.append(tok_vrp)
                    w2.append((PEV, PEV.n))
                if h == 0 and g2 == 1:
                    w2.append((PEV, PEV.n))
                k.mm(PS_B[:, h * 256:(h + 1) * 256], bdp[:, 0:128],
                     vpad[g2][:, h * 256:(h + 1) * 256],
                     start=True, stop=True, waits=tuple(w2), incs=(("avmm", 1),))
            tok_av = k.tok("avmm")
            for h in range(FH):
                w4 = ()
                if h == 0:
                    w4 = (tok_av, k.tok("crec"))
                    if g2 == 0 and dt >= 1:
                        w4 = w4 + (prevd["oT_mm"],)
                k.ts("vector", o_sb[:, g2 * 1024 + h * 256: g2 * 1024 + (h + 1) * 256],
                     PS_B[:, h * 256:(h + 1) * 256], cb[:, h:h + 1], ALU.mult,
                     waits=w4, incs=((PEV, 1),) if h == 3 else ())
        tok_oev = k.tok("PEV")
        for g2 in range(2):
            for c in range(8):
                dst = oT[:, c * 128 + g2 * 64: c * 128 + g2 * 64 + 64].rearrange(
                    "p (a l) -> p a l", a=4)
                tponce(dst,
                       o_sb[:, g2 * 1024 + c * 128: g2 * 1024 + (c + 1) * 128],
                       extra_waits=(tok_oev,) if (g2 == 0 and c == 0) else (),
                       evac_waits=(prevd["op_mm"],) if (g2 == 0 and c == 0 and dt >= 1) else (),
                       sel=lambda ps: ps.rearrange("p (a r) -> p a r", a=4)[:, :, 0:16])
        prevd["oT_mm"] = k.tok("tpmm")
        tok_oT = k.tok("TPEV")
        for h in range(2):
            ps = PS_B[:, (h % 2) * 512:(h % 2) * 512 + 512]
            k.wait("tensor", PEV, PEV.n - 1)
            for jc in range(8):
                w2 = [tok_oT] + ([tok_op, cld2] if dt == 0 else []) if (h == 0 and jc == 0) else []
                k.mm(ps, oT[:, jc * 128:(jc + 1) * 128], WT_mv(0, jc, h * 512, 512),
                     start=(jc == 0), stop=(jc == 7), waits=tuple(w2),
                     incs=((("PMM", 1),) if jc == 7 else ()))
            w3 = [k.tok("PMM")] + ([(SM("LSd"), SM("LSd").n)] if dt >= 1 else [])
            k.tt("vector", mha[:, h * 512:(h + 1) * 512], ps,
                 cslot(0)[:, h * 512:(h + 1) * 512], ALU.add, waits=tuple(w3),
                 incs=((PEV, 1),))
            k.tt("vector", mha[:, h * 512:(h + 1) * 512], mha[:, h * 512:(h + 1) * 512],
                 xd2[:, h * 512:(h + 1) * 512], ALU.add,
                 waits=((SM("xd2l"), SM("xd2l").n),), incs=(("mhax", 1),))
        prevd["op_mm"] = k.tok("PMM")
        prevd["sc_mm"] = k.tok("smm")
        pre_ln = [k.tok("mhax")] + ([prevd["y1_w"], prevd["y1T_mm"]] if dt >= 1 else [])
        tok_y1 = k.ln(y1b, mha, sqd, stat8(2), eps5, D, gamma=cslot(1), beta=cslot(2),
                      pre=tuple(pre_ln), sem_v="LVd", sem_s="LSd")
        k.dma("gpsimd", y1_scr[dt * 128:(dt + 1) * 128, :], y1b, sem=SM("y1w"),
              waits=(tok_y1,))
        if stage == "y1":
            k.dma("gpsimd", yout[dt * 128:(dt + 1) * 128, :], y1b, sem=SM("y1w"))
        prevd["y1_w"] = k.tok("y1w")
        for c in range(8):
            ev_w = ()
            if c == 0 and dt >= 1:
                ev_w = (prevd["gt_mm"], (SM("y1tw"), 16 * dt))
            tponce(y1Tb[:, c * 128:(c + 1) * 128], y1b[:, c * 128:(c + 1) * 128],
                   extra_waits=(tok_y1,) if c == 0 else (),
                   evac_waits=ev_w)
        prevd["y1T_mm"] = k.tok("tpmm")
        tok_y1T = k.tok("TPEV")
        k.dma("gpsimd", y1T_scr.rearrange("(c p) t -> p c t", c=8)[:, :, dt * 128:(dt + 1) * 128],
              y1Tb, sem=SM("y1tw"), waits=(tok_y1T,))
        for jc in range(8):
            w2 = []
            if jc == 0:
                w2 = [tok_y1T, (SM("idl"), SM("idl").n)]
                if dt >= 1:
                    w2.append((SM("gev"), SM("gev").n))
            k.mm(PS_F[:, 0:8], y1Tb[:, jc * 128:(jc + 1) * 128], wg0[:, jc * 8:(jc + 1) * 8],
                 start=(jc == 0), stop=(jc == 7), waits=tuple(w2),
                 incs=((("gmm", 1),) if jc == 7 else ()))
        gl = SMALL[:, 128:136]
        nmx = SMALL[:, 136:137]
        ge = SMALL[:, 144:152]
        den = SMALL[:, 152:153]
        rcp = SMALL[:, 153:154]
        emx = SMALL[:, 154:155]
        mask_sl = GT[:, dt * 24: dt * 24 + 8]
        pref_sl = GT[:, dt * 24 + 8: dt * 24 + 16]
        gv_sl = GT[:, dt * 24 + 16: dt * 24 + 17]
        prevd["gt_mm"] = k.tok("gmm")
        k.cp("vector", gl[:], PS_F[:, 0:8], waits=(k.tok("gmm"),))
        k.red("vector", nmx[:], gl[:], ALU.max, negate=True, incs=(("gv1", 1),))
        k.act(ge[:], gl[:], AF.Exp, bias=nmx[:], waits=(k.tok("gv1"),), incs=(("gs1", 1),))
        k.red("vector", den[:], ge[:], ALU.add, waits=(k.tok("gs1"),))
        k.emit("vector", lambda e, o=rcp[:], i=den[:]: e.reciprocal(o, i))
        k.red("vector", emx[:], ge[:], ALU.max, incs=(("gv2", 1),))
        k.tt("vector", gv_sl, emx[:], rcp[:], ALU.mult, waits=(k.tok("gv2"),))
        k.ts("vector", mask_sl, ge[:], emx[:], ALU.is_equal,
             waits=(((SM("tofz"), SM("tofz").n),) if dt == 0 else ()),
             incs=(("gev", 1),))
        for (reg, lhs, inc) in [((0, 16, 8, 16), selS[:], "cntmm"), ((0, 128, 16, 24), tri[:], "cntmm")]:
            p0, p1, c0, c1 = reg
            k.mm(PS_F[p0:p1, c0:c1], lhs, mask_sl, start=True, stop=True,
                 waits=((k.tok("gev"),) if c0 == 8 else ()) +
                       (((SM("cadd"), SM("cadd").n),) if (c0 == 8 and dt >= 1) else ()),
                 incs=((inc, 1),))
        tok_cnt = k.tok("cntmm")
        cntacc = RUN[:, 280:288]
        if dt == 0:
            k.memset("vector", RUN[0:16, 280:288], 0.0)
        k.cp("vector", RUN[0:16, dt * 8:(dt + 1) * 8], RUN[0:16, 280:288],
             waits=(tok_cnt,))
        k.tt("vector", RUN[0:16, 280:288], RUN[0:16, 280:288], PS_F[0:16, 8:16], ALU.add)
        k.cp("vector", pref_sl, PS_F[:, 16:24], incs=(("cadd", 1),))

    k.dma("gpsimd", agi_cnt, RUN[0:16, 280:288], sem=SM("cntw"), waits=(k.tok("cadd"),))
    k.collective("AllGather", [agi_cnt.opt()], [ago_cnt.opt()],
                 waits=((SM("cntw"), SM("cntw").n),), incs=(("A2Ac", 1),))
    agc = SMALL[:, 160:168]
    k.dma("sync", agc, ago_cnt, sem=SM("agcl"), waits=((SM("A2Ac"), 1),))
    k.mm(PS_F[0:16, 24:32], m1c[:], agc[:], start=True, stop=True,
         waits=((SM("agcl"), SM("agcl").n), k.tok("cadd")), incs=(("amm", 1),))
    A_sb = RUN[0:16, 288:296]
    k.cp("vector", A_sb, PS_F[0:16, 24:32], waits=(k.tok("amm"),), incs=(("aev", 1),))
    tmp16 = RUN[0:16, 296:304]
    rank = SMALL[:, 168:176]
    tmp8 = SMALL[:, 176:184]
    ro = SMALL[:, 184:185]
    kp = SMALL[:, 185:186]
    sc = SMALL[:, 186:187]
    for dt in range(NT):
        mask_sl = GT[:, dt * 24: dt * 24 + 8]
        pref_sl = GT[:, dt * 24 + 8: dt * 24 + 16]
        gv_sl = GT[:, dt * 24 + 16: dt * 24 + 17]
        k.tt("vector", tmp16, A_sb, RUN[0:16, dt * 8:(dt + 1) * 8], ALU.add,
             waits=(k.tok("aev"),) if dt == 0 else ((SM("rkmm"), SM("rkmm").n),),
             incs=(("t16", 1),))
        k.mm(PS_F[:, 32:40], selST[:], tmp16, start=True, stop=True,
             waits=(k.tok("t16"),), incs=(("rkmm", 1),))
        k.stt("vector", rank, PS_F[:, 32:40], 1.0, pref_sl, ALU.mult, ALU.add,
              waits=(k.tok("rkmm"),))
        k.tt("vector", tmp8, rank, mask_sl, ALU.mult)
        k.red("vector", ro[:], tmp8, ALU.add)
        k.ts("vector", kp[:], ro[:], 4096.0, ALU.is_lt)
        k.tt("vector", sc[:], kp[:], gv_sl, ALU.mult, incs=(("scv", 1),))
        k.ts("vector", S8[:, dt * 8:(dt + 1) * 8], mask_sl, sc[:], ALU.mult,
             incs=(("s8v", 1),))


    ytmp = [TRANS[:, 256:1280], TRANS[:, 1280:2304]]
    y1t_b = [TRANS[:, 3328:4352], TRANS[:, 4352:5376]]
    for e in range(E):
        for jc in range(8):
            w2 = [(SM("AG_we"), 1)] if (e == 0 and jc == 0) else []
            if e == 0 and jc == 0:
                w2 += [k.tok("avmm"), k.tok("PMM"), k.tok("bdev"), (SM("vrp"), SM("vrp").n)]
            if e >= 1 and jc == 0:
                w2.append(moe_prev_mm)
            k.dma("sync", BIGB[:, jc * 1024:(jc + 1) * 1024],
                  ago_we[e * 1024 + jc * 128: e * 1024 + (jc + 1) * 128, :],
                  sem=SM("wel"), waits=tuple(w2))
        k.dma("sync", rowbuf2[0:1, 0:1024], ago_ce[e:e + 1, :], sem=SM("cel"),
              waits=((SM("AG_ce"), 1),) if e == 0 else (k.tok("moemm"),))
        tok_wel = k.tok("wel")
        tok_cel = k.tok("cel")
        ROT4 = (PS_B[:, 0:512], PS_B[:, 512:1024], PS_C[:, 0:512], PS_E[:, 0:512])
        for dt in range(NT):
            par = dt % 2
            idx = e * NT + dt
            k.dma("sync", y1t_b[par], y1T_scr.rearrange("(c p) t -> p c t", c=8)
                  [:, :, dt * 128:(dt + 1) * 128], sem=SM("y1tl"),
                  waits=((SM("y1tw"), 16 * NT) if (e == 0 and dt == 0) else None,
                         (SM("moemm"), 2 * (idx - 1)) if idx >= 2 else None))
            tok_yl = k.tok("y1tl")
            for dh in range(2):
                ps = ROT4[(dt * 2 + dh) % 4]
                k.wait("tensor", PEV, PEV.n - 3)
                for jc in range(8):
                    w2 = []
                    if jc == 0:
                        w2 = [tok_yl]
                        if dt == 0:
                            w2.append(tok_wel)

                    k.mm(ps, y1t_b[par][:, jc * 128:(jc + 1) * 128],
                         BIGB[:, jc * 1024 + dh * 512: jc * 1024 + dh * 512 + 512],
                         start=(jc == 0), stop=False, waits=tuple(w2))
                k.mm(ps, onesrow[0:1, :], rowbuf2[0:1, dh * 512: dh * 512 + 512],
                     start=False, stop=True,
                     waits=(tok_cel,) if (dt == 0 and dh == 0) else (),
                     incs=(("moemm", 1),))
                k.ts("vector", ytmp[par][:, dh * 512:(dh + 1) * 512], ps,
                     S8[:, dt * 8 + e: dt * 8 + e + 1], ALU.mult,
                     waits=(k.tok("moemm"), k.tok("s8v") if (e == 0 and dt == 0) else None,
                            (SM("macc"), SM("macc").n - 16) if (e * NT + dt >= 2) else None),
                     incs=((PEV, 1),) + (() if dh == 0 else ()))
            k.dma("gpsimd", moe0_scr[dt * 128:(dt + 1) * 128, :], ytmp[par],
                  sem=SM("macc"), waits=(k.tok("PEV"),), accum=(e > 0))
        moe_prev_mm = k.tok("moemm")
    for si, row in [(0, 8), (1, 9)]:
        k.dma("sync", cslot(si), cvec[row], sem=SM("cld"),
              waits=((SM("LSd"), SM("LSd").n), k.tok("PEV")) if si == 0 else ())
    cld3 = k.tok("cld")
    d5a = BIGB[:, 0:1024]
    d5b = BIGB[:, 1024:2048]
    d5y = BIGB[:, 2048:3072]
    d5g = BIGB[:, 3072:4096]
    d5sq = BIGB[:, 4096:5120]
    for dt in range(NT):
        w2 = [(SM("macc"), SM("macc").n)] if dt == 0 else [(SM("lnf5"), SM("lnf5").n)]
        if dt == 0:
            w2 += [k.tok("moemm"), k.tok("PEV")]
        k.dma("sync", d5a, y1_scr[dt * 128:(dt + 1) * 128, :], sem=SM("d5l"), waits=tuple(w2))
        k.dma("sync", d5b, moe0_scr[dt * 128:(dt + 1) * 128, :], sem=SM("d5l"))
        k.tt("vector", d5y, d5a, d5b, ALU.add, waits=((SM("d5l"), SM("d5l").n),),
             incs=(("hy2", 1),))
        pre_ln = [k.tok("hy2")] + ([cld3] if dt == 0 else []) + \
                 ([(SM("ccb"), 16 * dt * (2 if stage == "g" else 1))] if dt >= 1 else [])
        tok_y2 = k.ln(d5y, d5y, d5sq, stat8(3), eps5, D, gamma=cslot(0), beta=cslot(1),
                      pre=tuple(pre_ln), sem_v="LV5", sem_s="LS5")
        k.act(d5g, d5y, AF.Silu, waits=(tok_y2,), incs=(("lnf5", 1),))
        k.dma("gpsimd", ccdet_ap(cc2_in, dt), d5g, sem=SM("ccb"),
              waits=(k.tok("lnf5"),))
        if stage == "g":
            k.dma("gpsimd", yout[dt * 128:(dt + 1) * 128, :], d5g, sem=SM("ccb"))
    if stage in ("y1", "g"):
        k.wait("gpsimd", SM("y1w" if stage == "y1" else "ccb"),
               SM("y1w" if stage == "y1" else "ccb").n)
        k.build_block()
        return nc

    k.collective("AllToAll", [cc2_in.opt()], [cc2_out.opt()],
                 waits=((SM("ccb"), SM("ccb").n),), incs=(("A2A2", 1),))

    for jc in range(8):
        k.dma("sync", BIGA[:, jc * 3072:(jc + 1) * 3072],
              ag["wqkv1"][1][jc * 128:(jc + 1) * 128, :], sem=SM("wq1l"),
              waits=((k.tok("AG_wqkv1"), k.tok("moemm"), k.tok("PMM"), k.tok("wemm"))
                     if jc == 0 else ()))
    tok_wq1 = k.tok("wq1l")
    c1x = BIGB[:, 0:1024]
    c1s = BIGB[:, 1024:2048]
    c1g = BIGB[:, 2048:3072]
    c1x1 = BIGB[:, 3072:4096]
    c1xn = BIGB[:, 4096:5120]
    c1xnT = BIGB[:, 5120:6144]
    c1qkv = BIGB[:, 6144:9216]
    c1sq = BIGB[:, 9216:10240]
    prev1 = {}
    for ti in range(NT):
        b_loc, s0 = ti // 16, (ti % 16) * 128
        i_blk = s0 // 256
        r0 = i_blk * 512 + b_loc * 256 + (s0 % 256)
        w0 = [(SM("A2A2"), 1)] if ti == 0 else [prev1["x1cmb"]]
        if ti == 0:
            w0 += [k.tok("lnf5"), k.tok("LS5"), k.tok("moemm"), k.tok("PEV"),
                   k.tok("hy2"), (SM("d5l"), SM("d5l").n)]
        k.dma("sync", c1x, xin[ti * 128:(ti + 1) * 128, :], sem=SM("c1l"), waits=tuple(w0))
        k.dma("sync", c1s, s_scr[ti * 128:(ti + 1) * 128, :], sem=SM("c1l"))
        k.dma("sync", c1g, cc2_out[r0:r0 + 128, :], sem=SM("c1l"))
        tok_l = k.tok("c1l")
        w1_ = [tok_l] + ([prev1["x1w"], prev1["xn1_tok"]] if ti >= 1 else [])
        k.tt("vector", c1s, c1s, c1g, ALU.mult, waits=tuple(w1_))
        k.tt("vector", c1x1, c1x, c1s, ALU.add, incs=(("x1c", 1),))
        prev1["x1cmb"] = k.tok("x1c")
        k.dma("gpsimd", x1_scr[ti * 128:(ti + 1) * 128, :], c1x1, sem=SM("x1w"),
              waits=(k.tok("x1c"),))
        if stage == "x1":
            if os.environ.get("KB_X1DUMP") == "cc2raw":
                k.dma("gpsimd", yout[ti * 128:(ti + 1) * 128, :],
                      cc2_in[ti * 128:(ti + 1) * 128, :], sem=SM("x1w"),
                      waits=((SM("ccb"), 16 * NT) if ti == 0 else None,))
            elif os.environ.get("KB_X1DUMP") == "cc2oraw":
                k.dma("gpsimd", yout[ti * 128:(ti + 1) * 128, :],
                      cc2_out[ti * 128:(ti + 1) * 128, :], sem=SM("x1w"),
                      waits=(((SM("A2A2"), 1)) if ti == 0 else None,))
            elif os.environ.get("KB_X1DUMP"):
                dumpsrc = {"g": c1g, "s": c1s, "x": c1x}.get(os.environ.get("KB_X1DUMP", ""), c1x1)
                k.dma("gpsimd", yout[ti * 128:(ti + 1) * 128, :], dumpsrc, sem=SM("x1w"))
        prev1["x1w"] = k.tok("x1w")
        pre = [k.tok("x1c")] + ([prev1["xnT_mm"]] if ti >= 1 else [])
        tok_xn1 = k.ln(c1xn, c1x1, c1sq, stat8(4), eps5, D, pre=tuple(pre),
                       sem_v="LV6", sem_s="LS6")
        prev1["xn1_tok"] = tok_xn1
        k.dma("gpsimd", xn1_scr[ti * 128:(ti + 1) * 128, :], c1xn, sem=SM("xnTw"),
              waits=(tok_xn1,))
        for c in range(8):
            tponce(c1xnT[:, c * 128:(c + 1) * 128], c1xn[:, c * 128:(c + 1) * 128],
                   extra_waits=(tok_xn1,) if c == 0 else (),
                   evac_waits=(prev1["qkv_mm"], (SM("xnTw"), 16 * ti)) if (c == 0 and ti >= 1) else ())
        prev1["xnT_mm"] = k.tok("tpmm")
        tok_xnT = k.tok("TPEV")
        def evq1(h, ps, w2):
            w2 = list(w2) + ([prev1["q1sp"]] if ti >= 1 else [])
            k.cp("vector", c1qkv[:, h * 512:(h + 1) * 512], ps, waits=tuple(w2),
                 incs=((PEV, 1),))
        proj_tm(c1xnT, 0, 3072, evq1, istride=3072,
                first_waits=(tok_xnT,) + ((tok_wq1,) if ti == 0 else ()))
        prev1["qkv_mm"] = k.tok("PMM")
        k.dma("gpsimd", qkv1_scr[ti * 128:(ti + 1) * 128, :], c1qkv, sem=SM("q1sp"),
              waits=(k.tok("PEV"),))
        prev1["q1sp"] = k.tok("q1sp")
        for jc in range(8):
            w2 = []
            if jc == 0:
                w2 = [tok_xnT, (SM("idl"), SM("idl").n)]
                if ti >= 1:
                    w2.append((SM("g1ev"), SM("g1ev").n))
            k.mm(PS_F[:, 0:8], c1xnT[:, jc * 128:(jc + 1) * 128], wg1[:, jc * 8:(jc + 1) * 8],
                 start=(jc == 0), stop=(jc == 7), waits=tuple(w2),
                 incs=((("g1mm", 1),) if jc == 7 else ()))
        gl = SMALL[:, 192:200]
        mm1 = SMALL[:, 200:201]
        mm2 = SMALL[:, 201:202]
        mlg = SMALL[:, 208:216]
        w12 = SMALL[:, 216:218]
        msk1 = SMALL[:, 224:232]
        msk2 = SMALL[:, 232:240]
        k.cp("vector", gl[:], PS_F[:, 0:8], waits=(k.tok("g1mm"),), incs=(("g1ev", 1),))
        k.red("vector", mm1[:], gl[:], ALU.max)
        k.ts("vector", msk1[:], gl[:], mm1[:], ALU.is_equal, incs=(("g1v", 1),))
        k.stt("vector", mlg[:], msk1[:], -1e30, gl[:], ALU.mult, ALU.add)
        k.red("vector", mm2[:], mlg[:], ALU.max)
        k.ts("vector", msk2[:], mlg[:], mm2[:], ALU.is_equal)
        k.tt("vector", w12[:, 0:1], mm1[:], mm2[:], ALU.subtract,
             waits=(((SM("slw"), SM("slw").n),) if ti >= 1 else ()),
             incs=(("g1v", 1),))
        k.act(w12[:, 0:1], w12[:, 0:1], AF.Sigmoid, waits=(k.tok("g1v"),),
              incs=(("g1s", 1),))
        k.ts("vector", w12[:, 1:2], w12[:, 0:1], -1.0, ALU.mult,
             waits=(k.tok("g1s"),))
        k.ts("vector", w12[:, 1:2], w12[:, 1:2], 1.0, ALU.add, incs=(("VM", 1),))
        um8 = SMALL[:, 256:264]
        k.tt("vector", um8[:], msk1[:], msk2[:], ALU.add,
             waits=((k.tok("pfmm"),) if ti >= 1 else ()), incs=(("umv", 1),))
        k.dma("gpsimd", wv_scr[ti * 128:(ti + 1) * 128, :], w12[:, 0:2], sem=SM("slw"),
              waits=(k.tok("g1s"), (SM("VM"), SM("VM").n)))
        k.mm(PS_D[:, 0:8], tri0[:], um8[:], start=True, stop=True,
             waits=(k.tok("umv"),) + ((k.tok("rkv"),) if ti >= 1 else ()),
             incs=(("pfmm", 1),))
        k.mm(PS_D[0:1, 8:16], onescol[:, 0:1], um8[:], start=True, stop=True,
             incs=(("pfmm", 1),))
        run8 = SMALL[:, 264:272]
        if ti == 0:
            k.memset("vector", run8[0:1, :], 0.0, incs=(("runv", 1),))
        k.tt("vector", SMALL[0:1, 272:280], run8[0:1, :], cpe_base[0:1, :], ALU.add,
             waits=((SM("runv"), SM("runv").n),) + ((k.tok("pfmm"),) if ti >= 1 else ()),
             incs=(("runv", 1),))
        k.mm(PS_E[:, 0:8], onesrow[0:1, :], SMALL[0:1, 272:280], start=True, stop=True,
             waits=(k.tok("runv"),), incs=(("pfmm", 1),))
        k.tt("vector", run8[0:1, :], run8[0:1, :], PS_D[0:1, 8:16], ALU.add,
             waits=(k.tok("pfmm"),), incs=(("runv", 1),))
        rank8 = SMALL[:, 280:288]
        k.cp("vector", rank8[:], PS_D[:, 0:8], waits=(k.tok("pfmm"),))
        k.tt("vector", rank8[:], rank8[:], PS_E[:, 0:8], ALU.add)
        tmp8b = SMALL[:, 288:296]
        slotf = SMALL[:, 296:298]
        for kk_, mk in ((0, msk1), (1, msk2)):
            k.tt("vector", tmp8b[:], rank8[:], mk[:], ALU.mult)
            k.red("vector", slotf[:, kk_:kk_ + 1], tmp8b[:], ALU.add)
        k.ts("vector", slotf[:, 0:2], slotf[:, 0:2], float(8 * 1536), ALU.min,
             incs=(("rkv", 1),))
        k.cp("vector", IDXS[:, 0:2], slotf[:, 0:2],
             waits=(k.tok("rkv"),) + (((SM("sctw"), SM("sctw").n),) if ti >= 1 else ()),
             incs=(("rkv", 1),))
        k.dma("gpsimd", slots_scr[ti * 128:(ti + 1) * 128, :], IDXS[:, 0:2],
              sem=SM("slw"), waits=(k.tok("rkv"),))
        k.emit("gpsimd", lambda e, o=IDXS[:, 2:3], b=ti * 128:
               e.iota(o, pattern=[[0, 1]], base=b, channel_multiplier=1),
               waits=((SM("tofz"), SM("tofz").n), (SM("sctw"), SM("sctw").n)),
               incs=(("sctw", 16),))
        from concourse.bass_core_rust import IndirectOffsetOnAxis as _IOA
        for kk_ in range(2):
            k.emit("gpsimd", lambda e, ix=IDXS[:, kk_:kk_ + 1], tk=IDXS[:, 2:3]:
                   e.indirect_dma_start(tok_slot[:, :], _IOA(ix, 0), tk, None),
                   waits=((SM("sctw"), SM("sctw").n), k.tok("rkv")),
                   incs=(("sctw", 16),), drain=False)
        k.ts("vector", msk1[:], msk1[:], w12[:, 0:1], ALU.mult,
             waits=((SM("slw"), SM("slw").n),))
        k.ts("vector", msk2[:], msk2[:], w12[:, 1:2], ALU.mult)
        k.tt("vector", S8[:, 256 + ti * 8: 256 + (ti + 1) * 8], msk1[:], msk2[:],
             ALU.add, incs=(("s81", 1),))
    if stage == "x1":
        if not os.environ.get("KB_X1DUMP"):
            for ti in range(NT):
                k.dma("gpsimd", yout[ti * 128:(ti + 1) * 128, :],
                      x1_scr[ti * 128:(ti + 1) * 128, :], sem=SM("yw"),
                      waits=((SM("x1w"), SM("x1w").n) if ti == 0 else None,))
            k.wait("gpsimd", SM("yw"), SM("yw").n)
        k.wait("gpsimd", SM("x1w"), SM("x1w").n)
        k.build_block()
        return nc

    def vtile(kt):
        if kt < 8:
            return BIGA[:, 16384 + kt * 1024: 16384 + (kt + 1) * 1024]
        return BIGB[:, (kt - 8) * 1024: (kt - 7) * 1024]

    qtb = BIGB[:, 8192:9216]
    qTb = BIGB[:, 9216:10240]
    probs = BIGB[:, 10240:12288]
    pT = [GT[:, 0:128], GT[:, 128:256]]
    o1sb = TRANS[:, 0:1024]
    rr1 = SMALL[:, 240:241]
    mx4 = SMALL[:, 244:248]
    nmb = SMALL[:, 248:249]
    prev2 = {}
    for r in range(B):
        for kt in range(16):
            tt_g = [(SM("q1sp"), 16 * NT)] if (r == 0 and kt == 0) else []
            if r == 1 and kt == 0:
                tt_g = [prev2["sc1_mm"], k.tok("PMM")]
            k.dma("sync", TRANS[:, 1024:2048],
                  qkv1_scr[r * 2048 + kt * 128: r * 2048 + (kt + 1) * 128, 1024:2048],
                  sem=SM("ktl"), waits=tuple(tt_g) + ((TPEV, TPEV.n),))
            tk = k.tok("ktl")
            for c in range(8):
                tponce(BIGA[:, c * 2048 + kt * 128: c * 2048 + (kt + 1) * 128],
                       TRANS[:, 1024 + c * 128: 1024 + (c + 1) * 128],
                       extra_waits=(tk,) if c == 0 else ())
        tok_kT = k.tok("TPEV")
        for kt in range(16):
            w2 = []
            if kt == 0:
                w2 = [tok_kT] + ([prev2["av_mm"]] if r == 1 else [k.tok("wq1l"), k.tok("g1mm")])
            k.dma("sync", vtile(kt),
                  qkv1_scr[r * 2048 + kt * 128: r * 2048 + (kt + 1) * 128, 2048:3072],
                  sem=SM("v1l"), waits=tuple(w2))
        tok_v1 = k.tok("v1l")
        for qt in range(16):
            ti_g = r * 16 + qt
            k.dma("sync", qtb, qkv1_scr[r * 2048 + qt * 128: r * 2048 + (qt + 1) * 128, 0:1024],
                  sem=SM("q1l"), waits=(prev2.get("qT_mm"),))
            tq = (SM("q1l"), SM("q1l").n)
            for c in range(8):
                tponce(qTb[:, c * 128:(c + 1) * 128], qtb[:, c * 128:(c + 1) * 128],
                       extra_waits=(tq, tok_v1) if c == 0 else (),
                       evac_waits=(prev2["sc1_mm"],) if "sc1_mm" in prev2 else ())
            prev2["qT_mm"] = k.tok("tpmm")
            tok_qT = k.tok("TPEV")
            for h in range(16):
                pb = 64 * (h % 2)
                segs = [PS_B[:, 0:512], PS_B[:, 512:1024], PS_C[:, 0:512], PS_E[:, 0:512]]
                for kb_i in range(4):
                    w2 = []
                    if kb_i == 0:
                        w2 = [tok_qT]
                        w2.append((SM("SCM"), SM("SCM").n))
                        if h == 0:
                            w2.append((PEV, PEV.n))
                    k.mm(segs[kb_i],
                         qTb[pb:pb + 64, (h // 2) * 128:(h // 2) * 128 + 128],
                         BIGA[pb:pb + 64, (h // 2) * 2048 + kb_i * 512: (h // 2) * 2048 + (kb_i + 1) * 512],
                         start=True, stop=True, waits=tuple(w2),
                         incs=((("sc1mm", 1),) if kb_i == 3 else ()))
                tok_sc = k.tok("sc1mm")
                for kb_i in range(4):
                    k.red("vector", mx4[:, kb_i:kb_i + 1], segs[kb_i], ALU.max,
                          waits=(tok_sc,) if kb_i == 0 else ())
                k.red("vector", nmb[:], mx4[:, 0:4], ALU.max, negate=True)
                k.ts("vector", nmb[:], nmb[:], 1.0 / 32.0, ALU.mult, incs=(("sm1v", 1),))
                for kb_i in range(4):
                    k.act(probs[:, kb_i * 512:(kb_i + 1) * 512], segs[kb_i], AF.Exp,
                          bias=nmb[:], scale=1.0 / 32.0,
                          waits=(k.tok("sm1v"),) if kb_i == 0 else (),
                          incs=((("SCM", 1),) if kb_i == 3 else ()))
                tok_exp = (SM("SCM"), SM("SCM").n)
                k.red("vector", rr1[:], probs[:], ALU.add, waits=(tok_exp,))
                k.emit("vector", lambda e, o=rr1[:], i=rr1[:]: e.reciprocal(o, i),
                       incs=(("VM", 1),))
                tok_rr = (SM("VM"), SM("VM").n)
                ops_ = PS_D if h % 2 == 0 else PS_F
                pTbuf = TRANS[:, 2304:4352]
                for kb4 in range(4):
                    k.pyctr["tpar"] = k.pyctr.get("tpar", 0) + 1
                    pslot = k.pyctr["tpar"] % 2
                    for q4 in range(4):
                        kt = kb4 * 4 + q4
                        k.emit("tensor",
                               lambda ee, o=PS_A[:, pslot * 512 + q4 * 128: pslot * 512 + (q4 + 1) * 128],
                               i=probs[:, kt * 128:(kt + 1) * 128]:
                                   ee.transpose(o, i, ident[0:128, 0:128]),
                               waits=((tok_exp,) if kt == 0 else ()) +
                                     (((TPEV, TPEV.n - 1),) if q4 == 0 else ()),
                               incs=(("tpmm", 1),))
                    lag = 4 * (3 - kb4)
                    gv_ = SM("av1").n - lag
                    k.cp("vector", pTbuf[:, kb4 * 512:(kb4 + 1) * 512],
                         PS_A[:, pslot * 512: pslot * 512 + 512],
                         waits=(k.tok("tpmm"),) +
                               (((SM("av1"), gv_),) if gv_ >= 1 else ()),
                         incs=((TPEV, 1),))
                tok_pT = k.tok("TPEV")
                for kt in range(16):
                    w2 = []
                    if kt == 0:
                        w2 = [tok_pT, (SM("oev1"), SM("oev1").n)]
                    elif kt % 4 == 0:
                        pass
                    k.mm(ops_[:, 0:64], pTbuf[:, kt * 128:(kt + 1) * 128],
                         vtile(kt)[:, h * 64:(h + 1) * 64],
                         start=(kt == 0), stop=(kt == 15), waits=tuple(w2),
                         incs=(("av1", 1),))
                k.ts("vector", o1sb[:, h * 64:(h + 1) * 64], ops_[:, 0:64], rr1[:],
                     ALU.mult, waits=(k.tok("av1"), tok_rr), incs=(("oev1", 1),))
            prev2["sc1_mm"] = k.tok("sc1mm")
            prev2["av_mm"] = k.tok("av1")
            k.dma("gpsimd", o1_scr[ti_g * 128:(ti_g + 1) * 128, :], o1sb,
                  sem=SM("o1w"), waits=(k.tok("oev1"),))
            prev2["o1_w"] = k.tok("o1w")
    for jc in range(8):
        k.dma("sync", BIGA[:, jc * 1024:(jc + 1) * 1024],
              ag["wout1"][1][jc * 128:(jc + 1) * 128, :], sem=SM("wo1l"),
              waits=((k.tok("AG_wout1"), k.tok("sc1mm"), k.tok("av1"), k.tok("TPEV"))
                     if jc == 0 else ()))
    tok_wo1 = k.tok("wo1l")
    c2b_o = BIGB[:, 0:1024]
    c2b_oT = BIGB[:, 1024:2048]
    c2b_s = BIGB[:, 2048:3072]
    prev3 = {}
    for ti in range(NT):
        k.dma("sync", c2b_o, o1_scr[ti * 128:(ti + 1) * 128, :], sem=SM("o1l"),
              waits=((SM("o1w"), 16 * NT), k.tok("v1l"), (SM("q1l"), SM("q1l").n))
                    if ti == 0 else (prev3["oT_mm"],))
        to_ = (SM("o1l"), SM("o1l").n)
        for c in range(8):
            tponce(c2b_oT[:, c * 128:(c + 1) * 128], c2b_o[:, c * 128:(c + 1) * 128],
                   extra_waits=(to_,) if c == 0 else (),
                   evac_waits=(prev3["sp_mm"],) if (c == 0 and ti >= 1) else ())
        prev3["oT_mm"] = k.tok("tpmm")
        tok_oT2 = k.tok("TPEV")

        def evs1(h, ps, w2):
            w2 = list(w2) + ([(SM("s1w"), 16 * ti)] if ti >= 1 else [])
            k.cp("vector", c2b_s[:, h * 512:(h + 1) * 512], ps, waits=tuple(w2),
                 incs=((PEV, 1),))
        proj_tm(c2b_oT, 0, 1024, evs1,
                first_waits=(tok_oT2,) + ((tok_wo1,) if ti == 0 else ()))
        prev3["sp_mm"] = k.tok("PMM")
        k.dma("gpsimd", s1_scr[ti * 128:(ti + 1) * 128, :], c2b_s, sem=SM("s1w"),
              waits=(k.tok("PEV"),))
        if stage == "s1":
            k.dma("gpsimd", yout[ti * 128:(ti + 1) * 128, :], c2b_s, sem=SM("s1y"))
    if stage == "s1":
        k.wait("gpsimd", SM("s1y"), SM("s1y").n)
        k.build_block()
        return nc

    from concourse.bass_core_rust import IndirectOffsetOnAxis as _IOA2
    CG = 3
    hq = TRANS[:, 0:4096]
    hq_h = TRANS[:, 0:2048].bitcast(BF16)
    hq_l = TRANS[:, 2048:4096].bitcast(BF16)
    stg = TRANS[:, 0:4096]
    ytmp1 = TRANS[:, 4096:5120]
    cxT = BIGB[:, 0:12288]
    for e in range(E):
        for ct in range(12):
            sbase = e * CPE + ct * 128
            w2 = []
            if e == 0 and ct == 0:
                w2 = [(SM("sctw"), SM("sctw").n), (SM("xnTw"), 16 * NT),
                      k.tok("PMM"), k.tok("PEV"), (SM("s1w"), SM("s1w").n),
                      (SM("o1l"), SM("o1l").n)]
            if ct >= 1 or e >= 1:
                w2.append(k.tok("cxg"))
            k.dma("sync", IDXS[:, 4:5], tok_slot[sbase:sbase + 128, :], sem=SM("ixl"),
                  waits=tuple(w2))
            gw = [(SM("ixl"), SM("ixl").n)]
            if ct >= 4 or e >= 1:
                gw.append((SM("tpmm"), SM("tpmm").n - 24))
            if e >= 1 and ct == 0:
                gw.append(k.tok("m1mm"))
            k.emit("gpsimd", lambda ee, o=stg[:, (ct % 4) * 1024:(ct % 4) * 1024 + 1024],
                   ix=IDXS[:, 4:5]:
                   ee.indirect_dma_start(o, None, xn1_scr[:, :], _IOA2(ix, 0)),
                   waits=tuple(gw), incs=(("cxg", 16),), drain=False)
            tok_g = k.tok("cxg")
            for c in range(8):
                ev_w = ()
                if c == 0 and e >= 1 and ct == 0:
                    ev_w = (k.tok("h1mm"),)
                k.pyctr["tpar"] = k.pyctr.get("tpar", 0) + 1
                pslot = k.pyctr["tpar"] % 2
                pst = PS_A[:, pslot * 512: pslot * 512 + 128]
                k.emit("tensor",
                       lambda ee, o=pst, i=stg[:, (ct % 4) * 1024 + c * 128:(ct % 4) * 1024 + (c + 1) * 128]:
                           ee.transpose(o, i, ident[0:128, 0:128]),
                       waits=((tok_g,) if c == 0 else ()) + ((TPEV, TPEV.n - 1),),
                       incs=(("tpmm", 1),))
                co = c * 1536 + ct * 128
                hi_ap = BIGBh[:, co:co + 128]
                lo_ap = BIGBh[:, 12288 + co: 12288 + co + 128]
                k.cp("vector", hi_ap, pst, waits=(k.tok("tpmm"),) + tuple(ev_w))
                k.stt("vector", lo_ap, hi_ap, -1.0, pst, ALU.mult, ALU.add,
                      incs=((TPEV, 1),))
        tok_cxT = k.tok("TPEV")
        k.dma("sync", rowbuf[0:1, 0:1024], bout_r[0:1, e * 1024:(e + 1) * 1024],
              sem=SM("bo1l"),
              waits=(((SM("idl"), SM("idl").n),) if e == 0 else (k.tok("m1mm"),)))
        tok_bo = k.tok("bo1l")
        for fq in range(4):
            for jc in range(8):
                w2 = [(SM("wspl"), SM("wspl").n)]
                if jc == 0:
                    w2 += [(SM("AG_win"), 1)]
                k.dma("sync", BIGA[:, 16384 + jc * 1024: 16384 + (jc + 1) * 1024],
                      ag["win"][1][e * 1024 + jc * 128: e * 1024 + (jc + 1) * 128,
                                   fq * 1024:(fq + 1) * 1024],
                      sem=SM("wi1l"), waits=tuple(w2))
            twl = k.tok("wi1l")
            for jc in range(8):
                stga = BIGA[:, 16384 + jc * 1024: 16384 + (jc + 1) * 1024]
                w2 = [twl] if jc == 0 else []
                if jc == 0:
                    w2.append(k.tok("h1mm"))
                k.cp("vector", BIGAh[:, jc * 1024:(jc + 1) * 1024], stga, waits=tuple(w2))
                k.stt("vector", BIGAh[:, 8192 + jc * 1024: 8192 + (jc + 1) * 1024],
                      BIGAh[:, jc * 1024:(jc + 1) * 1024], -1.0, stga,
                      ALU.mult, ALU.add, incs=(("wspl", 1),))
            for fc in range(8):
                w2 = [(SM("wspl"), SM("wspl").n)]
                if fc == 0:
                    w2 += [(SM("AG_wout"), 1)]
                k.dma("sync", BIGA[:, 16384 + fc * 1024: 16384 + (fc + 1) * 1024],
                      ag["wout"][1][e * 4096 + fq * 1024 + fc * 128:
                                    e * 4096 + fq * 1024 + (fc + 1) * 128, :],
                      sem=SM("wi2l"), waits=tuple(w2))
            two = k.tok("wi2l")
            for fc in range(8):
                stga = BIGA[:, 16384 + fc * 1024: 16384 + (fc + 1) * 1024]
                w2 = [two, k.tok("m1mm")] if fc == 0 else []
                k.cp("vector", BIGAh[:, 16384 + fc * 1024: 16384 + (fc + 1) * 1024],
                     stga, waits=tuple(w2))
                k.stt("vector", BIGAh[:, 24576 + fc * 1024: 24576 + (fc + 1) * 1024],
                      BIGAh[:, 16384 + fc * 1024: 16384 + (fc + 1) * 1024], -1.0, stga,
                      ALU.mult, ALU.add, incs=(("wspl", 1),))
            tok_wi = k.tok("wspl")
            ROT = (PS_B[:, 0:512], PS_B[:, 512:1024], PS_C[:, 0:512], PS_E[:, 0:512])
            for gi in range(CG):
                for fc in range(8):
                    ps = ROT[fc % 4]
                    k.wait("tensor", PEV, PEV.n - 3)
                    for jc in range(8):
                        w2 = [tok_wi, tok_cxT] if (fc == 0 and jc == 0) else []
                        wi_h = BIGAh[:, jc * 1024 + fc * 128: jc * 1024 + fc * 128 + 128]
                        wi_l = BIGAh[:, 8192 + jc * 1024 + fc * 128: 8192 + jc * 1024 + fc * 128 + 128]
                        cx_h = BIGBh[:, jc * 1536 + gi * 512: jc * 1536 + (gi + 1) * 512]
                        cx_l = BIGBh[:, 12288 + jc * 1536 + gi * 512: 12288 + jc * 1536 + (gi + 1) * 512]
                        k.mm(ps, wi_h, cx_h, start=(jc == 0), stop=False, waits=tuple(w2))
                        k.mm(ps, wi_h, cx_l, start=False, stop=False)
                        k.mm(ps, wi_l, cx_h, start=False, stop=(jc == 7),
                             incs=((("h1mm", 1),) if jc == 7 else ()))
                    k.act(ytmp1[:, 0:512], ps, AF.Gelu,
                          bias=binsb[:, e * 32 + fq * 8 + fc: e * 32 + fq * 8 + fc + 1],
                          waits=(k.tok("h1mm"), (SM("m1w"), SM("m1w").n)),
                          incs=(("SCM", 1),))
                    k.cp("vector", hq_h[:, fc * 512:(fc + 1) * 512], ytmp1[:, 0:512],
                         waits=((SM("SCM"), SM("SCM").n),))
                    k.stt("vector", hq_l[:, fc * 512:(fc + 1) * 512],
                          hq_h[:, fc * 512:(fc + 1) * 512], -1.0, ytmp1[:, 0:512],
                          ALU.mult, ALU.add, incs=((PEV, 1),))
                tok_h = k.tok("PEV")
                for tsub in range(4):
                    for dh in range(2):
                        ps = ROT[(tsub * 2 + dh) % 4]
                        k.wait("tensor", PEV, PEV.n - 3)
                        for fc in range(8):
                            w2 = [tok_h] if (fc == 0 and dh == 0 and tsub == 0) else []
                            last = (fc == 7 and fq > 0)
                            hh_ = hq_h[:, fc * 512 + tsub * 128: fc * 512 + tsub * 128 + 128]
                            hl_ = hq_l[:, fc * 512 + tsub * 128: fc * 512 + tsub * 128 + 128]
                            wo_h = BIGAh[:, 16384 + fc * 1024 + dh * 512: 16384 + fc * 1024 + dh * 512 + 512]
                            wo_l = BIGAh[:, 24576 + fc * 1024 + dh * 512: 24576 + fc * 1024 + dh * 512 + 512]
                            k.mm(ps, hh_, wo_h, start=(fc == 0), stop=False, waits=tuple(w2))
                            k.mm(ps, hh_, wo_l, start=False, stop=False)
                            k.mm(ps, hl_, wo_h, start=False, stop=last,
                                 incs=((("m1mm", 1),) if last else ()))
                        if fq == 0:
                            k.mm(ps, onesrow[0:1, :], rowbuf[0:1, dh * 512:(dh + 1) * 512],
                                 start=False, stop=True,
                                 waits=(tok_bo,) if (gi == 0 and tsub == 0 and dh == 0) else (),
                                 incs=(("m1mm", 1),))
                        wv2 = [k.tok("m1mm"), (SM("m1w"), SM("m1w").n)]
                        k.cp("vector", ytmp1[:, dh * 512:(dh + 1) * 512], ps,
                             waits=tuple(wv2), incs=((PEV, 1),))
                    rbase = e * CPE + gi * 512 + tsub * 128
                    k.dma("gpsimd", ycomp_scr[rbase:rbase + 128, :], ytmp1,
                          sem=SM("m1w"), waits=(k.tok("PEV"),), accum=(fq > 0))
    f4a = BIGB[:, 0:1024]
    f4b = BIGB[:, 1024:2048]
    f4c = BIGB[:, 2048:3072]
    f4g1 = BIGB[:, 3072:4096]
    f4g2 = BIGB[:, 4096:5120]
    f4w = SMALL[:, 304:306]
    for ti in range(NT):
        w2 = [(SM("m1w"), SM("m1w").n), k.tok("m1mm"), k.tok("PEV")] if ti == 0 else \
             [(SM("yw"), 16 * ti)]
        k.dma("sync", f4b, x1_scr[ti * 128:(ti + 1) * 128, :], sem=SM("f4l"),
              waits=tuple(w2))
        k.dma("sync", f4c, s1_scr[ti * 128:(ti + 1) * 128, :], sem=SM("f4l"),
              waits=((SM("s1w"), 16 * NT) if ti == 0 else None,))
        k.dma("sync", f4w, wv_scr[ti * 128:(ti + 1) * 128, :], sem=SM("f4l"),
              waits=((SM("slw"), SM("slw").n) if ti == 0 else None,))
        k.dma("sync", IDXS[:, 4:6], slots_scr[ti * 128:(ti + 1) * 128, :], sem=SM("f4i"),
              waits=((SM("c4g"), 32 * ti) if ti >= 1 else None,))
        ti4 = k.tok("f4i")
        k.emit("gpsimd", lambda ee, o=f4g1, ix=IDXS[:, 4:5]:
               ee.indirect_dma_start(o, None, ycomp_scr[:, :], _IOA2(ix, 0)),
               waits=(ti4, ((SM("VM"), SM("VM").n) if ti >= 1 else None)),
               incs=(("c4g", 16),), drain=False)
        k.emit("gpsimd", lambda ee, o=f4g2, ix=IDXS[:, 5:6]:
               ee.indirect_dma_start(o, None, ycomp_scr[:, :], _IOA2(ix, 0)),
               waits=(), incs=(("c4g", 16),), drain=False)
        tg4 = k.tok("c4g")
        tl = k.tok("f4l")
        k.ts("vector", f4a, f4g1, f4w[:, 0:1], ALU.mult,
             waits=(tg4, tl) + (((SM("yw"), 16 * ti),) if ti >= 1 else ()))
        k.stt("vector", f4a, f4g2, f4w[:, 1:2], f4a, ALU.mult, ALU.add,
              incs=(("hy4", 1),))
        k.act(f4a, f4a, AF.Silu, waits=(k.tok("hy4"),), incs=(("SCM", 1),))
        k.tt("vector", f4a, f4a, f4c, ALU.mult, waits=((SM("SCM"), SM("SCM").n),))
        k.tt("vector", f4a, f4a, f4b, ALU.add, incs=(("VM", 1),))
        k.dma("gpsimd", yout[ti * 128:(ti + 1) * 128, :], f4a, sem=SM("yw"),
              waits=(k.tok("VM"),))
    k.wait("gpsimd", SM("yw"), SM("yw").n)
    k.build_block()
    return nc


def _make_inmaps(ins):
    f32 = np.float32
    cores = []
    Wq, Wk, Wv, Wd = (ins["L0_attn_W" + n].astype(f32) for n in "qkvd")
    bvecs = [ins["L0_attn_gamma"], ins["L0_attn_beta"], ins["L0_attn_bq"], ins["L0_attn_bk"],
             ins["L0_attn_bv"], ins["L0_attn_bd"], ins["L0_ffn_ln1_g"], ins["L0_ffn_ln1_b"],
             ins["L0_ffn_ln2_g"], ins["L0_ffn_ln2_b"], ins["L0_ffn_out_proj_b"]]
    cvec = np.stack([np.ascontiguousarray(np.broadcast_to(np.asarray(v, f32), (128, 1024)))
                     for v in bvecs])
    x = np.asarray(ins["L0_attn_input_tensor"], f32)
    tt_ = np.arange(128)
    tri = ((tt_[:, None] % 16 == tt_[None, :] % 16) &
           (tt_[:, None] // 16 < tt_[None, :] // 16)).astype(f32)
    selS = (tt_[:, None] % 16 == np.arange(16)[None, :]).astype(f32)
    cl_ = np.arange(128) // 16
    ll_ = np.arange(128) % 16
    for c in range(8):
        m1 = ((ll_[:, None] < np.arange(16)[None, :]) |
              ((ll_[:, None] == np.arange(16)[None, :]) & (cl_[:, None] < c))).astype(f32)
        m = {
            "x_in": np.ascontiguousarray(x[2 * c:2 * c + 2].reshape(4096, 1024)),
            "wq_sl": np.ascontiguousarray(Wq[128 * c:128 * (c + 1)]),
            "wk_sl": np.ascontiguousarray(Wk[128 * c:128 * (c + 1)]),
            "wv_sl": np.ascontiguousarray(Wv[128 * c:128 * (c + 1)]),
            "wd_sl": np.ascontiguousarray(Wd[128 * c:128 * (c + 1)]),
            "ip_sl": np.ascontiguousarray(np.asarray(ins["L0_ffn_in_proj_w"], f32)[384 * c:384 * (c + 1)]),
            "op_sl": np.ascontiguousarray(np.asarray(ins["L0_ffn_out_proj_w"], f32)[128 * c:128 * (c + 1)]),
            "w1c": np.ascontiguousarray(np.asarray(ins["L0_ffn_w1"], f32)[c]),
            "w2c": np.ascontiguousarray(np.asarray(ins["L0_ffn_w2"], f32)[c]),
            "b1c": np.ascontiguousarray(np.asarray(ins["L0_ffn_b1"], f32)[c].reshape(32, 128).T),
            "b2_r": np.ascontiguousarray(np.asarray(ins["L0_ffn_b2"], f32)[c:c + 1]),
            "wg0_in": np.ascontiguousarray(np.asarray(ins["L0_ffn_wg"], f32)),
            "ipb_bc": np.ascontiguousarray(np.broadcast_to(np.asarray(ins["L0_ffn_in_proj_b"], f32), (128, 3072))),
            "m1c_in": m1,
            "selS_in": selS,
            "selST_in": np.ascontiguousarray(selS.T),
            "tri_in": tri,
            "tri0_in": (tt_[:, None] < tt_[None, :]).astype(f32),
            "ident_in": np.eye(128, dtype=f32),
            "cvec": cvec,
            "wqkv1_sl": np.ascontiguousarray(np.asarray(ins["L1_attn_w_qkv"], f32)[128 * c:128 * (c + 1)]),
            "wout1_sl": np.ascontiguousarray(np.asarray(ins["L1_attn_w_out"], f32)[128 * c:128 * (c + 1)]),
            "winc": np.ascontiguousarray(np.asarray(ins["L1_ffn_W_in"], f32)[c]),
            "woutc": np.ascontiguousarray(np.asarray(ins["L1_ffn_W_out"], f32)[c]),
            "binc": np.ascontiguousarray(
                np.asarray(ins["L1_ffn_b_in"], f32).reshape(8, 32, 128).transpose(2, 0, 1).reshape(128, 256)),
            "bout_r": np.ascontiguousarray(np.asarray(ins["L1_ffn_b_out"], f32).reshape(1, 8192)),
            "wg1_in": np.ascontiguousarray(np.asarray(ins["L1_ffn_W_gate"], f32)),
        }
        cores.append(m)
    return cores


def kernel(**inputs):
    from concourse.bass_utils import run_bass_kernel_spmd
    nc = build_core("full")
    in_maps = _make_inmaps(inputs)
    res = run_bass_kernel_spmd(nc, in_maps, core_ids=list(range(8)))
    out = np.stack([res.results[c]["y_out"].reshape(2, 2048, 1024) for c in range(8)])
    return np.ascontiguousarray(out.reshape(16, 2048, 1024))


# version 09
# speedup vs baseline: 1.0083x; 1.0038x over previous
import contextlib
import numpy as np
import concourse.bass as bass
import concourse.mybir as mybir

F32 = mybir.dt.float32
BF16 = mybir.dt.bfloat16
AF = mybir.ActivationFunctionType
ALU = mybir.AluOpType
AX = mybir.AxisListType

ENGS = ("sync", "tensor", "vector", "scalar", "gpsimd")

B = 2
S = 2048
D = 1024
T = B * S
NT = T // 128
NH0, HD0 = 16, 64
FH, FHD = 4, 256
E, FF = 8, 4096
NDET = 256
L = 16


class Sem:
    __slots__ = ("name", "h", "n")
    def __init__(self, name):
        self.name, self.h, self.n = name, None, 0


SEM_ALIAS = {}
for _n in ("LV2", "LVd", "LV5", "LV6"):
    SEM_ALIAS[_n] = "LNV"
for _n in ("LS2", "LSd", "LS5", "LS6"):
    SEM_ALIAS[_n] = "LNS"
for _n in ("gv1", "gv2", "g1v", "t16", "scv", "s8v", "aev", "cadd", "cedone",
           "ksdone", "dzero", "dmemset", "crec", "bdev", "zev", "hy2", "x1c",
           "QE", "KE", "ELUV", "ONES", "EPS"):
    SEM_ALIAS[_n] = "VM"
for _n in ("ELUS", "gs1", "g1s", "eexp", "lnf5"):
    SEM_ALIAS[_n] = "SCM"
for _n in ("oev1", "gev", "g1ev", "umv", "runv", "rkv", "hy4", "s81", "sm1v"):
    SEM_ALIAS[_n] = "VM"
for _n in ("PMM", "wemm", "cemm", "amm", "atmm", "cntmm", "dmm", "g1mm", "gmm",
           "h1mm", "sc1mm", "smm", "rkmm", "ctxmm", "av1", "kcons", "pfmm"):
    SEM_ALIAS[_n] = "TM"


class KB:
    def __init__(self, nc):
        self.nc = nc
        self.ops = {e: [] for e in ENGS}
        self.sems = {}
        self.stack = contextlib.ExitStack()
        self.sb = {}
        self.dr = {}
        self.pyctr = {}

    def sem(self, name):
        if isinstance(name, Sem):
            return name
        name = SEM_ALIAS.get(name, name)
        if name not in self.sems:
            s = Sem(name)
            s.h = self.stack.enter_context(self.nc.semaphore(name))
            self.sems[name] = s
        return self.sems[name]

    def sbuf(self, name, shape, dtype=F32):
        t = self.stack.enter_context(self.nc.sbuf_tensor(name, list(shape), dtype))
        self.sb[name] = t
        return t

    def psum(self, name, shape, dtype=F32):
        t = self.stack.enter_context(self.nc.psum_tensor(name, list(shape), dtype))
        self.sb[name] = t
        return t

    def dram(self, name, shape, dtype=F32, kind=None):
        if kind:
            t = self.nc.dram_tensor(name, list(shape), dtype, kind=kind)
        else:
            t = self.nc.dram_tensor(name, list(shape), dtype)
        self.dr[name] = t
        return t

    def rec(self, eng, fn):
        self.ops[eng].append(fn)

    def _waits(self, eng, waits):
        for w in waits:
            if not w:
                continue
            sem, val = w
            sem = self.sem(sem)
            v = sem.n if val is None else val
            if v <= 0:
                continue
            self.rec(eng, lambda e, h=sem.h, v=v: e.wait_ge(h, v))

    def wait(self, eng, sem, val=None):
        self._waits(eng, ((sem, val),))

    def emit(self, eng, fn, waits=(), incs=(), drain=None):
        if drain is None:
            drain = eng in ("vector", "scalar", "gpsimd")
        if drain:
            self.rec(eng, lambda e: e.drain())
        self._waits(eng, waits)
        incs = [(self.sem(s), a) for (s, a) in incs]
        assert len(incs) <= 1, "hardware allows one sync update per instruction"
        for (sem, amt) in incs:
            sem.n += amt
        def g(e, fn=fn, incs=tuple(incs)):
            i = fn(e)
            for (sem, amt) in incs:
                i.then_inc(sem.h, amt)
        self.rec(eng, g)

    def tok(self, name):
        s = self.sem(name)
        return (s, s.n)

    def dma(self, eng, out, in_, sem=None, waits=(), accum=False):
        incs = ((sem, 16),) if sem is not None else ()
        if accum:
            fn = lambda e, o=out, i=in_: e.dma_start(out=o, in_=i, accum_op=ALU.add)
        else:
            fn = lambda e, o=out, i=in_: e.dma_start(out=o, in_=i)
        self.emit(eng, fn, waits=waits, incs=incs, drain=False)

    def mm(self, out, lhsT, rhs, start, stop, waits=(), incs=(), tile_position=None):
        self.emit("tensor",
                  lambda e, o=out, l=lhsT, r=rhs, st=start, sp=stop, tp=tile_position:
                      e.matmul(o, l, r, start=st, stop=sp, tile_position=tp),
                  waits=waits, incs=incs)

    def transpose(self, out, in_, waits=(), incs=()):
        ident = self.sb["ident"]
        p = in_.shape[0]
        self.emit("tensor",
                  lambda e, o=out, i=in_, idn=ident, p=p: e.transpose(o, i, idn[0:p, 0:p]),
                  waits=waits, incs=incs)

    def tponce_sel(self, ps_full, dst_ap, src_ap, sel, extra_waits=(), evac_waits=()):
        raise NotImplementedError

    def cp(self, eng, out, in_, waits=(), incs=()):
        self.emit(eng, lambda e, o=out, i=in_: e.tensor_copy(o, i), waits=waits, incs=incs)

    def tt(self, eng, out, a, b, op, waits=(), incs=()):
        self.emit(eng, lambda e, o=out, x=a, y=b, p=op: e.tensor_tensor(o, x, y, op=p),
                  waits=waits, incs=incs)

    def ts(self, eng, out, a, scalar, op, waits=(), incs=()):
        self.emit(eng, lambda e, o=out, x=a, s=scalar, p=op: e.tensor_single_scalar(o, x, s, p),
                  waits=waits, incs=incs)

    def tsc(self, eng, out, in0, scalar1, scalar2, op0, op1, waits=(), incs=()):
        self.emit(eng,
                  lambda e, o=out, a=in0, s1=scalar1, s2=scalar2, p0=op0, p1=op1:
                      e.tensor_scalar(o, a, s1, s2, p0, p1),
                  waits=waits, incs=incs)

    def stt(self, eng, out, in0, scalar, in1, op0, op1, waits=(), incs=()):
        self.emit(eng,
                  lambda e, o=out, a=in0, s=scalar, b=in1, p0=op0, p1=op1:
                      e.scalar_tensor_tensor(o, a, s, b, p0, p1),
                  waits=waits, incs=incs)

    def act(self, out, in_, func, bias=0.0, scale=1.0, accum_out=None, waits=(), incs=()):
        if accum_out is not None:
            fn = lambda e, o=out, i=in_, f=func, b=bias, s=scale, a=accum_out: \
                e.activation(out=o, in_=i, func=f, bias=b, scale=s, accum_out=a)
        else:
            fn = lambda e, o=out, i=in_, f=func, b=bias, s=scale: \
                e.activation(out=o, in_=i, func=f, bias=b, scale=s)
        self.emit("scalar", fn, waits=waits, incs=incs)

    def red(self, eng, out, in_, op, negate=False, waits=(), incs=()):
        self.emit(eng,
                  lambda e, o=out, i=in_, p=op, n=negate: e.tensor_reduce(o, i, AX.X, p, negate=n),
                  waits=waits, incs=incs)

    def memset(self, eng, ap, val, waits=(), incs=()):
        self.emit(eng, lambda e, a=ap, v=val: e.memset(a, v), waits=waits, incs=incs)

    def collective(self, kind, ins, outs, op=ALU.bypass, waits=(), incs=()):
        from concourse.aot_env import allow_ici_all_to_all
        self._waits("gpsimd", waits)
        incs = [(self.sem(s), a) for (s, a) in incs]
        for (sem, amt) in incs:
            sem.n += amt
        def g(e, k=kind, i=ins, o=outs, p=op, incs=tuple(incs)):
            with allow_ici_all_to_all():
                inst = e.collective_compute(k, p, replica_groups=[list(range(8))], ins=i, outs=o)
            for (sem, amt) in incs:
                inst.then_inc(sem.h, amt)
        self.rec("gpsimd", g)

    def ln(self, out_ap, x_ap, sq_scratch, st, eps, cols, gamma=None, beta=None,
           pre=(), incs=(), sem_v="LNV", sem_s="LNS"):
        v = "vector"
        SV, SS = self.sem(sem_v), self.sem(sem_s)
        self.red(v, st[:, 0:1], x_ap, ALU.add, waits=pre, incs=((SV, 1),))
        self.act(sq_scratch, x_ap, AF.Square, accum_out=st[:, 1:2], waits=pre, incs=((SS, 1),))
        self.ts(v, st[:, 2:3], st[:, 0:1], 1.0 / cols, ALU.mult)
        self.ts(v, st[:, 3:4], st[:, 1:2], 1.0 / cols, ALU.mult, waits=(self.tok(sem_s),))
        self.tt(v, st[:, 4:5], st[:, 2:3], st[:, 2:3], ALU.mult)
        self.tt(v, st[:, 3:4], st[:, 3:4], st[:, 4:5], ALU.subtract, incs=((SV, 1),))
        self.act(st[:, 7:8], st[:, 3:4], AF.Sqrt,
                 bias=(eps if not isinstance(eps, float) else float(eps)),
                 waits=(self.tok(sem_v),), incs=((SS, 1),))
        self.emit(v, lambda e, o=st[:, 5:6], i=st[:, 7:8]: e.reciprocal(o, i),
                  waits=(self.tok(sem_s),))
        self.tt(v, st[:, 6:7], st[:, 2:3], st[:, 5:6], ALU.mult)
        self.ts(v, st[:, 6:7], st[:, 6:7], -1.0, ALU.mult, incs=((SV, 1),))
        self.act(out_ap, x_ap, AF.Identity, bias=st[:, 6:7], scale=st[:, 5:6],
                 waits=(self.tok(sem_v),), incs=((SS, 1),))
        if gamma is not None:
            self.tt(v, out_ap, out_ap, gamma, ALU.mult, waits=(self.tok(sem_s),))
            self.tt(v, out_ap, out_ap, beta, ALU.add, incs=((SV, 1),))
            return self.tok(sem_v)
        return self.tok(sem_s)

    def build_block(self):
        with self.nc.Block() as block:
            for eng in ENGS:
                fns = self.ops[eng]
                def runner(e, fns=fns):
                    for f in fns:
                        f(e)
                getattr(block, eng)(runner)
        self.stack.close()



STAGES = ["xn", "s", "qdet", "y1", "g", "x1", "s1", "full"]


def build_core(stage="full"):
    nc = bass.Bass("TRN2", target_bir_lowering=False)
    k = KB(nc)
    sti = STAGES.index(stage)
    SM = k.sem

    xin = k.dram("x_in", [T, D], kind="ExternalInput").ap()
    wq_sl = k.dram("wq_sl", [128, 1024], kind="ExternalInput").ap()
    wk_sl = k.dram("wk_sl", [128, 1024], kind="ExternalInput").ap()
    wv_sl = k.dram("wv_sl", [128, 1024], kind="ExternalInput").ap()
    wd_sl = k.dram("wd_sl", [128, 1024], kind="ExternalInput").ap()
    ip_sl = k.dram("ip_sl", [384, 1024], kind="ExternalInput").ap()
    op_sl = k.dram("op_sl", [128, 1024], kind="ExternalInput").ap()
    w1c = k.dram("w1c", [1024, 4096], kind="ExternalInput").ap()
    w2c = k.dram("w2c", [4096, 1024], kind="ExternalInput").ap()
    b1c = k.dram("b1c", [128, 32], kind="ExternalInput").ap()
    b2_r = k.dram("b2_r", [1, 1024], kind="ExternalInput").ap()
    wg0_in = k.dram("wg0_in", [1024, 8], kind="ExternalInput").ap()
    wqkv1_sl = k.dram("wqkv1_sl", [128, 3072], kind="ExternalInput").ap()
    wout1_sl = k.dram("wout1_sl", [128, 1024], kind="ExternalInput").ap()
    winc = k.dram("winc", [1024, 4096], kind="ExternalInput").ap()
    woutc = k.dram("woutc", [4096, 1024], kind="ExternalInput").ap()
    binc = k.dram("binc", [128, 256], kind="ExternalInput").ap()
    bout_r = k.dram("bout_r", [1, 8192], kind="ExternalInput").ap()
    wg1_in = k.dram("wg1_in", [1024, 8], kind="ExternalInput").ap()
    ipb_bc = k.dram("ipb_bc", [128, 3072], kind="ExternalInput").ap()
    m1c_in = k.dram("m1c_in", [128, 16], kind="ExternalInput").ap()
    selS_in = k.dram("selS_in", [128, 16], kind="ExternalInput").ap()
    selST_in = k.dram("selST_in", [16, 128], kind="ExternalInput").ap()
    tri_in = k.dram("tri_in", [128, 128], kind="ExternalInput").ap()
    tri0_in = k.dram("tri0_in", [128, 128], kind="ExternalInput").ap()
    ident_in = k.dram("ident_in", [128, 128], kind="ExternalInput").ap()
    cvec = k.dram("cvec", [11, 128, 1024], kind="ExternalInput").ap()

    yout = k.dram("y_out", [T, D], kind="ExternalOutput").ap()

    def idram(name, shape, dtype=F32):
        return k.dram(name, shape, dtype).ap()

    ag_specs = [("wq", wq_sl), ("wk", wk_sl), ("wv", wv_sl), ("wd", wd_sl),
                ("ip", ip_sl), ("op", op_sl),
                ("wqkv1", wqkv1_sl), ("wout1", wout1_sl),
                ("win", winc), ("wout", woutc)]
    ag_full = {"wq": [1024, 1024], "wk": [1024, 1024], "wv": [1024, 1024],
               "wd": [1024, 1024], "ip": [3072, 1024], "op": [1024, 1024],
               "wqkv1": [1024, 3072], "wout1": [1024, 1024],
               "win": [8192, 4096], "wout": [32768, 1024]}
    ag = {nm: (idram("agi_" + nm, list(sl.shape)),
               idram("ago_" + nm, ag_full[nm])) for nm, sl in ag_specs}
    q_scr = idram("q_scr", [T, D])
    s_scr = idram("s_scr", [T, D])
    cc1_in = idram("cc1_in", [T, D])
    cc1_out = idram("cc1_out", [T, D])
    qkvd_scr = idram("qkvd_scr", [T, 3 * D])
    y1_scr = idram("y1_scr", [T, D])
    y1T_scr = idram("y1T_scr", [D, T])
    moe0_scr = idram("moe0_scr", [T, D])
    cc2_in = idram("cc2_in", [T, D])
    cc2_out = idram("cc2_out", [T, D])
    agi_we = idram("agi_we", [1024, 1024])
    ago_we = idram("ago_we", [8 * 1024, 1024])
    agi_ce = idram("agi_ce", [1, 1024])
    ago_ce = idram("ago_ce", [8, 1024])
    agi_cnt = idram("agi_cnt", [16, 8])
    ago_cnt = idram("ago_cnt", [128, 8])
    x1_scr = idram("x1_scr", [T, D])
    xn1_scr = idram("xn1_scr", [T, D])
    qkv1_scr = idram("qkv1_scr", [T, 3 * D])
    s1_scr = idram("s1_scr", [T, D])
    o1_scr = idram("o1_scr", [T, D])
    CPE = 1536
    tok_slot = k.dram("tok_slot", [13312, 1], mybir.dt.int32).ap()
    slots_scr = k.dram("slots_scr", [T, 2], mybir.dt.int32).ap()
    wv_scr = idram("wv_scr", [T, 2])
    ycomp_scr = idram("ycomp_scr", [8 * CPE + 128, D])

    BIGA = k.sbuf("BIGA", [128, 24576])
    BIGB = k.sbuf("BIGB", [128, 12288])
    TRANS = k.sbuf("TRANS", [128, 5376])
    CONSTS = k.sbuf("CONSTS", [128, 3072])
    SMALL = k.sbuf("SMALL", [128, 1024])
    S8 = k.sbuf("S8", [128, 512])
    GT = k.sbuf("GT", [128, 1024])
    RUN = k.sbuf("RUN", [16, 512])
    ident = k.sbuf("ident", [128, 128])
    tri = k.sbuf("tri", [128, 128])
    tri0 = k.sbuf("tri0", [128, 128])
    selS = k.sbuf("selS", [128, 16])
    selST = k.sbuf("selST", [16, 128])
    m1c = k.sbuf("m1c", [128, 16])
    wg0 = k.sbuf("wg0", [128, 64])
    wg1 = k.sbuf("wg1", [128, 64])
    binsb = k.sbuf("binsb", [128, 256])
    IDXS = k.sbuf("IDXS", [128, 16], mybir.dt.int32)
    cpe_base = k.sbuf("cpe_base", [1, 8])
    BIGAh = BIGA[:, :].bitcast(BF16)
    BIGBh = BIGB[:, :].bitcast(BF16)
    b1sb = k.sbuf("b1sb", [128, 32])
    onescol = k.sbuf("onescol", [128, 8])
    onesrow = k.sbuf("onesrow", [1, 128])
    rowbuf = k.sbuf("rowbuf", [1, 1024])
    rowbuf2 = k.sbuf("rowbuf2", [1, 1024])

    PS_A = k.psum("PS_A", [128, 1024])
    PS_B = k.psum("PS_B", [128, 1024])
    PS_C = k.psum("PS_C", [128, 512])
    PS_D = k.psum("PS_D", [128, 512])
    PS_E = k.psum("PS_E", [128, 512])
    PS_F = k.psum("PS_F", [128, 512])

    TPEV, PEV = SM("TPEV"), SM("PEV")

    def cslot(si):
        return CONSTS[:, si * 1024:(si + 1) * 1024]

    def stat8(slot):
        return SMALL[:, slot * 8:(slot + 1) * 8]

    k.dma("sync", ident[:], ident_in, sem=SM("idl"))
    if sti >= STAGES.index("qdet"):
        k.dma("sync", tri[:], tri_in, sem=SM("idl"))
        k.dma("sync", tri0[:], tri0_in, sem=SM("idl"))
        k.dma("sync", selS[:], selS_in, sem=SM("idl"))
        k.dma("sync", selST[:], selST_in, sem=SM("idl"))
        k.dma("sync", m1c[:], m1c_in, sem=SM("idl"))
        k.dma("sync", wg0[:, 0:64].rearrange("p (jc e) -> p jc e", jc=8),
              wg0_in.rearrange("(jc p) e -> p jc e", p=128), sem=SM("idl"))
        k.dma("sync", b1sb[:], b1c, sem=SM("idl"))
        k.dma("sync", rowbuf[0:1, 0:1024], b2_r, sem=SM("idl"))
    if sti >= STAGES.index("x1"):
        k.dma("sync", wg1[:, 0:64].rearrange("p (jc e) -> p jc e", jc=8),
              wg1_in.rearrange("(jc p) e -> p jc e", p=128), sem=SM("idl"))
        k.dma("sync", binsb[:], binc, sem=SM("idl"))
        for e_ in range(8):
            k.memset("vector", cpe_base[0:1, e_:e_ + 1], float(e_ * 1536),
                     incs=((("VM", 1),) if e_ == 7 else ()))
        k.memset("vector", GT[:, 0:104], 0.0, incs=(("VM", 1),))
        zt = k.tok("VM")
        k.dma("gpsimd", tok_slot.rearrange("(p a) c -> p (a c)", p=128),
              GT[:, 0:104].bitcast(mybir.dt.int32), sem=SM("tofz"), waits=(zt,))
    k.memset("vector", onescol[:], 1.0, incs=(("ONES", 1),))
    k.memset("vector", onesrow[:], 1.0, incs=(("ONES", 1),))
    eps5 = SMALL[:, 96:97]; eps12 = SMALL[:, 97:98]
    k.memset("vector", eps5, 1e-5)
    k.memset("vector", eps12, 1e-12, incs=(("EPS", 1),))

    def load_consts(slots, waits=()):
        for si, row in slots:
            k.dma("sync", cslot(si), cvec[row], sem=SM("cl"), waits=waits)
            waits = ()
        return k.tok("cl")

    import os
    if stage != "xn" or os.environ.get("KB_FORCE_AG"):
        for nm, sl in ag_specs:
            k.dma("gpsimd", ag[nm][0], sl, sem=SM("agb"))
        k.wait("gpsimd", SM("agb"), SM("agb").n)
        for nm, sl in ag_specs:
            k.collective("AllGather", [ag[nm][0].opt()], [ag[nm][1].opt()],
                         incs=(("AG_" + nm, 1),))

    def tponce(dst_ap, src_ap, extra_waits=(), evac_waits=(), sel=None):
        k.pyctr["tpar"] = k.pyctr.get("tpar", 0) + 1
        pslot = k.pyctr["tpar"] % 2
        cols = src_ap.shape[1] if hasattr(src_ap, "shape") else 128
        ps = PS_A[:, pslot * 512: pslot * 512 + cols]
        k.emit("tensor",
               lambda e, o=ps, i=src_ap, idn=ident: e.transpose(o, i, idn[0:i.shape[0], 0:i.shape[0]]),
               waits=tuple(extra_waits) + ((TPEV, TPEV.n - 1),),
               incs=(("tpmm", 1),))
        rd = ps[0:dst_ap.shape[0], :] if sel is None else sel(ps)
        k.cp("vector", dst_ap, rd,
             waits=(k.tok("tpmm"),) + tuple(evac_waits),
             incs=((TPEV, 1),))

    def build_WT(base, ago_ap, first_waits, tag, rows=1024, dstbuf=None):
        ld = SM("wtl")
        dstbuf = BIGA if dstbuf is None else dstbuf
        sbufs = (TRANS[:, 2304:3328], TRANS[:, 3328:4352])
        import os as _os
        dbuf = bool(_os.environ.get("KB_WT_DBUF"))
        for ic in range(rows // 128):
            w = list(first_waits) if ic == 0 else []
            if dbuf:
                w.append((TPEV, TPEV.n - 8))
            else:
                w.append((TPEV, TPEV.n))
            stage_buf = sbufs[ic % 2 if dbuf else 0]
            k.dma("sync", stage_buf, ago_ap[ic * 128:(ic + 1) * 128, :], sem=ld, waits=tuple(w))
            tk = k.tok("wtl")
            for jc in range(8):
                o0 = base + jc * rows + ic * 128
                dst = dstbuf[:, o0:o0 + 128]
                tponce(dst, stage_buf[:, jc * 128:(jc + 1) * 128],
                       extra_waits=(tk,) if jc == 0 else ())
        return k.tok("TPEV")

    if sti >= STAGES.index("qdet"):
        idl_tok = k.tok("idl")
        for qq in range(4):
            t_w1 = build_WT(0, w1c[:, qq * 1024:(qq + 1) * 1024],
                            ((k.tok("wemm"),) if qq > 0 else (k.tok("idl"),)),
                            "W1Q%d" % qq, rows=1024, dstbuf=BIGB)
            for fc in range(8):
                k.dma("sync", BIGA[:, fc * 1024:(fc + 1) * 1024],
                      w2c[qq * 1024 + fc * 128: qq * 1024 + (fc + 1) * 128, :],
                      sem=SM("w2l"), waits=((k.tok("wemm"),) if qq > 0 and fc == 0 else ()))
            t_w2 = k.tok("w2l")
            for fc in range(8):
                for dh in range(2):
                    ceps = PS_C if dh == 0 else PS_E
                    w = [t_w2, idl_tok] if fc == 0 and dh == 0 else []
                    k.mm(ceps[0:1, 0:512], b1sb[:, qq * 8 + fc: qq * 8 + fc + 1],
                         BIGA[:, fc * 1024 + dh * 512: fc * 1024 + dh * 512 + 512],
                         start=(qq == 0 and fc == 0), stop=(qq == 3 and fc == 7),
                         waits=tuple(w),
                         incs=((("cemm", 1),) if (qq == 3 and fc == 7 and dh == 1) else ()))
            for jc in range(8):
                for dh in range(2):
                    ps = PS_B[:, (dh % 2) * 512:(dh % 2) * 512 + 512]
                    k.wait("tensor", PEV, PEV.n - 1)
                    for fc in range(8):
                        w = [t_w1, t_w2] if (jc == 0 and dh == 0 and fc == 0) else []
                        k.mm(ps, BIGB[:, fc * 1024 + jc * 128: fc * 1024 + jc * 128 + 128],
                             BIGA[:, fc * 1024 + dh * 512: fc * 1024 + dh * 512 + 512],
                             start=(fc == 0), stop=(fc == 7), waits=tuple(w),
                             incs=((("wemm", 1),) if fc == 7 else ()))
                    dst = BIGA[:, 8192 + jc * 1024 + dh * 512: 8192 + jc * 1024 + dh * 512 + 512]
                    if qq == 0:
                        k.cp("vector", dst, ps, waits=(k.tok("wemm"),), incs=((PEV, 1),))
                    else:
                        k.tt("vector", dst, dst, ps, ALU.add, waits=(k.tok("wemm"),),
                             incs=((PEV, 1),))
        k.cp("vector", rowbuf2[0:1, 0:512], PS_C[0:1, 0:512], waits=(k.tok("cemm"),))
        k.cp("vector", rowbuf2[0:1, 512:1024], PS_E[0:1, 0:512])
        k.tt("vector", rowbuf2[0:1, 0:1024], rowbuf2[0:1, 0:1024],
             rowbuf[0:1, 0:1024], ALU.add, incs=(("cedone", 1),))
        k.dma("gpsimd", agi_ce, rowbuf2[0:1, 0:1024], sem=SM("weout"),
              waits=(k.tok("cedone"),))
        for jc in range(8):
            k.dma("gpsimd", agi_we[jc * 128:(jc + 1) * 128, :],
                  BIGA[:, 8192 + jc * 1024: 8192 + (jc + 1) * 1024],
                  sem=SM("weout"), waits=((k.tok("PEV"),) if jc == 0 else ()))
        k.wait("gpsimd", SM("weout"), SM("weout").n)
        k.collective("AllGather", [agi_we.opt()], [ago_we.opt()], incs=(("AG_we", 1),))
        k.collective("AllGather", [agi_ce.opt()], [ago_ce.opt()], incs=(("AG_ce", 1),))
        tok_p2 = (k.tok("wemm"), k.tok("weout"), k.tok("cedone"), k.tok("PEV"))
    else:
        tok_p2 = ()

    if stage != "xn" or os.environ.get("KB_FORCE_WT"):
        tok_wq = build_WT(0, ag["wq"][1], (k.tok("AG_wq"), k.tok("idl")) + tuple(tok_p2), "TQ")
        tok_wk = build_WT(8192, ag["wk"][1], (k.tok("AG_wk"),), "TK")
        tok_wv = build_WT(16384, ag["wv"][1], (k.tok("AG_wv"),), "TV")
    else:
        tok_wq = tok_wk = tok_wv = None

    def WT_mv(wbase, jc, i0, iw, istride=1024, buf=None):
        buf = BIGA if buf is None else buf
        return buf[:, wbase + jc * istride + i0: wbase + jc * istride + i0 + iw]

    def proj_tm(xnT, wbase, nout, evac_fn, first_waits=(), istride=1024, wbuf=None):
        for h in range(nout // 512):
            ps = PS_B[:, (h % 2) * 512:(h % 2) * 512 + 512]
            k.wait("tensor", PEV, PEV.n - 1)
            for jc in range(8):
                w = list(first_waits) if (h == 0 and jc == 0) else []
                k.mm(ps, xnT[:, jc * 128:(jc + 1) * 128],
                     WT_mv(wbase, jc, h * 512, 512, istride, wbuf),
                     start=(jc == 0), stop=(jc == 7), waits=tuple(w),
                     incs=((("PMM", 1),) if jc == 7 else ()))
            evac_fn(h, ps, (k.tok("PMM"),))

    def elu_p1(dst, src, scratch, waits=(), incs=()):
        k.ts("vector", scratch, src, 0.0, ALU.min, waits=waits, incs=(("ELUV", 1),))
        k.act(scratch, scratch, AF.Exp, waits=(k.tok("ELUV"),), incs=(("ELUS", 1),))
        k.ts("vector", dst, src, 0.0, ALU.max, waits=(k.tok("ELUS"),))
        k.tt("vector", dst, dst, scratch, ALU.add, incs=incs)

    xb = BIGB[:, 0:1024]
    xnb = BIGB[:, 1024:2048]
    xnTb = BIGB[:, 2048:3072]
    qb = BIGB[:, 3072:4096]
    kbf = BIGB[:, 4096:5120]
    vbf = BIGB[:, 5120:6144]
    sqb = BIGB[:, 6144:7168]
    elub = BIGB[:, 7168:8192]
    kvev = TRANS[:, 0:1024]
    ksev = TRANS[:, 1024:1056]

    cl1 = load_consts([(0, 2), (1, 3), (2, 4)])

    prev = {}
    for ti in range(NT):
        b_loc, s0 = ti // 16, (ti % 16) * 128
        k.dma("sync", xb, xin[ti * 128:(ti + 1) * 128, :], sem=SM("xld"),
              waits=(prev.get("xn_done"),) + (tuple(tok_p2) if ti == 0 else ()))
        pre = [(SM("xld"), SM("xld").n)]
        if ti >= 1:
            pre += [(SM("ccw"), 16 * ti)] + ([prev["xnT_mm"]] if "xnT_mm" in prev else [])
            if stage == "xn":
                pre += [(SM("yw"), 16 * ti)]
        tok_xn = k.ln(xnb, xb, sqb, stat8(0), eps5, D,
                      pre=tuple(pre) + ((k.tok("EPS"),) if ti == 0 else ()))
        prev["xn_done"] = tok_xn
        j = s0 // 256
        r0 = j * 512 + b_loc * 256 + (s0 % 256)
        k.dma("gpsimd", cc1_in[r0:r0 + 128, :], xnb, sem=SM("ccw"), waits=(tok_xn,))
        if stage == "xn":
            k.dma("gpsimd", yout[ti * 128:(ti + 1) * 128, :], xnb, sem=SM("yw"),
                  waits=(tok_xn,))
            continue
        for c in range(8):
            tponce(xnTb[:, c * 128:(c + 1) * 128], xnb[:, c * 128:(c + 1) * 128],
                   extra_waits=(tok_xn,) if c == 0 else (),
                   evac_waits=(prev["proj_mm"],) if (c == 0 and ti >= 1) else ())
        prev["xnT_mm"] = k.tok("tpmm")
        tok_xnT = k.tok("TPEV")

        def mkev(buf, slot, extra):
            def ev(h, ps, w):
                k.tt("vector", buf[:, h * 512:(h + 1) * 512], ps,
                     cslot(slot)[:, h * 512:(h + 1) * 512], ALU.add,
                     waits=tuple(list(w) + (extra if h == 0 else [])),
                     incs=((PEV, 1),))
            return ev

        fw = [tok_xnT] + ([cl1, tok_wq, tok_wk, tok_wv] if ti == 0 else [])
        proj_tm(xnTb, 0, 1024, mkev(qb, 0, [(SM("qsp"), 16 * ti)] if ti >= 1 else []),
                first_waits=fw)
        tok_qev = k.tok("PEV")
        proj_tm(xnTb, 8192, 1024, mkev(kbf, 1, [prev["kcons"]] if ti >= 1 else []))
        proj_tm(xnTb, 16384, 1024, mkev(vbf, 2, [prev["kcons"]] if ti >= 1 else []))
        tok_vev = k.tok("PEV")
        prev["proj_mm"] = k.tok("PMM")
        elu_p1(qb, qb, elub, incs=(("QE", 1),))
        elu_p1(kbf, kbf, elub, incs=(("KE", 1),))
        k.dma("gpsimd", q_scr[ti * 128:(ti + 1) * 128, :], qb, sem=SM("qsp"),
              waits=(k.tok("QE"),))
        row_start = (ti % 16 == 0)
        par = ti % 2
        kvps = PS_C if par == 0 else PS_E
        for h in range(NH0):
            pp, cc_ = 64 * (h % 2), 64 * (h // 2)
            w = []
            if h == 0:
                w = [k.tok("KE"), tok_vev]
                if ti <= 1:
                    w.append(k.tok("cemm"))
                if ti >= 2:
                    w.append((SM("kvadd"), 17 * (ti - 1)))
            k.mm(kvps[pp:pp + 64, cc_:cc_ + 64],
                 kbf[:, h * 64:(h + 1) * 64], vbf[:, h * 64:(h + 1) * 64],
                 start=True, stop=True, waits=tuple(w),
                 incs=(("kcons", 1),))
        ksps = PS_D if par == 0 else PS_F
        for c in range(8):
            k.mm(ksps[:, c: c + 1],
                 kbf[:, c * 128:(c + 1) * 128], onescol[:, 0:1],
                 start=True, stop=True,
                 waits=((k.tok("ONES"),) if (ti == 0 and c == 0) else ()),
                 incs=(("kcons", 1),))
        prev["kcons"] = k.tok("kcons")
        tok_kvmm = k.tok("kcons")
        for h in range(NH0):
            pp, cc_ = 64 * (h % 2), 64 * (h // 2)
            dst = kvev[pp:pp + 64, b_loc * 512 + cc_: b_loc * 512 + cc_ + 64]
            srcp = kvps[pp:pp + 64, cc_:cc_ + 64]
            if row_start:
                k.cp("vector", dst, srcp, waits=(tok_kvmm,) if h == 0 else (),
                     incs=(("kvadd", 1),))
            else:
                k.tt("vector", dst, dst, srcp, ALU.add,
                     waits=(tok_kvmm,) if h == 0 else (), incs=(("kvadd", 1),))
        dst = ksev[:, b_loc * 8: b_loc * 8 + 8]
        srcp = ksps[:, 0:8]
        if row_start:
            k.cp("vector", dst, srcp, incs=(("kvadd", 1),))
        else:
            k.tt("vector", dst, dst, srcp, ALU.add, incs=(("kvadd", 1),))

    if stage == "xn":
        k.wait("gpsimd", SM("yw"), SM("yw").n)
        k.build_block()
        return nc

    k.ts("vector", ksev[:, 0:16], ksev[:, 0:16], 1e-6, ALU.add,
         waits=(k.tok("kvadd"),), incs=(("ksdone", 1),))

    if os.environ.get("KB_P1ONLY"):
        for ti in range(NT):
            k.dma("gpsimd", yout[ti * 128:(ti + 1) * 128, :], kvev[:, 0:1024],
                  sem=SM("yw"), waits=(k.tok("ksdone"), (SM("qsp"), 16 * NT)))
        k.wait("gpsimd", SM("yw"), SM("yw").n)
        k.build_block()
        return nc

    tok_wd = build_WT(0, ag["wd"][1], (k.tok("AG_wd"), k.tok("PEV")), "TD")
    cl2 = load_consts([(0, 5), (1, 0), (2, 1)], waits=(k.tok("PEV"),))

    qb2 = BIGB[:, 0:1024]
    qT2 = BIGB[:, 1024:2048]
    xn2 = BIGB[:, 2048:3072]
    ctxb = BIGB[:, 3072:4096]
    ctxT = BIGB[:, 4096:5120]
    hsb = BIGB[:, 5120:6144]
    sb_o = BIGB[:, 6144:7168]
    sq2 = BIGB[:, 7168:8192]
    zb = SMALL[:, 64:80]

    prev = {}
    for ti in range(NT):
        b_loc, s0 = ti // 16, (ti % 16) * 128
        j = s0 // 256
        r0 = j * 512 + b_loc * 256 + (s0 % 256)
        p1w = []
        if ti == 0:
            p1w = [(SM("qsp"), 16 * NT), k.tok("LNS"), k.tok("ccw"), k.tok("PMM"),
                   k.tok("kcons"), k.tok("ELUS")]
        k.dma("sync", qb2, q_scr[ti * 128:(ti + 1) * 128, :], sem=SM("q2l"),
              waits=tuple(p1w) if ti == 0 else (prev["qT_mm"],))
        k.dma("sync", xn2, cc1_in[r0:r0 + 128, :], sem=SM("xn2l"),
              waits=tuple(p1w) if ti == 0 else ((SM("hsx"), ti),))
        tq = (SM("q2l"), SM("q2l").n)
        for c in range(8):
            ev_w = ()
            if c == 0:
                ev_w = tuple(p1w) if ti == 0 else (prev["ctx_mm"],)
            tponce(qT2[:, c * 128:(c + 1) * 128], qb2[:, c * 128:(c + 1) * 128],
                   extra_waits=(tq,) if c == 0 else (),
                   evac_waits=ev_w)
        prev["qT_mm"] = k.tok("tpmm")
        tok_qT = k.tok("TPEV")
        for h in range(NH0):
            pb = 64 * (h % 2)
            w = []
            if h == 0:
                w = [tok_qT]
                w.append((SM("ksdone"), 1) if ti == 0 else (SM("zev"), SM("zev").n))
            dps = PS_D if h % 2 == 0 else PS_F
            k.mm(dps[:, h // 2: h // 2 + 1],
                 qT2[pb:pb + 64, (h // 2) * 128:(h // 2) * 128 + 128],
                 ksev[pb:pb + 64, b_loc * 8 + h // 2: b_loc * 8 + h // 2 + 1],
                 start=True, stop=True, waits=tuple(w),
                 incs=((("dmm", 1),) if h == NH0 - 1 else ()))
        k.emit("vector", lambda e, o=zb[:, 0:8], i=PS_D[:, 0:8]: e.reciprocal(o, i),
               waits=(k.tok("dmm"),))
        k.emit("vector", lambda e, o=zb[:, 8:16], i=PS_F[:, 0:8]: e.reciprocal(o, i),
               incs=(("zev", 1),))
        for h in range(NH0):
            pb = 64 * (h % 2)
            w = []
            if h == 0:
                w = [(PEV, PEV.n)]
                if ti == 0:
                    w.append(k.tok("kvadd"))
            pcol = (h % 2) * 512 + (h // 2) * 64
            k.mm(PS_B[:, pcol:pcol + 64],
                 qT2[pb:pb + 64, (h // 2) * 128:(h // 2) * 128 + 128],
                 kvev[pb:pb + 64, b_loc * 512 + (h // 2) * 64: b_loc * 512 + (h // 2) * 64 + 64],
                 start=True, stop=True,
                 waits=tuple(w), incs=(("ctxmm", 1),))
        prev["ctx_mm"] = k.tok("ctxmm")
        for half in range(2):
            wv_ = [(SM("ctxmm"), SM("ctxmm").n), k.tok("zev")]
            if ti == 0 and half == 0:
                wv_ += [(SM("qsp"), 16 * NT), k.tok("kcons")]
            for hh in range(8):
                h = half + 2 * hh
                pcol = half * 512 + hh * 64
                k.ts("vector", ctxb[:, h * 64:(h + 1) * 64], PS_B[:, pcol:pcol + 64],
                     zb[:, half * 8 + hh: half * 8 + hh + 1], ALU.mult,
                     waits=tuple(wv_) if hh == 0 else (),
                     incs=((PEV, 1),) if hh == 7 else ())
        tok_ctxev = k.tok("PEV")
        for c in range(8):
            ev_w = ()
            if c == 0:
                ev_w = (prev["oproj_mm"],) if ti >= 1 else (k.tok("PMM"),)
            tponce(ctxT[:, c * 128:(c + 1) * 128], ctxb[:, c * 128:(c + 1) * 128],
                   extra_waits=(tok_ctxev,) if c == 0 else (), evac_waits=ev_w)

        def evhs(h, ps, w):
            w = list(w) + ([cl2, k.tok("kcons")] if ti == 0 else []) + \
                ([(SM("LS2"), SM("LS2").n)] if ti >= 1 else [])
            k.tt("vector", hsb[:, h * 512:(h + 1) * 512], ps,
                 cslot(0)[:, h * 512:(h + 1) * 512], ALU.add, waits=tuple(w),
                 incs=((PEV, 1),))
            k.tt("vector", hsb[:, h * 512:(h + 1) * 512], hsb[:, h * 512:(h + 1) * 512],
                 xn2[:, h * 512:(h + 1) * 512], ALU.add,
                 waits=((SM("xn2l"), SM("xn2l").n),),
                 incs=((("hsx", 1),) if h == 1 else ()))
        proj_tm(ctxT, 0, 1024, evhs,
                first_waits=(k.tok("TPEV"),) + ((tok_wd,) if ti == 0 else ()))
        prev["oproj_mm"] = k.tok("PMM")
        pre_ln = [k.tok("hsx")]
        if ti >= 1:
            pre_ln.append(prev["s_w"])
        tok_s = k.ln(sb_o, hsb, sq2, stat8(1), eps12, D, gamma=cslot(1), beta=cslot(2),
                     pre=tuple(pre_ln), sem_v="LV2", sem_s="LS2")
        k.dma("gpsimd", s_scr[ti * 128:(ti + 1) * 128, :], sb_o, sem=SM("sw"),
              waits=(tok_s,))
        if stage == "s":
            k.dma("gpsimd", yout[ti * 128:(ti + 1) * 128, :], sb_o, sem=SM("sw"))
        prev["s_w"] = k.tok("sw")

    if stage == "s":
        k.wait("gpsimd", SM("sw"), SM("sw").n)
        k.build_block()
        return nc

    k.collective("AllToAll", [cc1_in.opt()], [cc1_out.opt()],
                 waits=((SM("ccw"), 16 * NT),), incs=(("A2A1", 1),))

    def ccdet_ap(buf, dt):
        return buf.rearrange("(i lp n) f -> n i lp f", i=8, lp=2, n=256)[8 * dt:8 * dt + 8]

    tok_ip = build_WT(0, ag["ip"][1], (k.tok("AG_ip"), prev["oproj_mm"], k.tok("PEV")),
                      "TIP", rows=3072)
    w_c = (k.tok("LS2"), k.tok("PEV"))
    for si in range(3):
        k.dma("sync", cslot(si), ipb_bc[:, si * 1024:(si + 1) * 1024], sem=SM("cld"),
              waits=w_c if si == 0 else ())
    cld1 = k.tok("cld")

    xd = BIGB[:, 0:1024]
    xdT = BIGB[:, 1024:2048]
    qkvsb = BIGB[:, 2048:5120]
    prevd = {}
    for dt in range(NT):
        w = [(SM("A2A1"), 1)] if dt == 0 else [prevd["xdT_mm"]]
        if dt == 0:
            w += [k.tok("LS2"), k.tok("sw"), k.tok("ctxmm"), k.tok("PEV"), k.tok("tpmm")]
        k.dma("sync", xd, ccdet_ap(cc1_out, dt), sem=SM("xdl"), waits=tuple(w))
        for c in range(8):
            tponce(xdT[:, c * 128:(c + 1) * 128], xd[:, c * 128:(c + 1) * 128],
                   extra_waits=((SM("xdl"), SM("xdl").n),) if c == 0 else (),
                   evac_waits=(prevd["qkv_mm"],) if (c == 0 and dt >= 1) else ())
        prevd["xdT_mm"] = k.tok("tpmm")
        tok_xdT = k.tok("TPEV")

        def evd(h, ps, w2):
            w2 = list(w2) + ([cld1] if dt == 0 else []) + \
                 ([prevd["qdsp_t"]] if dt >= 1 else [])
            cs = cslot(h // 2)[:, (h % 2) * 512:(h % 2) * 512 + 512]
            k.tt("vector", qkvsb[:, h * 512:(h + 1) * 512], ps, cs, ALU.add,
                 waits=tuple(w2), incs=((PEV, 1),))
        proj_tm(xdT, 0, 3072, evd, istride=3072,
                first_waits=(tok_xdT,) + ((tok_ip,) if dt == 0 else ()))
        prevd["qkv_mm"] = k.tok("PMM")
        tok_ev = k.tok("PEV")
        k.dma("gpsimd", qkvd_scr[dt * 128:(dt + 1) * 128, :], qkvsb, sem=SM("qdsp"),
              waits=(tok_ev,))
        if stage == "qdet":
            k.dma("gpsimd", yout[dt * 128:(dt + 1) * 128, :], qkvsb[:, 0:1024],
                  sem=SM("qdsp"))
        prevd["qdsp_t"] = k.tok("qdsp")
    if stage == "qdet":
        k.wait("gpsimd", SM("qdsp"), SM("qdsp").n)
        k.build_block()
        return nc

    tok_op = build_WT(0, ag["op"][1], (k.tok("AG_op"), prevd["qkv_mm"], k.tok("PEV")), "TOP")
    for si, row in [(0, 10), (1, 6), (2, 7)]:
        k.dma("sync", cslot(si), cvec[row], sem=SM("cld"),
              waits=((k.tok("PEV"),) if si == 0 else ()))
    cld2 = k.tok("cld")
    qd = BIGB[:, 0:1024]
    kd = BIGB[:, 1024:2048]
    vd = BIGB[:, 2048:3072]
    qdT = BIGB[:, 3072:4096]
    kdT = BIGB[:, 4096:5120]
    o_sb = BIGB[:, 5120:7168]
    oT = BIGB[:, 7168:9216]
    mha = BIGB[:, 9216:10240]
    y1b = BIGB[:, 10240:11264]
    y1Tb = BIGB[:, 11264:12288]
    BD = [TRANS[:, 0:128], TRANS[:, 128:256]]
    vpad = [TRANS[:, 256:1280], TRANS[:, 1280:2304]]
    xd2 = TRANS[:, 3328:4352]
    sqd = TRANS[:, 4352:5376]
    E_sb = SMALL[:, 104:120]
    cb = SMALL[:, 120:128]
    k.memset("vector", BD[0][:], 0.0,
             waits=(k.tok("qdsp"), k.tok("kvadd"), k.tok("ctxmm"), k.tok("LS2"),
                    k.tok("dmm")))
    k.memset("vector", BD[1][:], 0.0)
    k.memset("vector", vpad[0][:], 0.0)
    k.memset("vector", vpad[1][:], 0.0, incs=(("dzero", 1),))

    for dt in range(NT):
        w0 = [prevd["qdsp_t"], tok_op] if dt == 0 else [prevd["qkT_mm"]]
        k.dma("sync", qd, qkvd_scr[dt * 128:(dt + 1) * 128, 0:1024], sem=SM("d2l"),
              waits=tuple(w0))
        k.dma("sync", kd, qkvd_scr[dt * 128:(dt + 1) * 128, 1024:2048], sem=SM("d2l"))
        k.dma("sync", vd, qkvd_scr[dt * 128:(dt + 1) * 128, 2048:3072], sem=SM("d2l"),
              waits=((SM("vrp"), 128 * dt) if dt >= 1 else None,))
        k.dma("sync", xd2, ccdet_ap(cc1_out, dt), sem=SM("xd2l"),
              waits=((SM("mhax"), 2 * dt) if dt >= 1 else None,))
        tql = (SM("d2l"), SM("d2l").n)
        tvl = (SM("d2l"), SM("d2l").n)
        for c in range(8):
            tponce(qdT[:, c * 128:(c + 1) * 128], qd[:, c * 128:(c + 1) * 128],
                   extra_waits=(tql,) if c == 0 else (),
                   evac_waits=(prevd["sc_mm"],) if (c == 0 and dt >= 1) else ())
            tponce(kdT[:, c * 128:(c + 1) * 128], kd[:, c * 128:(c + 1) * 128],
                   extra_waits=(tql,) if c == 0 else ())
        prevd["qkT_mm"] = k.tok("tpmm")
        tok_qkT = k.tok("TPEV")
        for g2 in range(2):
            for a in range(4):
                nn = 4 * g2 + a
                k.dma("gpsimd", vpad[g2][32 * a:32 * a + 16, :],
                      vd[16 * nn:16 * nn + 16, :], sem=SM("vrp"),
                      waits=(tvl, (SM("avmm"), 8 * dt), k.tok("dzero")) if (g2 == 0 and a == 0) else ())
        tok_vrp = k.tok("vrp")
        for g2 in range(2):
            for h in range(FH):
                it = dt * 8 + g2 * 4 + h
                for a in range(4):
                    nn = 4 * g2 + a
                    for kc in range(2):
                        w2 = []
                        if a == 0 and kc == 0:
                            w2 = [tok_qkT]
                            if it >= 1:
                                w2.append(prevd["eexp"])
                        k.mm(PS_D[32 * a:32 * a + 16, 0:16],
                             qdT[:, (2 * h + kc) * 128 + nn * 16: (2 * h + kc) * 128 + nn * 16 + 16],
                             kdT[:, (2 * h + kc) * 128 + nn * 16: (2 * h + kc) * 128 + nn * 16 + 16],
                             start=(kc == 0), stop=(kc == 1), tile_position=(0, 32 * a),
                             waits=tuple(w2), incs=((("smm", 1),) if (a == 3 and kc == 1) else ()))
                tok_smm = k.tok("smm")
                k.memset("vector", E_sb[:], 1.0,
                         waits=(prevd["atmm"],) if it >= 1 else (),
                         incs=(("dmemset", 1),))
                for a in range(4):
                    k.act(E_sb[32 * a:32 * a + 16, :], PS_D[32 * a:32 * a + 16, 0:16],
                          AF.Exp, scale=1.0 / 16.0,
                          waits=(tok_smm, k.tok("dmemset")) if a == 0 else (),
                          incs=((("eexp", 1),) if a == 3 else ()))
                prevd["eexp"] = k.tok("eexp")
                k.red("vector", cb[:, 4:5], E_sb[:], ALU.add, waits=(prevd["eexp"],))
                k.emit("vector", lambda e, o=cb[:, h:h + 1], i=cb[:, 4:5]: e.reciprocal(o, i),
                       incs=(("crec", 1),))
                for a in range(4):
                    w2 = []
                    if a == 0:
                        w2 = [prevd["eexp"]]
                        if it >= 1:
                            w2.append(prevd["bdev"])
                    k.mm(PS_E[32 * a:32 * a + 16, 32 * a:32 * a + 16],
                         E_sb[:, 0:16], ident[:, 32 * a:32 * a + 16],
                         start=True, stop=True, tile_position=(0, 32 * a), waits=tuple(w2),
                         incs=((("atmm", 1),) if a == 3 else ()))
                prevd["atmm"] = k.tok("atmm")
                bdp = BD[it % 2]
                for a in range(4):
                    k.cp("vector", bdp[32 * a:32 * a + 16, 32 * a:32 * a + 16],
                         PS_E[32 * a:32 * a + 16, 32 * a:32 * a + 16],
                         waits=(prevd["atmm"],) if a == 0 else (),
                         incs=((("bdev", 1),) if a == 3 else ()))
                prevd["bdev"] = k.tok("bdev")
                w2 = [prevd["bdev"]]
                if g2 == 0 and h == 0:
                    w2.append(tok_vrp)
                    w2.append((PEV, PEV.n))
                if h == 0 and g2 == 1:
                    w2.append((PEV, PEV.n))
                k.mm(PS_B[:, h * 256:(h + 1) * 256], bdp[:, 0:128],
                     vpad[g2][:, h * 256:(h + 1) * 256],
                     start=True, stop=True, waits=tuple(w2), incs=(("avmm", 1),))
            tok_av = k.tok("avmm")
            for h in range(FH):
                w4 = ()
                if h == 0:
                    w4 = (tok_av, k.tok("crec"))
                    if g2 == 0 and dt >= 1:
                        w4 = w4 + (prevd["oT_mm"],)
                k.ts("vector", o_sb[:, g2 * 1024 + h * 256: g2 * 1024 + (h + 1) * 256],
                     PS_B[:, h * 256:(h + 1) * 256], cb[:, h:h + 1], ALU.mult,
                     waits=w4, incs=((PEV, 1),) if h == 3 else ())
        tok_oev = k.tok("PEV")
        for g2 in range(2):
            for c in range(8):
                dst = oT[:, c * 128 + g2 * 64: c * 128 + g2 * 64 + 64].rearrange(
                    "p (a l) -> p a l", a=4)
                tponce(dst,
                       o_sb[:, g2 * 1024 + c * 128: g2 * 1024 + (c + 1) * 128],
                       extra_waits=(tok_oev,) if (g2 == 0 and c == 0) else (),
                       evac_waits=(prevd["op_mm"],) if (g2 == 0 and c == 0 and dt >= 1) else (),
                       sel=lambda ps: ps.rearrange("p (a r) -> p a r", a=4)[:, :, 0:16])
        prevd["oT_mm"] = k.tok("tpmm")
        tok_oT = k.tok("TPEV")
        for h in range(2):
            ps = PS_B[:, (h % 2) * 512:(h % 2) * 512 + 512]
            k.wait("tensor", PEV, PEV.n - 1)
            for jc in range(8):
                w2 = [tok_oT] + ([tok_op, cld2] if dt == 0 else []) if (h == 0 and jc == 0) else []
                k.mm(ps, oT[:, jc * 128:(jc + 1) * 128], WT_mv(0, jc, h * 512, 512),
                     start=(jc == 0), stop=(jc == 7), waits=tuple(w2),
                     incs=((("PMM", 1),) if jc == 7 else ()))
            w3 = [k.tok("PMM")] + ([(SM("LSd"), SM("LSd").n)] if dt >= 1 else [])
            k.tt("vector", mha[:, h * 512:(h + 1) * 512], ps,
                 cslot(0)[:, h * 512:(h + 1) * 512], ALU.add, waits=tuple(w3),
                 incs=((PEV, 1),))
            k.tt("vector", mha[:, h * 512:(h + 1) * 512], mha[:, h * 512:(h + 1) * 512],
                 xd2[:, h * 512:(h + 1) * 512], ALU.add,
                 waits=((SM("xd2l"), SM("xd2l").n),), incs=(("mhax", 1),))
        prevd["op_mm"] = k.tok("PMM")
        prevd["sc_mm"] = k.tok("smm")
        pre_ln = [k.tok("mhax")] + ([prevd["y1_w"], prevd["y1T_mm"]] if dt >= 1 else [])
        tok_y1 = k.ln(y1b, mha, sqd, stat8(2), eps5, D, gamma=cslot(1), beta=cslot(2),
                      pre=tuple(pre_ln), sem_v="LVd", sem_s="LSd")
        k.dma("gpsimd", y1_scr[dt * 128:(dt + 1) * 128, :], y1b, sem=SM("y1w"),
              waits=(tok_y1,))
        if stage == "y1":
            k.dma("gpsimd", yout[dt * 128:(dt + 1) * 128, :], y1b, sem=SM("y1w"))
        prevd["y1_w"] = k.tok("y1w")
        for c in range(8):
            ev_w = ()
            if c == 0 and dt >= 1:
                ev_w = (prevd["gt_mm"], (SM("y1tw"), 16 * dt))
            tponce(y1Tb[:, c * 128:(c + 1) * 128], y1b[:, c * 128:(c + 1) * 128],
                   extra_waits=(tok_y1,) if c == 0 else (),
                   evac_waits=ev_w)
        prevd["y1T_mm"] = k.tok("tpmm")
        tok_y1T = k.tok("TPEV")
        k.dma("gpsimd", y1T_scr.rearrange("(c p) t -> p c t", c=8)[:, :, dt * 128:(dt + 1) * 128],
              y1Tb, sem=SM("y1tw"), waits=(tok_y1T,))
        for jc in range(8):
            w2 = []
            if jc == 0:
                w2 = [tok_y1T, (SM("idl"), SM("idl").n)]
                if dt >= 1:
                    w2.append((SM("gev"), SM("gev").n))
            k.mm(PS_F[:, 0:8], y1Tb[:, jc * 128:(jc + 1) * 128], wg0[:, jc * 8:(jc + 1) * 8],
                 start=(jc == 0), stop=(jc == 7), waits=tuple(w2),
                 incs=((("gmm", 1),) if jc == 7 else ()))
        gl = SMALL[:, 128:136]
        nmx = SMALL[:, 136:137]
        ge = SMALL[:, 144:152]
        den = SMALL[:, 152:153]
        rcp = SMALL[:, 153:154]
        emx = SMALL[:, 154:155]
        mask_sl = GT[:, dt * 24: dt * 24 + 8]
        pref_sl = GT[:, dt * 24 + 8: dt * 24 + 16]
        gv_sl = GT[:, dt * 24 + 16: dt * 24 + 17]
        prevd["gt_mm"] = k.tok("gmm")
        k.cp("vector", gl[:], PS_F[:, 0:8], waits=(k.tok("gmm"),))
        k.red("vector", nmx[:], gl[:], ALU.max, negate=True, incs=(("gv1", 1),))
        k.act(ge[:], gl[:], AF.Exp, bias=nmx[:], waits=(k.tok("gv1"),), incs=(("gs1", 1),))
        k.red("vector", den[:], ge[:], ALU.add, waits=(k.tok("gs1"),))
        k.emit("vector", lambda e, o=rcp[:], i=den[:]: e.reciprocal(o, i))
        k.red("vector", emx[:], ge[:], ALU.max, incs=(("gv2", 1),))
        k.tt("vector", gv_sl, emx[:], rcp[:], ALU.mult, waits=(k.tok("gv2"),))
        k.ts("vector", mask_sl, ge[:], emx[:], ALU.is_equal,
             waits=(((SM("tofz"), SM("tofz").n),) if dt == 0 else ()),
             incs=(("gev", 1),))
        for (reg, lhs, inc) in [((0, 16, 8, 16), selS[:], "cntmm"), ((0, 128, 16, 24), tri[:], "cntmm")]:
            p0, p1, c0, c1 = reg
            k.mm(PS_F[p0:p1, c0:c1], lhs, mask_sl, start=True, stop=True,
                 waits=((k.tok("gev"),) if c0 == 8 else ()) +
                       (((SM("cadd"), SM("cadd").n),) if (c0 == 8 and dt >= 1) else ()),
                 incs=((inc, 1),))
        tok_cnt = k.tok("cntmm")
        cntacc = RUN[:, 280:288]
        if dt == 0:
            k.memset("vector", RUN[0:16, 280:288], 0.0)
        k.cp("vector", RUN[0:16, dt * 8:(dt + 1) * 8], RUN[0:16, 280:288],
             waits=(tok_cnt,))
        k.tt("vector", RUN[0:16, 280:288], RUN[0:16, 280:288], PS_F[0:16, 8:16], ALU.add)
        k.cp("vector", pref_sl, PS_F[:, 16:24], incs=(("cadd", 1),))

    k.dma("gpsimd", agi_cnt, RUN[0:16, 280:288], sem=SM("cntw"), waits=(k.tok("cadd"),))
    k.collective("AllGather", [agi_cnt.opt()], [ago_cnt.opt()],
                 waits=((SM("cntw"), SM("cntw").n),), incs=(("A2Ac", 1),))
    agc = SMALL[:, 160:168]
    k.dma("sync", agc, ago_cnt, sem=SM("agcl"), waits=((SM("A2Ac"), 1),))
    k.mm(PS_F[0:16, 24:32], m1c[:], agc[:], start=True, stop=True,
         waits=((SM("agcl"), SM("agcl").n), k.tok("cadd")), incs=(("amm", 1),))
    A_sb = RUN[0:16, 288:296]
    k.cp("vector", A_sb, PS_F[0:16, 24:32], waits=(k.tok("amm"),), incs=(("aev", 1),))
    tmp16 = RUN[0:16, 296:304]
    rank = SMALL[:, 168:176]
    tmp8 = SMALL[:, 176:184]
    ro = SMALL[:, 184:185]
    kp = SMALL[:, 185:186]
    sc = SMALL[:, 186:187]
    for dt in range(NT):
        mask_sl = GT[:, dt * 24: dt * 24 + 8]
        pref_sl = GT[:, dt * 24 + 8: dt * 24 + 16]
        gv_sl = GT[:, dt * 24 + 16: dt * 24 + 17]
        k.tt("vector", tmp16, A_sb, RUN[0:16, dt * 8:(dt + 1) * 8], ALU.add,
             waits=(k.tok("aev"),) if dt == 0 else ((SM("rkmm"), SM("rkmm").n),),
             incs=(("t16", 1),))
        k.mm(PS_F[:, 32:40], selST[:], tmp16, start=True, stop=True,
             waits=(k.tok("t16"),), incs=(("rkmm", 1),))
        k.stt("vector", rank, PS_F[:, 32:40], 1.0, pref_sl, ALU.mult, ALU.add,
              waits=(k.tok("rkmm"),))
        k.tt("vector", tmp8, rank, mask_sl, ALU.mult)
        k.red("vector", ro[:], tmp8, ALU.add)
        k.ts("vector", kp[:], ro[:], 4096.0, ALU.is_lt)
        k.tt("vector", sc[:], kp[:], gv_sl, ALU.mult, incs=(("scv", 1),))
        k.ts("vector", S8[:, dt * 8:(dt + 1) * 8], mask_sl, sc[:], ALU.mult,
             incs=(("s8v", 1),))


    ytmp = [TRANS[:, 256:1280], TRANS[:, 1280:2304]]
    y1t_b = [TRANS[:, 3328:4352], TRANS[:, 4352:5376]]
    for e in range(E):
        for jc in range(8):
            w2 = [(SM("AG_we"), 1)] if (e == 0 and jc == 0) else []
            if e == 0 and jc == 0:
                w2 += [k.tok("avmm"), k.tok("PMM"), k.tok("bdev"), (SM("vrp"), SM("vrp").n)]
            if e >= 1 and jc == 0:
                w2.append(moe_prev_mm)
            k.dma("sync", BIGB[:, jc * 1024:(jc + 1) * 1024],
                  ago_we[e * 1024 + jc * 128: e * 1024 + (jc + 1) * 128, :],
                  sem=SM("wel"), waits=tuple(w2))
        k.dma("sync", rowbuf2[0:1, 0:1024], ago_ce[e:e + 1, :], sem=SM("cel"),
              waits=((SM("AG_ce"), 1),) if e == 0 else (k.tok("moemm"),))
        tok_wel = k.tok("wel")
        tok_cel = k.tok("cel")
        ROT4 = (PS_B[:, 0:512], PS_B[:, 512:1024], PS_C[:, 0:512], PS_E[:, 0:512])
        for dt in range(NT):
            par = dt % 2
            idx = e * NT + dt
            k.dma("sync", y1t_b[par], y1T_scr.rearrange("(c p) t -> p c t", c=8)
                  [:, :, dt * 128:(dt + 1) * 128], sem=SM("y1tl"),
                  waits=((SM("y1tw"), 16 * NT) if (e == 0 and dt == 0) else None,
                         (SM("moemm"), 2 * (idx - 1)) if idx >= 2 else None))
            tok_yl = k.tok("y1tl")
            for dh in range(2):
                ps = ROT4[(dt * 2 + dh) % 4]
                k.wait("tensor", PEV, PEV.n - 3)
                for jc in range(8):
                    w2 = []
                    if jc == 0:
                        w2 = [tok_yl]
                        if dt == 0:
                            w2.append(tok_wel)

                    k.mm(ps, y1t_b[par][:, jc * 128:(jc + 1) * 128],
                         BIGB[:, jc * 1024 + dh * 512: jc * 1024 + dh * 512 + 512],
                         start=(jc == 0), stop=False, waits=tuple(w2))
                k.mm(ps, onesrow[0:1, :], rowbuf2[0:1, dh * 512: dh * 512 + 512],
                     start=False, stop=True,
                     waits=(tok_cel,) if (dt == 0 and dh == 0) else (),
                     incs=(("moemm", 1),))
                k.ts("vector", ytmp[par][:, dh * 512:(dh + 1) * 512], ps,
                     S8[:, dt * 8 + e: dt * 8 + e + 1], ALU.mult,
                     waits=(k.tok("moemm"), k.tok("s8v") if (e == 0 and dt == 0) else None,
                            (SM("macc%d" % par), SM("macc%d" % par).n)
                            if (e * NT + dt >= 2) else None),
                     incs=((PEV, 1),) + (() if dh == 0 else ()))
            k.dma("gpsimd", moe0_scr[dt * 128:(dt + 1) * 128, :], ytmp[par],
                  sem=SM("macc%d" % par), waits=(k.tok("PEV"),), accum=(e > 0))
        moe_prev_mm = k.tok("moemm")
    for si, row in [(0, 8), (1, 9)]:
        k.dma("sync", cslot(si), cvec[row], sem=SM("cld"),
              waits=((SM("LSd"), SM("LSd").n), k.tok("PEV")) if si == 0 else ())
    cld3 = k.tok("cld")
    d5a = BIGB[:, 0:1024]
    d5b = BIGB[:, 1024:2048]
    d5y = BIGB[:, 2048:3072]
    d5g = BIGB[:, 3072:4096]
    d5sq = BIGB[:, 4096:5120]
    for dt in range(NT):
        w2 = ([(SM("macc0"), SM("macc0").n), (SM("macc1"), SM("macc1").n)]
              if dt == 0 else [(SM("lnf5"), SM("lnf5").n)])
        if dt == 0:
            w2 += [k.tok("moemm"), k.tok("PEV")]
        k.dma("sync", d5a, y1_scr[dt * 128:(dt + 1) * 128, :], sem=SM("d5l"), waits=tuple(w2))
        k.dma("sync", d5b, moe0_scr[dt * 128:(dt + 1) * 128, :], sem=SM("d5l"))
        k.tt("vector", d5y, d5a, d5b, ALU.add, waits=((SM("d5l"), SM("d5l").n),),
             incs=(("hy2", 1),))
        pre_ln = [k.tok("hy2")] + ([cld3] if dt == 0 else []) + \
                 ([(SM("ccb"), 16 * dt * (2 if stage == "g" else 1))] if dt >= 1 else [])
        tok_y2 = k.ln(d5y, d5y, d5sq, stat8(3), eps5, D, gamma=cslot(0), beta=cslot(1),
                      pre=tuple(pre_ln), sem_v="LV5", sem_s="LS5")
        k.act(d5g, d5y, AF.Silu, waits=(tok_y2,), incs=(("lnf5", 1),))
        k.dma("gpsimd", ccdet_ap(cc2_in, dt), d5g, sem=SM("ccb"),
              waits=(k.tok("lnf5"),))
        if stage == "g":
            k.dma("gpsimd", yout[dt * 128:(dt + 1) * 128, :], d5g, sem=SM("ccb"))
    if stage in ("y1", "g"):
        k.wait("gpsimd", SM("y1w" if stage == "y1" else "ccb"),
               SM("y1w" if stage == "y1" else "ccb").n)
        k.build_block()
        return nc

    k.collective("AllToAll", [cc2_in.opt()], [cc2_out.opt()],
                 waits=((SM("ccb"), SM("ccb").n),), incs=(("A2A2", 1),))

    for jc in range(8):
        k.dma("sync", BIGA[:, jc * 3072:(jc + 1) * 3072],
              ag["wqkv1"][1][jc * 128:(jc + 1) * 128, :], sem=SM("wq1l"),
              waits=((k.tok("AG_wqkv1"), k.tok("moemm"), k.tok("PMM"), k.tok("wemm"))
                     if jc == 0 else ()))
    tok_wq1 = k.tok("wq1l")
    c1x = BIGB[:, 0:1024]
    c1s = BIGB[:, 1024:2048]
    c1g = BIGB[:, 2048:3072]
    c1x1 = BIGB[:, 3072:4096]
    c1xn = BIGB[:, 4096:5120]
    c1xnT = BIGB[:, 5120:6144]
    c1qkv = BIGB[:, 6144:9216]
    c1sq = BIGB[:, 9216:10240]
    prev1 = {}
    for ti in range(NT):
        b_loc, s0 = ti // 16, (ti % 16) * 128
        i_blk = s0 // 256
        r0 = i_blk * 512 + b_loc * 256 + (s0 % 256)
        w0 = [(SM("A2A2"), 1)] if ti == 0 else [prev1["x1cmb"]]
        if ti == 0:
            w0 += [k.tok("lnf5"), k.tok("LS5"), k.tok("moemm"), k.tok("PEV"),
                   k.tok("hy2"), (SM("d5l"), SM("d5l").n)]
        k.dma("sync", c1x, xin[ti * 128:(ti + 1) * 128, :], sem=SM("c1l"), waits=tuple(w0))
        k.dma("sync", c1s, s_scr[ti * 128:(ti + 1) * 128, :], sem=SM("c1l"))
        k.dma("sync", c1g, cc2_out[r0:r0 + 128, :], sem=SM("c1l"))
        tok_l = k.tok("c1l")
        w1_ = [tok_l] + ([prev1["x1w"], prev1["xn1_tok"]] if ti >= 1 else [])
        k.tt("vector", c1s, c1s, c1g, ALU.mult, waits=tuple(w1_))
        k.tt("vector", c1x1, c1x, c1s, ALU.add, incs=(("x1c", 1),))
        prev1["x1cmb"] = k.tok("x1c")
        k.dma("gpsimd", x1_scr[ti * 128:(ti + 1) * 128, :], c1x1, sem=SM("x1w"),
              waits=(k.tok("x1c"),))
        if stage == "x1":
            if os.environ.get("KB_X1DUMP") == "cc2raw":
                k.dma("gpsimd", yout[ti * 128:(ti + 1) * 128, :],
                      cc2_in[ti * 128:(ti + 1) * 128, :], sem=SM("x1w"),
                      waits=((SM("ccb"), 16 * NT) if ti == 0 else None,))
            elif os.environ.get("KB_X1DUMP") == "cc2oraw":
                k.dma("gpsimd", yout[ti * 128:(ti + 1) * 128, :],
                      cc2_out[ti * 128:(ti + 1) * 128, :], sem=SM("x1w"),
                      waits=(((SM("A2A2"), 1)) if ti == 0 else None,))
            elif os.environ.get("KB_X1DUMP"):
                dumpsrc = {"g": c1g, "s": c1s, "x": c1x}.get(os.environ.get("KB_X1DUMP", ""), c1x1)
                k.dma("gpsimd", yout[ti * 128:(ti + 1) * 128, :], dumpsrc, sem=SM("x1w"))
        prev1["x1w"] = k.tok("x1w")
        pre = [k.tok("x1c")] + ([prev1["xnT_mm"]] if ti >= 1 else [])
        tok_xn1 = k.ln(c1xn, c1x1, c1sq, stat8(4), eps5, D, pre=tuple(pre),
                       sem_v="LV6", sem_s="LS6")
        prev1["xn1_tok"] = tok_xn1
        k.dma("gpsimd", xn1_scr[ti * 128:(ti + 1) * 128, :], c1xn, sem=SM("xnTw"),
              waits=(tok_xn1,))
        for c in range(8):
            tponce(c1xnT[:, c * 128:(c + 1) * 128], c1xn[:, c * 128:(c + 1) * 128],
                   extra_waits=(tok_xn1,) if c == 0 else (),
                   evac_waits=(prev1["qkv_mm"], (SM("xnTw"), 16 * ti)) if (c == 0 and ti >= 1) else ())
        prev1["xnT_mm"] = k.tok("tpmm")
        tok_xnT = k.tok("TPEV")
        def evq1(h, ps, w2):
            w2 = list(w2) + ([prev1["q1sp"]] if ti >= 1 else [])
            k.cp("vector", c1qkv[:, h * 512:(h + 1) * 512], ps, waits=tuple(w2),
                 incs=((PEV, 1),))
        proj_tm(c1xnT, 0, 3072, evq1, istride=3072,
                first_waits=(tok_xnT,) + ((tok_wq1,) if ti == 0 else ()))
        prev1["qkv_mm"] = k.tok("PMM")
        k.dma("gpsimd", qkv1_scr[ti * 128:(ti + 1) * 128, :], c1qkv, sem=SM("q1sp"),
              waits=(k.tok("PEV"),))
        prev1["q1sp"] = k.tok("q1sp")
        for jc in range(8):
            w2 = []
            if jc == 0:
                w2 = [tok_xnT, (SM("idl"), SM("idl").n)]
                if ti >= 1:
                    w2.append((SM("g1ev"), SM("g1ev").n))
            k.mm(PS_F[:, 0:8], c1xnT[:, jc * 128:(jc + 1) * 128], wg1[:, jc * 8:(jc + 1) * 8],
                 start=(jc == 0), stop=(jc == 7), waits=tuple(w2),
                 incs=((("g1mm", 1),) if jc == 7 else ()))
        gl = SMALL[:, 192:200]
        mm1 = SMALL[:, 200:201]
        mm2 = SMALL[:, 201:202]
        mlg = SMALL[:, 208:216]
        w12 = SMALL[:, 216:218]
        msk1 = SMALL[:, 224:232]
        msk2 = SMALL[:, 232:240]
        k.cp("vector", gl[:], PS_F[:, 0:8], waits=(k.tok("g1mm"),), incs=(("g1ev", 1),))
        k.red("vector", mm1[:], gl[:], ALU.max)
        k.ts("vector", msk1[:], gl[:], mm1[:], ALU.is_equal, incs=(("g1v", 1),))
        k.stt("vector", mlg[:], msk1[:], -1e30, gl[:], ALU.mult, ALU.add)
        k.red("vector", mm2[:], mlg[:], ALU.max)
        k.ts("vector", msk2[:], mlg[:], mm2[:], ALU.is_equal)
        k.tt("vector", w12[:, 0:1], mm1[:], mm2[:], ALU.subtract,
             waits=(((SM("slw"), SM("slw").n),) if ti >= 1 else ()),
             incs=(("g1v", 1),))
        k.act(w12[:, 0:1], w12[:, 0:1], AF.Sigmoid, waits=(k.tok("g1v"),),
              incs=(("g1s", 1),))
        k.ts("vector", w12[:, 1:2], w12[:, 0:1], -1.0, ALU.mult,
             waits=(k.tok("g1s"),))
        k.ts("vector", w12[:, 1:2], w12[:, 1:2], 1.0, ALU.add, incs=(("VM", 1),))
        um8 = SMALL[:, 256:264]
        k.tt("vector", um8[:], msk1[:], msk2[:], ALU.add,
             waits=((k.tok("pfmm"),) if ti >= 1 else ()), incs=(("umv", 1),))
        k.dma("gpsimd", wv_scr[ti * 128:(ti + 1) * 128, :], w12[:, 0:2], sem=SM("slw"),
              waits=(k.tok("g1s"), (SM("VM"), SM("VM").n)))
        k.mm(PS_D[:, 0:8], tri0[:], um8[:], start=True, stop=True,
             waits=(k.tok("umv"),) + ((k.tok("rkv"),) if ti >= 1 else ()),
             incs=(("pfmm", 1),))
        k.mm(PS_D[0:1, 8:16], onescol[:, 0:1], um8[:], start=True, stop=True,
             incs=(("pfmm", 1),))
        run8 = SMALL[:, 264:272]
        if ti == 0:
            k.memset("vector", run8[0:1, :], 0.0, incs=(("runv", 1),))
        k.tt("vector", SMALL[0:1, 272:280], run8[0:1, :], cpe_base[0:1, :], ALU.add,
             waits=((SM("runv"), SM("runv").n),) + ((k.tok("pfmm"),) if ti >= 1 else ()),
             incs=(("runv", 1),))
        k.mm(PS_E[:, 0:8], onesrow[0:1, :], SMALL[0:1, 272:280], start=True, stop=True,
             waits=(k.tok("runv"),), incs=(("pfmm", 1),))
        k.tt("vector", run8[0:1, :], run8[0:1, :], PS_D[0:1, 8:16], ALU.add,
             waits=(k.tok("pfmm"),), incs=(("runv", 1),))
        rank8 = SMALL[:, 280:288]
        k.cp("vector", rank8[:], PS_D[:, 0:8], waits=(k.tok("pfmm"),))
        k.tt("vector", rank8[:], rank8[:], PS_E[:, 0:8], ALU.add)
        tmp8b = SMALL[:, 288:296]
        slotf = SMALL[:, 296:298]
        for kk_, mk in ((0, msk1), (1, msk2)):
            k.tt("vector", tmp8b[:], rank8[:], mk[:], ALU.mult)
            k.red("vector", slotf[:, kk_:kk_ + 1], tmp8b[:], ALU.add)
        k.ts("vector", slotf[:, 0:2], slotf[:, 0:2], float(8 * 1536), ALU.min,
             incs=(("rkv", 1),))
        k.cp("vector", IDXS[:, 0:2], slotf[:, 0:2],
             waits=(k.tok("rkv"),) + (((SM("sctw"), SM("sctw").n),) if ti >= 1 else ()),
             incs=(("rkv", 1),))
        k.dma("gpsimd", slots_scr[ti * 128:(ti + 1) * 128, :], IDXS[:, 0:2],
              sem=SM("slw"), waits=(k.tok("rkv"),))
        k.emit("gpsimd", lambda e, o=IDXS[:, 2:3], b=ti * 128:
               e.iota(o, pattern=[[0, 1]], base=b, channel_multiplier=1),
               waits=((SM("tofz"), SM("tofz").n), (SM("sctw"), SM("sctw").n)),
               incs=(("sctw", 16),))
        from concourse.bass_core_rust import IndirectOffsetOnAxis as _IOA
        for kk_ in range(2):
            k.emit("gpsimd", lambda e, ix=IDXS[:, kk_:kk_ + 1], tk=IDXS[:, 2:3]:
                   e.indirect_dma_start(tok_slot[:, :], _IOA(ix, 0), tk, None),
                   waits=((SM("sctw"), SM("sctw").n), k.tok("rkv")),
                   incs=(("sctw", 16),), drain=False)
        k.ts("vector", msk1[:], msk1[:], w12[:, 0:1], ALU.mult,
             waits=((SM("slw"), SM("slw").n),))
        k.ts("vector", msk2[:], msk2[:], w12[:, 1:2], ALU.mult)
        k.tt("vector", S8[:, 256 + ti * 8: 256 + (ti + 1) * 8], msk1[:], msk2[:],
             ALU.add, incs=(("s81", 1),))
    if stage == "x1":
        if not os.environ.get("KB_X1DUMP"):
            for ti in range(NT):
                k.dma("gpsimd", yout[ti * 128:(ti + 1) * 128, :],
                      x1_scr[ti * 128:(ti + 1) * 128, :], sem=SM("yw"),
                      waits=((SM("x1w"), SM("x1w").n) if ti == 0 else None,))
            k.wait("gpsimd", SM("yw"), SM("yw").n)
        k.wait("gpsimd", SM("x1w"), SM("x1w").n)
        k.build_block()
        return nc

    def vtile(kt):
        if kt < 8:
            return BIGA[:, 16384 + kt * 1024: 16384 + (kt + 1) * 1024]
        return BIGB[:, (kt - 8) * 1024: (kt - 7) * 1024]

    qtb = BIGB[:, 8192:9216]
    qTb = BIGB[:, 9216:10240]
    probs = BIGB[:, 10240:12288]
    pT = [GT[:, 0:128], GT[:, 128:256]]
    o1sb = TRANS[:, 0:1024]
    rr1 = SMALL[:, 240:241]
    mx4 = SMALL[:, 244:248]
    nmb = SMALL[:, 248:249]
    prev2 = {}
    for r in range(B):
        for kt in range(16):
            tt_g = [(SM("q1sp"), 16 * NT)] if (r == 0 and kt == 0) else []
            if r == 1 and kt == 0:
                tt_g = [prev2["sc1_mm"], k.tok("PMM")]
            k.dma("sync", TRANS[:, 1024:2048],
                  qkv1_scr[r * 2048 + kt * 128: r * 2048 + (kt + 1) * 128, 1024:2048],
                  sem=SM("ktl"), waits=tuple(tt_g) + ((TPEV, TPEV.n),))
            tk = k.tok("ktl")
            for c in range(8):
                tponce(BIGA[:, c * 2048 + kt * 128: c * 2048 + (kt + 1) * 128],
                       TRANS[:, 1024 + c * 128: 1024 + (c + 1) * 128],
                       extra_waits=(tk,) if c == 0 else ())
        tok_kT = k.tok("TPEV")
        for kt in range(16):
            w2 = []
            if kt == 0:
                w2 = [tok_kT] + ([prev2["av_mm"]] if r == 1 else [k.tok("wq1l"), k.tok("g1mm")])
            k.dma("sync", vtile(kt),
                  qkv1_scr[r * 2048 + kt * 128: r * 2048 + (kt + 1) * 128, 2048:3072],
                  sem=SM("v1l"), waits=tuple(w2))
        tok_v1 = k.tok("v1l")
        for qt in range(16):
            ti_g = r * 16 + qt
            k.dma("sync", qtb, qkv1_scr[r * 2048 + qt * 128: r * 2048 + (qt + 1) * 128, 0:1024],
                  sem=SM("q1l"), waits=(prev2.get("qT_mm"),))
            tq = (SM("q1l"), SM("q1l").n)
            for c in range(8):
                tponce(qTb[:, c * 128:(c + 1) * 128], qtb[:, c * 128:(c + 1) * 128],
                       extra_waits=(tq, tok_v1) if c == 0 else (),
                       evac_waits=(prev2["sc1_mm"],) if "sc1_mm" in prev2 else ())
            prev2["qT_mm"] = k.tok("tpmm")
            tok_qT = k.tok("TPEV")
            for h in range(16):
                pb = 64 * (h % 2)
                segs = [PS_B[:, 0:512], PS_B[:, 512:1024], PS_C[:, 0:512], PS_E[:, 0:512]]
                for kb_i in range(4):
                    w2 = []
                    if kb_i == 0:
                        w2 = [tok_qT]
                        w2.append((SM("SCM"), SM("SCM").n))
                        if h == 0:
                            w2.append((PEV, PEV.n))
                    k.mm(segs[kb_i],
                         qTb[pb:pb + 64, (h // 2) * 128:(h // 2) * 128 + 128],
                         BIGA[pb:pb + 64, (h // 2) * 2048 + kb_i * 512: (h // 2) * 2048 + (kb_i + 1) * 512],
                         start=True, stop=True, waits=tuple(w2),
                         incs=((("sc1mm", 1),) if kb_i == 3 else ()))
                tok_sc = k.tok("sc1mm")
                for kb_i in range(4):
                    k.red("vector", mx4[:, kb_i:kb_i + 1], segs[kb_i], ALU.max,
                          waits=(tok_sc,) if kb_i == 0 else ())
                k.red("vector", nmb[:], mx4[:, 0:4], ALU.max, negate=True)
                k.ts("vector", nmb[:], nmb[:], 1.0 / 32.0, ALU.mult, incs=(("sm1v", 1),))
                for kb_i in range(4):
                    k.act(probs[:, kb_i * 512:(kb_i + 1) * 512], segs[kb_i], AF.Exp,
                          bias=nmb[:], scale=1.0 / 32.0,
                          waits=(k.tok("sm1v"),) if kb_i == 0 else (),
                          incs=((("SCM", 1),) if kb_i == 3 else ()))
                tok_exp = (SM("SCM"), SM("SCM").n)
                k.red("vector", rr1[:], probs[:], ALU.add, waits=(tok_exp,))
                k.emit("vector", lambda e, o=rr1[:], i=rr1[:]: e.reciprocal(o, i),
                       incs=(("VM", 1),))
                tok_rr = (SM("VM"), SM("VM").n)
                ops_ = PS_D if h % 2 == 0 else PS_F
                pTbuf = TRANS[:, 2304:4352]
                for kb4 in range(4):
                    k.pyctr["tpar"] = k.pyctr.get("tpar", 0) + 1
                    pslot = k.pyctr["tpar"] % 2
                    for q4 in range(4):
                        kt = kb4 * 4 + q4
                        k.emit("tensor",
                               lambda ee, o=PS_A[:, pslot * 512 + q4 * 128: pslot * 512 + (q4 + 1) * 128],
                               i=probs[:, kt * 128:(kt + 1) * 128]:
                                   ee.transpose(o, i, ident[0:128, 0:128]),
                               waits=((tok_exp,) if kt == 0 else ()) +
                                     (((TPEV, TPEV.n - 1),) if q4 == 0 else ()),
                               incs=(("tpmm", 1),))
                    lag = 4 * (3 - kb4)
                    gv_ = SM("av1").n - lag
                    k.cp("vector", pTbuf[:, kb4 * 512:(kb4 + 1) * 512],
                         PS_A[:, pslot * 512: pslot * 512 + 512],
                         waits=(k.tok("tpmm"),) +
                               (((SM("av1"), gv_),) if gv_ >= 1 else ()),
                         incs=((TPEV, 1),))
                tok_pT = k.tok("TPEV")
                for kt in range(16):
                    w2 = []
                    if kt == 0:
                        w2 = [tok_pT, (SM("oev1"), SM("oev1").n)]
                    elif kt % 4 == 0:
                        pass
                    k.mm(ops_[:, 0:64], pTbuf[:, kt * 128:(kt + 1) * 128],
                         vtile(kt)[:, h * 64:(h + 1) * 64],
                         start=(kt == 0), stop=(kt == 15), waits=tuple(w2),
                         incs=(("av1", 1),))
                k.ts("vector", o1sb[:, h * 64:(h + 1) * 64], ops_[:, 0:64], rr1[:],
                     ALU.mult, waits=(k.tok("av1"), tok_rr), incs=(("oev1", 1),))
            prev2["sc1_mm"] = k.tok("sc1mm")
            prev2["av_mm"] = k.tok("av1")
            k.dma("gpsimd", o1_scr[ti_g * 128:(ti_g + 1) * 128, :], o1sb,
                  sem=SM("o1w"), waits=(k.tok("oev1"),))
            prev2["o1_w"] = k.tok("o1w")
    for jc in range(8):
        k.dma("sync", BIGA[:, jc * 1024:(jc + 1) * 1024],
              ag["wout1"][1][jc * 128:(jc + 1) * 128, :], sem=SM("wo1l"),
              waits=((k.tok("AG_wout1"), k.tok("sc1mm"), k.tok("av1"), k.tok("TPEV"))
                     if jc == 0 else ()))
    tok_wo1 = k.tok("wo1l")
    c2b_o = BIGB[:, 0:1024]
    c2b_oT = BIGB[:, 1024:2048]
    c2b_s = BIGB[:, 2048:3072]
    prev3 = {}
    for ti in range(NT):
        k.dma("sync", c2b_o, o1_scr[ti * 128:(ti + 1) * 128, :], sem=SM("o1l"),
              waits=((SM("o1w"), 16 * NT), k.tok("v1l"), (SM("q1l"), SM("q1l").n))
                    if ti == 0 else (prev3["oT_mm"],))
        to_ = (SM("o1l"), SM("o1l").n)
        for c in range(8):
            tponce(c2b_oT[:, c * 128:(c + 1) * 128], c2b_o[:, c * 128:(c + 1) * 128],
                   extra_waits=(to_,) if c == 0 else (),
                   evac_waits=(prev3["sp_mm"],) if (c == 0 and ti >= 1) else ())
        prev3["oT_mm"] = k.tok("tpmm")
        tok_oT2 = k.tok("TPEV")

        def evs1(h, ps, w2):
            w2 = list(w2) + ([(SM("s1w"), 16 * ti)] if ti >= 1 else [])
            k.cp("vector", c2b_s[:, h * 512:(h + 1) * 512], ps, waits=tuple(w2),
                 incs=((PEV, 1),))
        proj_tm(c2b_oT, 0, 1024, evs1,
                first_waits=(tok_oT2,) + ((tok_wo1,) if ti == 0 else ()))
        prev3["sp_mm"] = k.tok("PMM")
        k.dma("gpsimd", s1_scr[ti * 128:(ti + 1) * 128, :], c2b_s, sem=SM("s1w"),
              waits=(k.tok("PEV"),))
        if stage == "s1":
            k.dma("gpsimd", yout[ti * 128:(ti + 1) * 128, :], c2b_s, sem=SM("s1y"))
    if stage == "s1":
        k.wait("gpsimd", SM("s1y"), SM("s1y").n)
        k.build_block()
        return nc

    from concourse.bass_core_rust import IndirectOffsetOnAxis as _IOA2
    CG = 3
    hq = TRANS[:, 0:4096]
    hq_h = TRANS[:, 0:2048].bitcast(BF16)
    hq_l = TRANS[:, 2048:4096].bitcast(BF16)
    stg = TRANS[:, 0:4096]
    ytmp1 = TRANS[:, 4096:5120]
    cxT = BIGB[:, 0:12288]
    for e in range(E):
        for ct in range(12):
            sbase = e * CPE + ct * 128
            w2 = []
            if e == 0 and ct == 0:
                w2 = [(SM("sctw"), SM("sctw").n), (SM("xnTw"), 16 * NT),
                      k.tok("PMM"), k.tok("PEV"), (SM("s1w"), SM("s1w").n),
                      (SM("o1l"), SM("o1l").n)]
            if ct >= 1 or e >= 1:
                w2.append(k.tok("cxg"))
            k.dma("sync", IDXS[:, 4:5], tok_slot[sbase:sbase + 128, :], sem=SM("ixl"),
                  waits=tuple(w2))
            gw = [(SM("ixl"), SM("ixl").n)]
            if ct >= 4 or e >= 1:
                gw.append((SM("tpmm"), SM("tpmm").n - 24))
            if e >= 1 and ct == 0:
                gw.append(k.tok("m1mm"))
            k.emit("gpsimd", lambda ee, o=stg[:, (ct % 4) * 1024:(ct % 4) * 1024 + 1024],
                   ix=IDXS[:, 4:5]:
                   ee.indirect_dma_start(o, None, xn1_scr[:, :], _IOA2(ix, 0)),
                   waits=tuple(gw), incs=(("cxg", 16),), drain=False)
            tok_g = k.tok("cxg")
            for c in range(8):
                ev_w = ()
                if c == 0 and e >= 1 and ct == 0:
                    ev_w = (k.tok("h1mm"),)
                k.pyctr["tpar"] = k.pyctr.get("tpar", 0) + 1
                pslot = k.pyctr["tpar"] % 2
                pst = PS_A[:, pslot * 512: pslot * 512 + 128]
                k.emit("tensor",
                       lambda ee, o=pst, i=stg[:, (ct % 4) * 1024 + c * 128:(ct % 4) * 1024 + (c + 1) * 128]:
                           ee.transpose(o, i, ident[0:128, 0:128]),
                       waits=((tok_g,) if c == 0 else ()) + ((TPEV, TPEV.n - 1),),
                       incs=(("tpmm", 1),))
                co = c * 1536 + ct * 128
                hi_ap = BIGBh[:, co:co + 128]
                lo_ap = BIGBh[:, 12288 + co: 12288 + co + 128]
                k.cp("vector", hi_ap, pst, waits=(k.tok("tpmm"),) + tuple(ev_w))
                k.stt("vector", lo_ap, hi_ap, -1.0, pst, ALU.mult, ALU.add,
                      incs=((TPEV, 1),))
        tok_cxT = k.tok("TPEV")
        k.dma("sync", rowbuf[0:1, 0:1024], bout_r[0:1, e * 1024:(e + 1) * 1024],
              sem=SM("bo1l"),
              waits=(((SM("idl"), SM("idl").n),) if e == 0 else (k.tok("m1mm"),)))
        tok_bo = k.tok("bo1l")
        for fq in range(4):
            for jc in range(8):
                w2 = [(SM("wspl"), SM("wspl").n)]
                if jc == 0:
                    w2 += [(SM("AG_win"), 1)]
                k.dma("sync", BIGA[:, 16384 + jc * 1024: 16384 + (jc + 1) * 1024],
                      ag["win"][1][e * 1024 + jc * 128: e * 1024 + (jc + 1) * 128,
                                   fq * 1024:(fq + 1) * 1024],
                      sem=SM("wi1l"), waits=tuple(w2))
            twl = k.tok("wi1l")
            for jc in range(8):
                stga = BIGA[:, 16384 + jc * 1024: 16384 + (jc + 1) * 1024]
                w2 = [twl] if jc == 0 else []
                if jc == 0:
                    w2.append(k.tok("h1mm"))
                k.cp("vector", BIGAh[:, jc * 1024:(jc + 1) * 1024], stga, waits=tuple(w2))
                k.stt("vector", BIGAh[:, 8192 + jc * 1024: 8192 + (jc + 1) * 1024],
                      BIGAh[:, jc * 1024:(jc + 1) * 1024], -1.0, stga,
                      ALU.mult, ALU.add, incs=(("wspl", 1),))
            for fc in range(8):
                w2 = [(SM("wspl"), SM("wspl").n)]
                if fc == 0:
                    w2 += [(SM("AG_wout"), 1)]
                k.dma("sync", BIGA[:, 16384 + fc * 1024: 16384 + (fc + 1) * 1024],
                      ag["wout"][1][e * 4096 + fq * 1024 + fc * 128:
                                    e * 4096 + fq * 1024 + (fc + 1) * 128, :],
                      sem=SM("wi2l"), waits=tuple(w2))
            two = k.tok("wi2l")
            for fc in range(8):
                stga = BIGA[:, 16384 + fc * 1024: 16384 + (fc + 1) * 1024]
                w2 = [two, k.tok("m1mm")] if fc == 0 else []
                k.cp("vector", BIGAh[:, 16384 + fc * 1024: 16384 + (fc + 1) * 1024],
                     stga, waits=tuple(w2))
                k.stt("vector", BIGAh[:, 24576 + fc * 1024: 24576 + (fc + 1) * 1024],
                      BIGAh[:, 16384 + fc * 1024: 16384 + (fc + 1) * 1024], -1.0, stga,
                      ALU.mult, ALU.add, incs=(("wspl", 1),))
            tok_wi = k.tok("wspl")
            ROT = (PS_B[:, 0:512], PS_B[:, 512:1024], PS_C[:, 0:512], PS_E[:, 0:512])
            for gi in range(CG):
                for fc in range(8):
                    ps = ROT[fc % 4]
                    k.wait("tensor", PEV, PEV.n - 3)
                    for jc in range(8):
                        w2 = [tok_wi, tok_cxT] if (fc == 0 and jc == 0) else []
                        wi_h = BIGAh[:, jc * 1024 + fc * 128: jc * 1024 + fc * 128 + 128]
                        wi_l = BIGAh[:, 8192 + jc * 1024 + fc * 128: 8192 + jc * 1024 + fc * 128 + 128]
                        cx_h = BIGBh[:, jc * 1536 + gi * 512: jc * 1536 + (gi + 1) * 512]
                        cx_l = BIGBh[:, 12288 + jc * 1536 + gi * 512: 12288 + jc * 1536 + (gi + 1) * 512]
                        k.mm(ps, wi_h, cx_h, start=(jc == 0), stop=False, waits=tuple(w2))
                        k.mm(ps, wi_h, cx_l, start=False, stop=False)
                        k.mm(ps, wi_l, cx_h, start=False, stop=(jc == 7),
                             incs=((("h1mm", 1),) if jc == 7 else ()))
                    k.act(ytmp1[:, 0:512], ps, AF.Gelu,
                          bias=binsb[:, e * 32 + fq * 8 + fc: e * 32 + fq * 8 + fc + 1],
                          waits=(k.tok("h1mm"), (SM("m1w"), SM("m1w").n)),
                          incs=(("SCM", 1),))
                    k.cp("vector", hq_h[:, fc * 512:(fc + 1) * 512], ytmp1[:, 0:512],
                         waits=((SM("SCM"), SM("SCM").n),))
                    k.stt("vector", hq_l[:, fc * 512:(fc + 1) * 512],
                          hq_h[:, fc * 512:(fc + 1) * 512], -1.0, ytmp1[:, 0:512],
                          ALU.mult, ALU.add, incs=((PEV, 1),))
                tok_h = k.tok("PEV")
                for tsub in range(4):
                    for dh in range(2):
                        ps = ROT[(tsub * 2 + dh) % 4]
                        k.wait("tensor", PEV, PEV.n - 3)
                        for fc in range(8):
                            w2 = [tok_h] if (fc == 0 and dh == 0 and tsub == 0) else []
                            last = (fc == 7 and fq > 0)
                            hh_ = hq_h[:, fc * 512 + tsub * 128: fc * 512 + tsub * 128 + 128]
                            hl_ = hq_l[:, fc * 512 + tsub * 128: fc * 512 + tsub * 128 + 128]
                            wo_h = BIGAh[:, 16384 + fc * 1024 + dh * 512: 16384 + fc * 1024 + dh * 512 + 512]
                            wo_l = BIGAh[:, 24576 + fc * 1024 + dh * 512: 24576 + fc * 1024 + dh * 512 + 512]
                            k.mm(ps, hh_, wo_h, start=(fc == 0), stop=False, waits=tuple(w2))
                            k.mm(ps, hh_, wo_l, start=False, stop=False)
                            k.mm(ps, hl_, wo_h, start=False, stop=last,
                                 incs=((("m1mm", 1),) if last else ()))
                        if fq == 0:
                            k.mm(ps, onesrow[0:1, :], rowbuf[0:1, dh * 512:(dh + 1) * 512],
                                 start=False, stop=True,
                                 waits=(tok_bo,) if (gi == 0 and tsub == 0 and dh == 0) else (),
                                 incs=(("m1mm", 1),))
                        wv2 = [k.tok("m1mm"), (SM("m1w"), SM("m1w").n)]
                        k.cp("vector", ytmp1[:, dh * 512:(dh + 1) * 512], ps,
                             waits=tuple(wv2), incs=((PEV, 1),))
                    rbase = e * CPE + gi * 512 + tsub * 128
                    k.dma("gpsimd", ycomp_scr[rbase:rbase + 128, :], ytmp1,
                          sem=SM("m1w"), waits=(k.tok("PEV"),), accum=(fq > 0))
    f4a = BIGB[:, 0:1024]
    f4b = BIGB[:, 1024:2048]
    f4c = BIGB[:, 2048:3072]
    f4g1 = BIGB[:, 3072:4096]
    f4g2 = BIGB[:, 4096:5120]
    f4w = SMALL[:, 304:306]
    for ti in range(NT):
        w2 = [(SM("m1w"), SM("m1w").n), k.tok("m1mm"), k.tok("PEV")] if ti == 0 else \
             [(SM("yw"), 16 * ti)]
        k.dma("sync", f4b, x1_scr[ti * 128:(ti + 1) * 128, :], sem=SM("f4l"),
              waits=tuple(w2))
        k.dma("sync", f4c, s1_scr[ti * 128:(ti + 1) * 128, :], sem=SM("f4l"),
              waits=((SM("s1w"), 16 * NT) if ti == 0 else None,))
        k.dma("sync", f4w, wv_scr[ti * 128:(ti + 1) * 128, :], sem=SM("f4l"),
              waits=((SM("slw"), SM("slw").n) if ti == 0 else None,))
        k.dma("sync", IDXS[:, 4:6], slots_scr[ti * 128:(ti + 1) * 128, :], sem=SM("f4i"),
              waits=((SM("c4g"), 32 * ti) if ti >= 1 else None,))
        ti4 = k.tok("f4i")
        k.emit("gpsimd", lambda ee, o=f4g1, ix=IDXS[:, 4:5]:
               ee.indirect_dma_start(o, None, ycomp_scr[:, :], _IOA2(ix, 0)),
               waits=(ti4, ((SM("VM"), SM("VM").n) if ti >= 1 else None)),
               incs=(("c4g", 16),), drain=False)
        k.emit("gpsimd", lambda ee, o=f4g2, ix=IDXS[:, 5:6]:
               ee.indirect_dma_start(o, None, ycomp_scr[:, :], _IOA2(ix, 0)),
               waits=(), incs=(("c4g", 16),), drain=False)
        tg4 = k.tok("c4g")
        tl = k.tok("f4l")
        k.ts("vector", f4a, f4g1, f4w[:, 0:1], ALU.mult,
             waits=(tg4, tl) + (((SM("yw"), 16 * ti),) if ti >= 1 else ()))
        k.stt("vector", f4a, f4g2, f4w[:, 1:2], f4a, ALU.mult, ALU.add,
              incs=(("hy4", 1),))
        k.act(f4a, f4a, AF.Silu, waits=(k.tok("hy4"),), incs=(("SCM", 1),))
        k.tt("vector", f4a, f4a, f4c, ALU.mult, waits=((SM("SCM"), SM("SCM").n),))
        k.tt("vector", f4a, f4a, f4b, ALU.add, incs=(("VM", 1),))
        k.dma("gpsimd", yout[ti * 128:(ti + 1) * 128, :], f4a, sem=SM("yw"),
              waits=(k.tok("VM"),))
    k.wait("gpsimd", SM("yw"), SM("yw").n)
    k.build_block()
    return nc


def _make_inmaps(ins):
    f32 = np.float32
    cores = []
    Wq, Wk, Wv, Wd = (ins["L0_attn_W" + n].astype(f32) for n in "qkvd")
    bvecs = [ins["L0_attn_gamma"], ins["L0_attn_beta"], ins["L0_attn_bq"], ins["L0_attn_bk"],
             ins["L0_attn_bv"], ins["L0_attn_bd"], ins["L0_ffn_ln1_g"], ins["L0_ffn_ln1_b"],
             ins["L0_ffn_ln2_g"], ins["L0_ffn_ln2_b"], ins["L0_ffn_out_proj_b"]]
    cvec = np.stack([np.ascontiguousarray(np.broadcast_to(np.asarray(v, f32), (128, 1024)))
                     for v in bvecs])
    x = np.asarray(ins["L0_attn_input_tensor"], f32)
    tt_ = np.arange(128)
    tri = ((tt_[:, None] % 16 == tt_[None, :] % 16) &
           (tt_[:, None] // 16 < tt_[None, :] // 16)).astype(f32)
    selS = (tt_[:, None] % 16 == np.arange(16)[None, :]).astype(f32)
    cl_ = np.arange(128) // 16
    ll_ = np.arange(128) % 16
    for c in range(8):
        m1 = ((ll_[:, None] < np.arange(16)[None, :]) |
              ((ll_[:, None] == np.arange(16)[None, :]) & (cl_[:, None] < c))).astype(f32)
        m = {
            "x_in": np.ascontiguousarray(x[2 * c:2 * c + 2].reshape(4096, 1024)),
            "wq_sl": np.ascontiguousarray(Wq[128 * c:128 * (c + 1)]),
            "wk_sl": np.ascontiguousarray(Wk[128 * c:128 * (c + 1)]),
            "wv_sl": np.ascontiguousarray(Wv[128 * c:128 * (c + 1)]),
            "wd_sl": np.ascontiguousarray(Wd[128 * c:128 * (c + 1)]),
            "ip_sl": np.ascontiguousarray(np.asarray(ins["L0_ffn_in_proj_w"], f32)[384 * c:384 * (c + 1)]),
            "op_sl": np.ascontiguousarray(np.asarray(ins["L0_ffn_out_proj_w"], f32)[128 * c:128 * (c + 1)]),
            "w1c": np.ascontiguousarray(np.asarray(ins["L0_ffn_w1"], f32)[c]),
            "w2c": np.ascontiguousarray(np.asarray(ins["L0_ffn_w2"], f32)[c]),
            "b1c": np.ascontiguousarray(np.asarray(ins["L0_ffn_b1"], f32)[c].reshape(32, 128).T),
            "b2_r": np.ascontiguousarray(np.asarray(ins["L0_ffn_b2"], f32)[c:c + 1]),
            "wg0_in": np.ascontiguousarray(np.asarray(ins["L0_ffn_wg"], f32)),
            "ipb_bc": np.ascontiguousarray(np.broadcast_to(np.asarray(ins["L0_ffn_in_proj_b"], f32), (128, 3072))),
            "m1c_in": m1,
            "selS_in": selS,
            "selST_in": np.ascontiguousarray(selS.T),
            "tri_in": tri,
            "tri0_in": (tt_[:, None] < tt_[None, :]).astype(f32),
            "ident_in": np.eye(128, dtype=f32),
            "cvec": cvec,
            "wqkv1_sl": np.ascontiguousarray(np.asarray(ins["L1_attn_w_qkv"], f32)[128 * c:128 * (c + 1)]),
            "wout1_sl": np.ascontiguousarray(np.asarray(ins["L1_attn_w_out"], f32)[128 * c:128 * (c + 1)]),
            "winc": np.ascontiguousarray(np.asarray(ins["L1_ffn_W_in"], f32)[c]),
            "woutc": np.ascontiguousarray(np.asarray(ins["L1_ffn_W_out"], f32)[c]),
            "binc": np.ascontiguousarray(
                np.asarray(ins["L1_ffn_b_in"], f32).reshape(8, 32, 128).transpose(2, 0, 1).reshape(128, 256)),
            "bout_r": np.ascontiguousarray(np.asarray(ins["L1_ffn_b_out"], f32).reshape(1, 8192)),
            "wg1_in": np.ascontiguousarray(np.asarray(ins["L1_ffn_W_gate"], f32)),
        }
        cores.append(m)
    return cores


def kernel(**inputs):
    from concourse.bass_utils import run_bass_kernel_spmd
    nc = build_core("full")
    in_maps = _make_inmaps(inputs)
    res = run_bass_kernel_spmd(nc, in_maps, core_ids=list(range(8)))
    out = np.stack([res.results[c]["y_out"].reshape(2, 2048, 1024) for c in range(8)])
    return np.ascontiguousarray(out.reshape(16, 2048, 1024))


# version 10 (final)
# speedup vs baseline: 1.0126x; 1.0042x over previous
import contextlib
import numpy as np
import concourse.bass as bass
import concourse.mybir as mybir

F32 = mybir.dt.float32
BF16 = mybir.dt.bfloat16
AF = mybir.ActivationFunctionType
ALU = mybir.AluOpType
AX = mybir.AxisListType

ENGS = ("sync", "tensor", "vector", "scalar", "gpsimd")

B = 2
S = 2048
D = 1024
T = B * S
NT = T // 128
NH0, HD0 = 16, 64
FH, FHD = 4, 256
E, FF = 8, 4096
NDET = 256
L = 16


class Sem:
    __slots__ = ("name", "h", "n")
    def __init__(self, name):
        self.name, self.h, self.n = name, None, 0


SEM_ALIAS = {}
for _n in ("LV2", "LVd", "LV5", "LV6"):
    SEM_ALIAS[_n] = "LNV"
for _n in ("LS2", "LSd", "LS5", "LS6"):
    SEM_ALIAS[_n] = "LNS"
for _n in ("gv1", "gv2", "g1v", "t16", "scv", "s8v", "aev", "cadd", "cedone",
           "ksdone", "dzero", "dmemset", "crec", "bdev", "zev", "hy2", "x1c",
           "QE", "KE", "ELUV", "ONES", "EPS"):
    SEM_ALIAS[_n] = "VM"
for _n in ("ELUS", "gs1", "g1s", "eexp", "lnf5"):
    SEM_ALIAS[_n] = "SCM"
for _n in ("oev1", "gev", "g1ev", "umv", "runv", "rkv", "hy4", "s81", "sm1v"):
    SEM_ALIAS[_n] = "VM"
for _n in ("PMM", "wemm", "cemm", "amm", "atmm", "cntmm", "dmm", "g1mm", "gmm",
           "h1mm", "sc1mm", "smm", "rkmm", "ctxmm", "av1", "kcons", "pfmm"):
    SEM_ALIAS[_n] = "TM"


class KB:
    def __init__(self, nc):
        self.nc = nc
        self.ops = {e: [] for e in ENGS}
        self.sems = {}
        self.stack = contextlib.ExitStack()
        self.sb = {}
        self.dr = {}
        self.pyctr = {}

    def sem(self, name):
        if isinstance(name, Sem):
            return name
        name = SEM_ALIAS.get(name, name)
        if name not in self.sems:
            s = Sem(name)
            s.h = self.stack.enter_context(self.nc.semaphore(name))
            self.sems[name] = s
        return self.sems[name]

    def sbuf(self, name, shape, dtype=F32):
        t = self.stack.enter_context(self.nc.sbuf_tensor(name, list(shape), dtype))
        self.sb[name] = t
        return t

    def psum(self, name, shape, dtype=F32):
        t = self.stack.enter_context(self.nc.psum_tensor(name, list(shape), dtype))
        self.sb[name] = t
        return t

    def dram(self, name, shape, dtype=F32, kind=None):
        if kind:
            t = self.nc.dram_tensor(name, list(shape), dtype, kind=kind)
        else:
            t = self.nc.dram_tensor(name, list(shape), dtype)
        self.dr[name] = t
        return t

    def rec(self, eng, fn):
        self.ops[eng].append(fn)

    def _waits(self, eng, waits):
        for w in waits:
            if not w:
                continue
            sem, val = w
            sem = self.sem(sem)
            v = sem.n if val is None else val
            if v <= 0:
                continue
            self.rec(eng, lambda e, h=sem.h, v=v: e.wait_ge(h, v))

    def wait(self, eng, sem, val=None):
        self._waits(eng, ((sem, val),))

    def emit(self, eng, fn, waits=(), incs=(), drain=None):
        if drain is None:
            drain = eng in ("vector", "scalar", "gpsimd")
        if drain:
            self.rec(eng, lambda e: e.drain())
        self._waits(eng, waits)
        incs = [(self.sem(s), a) for (s, a) in incs]
        assert len(incs) <= 1, "hardware allows one sync update per instruction"
        for (sem, amt) in incs:
            sem.n += amt
        def g(e, fn=fn, incs=tuple(incs)):
            i = fn(e)
            for (sem, amt) in incs:
                i.then_inc(sem.h, amt)
        self.rec(eng, g)

    def tok(self, name):
        s = self.sem(name)
        return (s, s.n)

    def dma(self, eng, out, in_, sem=None, waits=(), accum=False):
        incs = ((sem, 16),) if sem is not None else ()
        if accum:
            fn = lambda e, o=out, i=in_: e.dma_start(out=o, in_=i, accum_op=ALU.add)
        else:
            fn = lambda e, o=out, i=in_: e.dma_start(out=o, in_=i)
        self.emit(eng, fn, waits=waits, incs=incs, drain=False)

    def mm(self, out, lhsT, rhs, start, stop, waits=(), incs=(), tile_position=None):
        self.emit("tensor",
                  lambda e, o=out, l=lhsT, r=rhs, st=start, sp=stop, tp=tile_position:
                      e.matmul(o, l, r, start=st, stop=sp, tile_position=tp),
                  waits=waits, incs=incs)

    def transpose(self, out, in_, waits=(), incs=()):
        ident = self.sb["ident"]
        p = in_.shape[0]
        self.emit("tensor",
                  lambda e, o=out, i=in_, idn=ident, p=p: e.transpose(o, i, idn[0:p, 0:p]),
                  waits=waits, incs=incs)

    def tponce_sel(self, ps_full, dst_ap, src_ap, sel, extra_waits=(), evac_waits=()):
        raise NotImplementedError

    def cp(self, eng, out, in_, waits=(), incs=()):
        self.emit(eng, lambda e, o=out, i=in_: e.tensor_copy(o, i), waits=waits, incs=incs)

    def tt(self, eng, out, a, b, op, waits=(), incs=()):
        self.emit(eng, lambda e, o=out, x=a, y=b, p=op: e.tensor_tensor(o, x, y, op=p),
                  waits=waits, incs=incs)

    def ts(self, eng, out, a, scalar, op, waits=(), incs=()):
        self.emit(eng, lambda e, o=out, x=a, s=scalar, p=op: e.tensor_single_scalar(o, x, s, p),
                  waits=waits, incs=incs)

    def tsc(self, eng, out, in0, scalar1, scalar2, op0, op1, waits=(), incs=()):
        self.emit(eng,
                  lambda e, o=out, a=in0, s1=scalar1, s2=scalar2, p0=op0, p1=op1:
                      e.tensor_scalar(o, a, s1, s2, p0, p1),
                  waits=waits, incs=incs)

    def stt(self, eng, out, in0, scalar, in1, op0, op1, waits=(), incs=()):
        self.emit(eng,
                  lambda e, o=out, a=in0, s=scalar, b=in1, p0=op0, p1=op1:
                      e.scalar_tensor_tensor(o, a, s, b, p0, p1),
                  waits=waits, incs=incs)

    def act(self, out, in_, func, bias=0.0, scale=1.0, accum_out=None, waits=(), incs=()):
        if accum_out is not None:
            fn = lambda e, o=out, i=in_, f=func, b=bias, s=scale, a=accum_out: \
                e.activation(out=o, in_=i, func=f, bias=b, scale=s, accum_out=a)
        else:
            fn = lambda e, o=out, i=in_, f=func, b=bias, s=scale: \
                e.activation(out=o, in_=i, func=f, bias=b, scale=s)
        self.emit("scalar", fn, waits=waits, incs=incs)

    def red(self, eng, out, in_, op, negate=False, waits=(), incs=()):
        self.emit(eng,
                  lambda e, o=out, i=in_, p=op, n=negate: e.tensor_reduce(o, i, AX.X, p, negate=n),
                  waits=waits, incs=incs)

    def memset(self, eng, ap, val, waits=(), incs=()):
        self.emit(eng, lambda e, a=ap, v=val: e.memset(a, v), waits=waits, incs=incs)

    def collective(self, kind, ins, outs, op=ALU.bypass, waits=(), incs=()):
        from concourse.aot_env import allow_ici_all_to_all
        self._waits("gpsimd", waits)
        incs = [(self.sem(s), a) for (s, a) in incs]
        for (sem, amt) in incs:
            sem.n += amt
        def g(e, k=kind, i=ins, o=outs, p=op, incs=tuple(incs)):
            with allow_ici_all_to_all():
                inst = e.collective_compute(k, p, replica_groups=[list(range(8))], ins=i, outs=o)
            for (sem, amt) in incs:
                inst.then_inc(sem.h, amt)
        self.rec("gpsimd", g)

    def ln(self, out_ap, x_ap, sq_scratch, st, eps, cols, gamma=None, beta=None,
           pre=(), incs=(), sem_v="LNV", sem_s="LNS"):
        v = "vector"
        SV, SS = self.sem(sem_v), self.sem(sem_s)
        self.red(v, st[:, 0:1], x_ap, ALU.add, waits=pre, incs=((SV, 1),))
        self.act(sq_scratch, x_ap, AF.Square, accum_out=st[:, 1:2], waits=pre, incs=((SS, 1),))
        self.ts(v, st[:, 2:3], st[:, 0:1], 1.0 / cols, ALU.mult)
        self.ts(v, st[:, 3:4], st[:, 1:2], 1.0 / cols, ALU.mult, waits=(self.tok(sem_s),))
        self.tt(v, st[:, 4:5], st[:, 2:3], st[:, 2:3], ALU.mult)
        self.tt(v, st[:, 3:4], st[:, 3:4], st[:, 4:5], ALU.subtract, incs=((SV, 1),))
        self.act(st[:, 7:8], st[:, 3:4], AF.Sqrt,
                 bias=(eps if not isinstance(eps, float) else float(eps)),
                 waits=(self.tok(sem_v),), incs=((SS, 1),))
        self.emit(v, lambda e, o=st[:, 5:6], i=st[:, 7:8]: e.reciprocal(o, i),
                  waits=(self.tok(sem_s),))
        self.tt(v, st[:, 6:7], st[:, 2:3], st[:, 5:6], ALU.mult)
        self.ts(v, st[:, 6:7], st[:, 6:7], -1.0, ALU.mult, incs=((SV, 1),))
        self.act(out_ap, x_ap, AF.Identity, bias=st[:, 6:7], scale=st[:, 5:6],
                 waits=(self.tok(sem_v),), incs=((SS, 1),))
        if gamma is not None:
            self.tt(v, out_ap, out_ap, gamma, ALU.mult, waits=(self.tok(sem_s),))
            self.tt(v, out_ap, out_ap, beta, ALU.add, incs=((SV, 1),))
            return self.tok(sem_v)
        return self.tok(sem_s)

    def build_block(self):
        with self.nc.Block() as block:
            for eng in ENGS:
                fns = self.ops[eng]
                def runner(e, fns=fns):
                    for f in fns:
                        f(e)
                getattr(block, eng)(runner)
        self.stack.close()



STAGES = ["xn", "s", "qdet", "y1", "g", "x1", "s1", "full"]


def build_core(stage="full"):
    nc = bass.Bass("TRN2", target_bir_lowering=False)
    k = KB(nc)
    sti = STAGES.index(stage)
    SM = k.sem

    xin = k.dram("x_in", [T, D], kind="ExternalInput").ap()
    wq_sl = k.dram("wq_sl", [128, 1024], kind="ExternalInput").ap()
    wk_sl = k.dram("wk_sl", [128, 1024], kind="ExternalInput").ap()
    wv_sl = k.dram("wv_sl", [128, 1024], kind="ExternalInput").ap()
    wd_sl = k.dram("wd_sl", [128, 1024], kind="ExternalInput").ap()
    ip_sl = k.dram("ip_sl", [384, 1024], kind="ExternalInput").ap()
    op_sl = k.dram("op_sl", [128, 1024], kind="ExternalInput").ap()
    w1c = k.dram("w1c", [1024, 4096], kind="ExternalInput").ap()
    w2c = k.dram("w2c", [4096, 1024], kind="ExternalInput").ap()
    b1c = k.dram("b1c", [128, 32], kind="ExternalInput").ap()
    b2_r = k.dram("b2_r", [1, 1024], kind="ExternalInput").ap()
    wg0_in = k.dram("wg0_in", [1024, 8], kind="ExternalInput").ap()
    wqkv1_sl = k.dram("wqkv1_sl", [128, 3072], kind="ExternalInput").ap()
    wout1_sl = k.dram("wout1_sl", [128, 1024], kind="ExternalInput").ap()
    winc = k.dram("winc", [1024, 4096], kind="ExternalInput").ap()
    woutc = k.dram("woutc", [4096, 1024], kind="ExternalInput").ap()
    binc = k.dram("binc", [128, 256], kind="ExternalInput").ap()
    bout_r = k.dram("bout_r", [1, 8192], kind="ExternalInput").ap()
    wg1_in = k.dram("wg1_in", [1024, 8], kind="ExternalInput").ap()
    ipb_bc = k.dram("ipb_bc", [128, 3072], kind="ExternalInput").ap()
    m1c_in = k.dram("m1c_in", [128, 16], kind="ExternalInput").ap()
    selS_in = k.dram("selS_in", [128, 16], kind="ExternalInput").ap()
    selST_in = k.dram("selST_in", [16, 128], kind="ExternalInput").ap()
    tri_in = k.dram("tri_in", [128, 128], kind="ExternalInput").ap()
    tri0_in = k.dram("tri0_in", [128, 128], kind="ExternalInput").ap()
    ident_in = k.dram("ident_in", [128, 128], kind="ExternalInput").ap()
    cvec = k.dram("cvec", [11, 128, 1024], kind="ExternalInput").ap()

    yout = k.dram("y_out", [T, D], kind="ExternalOutput").ap()

    def idram(name, shape, dtype=F32):
        return k.dram(name, shape, dtype).ap()

    ag_specs = [("wq", wq_sl), ("wk", wk_sl), ("wv", wv_sl), ("wd", wd_sl),
                ("ip", ip_sl), ("op", op_sl),
                ("wqkv1", wqkv1_sl), ("wout1", wout1_sl),
                ("win", winc), ("wout", woutc)]
    ag_full = {"wq": [1024, 1024], "wk": [1024, 1024], "wv": [1024, 1024],
               "wd": [1024, 1024], "ip": [3072, 1024], "op": [1024, 1024],
               "wqkv1": [1024, 3072], "wout1": [1024, 1024],
               "win": [8192, 4096], "wout": [32768, 1024]}
    ag = {nm: (idram("agi_" + nm, list(sl.shape)),
               idram("ago_" + nm, ag_full[nm])) for nm, sl in ag_specs}
    q_scr = idram("q_scr", [T, D])
    s_scr = idram("s_scr", [T, D])
    cc1_in = idram("cc1_in", [T, D])
    cc1_out = idram("cc1_out", [T, D])
    qkvd_scr = idram("qkvd_scr", [T, 3 * D])
    y1_scr = idram("y1_scr", [T, D])
    y1T_scr = idram("y1T_scr", [D, T])
    moe0_scr = idram("moe0_scr", [T, D])
    cc2_in = idram("cc2_in", [T, D])
    cc2_out = idram("cc2_out", [T, D])
    agi_we = idram("agi_we", [1024, 1024])
    ago_we = idram("ago_we", [8 * 1024, 1024])
    agi_ce = idram("agi_ce", [1, 1024])
    ago_ce = idram("ago_ce", [8, 1024])
    agi_cnt = idram("agi_cnt", [16, 8])
    ago_cnt = idram("ago_cnt", [128, 8])
    x1_scr = idram("x1_scr", [T, D])
    xn1_scr = idram("xn1_scr", [T, D])
    qkv1_scr = idram("qkv1_scr", [T, 3 * D])
    s1_scr = idram("s1_scr", [T, D])
    o1_scr = idram("o1_scr", [T, D])
    CPE = 1536
    tok_slot = k.dram("tok_slot", [13312, 1], mybir.dt.int32).ap()
    slots_scr = k.dram("slots_scr", [T, 2], mybir.dt.int32).ap()
    wv_scr = idram("wv_scr", [T, 2])
    ycomp_scr = idram("ycomp_scr", [8 * CPE + 128, D])

    BIGA = k.sbuf("BIGA", [128, 24576])
    BIGB = k.sbuf("BIGB", [128, 12288])
    TRANS = k.sbuf("TRANS", [128, 5376])
    CONSTS = k.sbuf("CONSTS", [128, 3072])
    SMALL = k.sbuf("SMALL", [128, 1024])
    S8 = k.sbuf("S8", [128, 512])
    GT = k.sbuf("GT", [128, 1024])
    RUN = k.sbuf("RUN", [16, 512])
    ident = k.sbuf("ident", [128, 128])
    tri = k.sbuf("tri", [128, 128])
    tri0 = k.sbuf("tri0", [128, 128])
    selS = k.sbuf("selS", [128, 16])
    selST = k.sbuf("selST", [16, 128])
    m1c = k.sbuf("m1c", [128, 16])
    wg0 = k.sbuf("wg0", [128, 64])
    wg1 = k.sbuf("wg1", [128, 64])
    binsb = k.sbuf("binsb", [128, 256])
    IDXS = k.sbuf("IDXS", [128, 16], mybir.dt.int32)
    cpe_base = k.sbuf("cpe_base", [1, 8])
    BIGAh = BIGA[:, :].bitcast(BF16)
    BIGBh = BIGB[:, :].bitcast(BF16)
    b1sb = k.sbuf("b1sb", [128, 32])
    onescol = k.sbuf("onescol", [128, 8])
    onesrow = k.sbuf("onesrow", [1, 128])
    rowbuf = k.sbuf("rowbuf", [1, 1024])
    rowbuf2 = k.sbuf("rowbuf2", [1, 1024])

    PS_A = k.psum("PS_A", [128, 1024])
    PS_B = k.psum("PS_B", [128, 1024])
    PS_C = k.psum("PS_C", [128, 512])
    PS_D = k.psum("PS_D", [128, 512])
    PS_E = k.psum("PS_E", [128, 512])
    PS_F = k.psum("PS_F", [128, 512])

    TPEV, PEV = SM("TPEV"), SM("PEV")

    def cslot(si):
        return CONSTS[:, si * 1024:(si + 1) * 1024]

    def stat8(slot):
        return SMALL[:, slot * 8:(slot + 1) * 8]

    k.dma("sync", ident[:], ident_in, sem=SM("idl"))
    if sti >= STAGES.index("qdet"):
        k.dma("sync", tri[:], tri_in, sem=SM("idl"))
        k.dma("sync", tri0[:], tri0_in, sem=SM("idl"))
        k.dma("sync", selS[:], selS_in, sem=SM("idl"))
        k.dma("sync", selST[:], selST_in, sem=SM("idl"))
        k.dma("sync", m1c[:], m1c_in, sem=SM("idl"))
        k.dma("sync", wg0[:, 0:64].rearrange("p (jc e) -> p jc e", jc=8),
              wg0_in.rearrange("(jc p) e -> p jc e", p=128), sem=SM("idl"))
        k.dma("sync", b1sb[:], b1c, sem=SM("idl"))
        k.dma("sync", rowbuf[0:1, 0:1024], b2_r, sem=SM("idl"))
    if sti >= STAGES.index("x1"):
        k.dma("sync", wg1[:, 0:64].rearrange("p (jc e) -> p jc e", jc=8),
              wg1_in.rearrange("(jc p) e -> p jc e", p=128), sem=SM("idl"))
        k.dma("sync", binsb[:], binc, sem=SM("idl"))
        for e_ in range(8):
            k.memset("vector", cpe_base[0:1, e_:e_ + 1], float(e_ * 1536),
                     incs=((("VM", 1),) if e_ == 7 else ()))
        k.memset("vector", GT[:, 0:104], 0.0, incs=(("VM", 1),))
        zt = k.tok("VM")
        k.dma("gpsimd", tok_slot.rearrange("(p a) c -> p (a c)", p=128),
              GT[:, 0:104].bitcast(mybir.dt.int32), sem=SM("tofz"), waits=(zt,))
    k.memset("vector", onescol[:], 1.0, incs=(("ONES", 1),))
    k.memset("vector", onesrow[:], 1.0, incs=(("ONES", 1),))
    eps5 = SMALL[:, 96:97]; eps12 = SMALL[:, 97:98]
    k.memset("vector", eps5, 1e-5)
    k.memset("vector", eps12, 1e-12, incs=(("EPS", 1),))

    def load_consts(slots, waits=()):
        for si, row in slots:
            k.dma("sync", cslot(si), cvec[row], sem=SM("cl"), waits=waits)
            waits = ()
        return k.tok("cl")

    import os
    if stage != "xn" or os.environ.get("KB_FORCE_AG"):
        for nm, sl in ag_specs:
            k.dma("gpsimd", ag[nm][0], sl, sem=SM("agb"))
        k.wait("gpsimd", SM("agb"), SM("agb").n)
        for nm, sl in ag_specs:
            k.collective("AllGather", [ag[nm][0].opt()], [ag[nm][1].opt()],
                         incs=(("AG_" + nm, 1),))

    def tponce(dst_ap, src_ap, extra_waits=(), evac_waits=(), sel=None):
        k.pyctr["tpar"] = k.pyctr.get("tpar", 0) + 1
        pslot = k.pyctr["tpar"] % 2
        cols = src_ap.shape[1] if hasattr(src_ap, "shape") else 128
        ps = PS_A[:, pslot * 512: pslot * 512 + cols]
        k.emit("tensor",
               lambda e, o=ps, i=src_ap, idn=ident: e.transpose(o, i, idn[0:i.shape[0], 0:i.shape[0]]),
               waits=tuple(extra_waits) + ((TPEV, TPEV.n - 1),),
               incs=(("tpmm", 1),))
        rd = ps[0:dst_ap.shape[0], :] if sel is None else sel(ps)
        k.cp("vector", dst_ap, rd,
             waits=(k.tok("tpmm"),) + tuple(evac_waits),
             incs=((TPEV, 1),))

    def build_WT(base, ago_ap, first_waits, tag, rows=1024, dstbuf=None):
        ld = SM("wtl")
        dstbuf = BIGA if dstbuf is None else dstbuf
        sbufs = (TRANS[:, 2304:3328], TRANS[:, 3328:4352])
        import os as _os
        dbuf = bool(_os.environ.get("KB_WT_DBUF"))
        for ic in range(rows // 128):
            w = list(first_waits) if ic == 0 else []
            if dbuf:
                w.append((TPEV, TPEV.n - 8))
            else:
                w.append((TPEV, TPEV.n))
            stage_buf = sbufs[ic % 2 if dbuf else 0]
            k.dma("sync", stage_buf, ago_ap[ic * 128:(ic + 1) * 128, :], sem=ld, waits=tuple(w))
            tk = k.tok("wtl")
            for jc in range(8):
                o0 = base + jc * rows + ic * 128
                dst = dstbuf[:, o0:o0 + 128]
                tponce(dst, stage_buf[:, jc * 128:(jc + 1) * 128],
                       extra_waits=(tk,) if jc == 0 else ())
        return k.tok("TPEV")

    if sti >= STAGES.index("qdet"):
        idl_tok = k.tok("idl")
        for qq in range(4):
            t_w1 = build_WT(0, w1c[:, qq * 1024:(qq + 1) * 1024],
                            ((k.tok("wemm"),) if qq > 0 else (k.tok("idl"),)),
                            "W1Q%d" % qq, rows=1024, dstbuf=BIGB)
            for fc in range(8):
                k.dma("sync", BIGA[:, fc * 1024:(fc + 1) * 1024],
                      w2c[qq * 1024 + fc * 128: qq * 1024 + (fc + 1) * 128, :],
                      sem=SM("w2l"), waits=((k.tok("wemm"),) if qq > 0 and fc == 0 else ()))
            t_w2 = k.tok("w2l")
            for fc in range(8):
                for dh in range(2):
                    ceps = PS_C if dh == 0 else PS_E
                    w = [t_w2, idl_tok] if fc == 0 and dh == 0 else []
                    k.mm(ceps[0:1, 0:512], b1sb[:, qq * 8 + fc: qq * 8 + fc + 1],
                         BIGA[:, fc * 1024 + dh * 512: fc * 1024 + dh * 512 + 512],
                         start=(qq == 0 and fc == 0), stop=(qq == 3 and fc == 7),
                         waits=tuple(w),
                         incs=((("cemm", 1),) if (qq == 3 and fc == 7 and dh == 1) else ()))
            for jc in range(8):
                for dh in range(2):
                    ps = PS_B[:, (dh % 2) * 512:(dh % 2) * 512 + 512]
                    k.wait("tensor", PEV, PEV.n - 1)
                    for fc in range(8):
                        w = [t_w1, t_w2] if (jc == 0 and dh == 0 and fc == 0) else []
                        k.mm(ps, BIGB[:, fc * 1024 + jc * 128: fc * 1024 + jc * 128 + 128],
                             BIGA[:, fc * 1024 + dh * 512: fc * 1024 + dh * 512 + 512],
                             start=(fc == 0), stop=(fc == 7), waits=tuple(w),
                             incs=((("wemm", 1),) if fc == 7 else ()))
                    dst = BIGA[:, 8192 + jc * 1024 + dh * 512: 8192 + jc * 1024 + dh * 512 + 512]
                    if qq == 0:
                        k.cp("vector", dst, ps, waits=(k.tok("wemm"),), incs=((PEV, 1),))
                    else:
                        k.tt("vector", dst, dst, ps, ALU.add, waits=(k.tok("wemm"),),
                             incs=((PEV, 1),))
        k.cp("vector", rowbuf2[0:1, 0:512], PS_C[0:1, 0:512], waits=(k.tok("cemm"),))
        k.cp("vector", rowbuf2[0:1, 512:1024], PS_E[0:1, 0:512])
        k.tt("vector", rowbuf2[0:1, 0:1024], rowbuf2[0:1, 0:1024],
             rowbuf[0:1, 0:1024], ALU.add, incs=(("cedone", 1),))
        k.dma("gpsimd", agi_ce, rowbuf2[0:1, 0:1024], sem=SM("weout"),
              waits=(k.tok("cedone"),))
        for jc in range(8):
            k.dma("gpsimd", agi_we[jc * 128:(jc + 1) * 128, :],
                  BIGA[:, 8192 + jc * 1024: 8192 + (jc + 1) * 1024],
                  sem=SM("weout"), waits=((k.tok("PEV"),) if jc == 0 else ()))
        k.wait("gpsimd", SM("weout"), SM("weout").n)
        k.collective("AllGather", [agi_we.opt()], [ago_we.opt()], incs=(("AG_we", 1),))
        k.collective("AllGather", [agi_ce.opt()], [ago_ce.opt()], incs=(("AG_ce", 1),))
        tok_p2 = (k.tok("wemm"), k.tok("weout"), k.tok("cedone"), k.tok("PEV"))
    else:
        tok_p2 = ()

    if stage != "xn" or os.environ.get("KB_FORCE_WT"):
        tok_wq = build_WT(0, ag["wq"][1], (k.tok("AG_wq"), k.tok("idl")) + tuple(tok_p2), "TQ")
        tok_wk = build_WT(8192, ag["wk"][1], (k.tok("AG_wk"),), "TK")
        tok_wv = build_WT(16384, ag["wv"][1], (k.tok("AG_wv"),), "TV")
    else:
        tok_wq = tok_wk = tok_wv = None

    def WT_mv(wbase, jc, i0, iw, istride=1024, buf=None):
        buf = BIGA if buf is None else buf
        return buf[:, wbase + jc * istride + i0: wbase + jc * istride + i0 + iw]

    def proj_tm(xnT, wbase, nout, evac_fn, first_waits=(), istride=1024, wbuf=None):
        for h in range(nout // 512):
            ps = PS_B[:, (h % 2) * 512:(h % 2) * 512 + 512]
            k.wait("tensor", PEV, PEV.n - 1)
            for jc in range(8):
                w = list(first_waits) if (h == 0 and jc == 0) else []
                k.mm(ps, xnT[:, jc * 128:(jc + 1) * 128],
                     WT_mv(wbase, jc, h * 512, 512, istride, wbuf),
                     start=(jc == 0), stop=(jc == 7), waits=tuple(w),
                     incs=((("PMM", 1),) if jc == 7 else ()))
            evac_fn(h, ps, (k.tok("PMM"),))

    def elu_p1(dst, src, scratch, waits=(), incs=()):
        k.ts("vector", scratch, src, 0.0, ALU.min, waits=waits, incs=(("ELUV", 1),))
        k.act(scratch, scratch, AF.Exp, waits=(k.tok("ELUV"),), incs=(("ELUS", 1),))
        k.ts("vector", dst, src, 0.0, ALU.max, waits=(k.tok("ELUS"),))
        k.tt("vector", dst, dst, scratch, ALU.add, incs=incs)

    xb = BIGB[:, 0:1024]
    xnb = BIGB[:, 1024:2048]
    xnTb = BIGB[:, 2048:3072]
    qb = BIGB[:, 3072:4096]
    kbf = BIGB[:, 4096:5120]
    vbf = BIGB[:, 5120:6144]
    sqb = BIGB[:, 6144:7168]
    elub = BIGB[:, 7168:8192]
    kvev = TRANS[:, 0:1024]
    ksev = TRANS[:, 1024:1056]

    cl1 = load_consts([(0, 2), (1, 3), (2, 4)])

    prev = {}
    for ti in range(NT):
        b_loc, s0 = ti // 16, (ti % 16) * 128
        k.dma("sync", xb, xin[ti * 128:(ti + 1) * 128, :], sem=SM("xld"),
              waits=(prev.get("xn_done"),) + (tuple(tok_p2) if ti == 0 else ()))
        pre = [(SM("xld"), SM("xld").n)]
        if ti >= 1:
            pre += [(SM("ccw"), 16 * ti)] + ([prev["xnT_mm"]] if "xnT_mm" in prev else [])
            if stage == "xn":
                pre += [(SM("yw"), 16 * ti)]
        tok_xn = k.ln(xnb, xb, sqb, stat8(0), eps5, D,
                      pre=tuple(pre) + ((k.tok("EPS"),) if ti == 0 else ()))
        prev["xn_done"] = tok_xn
        j = s0 // 256
        r0 = j * 512 + b_loc * 256 + (s0 % 256)
        k.dma("gpsimd", cc1_in[r0:r0 + 128, :], xnb, sem=SM("ccw"), waits=(tok_xn,))
        if stage == "xn":
            k.dma("gpsimd", yout[ti * 128:(ti + 1) * 128, :], xnb, sem=SM("yw"),
                  waits=(tok_xn,))
            continue
        for c in range(8):
            tponce(xnTb[:, c * 128:(c + 1) * 128], xnb[:, c * 128:(c + 1) * 128],
                   extra_waits=(tok_xn,) if c == 0 else (),
                   evac_waits=(prev["proj_mm"],) if (c == 0 and ti >= 1) else ())
        prev["xnT_mm"] = k.tok("tpmm")
        tok_xnT = k.tok("TPEV")

        def mkev(buf, slot, extra):
            def ev(h, ps, w):
                k.tt("vector", buf[:, h * 512:(h + 1) * 512], ps,
                     cslot(slot)[:, h * 512:(h + 1) * 512], ALU.add,
                     waits=tuple(list(w) + (extra if h == 0 else [])),
                     incs=((PEV, 1),))
            return ev

        fw = [tok_xnT] + ([cl1, tok_wq, tok_wk, tok_wv] if ti == 0 else [])
        proj_tm(xnTb, 0, 1024, mkev(qb, 0, [(SM("qsp"), 16 * ti)] if ti >= 1 else []),
                first_waits=fw)
        tok_qev = k.tok("PEV")
        proj_tm(xnTb, 8192, 1024, mkev(kbf, 1, [prev["kcons"]] if ti >= 1 else []))
        proj_tm(xnTb, 16384, 1024, mkev(vbf, 2, [prev["kcons"]] if ti >= 1 else []))
        tok_vev = k.tok("PEV")
        prev["proj_mm"] = k.tok("PMM")
        elu_p1(qb, qb, elub, incs=(("QE", 1),))
        elu_p1(kbf, kbf, elub, incs=(("KE", 1),))
        k.dma("gpsimd", q_scr[ti * 128:(ti + 1) * 128, :], qb, sem=SM("qsp"),
              waits=(k.tok("QE"),))
        row_start = (ti % 16 == 0)
        par = ti % 2
        kvps = PS_C if par == 0 else PS_E
        for h in range(NH0):
            pp, cc_ = 64 * (h % 2), 64 * (h // 2)
            w = []
            if h == 0:
                w = [k.tok("KE"), tok_vev]
                if ti <= 1:
                    w.append(k.tok("cemm"))
                if ti >= 2:
                    w.append((SM("kvadd"), 17 * (ti - 1)))
            k.mm(kvps[pp:pp + 64, cc_:cc_ + 64],
                 kbf[:, h * 64:(h + 1) * 64], vbf[:, h * 64:(h + 1) * 64],
                 start=True, stop=True, waits=tuple(w),
                 incs=(("kcons", 1),))
        ksps = PS_D if par == 0 else PS_F
        for c in range(8):
            k.mm(ksps[:, c: c + 1],
                 kbf[:, c * 128:(c + 1) * 128], onescol[:, 0:1],
                 start=True, stop=True,
                 waits=((k.tok("ONES"),) if (ti == 0 and c == 0) else ()),
                 incs=(("kcons", 1),))
        prev["kcons"] = k.tok("kcons")
        tok_kvmm = k.tok("kcons")
        for h in range(NH0):
            pp, cc_ = 64 * (h % 2), 64 * (h // 2)
            dst = kvev[pp:pp + 64, b_loc * 512 + cc_: b_loc * 512 + cc_ + 64]
            srcp = kvps[pp:pp + 64, cc_:cc_ + 64]
            if row_start:
                k.cp("vector", dst, srcp, waits=(tok_kvmm,) if h == 0 else (),
                     incs=(("kvadd", 1),))
            else:
                k.tt("vector", dst, dst, srcp, ALU.add,
                     waits=(tok_kvmm,) if h == 0 else (), incs=(("kvadd", 1),))
        dst = ksev[:, b_loc * 8: b_loc * 8 + 8]
        srcp = ksps[:, 0:8]
        if row_start:
            k.cp("vector", dst, srcp, incs=(("kvadd", 1),))
        else:
            k.tt("vector", dst, dst, srcp, ALU.add, incs=(("kvadd", 1),))

    if stage == "xn":
        k.wait("gpsimd", SM("yw"), SM("yw").n)
        k.build_block()
        return nc

    k.ts("vector", ksev[:, 0:16], ksev[:, 0:16], 1e-6, ALU.add,
         waits=(k.tok("kvadd"),), incs=(("ksdone", 1),))

    if os.environ.get("KB_P1ONLY"):
        for ti in range(NT):
            k.dma("gpsimd", yout[ti * 128:(ti + 1) * 128, :], kvev[:, 0:1024],
                  sem=SM("yw"), waits=(k.tok("ksdone"), (SM("qsp"), 16 * NT)))
        k.wait("gpsimd", SM("yw"), SM("yw").n)
        k.build_block()
        return nc

    tok_wd = build_WT(0, ag["wd"][1], (k.tok("AG_wd"), k.tok("PEV")), "TD")
    cl2 = load_consts([(0, 5), (1, 0), (2, 1)], waits=(k.tok("PEV"),))

    qb2 = BIGB[:, 0:1024]
    qT2 = BIGB[:, 1024:2048]
    xn2 = BIGB[:, 2048:3072]
    ctxb = BIGB[:, 3072:4096]
    ctxT = BIGB[:, 4096:5120]
    hsb = BIGB[:, 5120:6144]
    sb_o = BIGB[:, 6144:7168]
    sq2 = BIGB[:, 7168:8192]
    zb = SMALL[:, 64:80]

    prev = {}
    for ti in range(NT):
        b_loc, s0 = ti // 16, (ti % 16) * 128
        j = s0 // 256
        r0 = j * 512 + b_loc * 256 + (s0 % 256)
        p1w = []
        if ti == 0:
            p1w = [(SM("qsp"), 16 * NT), k.tok("LNS"), k.tok("ccw"), k.tok("PMM"),
                   k.tok("kcons"), k.tok("ELUS")]
        k.dma("sync", qb2, q_scr[ti * 128:(ti + 1) * 128, :], sem=SM("q2l"),
              waits=tuple(p1w) if ti == 0 else (prev["qT_mm"],))
        k.dma("sync", xn2, cc1_in[r0:r0 + 128, :], sem=SM("xn2l"),
              waits=tuple(p1w) if ti == 0 else ((SM("hsx"), ti),))
        tq = (SM("q2l"), SM("q2l").n)
        for c in range(8):
            ev_w = ()
            if c == 0:
                ev_w = tuple(p1w) if ti == 0 else (prev["ctx_mm"],)
            tponce(qT2[:, c * 128:(c + 1) * 128], qb2[:, c * 128:(c + 1) * 128],
                   extra_waits=(tq,) if c == 0 else (),
                   evac_waits=ev_w)
        prev["qT_mm"] = k.tok("tpmm")
        tok_qT = k.tok("TPEV")
        for h in range(NH0):
            pb = 64 * (h % 2)
            w = []
            if h == 0:
                w = [tok_qT]
                w.append((SM("ksdone"), 1) if ti == 0 else (SM("zev"), SM("zev").n))
            dps = PS_D if h % 2 == 0 else PS_F
            k.mm(dps[:, h // 2: h // 2 + 1],
                 qT2[pb:pb + 64, (h // 2) * 128:(h // 2) * 128 + 128],
                 ksev[pb:pb + 64, b_loc * 8 + h // 2: b_loc * 8 + h // 2 + 1],
                 start=True, stop=True, waits=tuple(w),
                 incs=((("dmm", 1),) if h == NH0 - 1 else ()))
        k.emit("vector", lambda e, o=zb[:, 0:8], i=PS_D[:, 0:8]: e.reciprocal(o, i),
               waits=(k.tok("dmm"),))
        k.emit("vector", lambda e, o=zb[:, 8:16], i=PS_F[:, 0:8]: e.reciprocal(o, i),
               incs=(("zev", 1),))
        for h in range(NH0):
            pb = 64 * (h % 2)
            w = []
            if h == 0:
                w = [(PEV, PEV.n)]
                if ti == 0:
                    w.append(k.tok("kvadd"))
            pcol = (h % 2) * 512 + (h // 2) * 64
            k.mm(PS_B[:, pcol:pcol + 64],
                 qT2[pb:pb + 64, (h // 2) * 128:(h // 2) * 128 + 128],
                 kvev[pb:pb + 64, b_loc * 512 + (h // 2) * 64: b_loc * 512 + (h // 2) * 64 + 64],
                 start=True, stop=True,
                 waits=tuple(w), incs=(("ctxmm", 1),))
        prev["ctx_mm"] = k.tok("ctxmm")
        for half in range(2):
            wv_ = [(SM("ctxmm"), SM("ctxmm").n), k.tok("zev")]
            if ti == 0 and half == 0:
                wv_ += [(SM("qsp"), 16 * NT), k.tok("kcons")]
            for hh in range(8):
                h = half + 2 * hh
                pcol = half * 512 + hh * 64
                k.ts("vector", ctxb[:, h * 64:(h + 1) * 64], PS_B[:, pcol:pcol + 64],
                     zb[:, half * 8 + hh: half * 8 + hh + 1], ALU.mult,
                     waits=tuple(wv_) if hh == 0 else (),
                     incs=((PEV, 1),) if hh == 7 else ())
        tok_ctxev = k.tok("PEV")
        for c in range(8):
            ev_w = ()
            if c == 0:
                ev_w = (prev["oproj_mm"],) if ti >= 1 else (k.tok("PMM"),)
            tponce(ctxT[:, c * 128:(c + 1) * 128], ctxb[:, c * 128:(c + 1) * 128],
                   extra_waits=(tok_ctxev,) if c == 0 else (), evac_waits=ev_w)

        def evhs(h, ps, w):
            w = list(w) + ([cl2, k.tok("kcons")] if ti == 0 else []) + \
                ([(SM("LS2"), SM("LS2").n)] if ti >= 1 else [])
            k.tt("vector", hsb[:, h * 512:(h + 1) * 512], ps,
                 cslot(0)[:, h * 512:(h + 1) * 512], ALU.add, waits=tuple(w),
                 incs=((PEV, 1),))
            k.tt("vector", hsb[:, h * 512:(h + 1) * 512], hsb[:, h * 512:(h + 1) * 512],
                 xn2[:, h * 512:(h + 1) * 512], ALU.add,
                 waits=((SM("xn2l"), SM("xn2l").n),),
                 incs=((("hsx", 1),) if h == 1 else ()))
        proj_tm(ctxT, 0, 1024, evhs,
                first_waits=(k.tok("TPEV"),) + ((tok_wd,) if ti == 0 else ()))
        prev["oproj_mm"] = k.tok("PMM")
        pre_ln = [k.tok("hsx")]
        if ti >= 1:
            pre_ln.append(prev["s_w"])
        tok_s = k.ln(sb_o, hsb, sq2, stat8(1), eps12, D, gamma=cslot(1), beta=cslot(2),
                     pre=tuple(pre_ln), sem_v="LV2", sem_s="LS2")
        k.dma("gpsimd", s_scr[ti * 128:(ti + 1) * 128, :], sb_o, sem=SM("sw"),
              waits=(tok_s,))
        if stage == "s":
            k.dma("gpsimd", yout[ti * 128:(ti + 1) * 128, :], sb_o, sem=SM("sw"))
        prev["s_w"] = k.tok("sw")

    if stage == "s":
        k.wait("gpsimd", SM("sw"), SM("sw").n)
        k.build_block()
        return nc

    k.collective("AllToAll", [cc1_in.opt()], [cc1_out.opt()],
                 waits=((SM("ccw"), 16 * NT),), incs=(("A2A1", 1),))

    def ccdet_ap(buf, dt):
        return buf.rearrange("(i lp n) f -> n i lp f", i=8, lp=2, n=256)[8 * dt:8 * dt + 8]

    tok_ip = build_WT(0, ag["ip"][1], (k.tok("AG_ip"), prev["oproj_mm"], k.tok("PEV")),
                      "TIP", rows=3072)
    w_c = (k.tok("LS2"), k.tok("PEV"))
    for si in range(3):
        k.dma("sync", cslot(si), ipb_bc[:, si * 1024:(si + 1) * 1024], sem=SM("cld"),
              waits=w_c if si == 0 else ())
    cld1 = k.tok("cld")

    xd = BIGB[:, 0:1024]
    xdT = BIGB[:, 1024:2048]
    qkvsb = BIGB[:, 2048:5120]
    prevd = {}
    for dt in range(NT):
        w = [(SM("A2A1"), 1)] if dt == 0 else [prevd["xdT_mm"]]
        if dt == 0:
            w += [k.tok("LS2"), k.tok("sw"), k.tok("ctxmm"), k.tok("PEV"), k.tok("tpmm")]
        k.dma("sync", xd, ccdet_ap(cc1_out, dt), sem=SM("xdl"), waits=tuple(w))
        for c in range(8):
            tponce(xdT[:, c * 128:(c + 1) * 128], xd[:, c * 128:(c + 1) * 128],
                   extra_waits=((SM("xdl"), SM("xdl").n),) if c == 0 else (),
                   evac_waits=(prevd["qkv_mm"],) if (c == 0 and dt >= 1) else ())
        prevd["xdT_mm"] = k.tok("tpmm")
        tok_xdT = k.tok("TPEV")

        def evd(h, ps, w2):
            w2 = list(w2) + ([cld1] if dt == 0 else []) + \
                 ([prevd["qdsp_t"]] if dt >= 1 else [])
            cs = cslot(h // 2)[:, (h % 2) * 512:(h % 2) * 512 + 512]
            k.tt("vector", qkvsb[:, h * 512:(h + 1) * 512], ps, cs, ALU.add,
                 waits=tuple(w2), incs=((PEV, 1),))
        proj_tm(xdT, 0, 3072, evd, istride=3072,
                first_waits=(tok_xdT,) + ((tok_ip,) if dt == 0 else ()))
        prevd["qkv_mm"] = k.tok("PMM")
        tok_ev = k.tok("PEV")
        k.dma("gpsimd", qkvd_scr[dt * 128:(dt + 1) * 128, :], qkvsb, sem=SM("qdsp"),
              waits=(tok_ev,))
        if stage == "qdet":
            k.dma("gpsimd", yout[dt * 128:(dt + 1) * 128, :], qkvsb[:, 0:1024],
                  sem=SM("qdsp"))
        prevd["qdsp_t"] = k.tok("qdsp")
    if stage == "qdet":
        k.wait("gpsimd", SM("qdsp"), SM("qdsp").n)
        k.build_block()
        return nc

    tok_op = build_WT(0, ag["op"][1], (k.tok("AG_op"), prevd["qkv_mm"], k.tok("PEV")), "TOP")
    for si, row in [(0, 10), (1, 6), (2, 7)]:
        k.dma("sync", cslot(si), cvec[row], sem=SM("cld"),
              waits=((k.tok("PEV"),) if si == 0 else ()))
    cld2 = k.tok("cld")
    qd = BIGB[:, 0:1024]
    kd = BIGB[:, 1024:2048]
    vd = BIGB[:, 2048:3072]
    qdT = BIGB[:, 3072:4096]
    kdT = BIGB[:, 4096:5120]
    o_sb = BIGB[:, 5120:7168]
    oT = BIGB[:, 7168:9216]
    mha = BIGB[:, 9216:10240]
    y1b = BIGB[:, 10240:11264]
    y1Tb = BIGB[:, 11264:12288]
    BD = [TRANS[:, 0:128], TRANS[:, 128:256]]
    vpad = [TRANS[:, 256:1280], TRANS[:, 1280:2304]]
    xd2 = TRANS[:, 3328:4352]
    sqd = TRANS[:, 4352:5376]
    E_sb = SMALL[:, 104:120]
    cb = SMALL[:, 120:128]
    k.memset("vector", BD[0][:], 0.0,
             waits=(k.tok("qdsp"), k.tok("kvadd"), k.tok("ctxmm"), k.tok("LS2"),
                    k.tok("dmm")))
    k.memset("vector", BD[1][:], 0.0)
    k.memset("vector", vpad[0][:], 0.0)
    k.memset("vector", vpad[1][:], 0.0, incs=(("dzero", 1),))

    for dt in range(NT):
        w0 = [prevd["qdsp_t"], tok_op] if dt == 0 else [prevd["qkT_mm"]]
        k.dma("sync", qd, qkvd_scr[dt * 128:(dt + 1) * 128, 0:1024], sem=SM("d2l"),
              waits=tuple(w0))
        k.dma("sync", kd, qkvd_scr[dt * 128:(dt + 1) * 128, 1024:2048], sem=SM("d2l"))
        k.dma("sync", vd, qkvd_scr[dt * 128:(dt + 1) * 128, 2048:3072], sem=SM("d2l"),
              waits=((SM("vrp"), 128 * dt) if dt >= 1 else None,))
        k.dma("sync", xd2, ccdet_ap(cc1_out, dt), sem=SM("xd2l"),
              waits=((SM("mhax"), 2 * dt) if dt >= 1 else None,))
        tql = (SM("d2l"), SM("d2l").n)
        tvl = (SM("d2l"), SM("d2l").n)
        for c in range(8):
            tponce(qdT[:, c * 128:(c + 1) * 128], qd[:, c * 128:(c + 1) * 128],
                   extra_waits=(tql,) if c == 0 else (),
                   evac_waits=(prevd["sc_mm"],) if (c == 0 and dt >= 1) else ())
            tponce(kdT[:, c * 128:(c + 1) * 128], kd[:, c * 128:(c + 1) * 128],
                   extra_waits=(tql,) if c == 0 else ())
        prevd["qkT_mm"] = k.tok("tpmm")
        tok_qkT = k.tok("TPEV")
        for g2 in range(2):
            for a in range(4):
                nn = 4 * g2 + a
                k.dma("gpsimd", vpad[g2][32 * a:32 * a + 16, :],
                      vd[16 * nn:16 * nn + 16, :], sem=SM("vrp"),
                      waits=(tvl, (SM("avmm"), 8 * dt), k.tok("dzero")) if (g2 == 0 and a == 0) else ())
        tok_vrp = k.tok("vrp")
        for g2 in range(2):
            for h in range(FH):
                it = dt * 8 + g2 * 4 + h
                for a in range(4):
                    nn = 4 * g2 + a
                    for kc in range(2):
                        w2 = []
                        if a == 0 and kc == 0:
                            w2 = [tok_qkT]
                            if it >= 1:
                                w2.append(prevd["eexp"])
                        k.mm(PS_D[32 * a:32 * a + 16, 0:16],
                             qdT[:, (2 * h + kc) * 128 + nn * 16: (2 * h + kc) * 128 + nn * 16 + 16],
                             kdT[:, (2 * h + kc) * 128 + nn * 16: (2 * h + kc) * 128 + nn * 16 + 16],
                             start=(kc == 0), stop=(kc == 1), tile_position=(0, 32 * a),
                             waits=tuple(w2), incs=((("smm", 1),) if (a == 3 and kc == 1) else ()))
                tok_smm = k.tok("smm")
                k.memset("vector", E_sb[:], 1.0,
                         waits=(prevd["atmm"],) if it >= 1 else (),
                         incs=(("dmemset", 1),))
                for a in range(4):
                    k.act(E_sb[32 * a:32 * a + 16, :], PS_D[32 * a:32 * a + 16, 0:16],
                          AF.Exp, scale=1.0 / 16.0,
                          waits=(tok_smm, k.tok("dmemset")) if a == 0 else (),
                          incs=((("eexp", 1),) if a == 3 else ()))
                prevd["eexp"] = k.tok("eexp")
                k.red("vector", cb[:, 4:5], E_sb[:], ALU.add, waits=(prevd["eexp"],))
                k.emit("vector", lambda e, o=cb[:, h:h + 1], i=cb[:, 4:5]: e.reciprocal(o, i),
                       incs=(("crec", 1),))
                for a in range(4):
                    w2 = []
                    if a == 0:
                        w2 = [prevd["eexp"]]
                        if it >= 1:
                            w2.append(prevd["bdev"])
                    k.mm(PS_E[32 * a:32 * a + 16, 32 * a:32 * a + 16],
                         E_sb[:, 0:16], ident[:, 32 * a:32 * a + 16],
                         start=True, stop=True, tile_position=(0, 32 * a), waits=tuple(w2),
                         incs=((("atmm", 1),) if a == 3 else ()))
                prevd["atmm"] = k.tok("atmm")
                bdp = BD[it % 2]
                for a in range(4):
                    k.cp("vector", bdp[32 * a:32 * a + 16, 32 * a:32 * a + 16],
                         PS_E[32 * a:32 * a + 16, 32 * a:32 * a + 16],
                         waits=(prevd["atmm"],) if a == 0 else (),
                         incs=((("bdev", 1),) if a == 3 else ()))
                prevd["bdev"] = k.tok("bdev")
                w2 = [prevd["bdev"]]
                if g2 == 0 and h == 0:
                    w2.append(tok_vrp)
                    w2.append((PEV, PEV.n))
                if h == 0 and g2 == 1:
                    w2.append((PEV, PEV.n))
                k.mm(PS_B[:, h * 256:(h + 1) * 256], bdp[:, 0:128],
                     vpad[g2][:, h * 256:(h + 1) * 256],
                     start=True, stop=True, waits=tuple(w2), incs=(("avmm", 1),))
            tok_av = k.tok("avmm")
            for h in range(FH):
                w4 = ()
                if h == 0:
                    w4 = (tok_av, k.tok("crec"))
                    if g2 == 0 and dt >= 1:
                        w4 = w4 + (prevd["oT_mm"],)
                k.ts("vector", o_sb[:, g2 * 1024 + h * 256: g2 * 1024 + (h + 1) * 256],
                     PS_B[:, h * 256:(h + 1) * 256], cb[:, h:h + 1], ALU.mult,
                     waits=w4, incs=((PEV, 1),) if h == 3 else ())
        tok_oev = k.tok("PEV")
        for g2 in range(2):
            for c in range(8):
                dst = oT[:, c * 128 + g2 * 64: c * 128 + g2 * 64 + 64].rearrange(
                    "p (a l) -> p a l", a=4)
                tponce(dst,
                       o_sb[:, g2 * 1024 + c * 128: g2 * 1024 + (c + 1) * 128],
                       extra_waits=(tok_oev,) if (g2 == 0 and c == 0) else (),
                       evac_waits=(prevd["op_mm"],) if (g2 == 0 and c == 0 and dt >= 1) else (),
                       sel=lambda ps: ps.rearrange("p (a r) -> p a r", a=4)[:, :, 0:16])
        prevd["oT_mm"] = k.tok("tpmm")
        tok_oT = k.tok("TPEV")
        for h in range(2):
            ps = PS_B[:, (h % 2) * 512:(h % 2) * 512 + 512]
            k.wait("tensor", PEV, PEV.n - 1)
            for jc in range(8):
                w2 = [tok_oT] + ([tok_op, cld2] if dt == 0 else []) if (h == 0 and jc == 0) else []
                k.mm(ps, oT[:, jc * 128:(jc + 1) * 128], WT_mv(0, jc, h * 512, 512),
                     start=(jc == 0), stop=(jc == 7), waits=tuple(w2),
                     incs=((("PMM", 1),) if jc == 7 else ()))
            w3 = [k.tok("PMM")] + ([(SM("LSd"), SM("LSd").n)] if dt >= 1 else [])
            k.tt("vector", mha[:, h * 512:(h + 1) * 512], ps,
                 cslot(0)[:, h * 512:(h + 1) * 512], ALU.add, waits=tuple(w3),
                 incs=((PEV, 1),))
            k.tt("vector", mha[:, h * 512:(h + 1) * 512], mha[:, h * 512:(h + 1) * 512],
                 xd2[:, h * 512:(h + 1) * 512], ALU.add,
                 waits=((SM("xd2l"), SM("xd2l").n),), incs=(("mhax", 1),))
        prevd["op_mm"] = k.tok("PMM")
        prevd["sc_mm"] = k.tok("smm")
        pre_ln = [k.tok("mhax")] + ([prevd["y1_w"], prevd["y1T_mm"]] if dt >= 1 else [])
        tok_y1 = k.ln(y1b, mha, sqd, stat8(2), eps5, D, gamma=cslot(1), beta=cslot(2),
                      pre=tuple(pre_ln), sem_v="LVd", sem_s="LSd")
        k.dma("gpsimd", y1_scr[dt * 128:(dt + 1) * 128, :], y1b, sem=SM("y1w"),
              waits=(tok_y1,))
        if stage == "y1":
            k.dma("gpsimd", yout[dt * 128:(dt + 1) * 128, :], y1b, sem=SM("y1w"))
        prevd["y1_w"] = k.tok("y1w")
        for c in range(8):
            ev_w = ()
            if c == 0 and dt >= 1:
                ev_w = (prevd["gt_mm"], (SM("y1tw"), 16 * dt))
            tponce(y1Tb[:, c * 128:(c + 1) * 128], y1b[:, c * 128:(c + 1) * 128],
                   extra_waits=(tok_y1,) if c == 0 else (),
                   evac_waits=ev_w)
        prevd["y1T_mm"] = k.tok("tpmm")
        tok_y1T = k.tok("TPEV")
        k.dma("gpsimd", y1T_scr.rearrange("(c p) t -> p c t", c=8)[:, :, dt * 128:(dt + 1) * 128],
              y1Tb, sem=SM("y1tw"), waits=(tok_y1T,))
        for jc in range(8):
            w2 = []
            if jc == 0:
                w2 = [tok_y1T, (SM("idl"), SM("idl").n)]
                if dt >= 1:
                    w2.append((SM("gev"), SM("gev").n))
            k.mm(PS_F[:, 0:8], y1Tb[:, jc * 128:(jc + 1) * 128], wg0[:, jc * 8:(jc + 1) * 8],
                 start=(jc == 0), stop=(jc == 7), waits=tuple(w2),
                 incs=((("gmm", 1),) if jc == 7 else ()))
        gl = SMALL[:, 128:136]
        nmx = SMALL[:, 136:137]
        ge = SMALL[:, 144:152]
        den = SMALL[:, 152:153]
        rcp = SMALL[:, 153:154]
        emx = SMALL[:, 154:155]
        mask_sl = GT[:, dt * 24: dt * 24 + 8]
        pref_sl = GT[:, dt * 24 + 8: dt * 24 + 16]
        gv_sl = GT[:, dt * 24 + 16: dt * 24 + 17]
        prevd["gt_mm"] = k.tok("gmm")
        k.cp("vector", gl[:], PS_F[:, 0:8], waits=(k.tok("gmm"),))
        k.red("vector", nmx[:], gl[:], ALU.max, negate=True, incs=(("gv1", 1),))
        k.act(ge[:], gl[:], AF.Exp, bias=nmx[:], waits=(k.tok("gv1"),), incs=(("gs1", 1),))
        k.red("vector", den[:], ge[:], ALU.add, waits=(k.tok("gs1"),))
        k.emit("vector", lambda e, o=rcp[:], i=den[:]: e.reciprocal(o, i))
        k.red("vector", emx[:], ge[:], ALU.max, incs=(("gv2", 1),))
        k.tt("vector", gv_sl, emx[:], rcp[:], ALU.mult,
             waits=(k.tok("gv2"),) + (((SM("tofz"), SM("tofz").n),) if dt == 0 else ()))
        k.ts("vector", mask_sl, ge[:], emx[:], ALU.is_equal,
             waits=(((SM("tofz"), SM("tofz").n),) if dt == 0 else ()),
             incs=(("gev", 1),))
        for (reg, lhs, inc) in [((0, 16, 8, 16), selS[:], "cntmm"), ((0, 128, 16, 24), tri[:], "cntmm")]:
            p0, p1, c0, c1 = reg
            k.mm(PS_F[p0:p1, c0:c1], lhs, mask_sl, start=True, stop=True,
                 waits=((k.tok("gev"),) if c0 == 8 else ()) +
                       (((SM("cadd"), SM("cadd").n),) if (c0 == 8 and dt >= 1) else ()),
                 incs=((inc, 1),))
        tok_cnt = k.tok("cntmm")
        cntacc = RUN[:, 280:288]
        if dt == 0:
            k.memset("vector", RUN[0:16, 280:288], 0.0)
        k.cp("vector", RUN[0:16, dt * 8:(dt + 1) * 8], RUN[0:16, 280:288],
             waits=(tok_cnt,))
        k.tt("vector", RUN[0:16, 280:288], RUN[0:16, 280:288], PS_F[0:16, 8:16], ALU.add)
        k.cp("vector", pref_sl, PS_F[:, 16:24], incs=(("cadd", 1),))

    k.dma("gpsimd", agi_cnt, RUN[0:16, 280:288], sem=SM("cntw"), waits=(k.tok("cadd"),))
    k.collective("AllGather", [agi_cnt.opt()], [ago_cnt.opt()],
                 waits=((SM("cntw"), SM("cntw").n),), incs=(("A2Ac", 1),))
    agc = SMALL[:, 160:168]
    k.dma("sync", agc, ago_cnt, sem=SM("agcl"), waits=((SM("A2Ac"), 1),))
    k.mm(PS_F[0:16, 24:32], m1c[:], agc[:], start=True, stop=True,
         waits=((SM("agcl"), SM("agcl").n), k.tok("cadd")), incs=(("amm", 1),))
    A_sb = RUN[0:16, 288:296]
    k.cp("vector", A_sb, PS_F[0:16, 24:32], waits=(k.tok("amm"),), incs=(("aev", 1),))
    tmp16 = RUN[0:16, 296:304]
    rank = SMALL[:, 168:176]
    tmp8 = SMALL[:, 176:184]
    ro = SMALL[:, 184:185]
    kp = SMALL[:, 185:186]
    sc = SMALL[:, 186:187]
    for dt in range(NT):
        mask_sl = GT[:, dt * 24: dt * 24 + 8]
        pref_sl = GT[:, dt * 24 + 8: dt * 24 + 16]
        gv_sl = GT[:, dt * 24 + 16: dt * 24 + 17]
        k.tt("vector", tmp16, A_sb, RUN[0:16, dt * 8:(dt + 1) * 8], ALU.add,
             waits=(k.tok("aev"),) if dt == 0 else ((SM("rkmm"), SM("rkmm").n),),
             incs=(("t16", 1),))
        k.mm(PS_F[:, 32:40], selST[:], tmp16, start=True, stop=True,
             waits=(k.tok("t16"),), incs=(("rkmm", 1),))
        k.stt("vector", rank, PS_F[:, 32:40], 1.0, pref_sl, ALU.mult, ALU.add,
              waits=(k.tok("rkmm"),))
        k.tt("vector", tmp8, rank, mask_sl, ALU.mult)
        k.red("vector", ro[:], tmp8, ALU.add)
        k.ts("vector", kp[:], ro[:], 4096.0, ALU.is_lt)
        k.tt("vector", sc[:], kp[:], gv_sl, ALU.mult, incs=(("scv", 1),))
        k.ts("vector", S8[:, dt * 8:(dt + 1) * 8], mask_sl, sc[:], ALU.mult,
             incs=(("s8v", 1),))


    ytmp = [TRANS[:, 256:1280], TRANS[:, 1280:2304]]
    y1t_b = [TRANS[:, 3328:4352], TRANS[:, 4352:5376]]
    for e in range(E):
        for jc in range(8):
            w2 = [(SM("AG_we"), 1)] if (e == 0 and jc == 0) else []
            if e == 0 and jc == 0:
                w2 += [k.tok("avmm"), k.tok("PMM"), k.tok("bdev"), (SM("vrp"), SM("vrp").n)]
            if e >= 1 and jc == 0:
                w2.append(moe_prev_mm)
            k.dma("sync", BIGB[:, jc * 1024:(jc + 1) * 1024],
                  ago_we[e * 1024 + jc * 128: e * 1024 + (jc + 1) * 128, :],
                  sem=SM("wel"), waits=tuple(w2))
        k.dma("sync", rowbuf2[0:1, 0:1024], ago_ce[e:e + 1, :], sem=SM("cel"),
              waits=((SM("AG_ce"), 1),) if e == 0 else (k.tok("moemm"),))
        tok_wel = k.tok("wel")
        tok_cel = k.tok("cel")
        ROT4 = (PS_B[:, 0:512], PS_B[:, 512:1024], PS_C[:, 0:512], PS_E[:, 0:512])
        for dt in range(NT):
            par = dt % 2
            idx = e * NT + dt
            k.dma("sync", y1t_b[par], y1T_scr.rearrange("(c p) t -> p c t", c=8)
                  [:, :, dt * 128:(dt + 1) * 128], sem=SM("y1tl%d" % par),
                  waits=((SM("y1tw"), 16 * NT) if (e == 0 and dt == 0) else None,
                         (SM("moemm"), 2 * (idx - 1)) if idx >= 2 else None))
            tok_yl = k.tok("y1tl%d" % par)
            for dh in range(2):
                ps = ROT4[(dt * 2 + dh) % 4]
                k.wait("tensor", PEV, PEV.n - 3)
                for jc in range(8):
                    w2 = []
                    if jc == 0:
                        w2 = [tok_yl]
                        if dt == 0:
                            w2.append(tok_wel)

                    k.mm(ps, y1t_b[par][:, jc * 128:(jc + 1) * 128],
                         BIGB[:, jc * 1024 + dh * 512: jc * 1024 + dh * 512 + 512],
                         start=(jc == 0), stop=False, waits=tuple(w2))
                k.mm(ps, onesrow[0:1, :], rowbuf2[0:1, dh * 512: dh * 512 + 512],
                     start=False, stop=True,
                     waits=(tok_cel,) if (dt == 0 and dh == 0) else (),
                     incs=(("moemm", 1),))
                k.ts("vector", ytmp[par][:, dh * 512:(dh + 1) * 512], ps,
                     S8[:, dt * 8 + e: dt * 8 + e + 1], ALU.mult,
                     waits=(k.tok("moemm"), k.tok("s8v") if (e == 0 and dt == 0) else None,
                            (SM("macc%d" % par), SM("macc%d" % par).n)
                            if (e * NT + dt >= 2) else None),
                     incs=((PEV, 1),) + (() if dh == 0 else ()))
            k.dma("gpsimd", moe0_scr[dt * 128:(dt + 1) * 128, :], ytmp[par],
                  sem=SM("macc%d" % par), waits=(k.tok("PEV"),), accum=(e > 0))
        moe_prev_mm = k.tok("moemm")
    for si, row in [(0, 8), (1, 9)]:
        k.dma("sync", cslot(si), cvec[row], sem=SM("cld"),
              waits=((SM("LSd"), SM("LSd").n), k.tok("PEV")) if si == 0 else ())
    cld3 = k.tok("cld")
    d5a = BIGB[:, 0:1024]
    d5b = BIGB[:, 1024:2048]
    d5y = BIGB[:, 2048:3072]
    d5g = BIGB[:, 3072:4096]
    d5sq = BIGB[:, 4096:5120]
    for dt in range(NT):
        w2 = ([(SM("macc0"), SM("macc0").n), (SM("macc1"), SM("macc1").n),
               (SM("y1w"), SM("y1w").n)]
              if dt == 0 else [(SM("lnf5"), SM("lnf5").n)])
        if dt == 0:
            w2 += [k.tok("moemm"), k.tok("PEV")]
        k.dma("sync", d5a, y1_scr[dt * 128:(dt + 1) * 128, :], sem=SM("d5l"), waits=tuple(w2))
        k.dma("sync", d5b, moe0_scr[dt * 128:(dt + 1) * 128, :], sem=SM("d5l"))
        k.tt("vector", d5y, d5a, d5b, ALU.add, waits=((SM("d5l"), SM("d5l").n),),
             incs=(("hy2", 1),))
        pre_ln = [k.tok("hy2")] + ([cld3] if dt == 0 else []) + \
                 ([(SM("ccb"), 16 * dt * (2 if stage == "g" else 1))] if dt >= 1 else [])
        tok_y2 = k.ln(d5y, d5y, d5sq, stat8(3), eps5, D, gamma=cslot(0), beta=cslot(1),
                      pre=tuple(pre_ln), sem_v="LV5", sem_s="LS5")
        k.act(d5g, d5y, AF.Silu, waits=(tok_y2,), incs=(("lnf5", 1),))
        k.dma("gpsimd", ccdet_ap(cc2_in, dt), d5g, sem=SM("ccb"),
              waits=(k.tok("lnf5"),))
        if stage == "g":
            k.dma("gpsimd", yout[dt * 128:(dt + 1) * 128, :], d5g, sem=SM("ccb"))
    if stage in ("y1", "g"):
        k.wait("gpsimd", SM("y1w" if stage == "y1" else "ccb"),
               SM("y1w" if stage == "y1" else "ccb").n)
        k.build_block()
        return nc

    k.collective("AllToAll", [cc2_in.opt()], [cc2_out.opt()],
                 waits=((SM("ccb"), SM("ccb").n),), incs=(("A2A2", 1),))

    for jc in range(8):
        k.dma("sync", BIGA[:, jc * 3072:(jc + 1) * 3072],
              ag["wqkv1"][1][jc * 128:(jc + 1) * 128, :], sem=SM("wq1l"),
              waits=((k.tok("AG_wqkv1"), k.tok("moemm"), k.tok("PMM"), k.tok("wemm"))
                     if jc == 0 else ()))
    tok_wq1 = k.tok("wq1l")
    c1x = BIGB[:, 0:1024]
    c1s = BIGB[:, 1024:2048]
    c1g = BIGB[:, 2048:3072]
    c1x1 = BIGB[:, 3072:4096]
    c1xn = BIGB[:, 4096:5120]
    c1xnT = BIGB[:, 5120:6144]
    c1qkv = BIGB[:, 6144:9216]
    c1sq = BIGB[:, 9216:10240]
    prev1 = {}
    for ti in range(NT):
        b_loc, s0 = ti // 16, (ti % 16) * 128
        i_blk = s0 // 256
        r0 = i_blk * 512 + b_loc * 256 + (s0 % 256)
        w0 = [(SM("A2A2"), 1)] if ti == 0 else [prev1["x1cmb"]]
        if ti == 0:
            w0 += [k.tok("lnf5"), k.tok("LS5"), k.tok("moemm"), k.tok("PEV"),
                   k.tok("hy2"), (SM("d5l"), SM("d5l").n)]
        k.dma("sync", c1x, xin[ti * 128:(ti + 1) * 128, :], sem=SM("c1l"), waits=tuple(w0))
        k.dma("sync", c1s, s_scr[ti * 128:(ti + 1) * 128, :], sem=SM("c1l"))
        k.dma("sync", c1g, cc2_out[r0:r0 + 128, :], sem=SM("c1l"))
        tok_l = k.tok("c1l")
        w1_ = [tok_l] + ([prev1["x1w"], prev1["xn1_tok"]] if ti >= 1 else [])
        k.tt("vector", c1s, c1s, c1g, ALU.mult, waits=tuple(w1_))
        k.tt("vector", c1x1, c1x, c1s, ALU.add, incs=(("x1c", 1),))
        prev1["x1cmb"] = k.tok("x1c")
        k.dma("gpsimd", x1_scr[ti * 128:(ti + 1) * 128, :], c1x1, sem=SM("x1w"),
              waits=(k.tok("x1c"),))
        if stage == "x1":
            if os.environ.get("KB_X1DUMP") == "cc2raw":
                k.dma("gpsimd", yout[ti * 128:(ti + 1) * 128, :],
                      cc2_in[ti * 128:(ti + 1) * 128, :], sem=SM("x1w"),
                      waits=((SM("ccb"), 16 * NT) if ti == 0 else None,))
            elif os.environ.get("KB_X1DUMP") == "cc2oraw":
                k.dma("gpsimd", yout[ti * 128:(ti + 1) * 128, :],
                      cc2_out[ti * 128:(ti + 1) * 128, :], sem=SM("x1w"),
                      waits=(((SM("A2A2"), 1)) if ti == 0 else None,))
            elif os.environ.get("KB_X1DUMP"):
                dumpsrc = {"g": c1g, "s": c1s, "x": c1x}.get(os.environ.get("KB_X1DUMP", ""), c1x1)
                k.dma("gpsimd", yout[ti * 128:(ti + 1) * 128, :], dumpsrc, sem=SM("x1w"))
        prev1["x1w"] = k.tok("x1w")
        pre = [k.tok("x1c")] + ([prev1["xnT_mm"], (SM("xnTw"), 16 * ti)] if ti >= 1 else [])
        tok_xn1 = k.ln(c1xn, c1x1, c1sq, stat8(4), eps5, D, pre=tuple(pre),
                       sem_v="LV6", sem_s="LS6")
        prev1["xn1_tok"] = tok_xn1
        k.dma("gpsimd", xn1_scr[ti * 128:(ti + 1) * 128, :], c1xn, sem=SM("xnTw"),
              waits=(tok_xn1,))
        for c in range(8):
            tponce(c1xnT[:, c * 128:(c + 1) * 128], c1xn[:, c * 128:(c + 1) * 128],
                   extra_waits=(tok_xn1,) if c == 0 else (),
                   evac_waits=(prev1["qkv_mm"], (SM("xnTw"), 16 * ti)) if (c == 0 and ti >= 1) else ())
        prev1["xnT_mm"] = k.tok("tpmm")
        tok_xnT = k.tok("TPEV")
        def evq1(h, ps, w2):
            w2 = list(w2) + ([prev1["q1sp"]] if ti >= 1 else [])
            k.cp("vector", c1qkv[:, h * 512:(h + 1) * 512], ps, waits=tuple(w2),
                 incs=((PEV, 1),))
        proj_tm(c1xnT, 0, 3072, evq1, istride=3072,
                first_waits=(tok_xnT,) + ((tok_wq1,) if ti == 0 else ()))
        prev1["qkv_mm"] = k.tok("PMM")
        k.dma("gpsimd", qkv1_scr[ti * 128:(ti + 1) * 128, :], c1qkv, sem=SM("q1sp"),
              waits=(k.tok("PEV"),))
        prev1["q1sp"] = k.tok("q1sp")
        for jc in range(8):
            w2 = []
            if jc == 0:
                w2 = [tok_xnT, (SM("idl"), SM("idl").n)]
                if ti >= 1:
                    w2.append((SM("g1ev"), SM("g1ev").n))
            k.mm(PS_F[:, 0:8], c1xnT[:, jc * 128:(jc + 1) * 128], wg1[:, jc * 8:(jc + 1) * 8],
                 start=(jc == 0), stop=(jc == 7), waits=tuple(w2),
                 incs=((("g1mm", 1),) if jc == 7 else ()))
        gl = SMALL[:, 192:200]
        mm1 = SMALL[:, 200:201]
        mm2 = SMALL[:, 201:202]
        mlg = SMALL[:, 208:216]
        w12 = SMALL[:, 216:218]
        msk1 = SMALL[:, 224:232]
        msk2 = SMALL[:, 232:240]
        k.cp("vector", gl[:], PS_F[:, 0:8], waits=(k.tok("g1mm"),), incs=(("g1ev", 1),))
        k.red("vector", mm1[:], gl[:], ALU.max)
        k.ts("vector", msk1[:], gl[:], mm1[:], ALU.is_equal, incs=(("g1v", 1),))
        k.stt("vector", mlg[:], msk1[:], -1e30, gl[:], ALU.mult, ALU.add)
        k.red("vector", mm2[:], mlg[:], ALU.max)
        k.ts("vector", msk2[:], mlg[:], mm2[:], ALU.is_equal)
        k.tt("vector", w12[:, 0:1], mm1[:], mm2[:], ALU.subtract,
             waits=(((SM("slw"), SM("slw").n),) if ti >= 1 else ()),
             incs=(("g1v", 1),))
        k.act(w12[:, 0:1], w12[:, 0:1], AF.Sigmoid, waits=(k.tok("g1v"),),
              incs=(("g1s", 1),))
        k.ts("vector", w12[:, 1:2], w12[:, 0:1], -1.0, ALU.mult,
             waits=(k.tok("g1s"),))
        k.ts("vector", w12[:, 1:2], w12[:, 1:2], 1.0, ALU.add, incs=(("VM", 1),))
        um8 = SMALL[:, 256:264]
        k.tt("vector", um8[:], msk1[:], msk2[:], ALU.add,
             waits=((k.tok("pfmm"),) if ti >= 1 else ()), incs=(("umv", 1),))
        k.dma("gpsimd", wv_scr[ti * 128:(ti + 1) * 128, :], w12[:, 0:2], sem=SM("slw"),
              waits=(k.tok("g1s"), (SM("VM"), SM("VM").n)))
        k.mm(PS_D[:, 0:8], tri0[:], um8[:], start=True, stop=True,
             waits=(k.tok("umv"),) + ((k.tok("rkv"),) if ti >= 1 else ()),
             incs=(("pfmm", 1),))
        k.mm(PS_D[0:1, 8:16], onescol[:, 0:1], um8[:], start=True, stop=True,
             incs=(("pfmm", 1),))
        run8 = SMALL[:, 264:272]
        if ti == 0:
            k.memset("vector", run8[0:1, :], 0.0, incs=(("runv", 1),))
        k.tt("vector", SMALL[0:1, 272:280], run8[0:1, :], cpe_base[0:1, :], ALU.add,
             waits=((SM("runv"), SM("runv").n),) + ((k.tok("pfmm"),) if ti >= 1 else ()),
             incs=(("runv", 1),))
        k.mm(PS_E[:, 0:8], onesrow[0:1, :], SMALL[0:1, 272:280], start=True, stop=True,
             waits=(k.tok("runv"),), incs=(("pfmm", 1),))
        k.tt("vector", run8[0:1, :], run8[0:1, :], PS_D[0:1, 8:16], ALU.add,
             waits=(k.tok("pfmm"),), incs=(("runv", 1),))
        rank8 = SMALL[:, 280:288]
        k.cp("vector", rank8[:], PS_D[:, 0:8], waits=(k.tok("pfmm"),))
        k.tt("vector", rank8[:], rank8[:], PS_E[:, 0:8], ALU.add)
        tmp8b = SMALL[:, 288:296]
        slotf = SMALL[:, 296:298]
        for kk_, mk in ((0, msk1), (1, msk2)):
            k.tt("vector", tmp8b[:], rank8[:], mk[:], ALU.mult)
            k.red("vector", slotf[:, kk_:kk_ + 1], tmp8b[:], ALU.add)
        k.ts("vector", slotf[:, 0:2], slotf[:, 0:2], float(8 * 1536), ALU.min,
             incs=(("rkv", 1),))
        k.cp("vector", IDXS[:, 0:2], slotf[:, 0:2],
             waits=(k.tok("rkv"),) + (((SM("sctw"), SM("sctw").n),) if ti >= 1 else ()),
             incs=(("rkv", 1),))
        k.dma("gpsimd", slots_scr[ti * 128:(ti + 1) * 128, :], IDXS[:, 0:2],
              sem=SM("slw"), waits=(k.tok("rkv"),))
        k.emit("gpsimd", lambda e, o=IDXS[:, 2:3], b=ti * 128:
               e.iota(o, pattern=[[0, 1]], base=b, channel_multiplier=1),
               waits=((SM("tofz"), SM("tofz").n), (SM("sctw"), SM("sctw").n)),
               incs=(("iot", 1),))
        from concourse.bass_core_rust import IndirectOffsetOnAxis as _IOA
        for kk_ in range(2):
            k.emit("gpsimd", lambda e, ix=IDXS[:, kk_:kk_ + 1], tk=IDXS[:, 2:3]:
                   e.indirect_dma_start(tok_slot[:, :], _IOA(ix, 0), tk, None),
                   waits=(k.tok("iot"), k.tok("rkv")),
                   incs=(("sctw", 16),), drain=False)
        k.ts("vector", msk1[:], msk1[:], w12[:, 0:1], ALU.mult,
             waits=((SM("slw"), SM("slw").n),))
        k.ts("vector", msk2[:], msk2[:], w12[:, 1:2], ALU.mult)
        k.tt("vector", S8[:, 256 + ti * 8: 256 + (ti + 1) * 8], msk1[:], msk2[:],
             ALU.add, incs=(("s81", 1),))
    if stage == "x1":
        if not os.environ.get("KB_X1DUMP"):
            for ti in range(NT):
                k.dma("gpsimd", yout[ti * 128:(ti + 1) * 128, :],
                      x1_scr[ti * 128:(ti + 1) * 128, :], sem=SM("yw"),
                      waits=((SM("x1w"), SM("x1w").n) if ti == 0 else None,))
            k.wait("gpsimd", SM("yw"), SM("yw").n)
        k.wait("gpsimd", SM("x1w"), SM("x1w").n)
        k.build_block()
        return nc

    def vtile(kt):
        if kt < 8:
            return BIGA[:, 16384 + kt * 1024: 16384 + (kt + 1) * 1024]
        return BIGB[:, (kt - 8) * 1024: (kt - 7) * 1024]

    qtb = BIGB[:, 8192:9216]
    qTb = BIGB[:, 9216:10240]
    probs = BIGB[:, 10240:12288]
    pT = [GT[:, 0:128], GT[:, 128:256]]
    o1sb = TRANS[:, 0:1024]
    rr1 = SMALL[:, 240:241]
    mx4 = SMALL[:, 244:248]
    nmb = SMALL[:, 248:249]
    prev2 = {}
    for r in range(B):
        for kt in range(16):
            tt_g = [(SM("q1sp"), 16 * NT)] if (r == 0 and kt == 0) else []
            if r == 1 and kt == 0:
                tt_g = [prev2["sc1_mm"], k.tok("PMM")]
            k.dma("sync", TRANS[:, 1024:2048],
                  qkv1_scr[r * 2048 + kt * 128: r * 2048 + (kt + 1) * 128, 1024:2048],
                  sem=SM("ktl"), waits=tuple(tt_g) + ((TPEV, TPEV.n),))
            tk = k.tok("ktl")
            for c in range(8):
                tponce(BIGA[:, c * 2048 + kt * 128: c * 2048 + (kt + 1) * 128],
                       TRANS[:, 1024 + c * 128: 1024 + (c + 1) * 128],
                       extra_waits=(tk,) if c == 0 else ())
        tok_kT = k.tok("TPEV")
        for kt in range(16):
            w2 = []
            if kt == 0:
                w2 = [tok_kT] + ([prev2["av_mm"]] if r == 1
                                 else [k.tok("wq1l"), k.tok("g1mm"), k.tok("x1w"),
                                       k.tok("xnTw"), k.tok("PMM")])
            k.dma("sync", vtile(kt),
                  qkv1_scr[r * 2048 + kt * 128: r * 2048 + (kt + 1) * 128, 2048:3072],
                  sem=SM("v1l"), waits=tuple(w2))
        tok_v1 = k.tok("v1l")
        for qt in range(16):
            ti_g = r * 16 + qt
            k.dma("sync", qtb, qkv1_scr[r * 2048 + qt * 128: r * 2048 + (qt + 1) * 128, 0:1024],
                  sem=SM("q1l"), waits=(prev2.get("qT_mm"),))
            tq = (SM("q1l"), SM("q1l").n)
            for c in range(8):
                tponce(qTb[:, c * 128:(c + 1) * 128], qtb[:, c * 128:(c + 1) * 128],
                       extra_waits=(tq, tok_v1) if c == 0 else (),
                       evac_waits=(prev2["sc1_mm"],) if "sc1_mm" in prev2 else ())
            prev2["qT_mm"] = k.tok("tpmm")
            tok_qT = k.tok("TPEV")
            for h in range(16):
                pb = 64 * (h % 2)
                segs = [PS_B[:, 0:512], PS_B[:, 512:1024], PS_C[:, 0:512], PS_E[:, 0:512]]
                for kb_i in range(4):
                    w2 = []
                    if kb_i == 0:
                        w2 = [tok_qT]
                        w2.append((SM("SCM"), SM("SCM").n))
                        if h == 0:
                            w2.append((PEV, PEV.n))
                    k.mm(segs[kb_i],
                         qTb[pb:pb + 64, (h // 2) * 128:(h // 2) * 128 + 128],
                         BIGA[pb:pb + 64, (h // 2) * 2048 + kb_i * 512: (h // 2) * 2048 + (kb_i + 1) * 512],
                         start=True, stop=True, waits=tuple(w2),
                         incs=((("sc1mm", 1),) if kb_i == 3 else ()))
                tok_sc = k.tok("sc1mm")
                for kb_i in range(4):
                    k.red("vector", mx4[:, kb_i:kb_i + 1], segs[kb_i], ALU.max,
                          waits=(tok_sc,) if kb_i == 0 else ())
                k.red("vector", nmb[:], mx4[:, 0:4], ALU.max, negate=True)
                k.ts("vector", nmb[:], nmb[:], 1.0 / 32.0, ALU.mult, incs=(("sm1v", 1),))
                for kb_i in range(4):
                    k.act(probs[:, kb_i * 512:(kb_i + 1) * 512], segs[kb_i], AF.Exp,
                          bias=nmb[:], scale=1.0 / 32.0,
                          waits=(k.tok("sm1v"),) if kb_i == 0 else (),
                          incs=((("SCM", 1),) if kb_i == 3 else ()))
                tok_exp = (SM("SCM"), SM("SCM").n)
                k.red("vector", rr1[:], probs[:], ALU.add, waits=(tok_exp,))
                k.emit("vector", lambda e, o=rr1[:], i=rr1[:]: e.reciprocal(o, i),
                       incs=(("VM", 1),))
                tok_rr = (SM("VM"), SM("VM").n)
                ops_ = PS_D if h % 2 == 0 else PS_F
                pTbuf = TRANS[:, 2304:4352]
                for kb4 in range(4):
                    k.pyctr["tpar"] = k.pyctr.get("tpar", 0) + 1
                    pslot = k.pyctr["tpar"] % 2
                    for q4 in range(4):
                        kt = kb4 * 4 + q4
                        k.emit("tensor",
                               lambda ee, o=PS_A[:, pslot * 512 + q4 * 128: pslot * 512 + (q4 + 1) * 128],
                               i=probs[:, kt * 128:(kt + 1) * 128]:
                                   ee.transpose(o, i, ident[0:128, 0:128]),
                               waits=((tok_exp,) if kt == 0 else ()) +
                                     (((TPEV, TPEV.n - 1),) if q4 == 0 else ()),
                               incs=(("tpmm", 1),))
                    lag = 4 * (3 - kb4)
                    gv_ = SM("av1").n - lag
                    k.cp("vector", pTbuf[:, kb4 * 512:(kb4 + 1) * 512],
                         PS_A[:, pslot * 512: pslot * 512 + 512],
                         waits=(k.tok("tpmm"),) +
                               (((SM("av1"), gv_),) if gv_ >= 1 else ()),
                         incs=((TPEV, 1),))
                tok_pT = k.tok("TPEV")
                for kt in range(16):
                    w2 = []
                    if kt == 0:
                        w2 = [tok_pT, (SM("oev1"), SM("oev1").n)]
                    elif kt % 4 == 0:
                        pass
                    k.mm(ops_[:, 0:64], pTbuf[:, kt * 128:(kt + 1) * 128],
                         vtile(kt)[:, h * 64:(h + 1) * 64],
                         start=(kt == 0), stop=(kt == 15), waits=tuple(w2),
                         incs=(("av1", 1),))
                k.ts("vector", o1sb[:, h * 64:(h + 1) * 64], ops_[:, 0:64], rr1[:],
                     ALU.mult,
                     waits=(k.tok("av1"), tok_rr)
                           + ((prev2["o1_w"],) if (h == 0 and "o1_w" in prev2) else ()),
                     incs=(("oev1", 1),))
            prev2["sc1_mm"] = k.tok("sc1mm")
            prev2["av_mm"] = k.tok("av1")
            k.dma("gpsimd", o1_scr[ti_g * 128:(ti_g + 1) * 128, :], o1sb,
                  sem=SM("o1w"), waits=(k.tok("oev1"),))
            prev2["o1_w"] = k.tok("o1w")
    for jc in range(8):
        k.dma("sync", BIGA[:, jc * 1024:(jc + 1) * 1024],
              ag["wout1"][1][jc * 128:(jc + 1) * 128, :], sem=SM("wo1l"),
              waits=((k.tok("AG_wout1"), k.tok("sc1mm"), k.tok("av1"), k.tok("TPEV"))
                     if jc == 0 else ()))
    tok_wo1 = k.tok("wo1l")
    c2b_o = BIGB[:, 0:1024]
    c2b_oT = BIGB[:, 1024:2048]
    c2b_s = BIGB[:, 2048:3072]
    prev3 = {}
    for ti in range(NT):
        k.dma("sync", c2b_o, o1_scr[ti * 128:(ti + 1) * 128, :], sem=SM("o1l"),
              waits=((SM("o1w"), 16 * NT), k.tok("v1l"), (SM("q1l"), SM("q1l").n))
                    if ti == 0 else (prev3["oT_mm"],))
        to_ = (SM("o1l"), SM("o1l").n)
        for c in range(8):
            tponce(c2b_oT[:, c * 128:(c + 1) * 128], c2b_o[:, c * 128:(c + 1) * 128],
                   extra_waits=(to_,) if c == 0 else (),
                   evac_waits=(prev3["sp_mm"],) if (c == 0 and ti >= 1) else ())
        prev3["oT_mm"] = k.tok("tpmm")
        tok_oT2 = k.tok("TPEV")

        def evs1(h, ps, w2):
            w2 = list(w2) + ([(SM("s1w"), 16 * ti)] if ti >= 1 else [])
            k.cp("vector", c2b_s[:, h * 512:(h + 1) * 512], ps, waits=tuple(w2),
                 incs=((PEV, 1),))
        proj_tm(c2b_oT, 0, 1024, evs1,
                first_waits=(tok_oT2,) + ((tok_wo1,) if ti == 0 else ()))
        prev3["sp_mm"] = k.tok("PMM")
        k.dma("gpsimd", s1_scr[ti * 128:(ti + 1) * 128, :], c2b_s, sem=SM("s1w"),
              waits=(k.tok("PEV"),))
        if stage == "s1":
            k.dma("gpsimd", yout[ti * 128:(ti + 1) * 128, :], c2b_s, sem=SM("s1y"))
    if stage == "s1":
        k.wait("gpsimd", SM("s1y"), SM("s1y").n)
        k.build_block()
        return nc

    from concourse.bass_core_rust import IndirectOffsetOnAxis as _IOA2
    CG = 3
    hq = TRANS[:, 0:4096]
    hq_h = TRANS[:, 0:2048].bitcast(BF16)
    hq_l = TRANS[:, 2048:4096].bitcast(BF16)
    stg = TRANS[:, 0:4096]
    ytmp1 = TRANS[:, 4096:5120]
    cxT = BIGB[:, 0:12288]
    for e in range(E):
        for ct in range(12):
            sbase = e * CPE + ct * 128
            w2 = []
            if e == 0 and ct == 0:
                w2 = [(SM("sctw"), SM("sctw").n), (SM("xnTw"), 16 * NT),
                      k.tok("PMM"), k.tok("PEV"), (SM("s1w"), SM("s1w").n),
                      (SM("o1l"), SM("o1l").n)]
            if ct >= 1 or e >= 1:
                w2.append(k.tok("cxg"))
            k.dma("sync", IDXS[:, 4:5], tok_slot[sbase:sbase + 128, :], sem=SM("ixl"),
                  waits=tuple(w2))
            gw = [(SM("ixl"), SM("ixl").n)]
            if ct >= 4 or e >= 1:
                gw.append((SM("tpmm"), SM("tpmm").n - 24))
            if e >= 1 and ct == 0:
                gw.append(k.tok("m1mm"))
            k.emit("gpsimd", lambda ee, o=stg[:, (ct % 4) * 1024:(ct % 4) * 1024 + 1024],
                   ix=IDXS[:, 4:5]:
                   ee.indirect_dma_start(o, None, xn1_scr[:, :], _IOA2(ix, 0)),
                   waits=tuple(gw), incs=(("cxg", 16),), drain=False)
            tok_g = k.tok("cxg")
            for c in range(8):
                ev_w = ()
                if c == 0 and e >= 1 and ct == 0:
                    ev_w = (k.tok("h1mm"),)
                k.pyctr["tpar"] = k.pyctr.get("tpar", 0) + 1
                pslot = k.pyctr["tpar"] % 2
                pst = PS_A[:, pslot * 512: pslot * 512 + 128]
                k.emit("tensor",
                       lambda ee, o=pst, i=stg[:, (ct % 4) * 1024 + c * 128:(ct % 4) * 1024 + (c + 1) * 128]:
                           ee.transpose(o, i, ident[0:128, 0:128]),
                       waits=((tok_g,) if c == 0 else ()) + ((TPEV, TPEV.n - 1),),
                       incs=(("tpmm", 1),))
                co = c * 1536 + ct * 128
                hi_ap = BIGBh[:, co:co + 128]
                lo_ap = BIGBh[:, 12288 + co: 12288 + co + 128]
                k.cp("vector", hi_ap, pst, waits=(k.tok("tpmm"),) + tuple(ev_w))
                k.stt("vector", lo_ap, hi_ap, -1.0, pst, ALU.mult, ALU.add,
                      incs=((TPEV, 1),))
        tok_cxT = k.tok("TPEV")
        k.dma("sync", rowbuf[0:1, 0:1024], bout_r[0:1, e * 1024:(e + 1) * 1024],
              sem=SM("bo1l"),
              waits=(((SM("idl"), SM("idl").n),) if e == 0 else (k.tok("m1mm"),)))
        tok_bo = k.tok("bo1l")
        for fq in range(4):
            for jc in range(8):
                w2 = [(SM("wspl"), SM("wspl").n)]
                if jc == 0:
                    w2 += [(SM("AG_win"), 1)]
                k.dma("sync", BIGA[:, 16384 + jc * 1024: 16384 + (jc + 1) * 1024],
                      ag["win"][1][e * 1024 + jc * 128: e * 1024 + (jc + 1) * 128,
                                   fq * 1024:(fq + 1) * 1024],
                      sem=SM("wi1l"), waits=tuple(w2))
            twl = k.tok("wi1l")
            for jc in range(8):
                stga = BIGA[:, 16384 + jc * 1024: 16384 + (jc + 1) * 1024]
                w2 = [twl] if jc == 0 else []
                if jc == 0:
                    w2.append(k.tok("h1mm"))
                k.cp("vector", BIGAh[:, jc * 1024:(jc + 1) * 1024], stga, waits=tuple(w2))
                k.stt("vector", BIGAh[:, 8192 + jc * 1024: 8192 + (jc + 1) * 1024],
                      BIGAh[:, jc * 1024:(jc + 1) * 1024], -1.0, stga,
                      ALU.mult, ALU.add, incs=(("wspl", 1),))
            for fc in range(8):
                w2 = [(SM("wspl"), SM("wspl").n)]
                if fc == 0:
                    w2 += [(SM("AG_wout"), 1)]
                k.dma("sync", BIGA[:, 16384 + fc * 1024: 16384 + (fc + 1) * 1024],
                      ag["wout"][1][e * 4096 + fq * 1024 + fc * 128:
                                    e * 4096 + fq * 1024 + (fc + 1) * 128, :],
                      sem=SM("wi2l"), waits=tuple(w2))
            two = k.tok("wi2l")
            for fc in range(8):
                stga = BIGA[:, 16384 + fc * 1024: 16384 + (fc + 1) * 1024]
                w2 = [two, k.tok("m1mm")] if fc == 0 else []
                k.cp("vector", BIGAh[:, 16384 + fc * 1024: 16384 + (fc + 1) * 1024],
                     stga, waits=tuple(w2))
                k.stt("vector", BIGAh[:, 24576 + fc * 1024: 24576 + (fc + 1) * 1024],
                      BIGAh[:, 16384 + fc * 1024: 16384 + (fc + 1) * 1024], -1.0, stga,
                      ALU.mult, ALU.add, incs=(("wspl", 1),))
            tok_wi = k.tok("wspl")
            ROT = (PS_B[:, 0:512], PS_B[:, 512:1024], PS_C[:, 0:512], PS_E[:, 0:512])
            for gi in range(CG):
                for fc in range(8):
                    ps = ROT[fc % 4]
                    k.wait("tensor", PEV, PEV.n - 3)
                    for jc in range(8):
                        w2 = [tok_wi, tok_cxT] if (fc == 0 and jc == 0) else []
                        wi_h = BIGAh[:, jc * 1024 + fc * 128: jc * 1024 + fc * 128 + 128]
                        wi_l = BIGAh[:, 8192 + jc * 1024 + fc * 128: 8192 + jc * 1024 + fc * 128 + 128]
                        cx_h = BIGBh[:, jc * 1536 + gi * 512: jc * 1536 + (gi + 1) * 512]
                        cx_l = BIGBh[:, 12288 + jc * 1536 + gi * 512: 12288 + jc * 1536 + (gi + 1) * 512]
                        k.mm(ps, wi_h, cx_h, start=(jc == 0), stop=False, waits=tuple(w2))
                        k.mm(ps, wi_h, cx_l, start=False, stop=False)
                        k.mm(ps, wi_l, cx_h, start=False, stop=(jc == 7),
                             incs=((("h1mm", 1),) if jc == 7 else ()))
                    k.act(ytmp1[:, 0:512], ps, AF.Gelu,
                          bias=binsb[:, e * 32 + fq * 8 + fc: e * 32 + fq * 8 + fc + 1],
                          waits=(k.tok("h1mm"), (SM("m1w"), SM("m1w").n)),
                          incs=(("SCM", 1),))
                    k.cp("vector", hq_h[:, fc * 512:(fc + 1) * 512], ytmp1[:, 0:512],
                         waits=((SM("SCM"), SM("SCM").n),))
                    k.stt("vector", hq_l[:, fc * 512:(fc + 1) * 512],
                          hq_h[:, fc * 512:(fc + 1) * 512], -1.0, ytmp1[:, 0:512],
                          ALU.mult, ALU.add, incs=((PEV, 1),))
                tok_h = k.tok("PEV")
                for tsub in range(4):
                    for dh in range(2):
                        ps = ROT[(tsub * 2 + dh) % 4]
                        k.wait("tensor", PEV, PEV.n - 3)
                        for fc in range(8):
                            w2 = [tok_h] if (fc == 0 and dh == 0 and tsub == 0) else []
                            last = (fc == 7 and fq > 0)
                            hh_ = hq_h[:, fc * 512 + tsub * 128: fc * 512 + tsub * 128 + 128]
                            hl_ = hq_l[:, fc * 512 + tsub * 128: fc * 512 + tsub * 128 + 128]
                            wo_h = BIGAh[:, 16384 + fc * 1024 + dh * 512: 16384 + fc * 1024 + dh * 512 + 512]
                            wo_l = BIGAh[:, 24576 + fc * 1024 + dh * 512: 24576 + fc * 1024 + dh * 512 + 512]
                            k.mm(ps, hh_, wo_h, start=(fc == 0), stop=False, waits=tuple(w2))
                            k.mm(ps, hh_, wo_l, start=False, stop=False)
                            k.mm(ps, hl_, wo_h, start=False, stop=last,
                                 incs=((("m1mm", 1),) if last else ()))
                        if fq == 0:
                            k.mm(ps, onesrow[0:1, :], rowbuf[0:1, dh * 512:(dh + 1) * 512],
                                 start=False, stop=True,
                                 waits=(tok_bo,) if (gi == 0 and tsub == 0 and dh == 0) else (),
                                 incs=(("m1mm", 1),))
                        wv2 = [k.tok("m1mm"), (SM("m1w"), SM("m1w").n)]
                        k.cp("vector", ytmp1[:, dh * 512:(dh + 1) * 512], ps,
                             waits=tuple(wv2), incs=((PEV, 1),))
                    rbase = e * CPE + gi * 512 + tsub * 128
                    k.dma("gpsimd", ycomp_scr[rbase:rbase + 128, :], ytmp1,
                          sem=SM("m1w"), waits=(k.tok("PEV"),), accum=(fq > 0))
    f4a = BIGB[:, 0:1024]
    f4b = BIGB[:, 1024:2048]
    f4c = BIGB[:, 2048:3072]
    f4g1 = BIGB[:, 3072:4096]
    f4g2 = BIGB[:, 4096:5120]
    f4w = SMALL[:, 304:306]
    for ti in range(NT):
        w2 = [(SM("m1w"), SM("m1w").n), k.tok("m1mm"), k.tok("PEV")] if ti == 0 else \
             [(SM("yw"), 16 * ti)]
        k.dma("sync", f4b, x1_scr[ti * 128:(ti + 1) * 128, :], sem=SM("f4l"),
              waits=tuple(w2))
        k.dma("sync", f4c, s1_scr[ti * 128:(ti + 1) * 128, :], sem=SM("f4l"),
              waits=((SM("s1w"), 16 * NT) if ti == 0 else None,))
        k.dma("sync", f4w, wv_scr[ti * 128:(ti + 1) * 128, :], sem=SM("f4l"),
              waits=((SM("slw"), SM("slw").n) if ti == 0 else None,))
        k.dma("sync", IDXS[:, 4:6], slots_scr[ti * 128:(ti + 1) * 128, :], sem=SM("f4i"),
              waits=((SM("c4g"), 32 * ti) if ti >= 1 else None,))
        ti4 = k.tok("f4i")
        k.emit("gpsimd", lambda ee, o=f4g1, ix=IDXS[:, 4:5]:
               ee.indirect_dma_start(o, None, ycomp_scr[:, :], _IOA2(ix, 0)),
               waits=(ti4, ((SM("VM"), SM("VM").n) if ti >= 1 else None)),
               incs=(("c4g", 16),), drain=False)
        k.emit("gpsimd", lambda ee, o=f4g2, ix=IDXS[:, 5:6]:
               ee.indirect_dma_start(o, None, ycomp_scr[:, :], _IOA2(ix, 0)),
               waits=(), incs=(("c4g", 16),), drain=False)
        tg4 = k.tok("c4g")
        tl = k.tok("f4l")
        k.ts("vector", f4a, f4g1, f4w[:, 0:1], ALU.mult,
             waits=(tg4, tl) + (((SM("yw"), 16 * ti),) if ti >= 1 else ()))
        k.stt("vector", f4a, f4g2, f4w[:, 1:2], f4a, ALU.mult, ALU.add,
              incs=(("hy4", 1),))
        k.act(f4a, f4a, AF.Silu, waits=(k.tok("hy4"),), incs=(("SCM", 1),))
        k.tt("vector", f4a, f4a, f4c, ALU.mult, waits=((SM("SCM"), SM("SCM").n),))
        k.tt("vector", f4a, f4a, f4b, ALU.add, incs=(("VM", 1),))
        k.dma("gpsimd", yout[ti * 128:(ti + 1) * 128, :], f4a, sem=SM("yw"),
              waits=(k.tok("VM"),))
    k.wait("gpsimd", SM("yw"), SM("yw").n)
    k.build_block()
    return nc


def _make_inmaps(ins):
    f32 = np.float32
    cores = []
    Wq, Wk, Wv, Wd = (ins["L0_attn_W" + n].astype(f32) for n in "qkvd")
    bvecs = [ins["L0_attn_gamma"], ins["L0_attn_beta"], ins["L0_attn_bq"], ins["L0_attn_bk"],
             ins["L0_attn_bv"], ins["L0_attn_bd"], ins["L0_ffn_ln1_g"], ins["L0_ffn_ln1_b"],
             ins["L0_ffn_ln2_g"], ins["L0_ffn_ln2_b"], ins["L0_ffn_out_proj_b"]]
    cvec = np.stack([np.ascontiguousarray(np.broadcast_to(np.asarray(v, f32), (128, 1024)))
                     for v in bvecs])
    x = np.asarray(ins["L0_attn_input_tensor"], f32)
    tt_ = np.arange(128)
    tri = ((tt_[:, None] % 16 == tt_[None, :] % 16) &
           (tt_[:, None] // 16 < tt_[None, :] // 16)).astype(f32)
    selS = (tt_[:, None] % 16 == np.arange(16)[None, :]).astype(f32)
    cl_ = np.arange(128) // 16
    ll_ = np.arange(128) % 16
    for c in range(8):
        m1 = ((ll_[:, None] < np.arange(16)[None, :]) |
              ((ll_[:, None] == np.arange(16)[None, :]) & (cl_[:, None] < c))).astype(f32)
        m = {
            "x_in": np.ascontiguousarray(x[2 * c:2 * c + 2].reshape(4096, 1024)),
            "wq_sl": np.ascontiguousarray(Wq[128 * c:128 * (c + 1)]),
            "wk_sl": np.ascontiguousarray(Wk[128 * c:128 * (c + 1)]),
            "wv_sl": np.ascontiguousarray(Wv[128 * c:128 * (c + 1)]),
            "wd_sl": np.ascontiguousarray(Wd[128 * c:128 * (c + 1)]),
            "ip_sl": np.ascontiguousarray(np.asarray(ins["L0_ffn_in_proj_w"], f32)[384 * c:384 * (c + 1)]),
            "op_sl": np.ascontiguousarray(np.asarray(ins["L0_ffn_out_proj_w"], f32)[128 * c:128 * (c + 1)]),
            "w1c": np.ascontiguousarray(np.asarray(ins["L0_ffn_w1"], f32)[c]),
            "w2c": np.ascontiguousarray(np.asarray(ins["L0_ffn_w2"], f32)[c]),
            "b1c": np.ascontiguousarray(np.asarray(ins["L0_ffn_b1"], f32)[c].reshape(32, 128).T),
            "b2_r": np.ascontiguousarray(np.asarray(ins["L0_ffn_b2"], f32)[c:c + 1]),
            "wg0_in": np.ascontiguousarray(np.asarray(ins["L0_ffn_wg"], f32)),
            "ipb_bc": np.ascontiguousarray(np.broadcast_to(np.asarray(ins["L0_ffn_in_proj_b"], f32), (128, 3072))),
            "m1c_in": m1,
            "selS_in": selS,
            "selST_in": np.ascontiguousarray(selS.T),
            "tri_in": tri,
            "tri0_in": (tt_[:, None] < tt_[None, :]).astype(f32),
            "ident_in": np.eye(128, dtype=f32),
            "cvec": cvec,
            "wqkv1_sl": np.ascontiguousarray(np.asarray(ins["L1_attn_w_qkv"], f32)[128 * c:128 * (c + 1)]),
            "wout1_sl": np.ascontiguousarray(np.asarray(ins["L1_attn_w_out"], f32)[128 * c:128 * (c + 1)]),
            "winc": np.ascontiguousarray(np.asarray(ins["L1_ffn_W_in"], f32)[c]),
            "woutc": np.ascontiguousarray(np.asarray(ins["L1_ffn_W_out"], f32)[c]),
            "binc": np.ascontiguousarray(
                np.asarray(ins["L1_ffn_b_in"], f32).reshape(8, 32, 128).transpose(2, 0, 1).reshape(128, 256)),
            "bout_r": np.ascontiguousarray(np.asarray(ins["L1_ffn_b_out"], f32).reshape(1, 8192)),
            "wg1_in": np.ascontiguousarray(np.asarray(ins["L1_ffn_W_gate"], f32)),
        }
        cores.append(m)
    return cores


def kernel(**inputs):
    from concourse.bass_utils import run_bass_kernel_spmd
    nc = build_core("full")
    in_maps = _make_inmaps(inputs)
    res = run_bass_kernel_spmd(nc, in_maps, core_ids=list(range(8)))
    out = np.stack([res.results[c]["y_out"].reshape(2, 2048, 1024) for c in range(8)])
    return np.ascontiguousarray(out.reshape(16, 2048, 1024))
